# version 08
# baseline (speedup 1.0000x reference)
import numpy as np
from contextlib import ExitStack

import concourse.bass as bass
import concourse.mybir as mybir
from concourse.bass_utils import run_bass_kernel_spmd

F32 = mybir.dt.float32
BF16 = mybir.dt.bfloat16
I32 = mybir.dt.int32
U32 = mybir.dt.uint32
AF = mybir.ActivationFunctionType
ALU = mybir.AluOpType
AX = mybir.AxisListType
IOffs = bass.IndirectOffsetOnAxis

MAGIC = float(np.float32(1.5 * 2 ** 23))
T = 2048
D = 1024
NT = T // 128
KC = D // 128
NCORES = 8

ALL_BLOCKS = ["l0a", "l0f", "l1a", "l1f", "l2a", "l2f", "l3a", "l3f"]


class Prog:
    def __init__(self, nc, ctx):
        self.nc = nc
        self.ctx = ctx
        self.e = {"pe": nc.tensor, "dve": nc.vector, "act": nc.scalar,
                  "pool": nc.gpsimd, "sp": nc.sync}
        self.sem = {}
        self.cnt = {}
        self.waited = {}
        names = ["pe", "dve", "act", "pool", "sp", "cc"]
        for q in ["sp", "pool", "act"]:
            for i in range(6):
                names.append(f"dma_{q}{i}")
        for n in names:
            self.sem[n] = ctx.enter_context(nc.semaphore("s_" + n))
            self.cnt[n] = 0
        self.dma_rr = {"sp": 0, "pool": 0, "act": 0}

    def tok(self, name, instr, n=1):
        self.cnt[name] += n
        instr.then_inc(self.sem[name], n)
        return (name, self.cnt[name])

    def last(self, name):
        return (name, self.cnt[name])

    def dma(self, queue, out, in_, **kw):
        i = self.dma_rr[queue]
        self.dma_rr[queue] = (i + 1) % 6
        name = f"dma_{queue}{i}"
        self.wait(queue, (name, self.cnt[name]))
        ins = self.e[queue].dma_start(out, in_, **kw)
        return self.tok(name, ins, 16)

    def dma_indirect_add(self, out_ap, table_dram, idx_ap):
        i = self.dma_rr["pool"]
        self.dma_rr["pool"] = (i + 1) % 6
        name = f"dma_pool{i}"
        self.wait("pool", (name, self.cnt[name]))
        ins = self.nc.gpsimd.indirect_dma_start(
            out_ap, None, table_dram, IOffs(idx_ap, 0), compute_op=ALU.add)
        return self.tok(name, ins, 16)

    def dma_gather_rows(self, out_ap, table_dram, idx_ap, element_offset=0):
        i = self.dma_rr["pool"]
        self.dma_rr["pool"] = (i + 1) % 6
        name = f"dma_pool{i}"
        self.wait("pool", (name, self.cnt[name]))
        ins = self.nc.gpsimd.indirect_dma_start(
            out_ap, None, table_dram, IOffs(idx_ap, 0),
            element_offset=element_offset)
        return self.tok(name, ins, 16)

    def wait(self, eng, *toks):
        for t in toks:
            if t is None:
                continue
            sname, val = t
            if val <= 0:
                continue
            key = (eng, sname)
            if self.waited.get(key, 0) >= val:
                continue
            self.e[eng].wait_ge(self.sem[sname], val)
            self.waited[key] = val

    def barrier(self):
        toks = [self.last(n) for n in self.sem if n != "cc"]
        for eng in ["pe", "dve", "act", "pool", "sp"]:
            self.wait(eng, *toks)

    def sb(self, name, shape, dt):
        return self.ctx.enter_context(self.nc.sbuf_tensor(name, shape, dt))

    def ccol(self, value):
        if not hasattr(self, "_cpool"):
            self._cpool = self.sb("cpool", [128, 16], F32)[:]
            self._cvals = {}
        v = float(value)
        if v not in self._cvals:
            i = len(self._cvals)
            assert i < 16
            ap = self._cpool[:, i:i + 1]
            t = self.nc.gpsimd.memset(ap, v)
            tk = self.tok("pool", t)
            self._cvals[v] = (ap, tk)
        return self._cvals[v]

    def ps(self, name, shape, dt=F32):
        return self.ctx.enter_context(self.nc.psum_tensor(name, shape, dt))

    def bps(self, name, shape, dt=F32):
        return self.bctx.enter_context(self.nc.psum_tensor(name, shape, dt))[:]


def emit_common(p, R):
    nc = p.nc
    t = nc.vector.memset(R["ones_f"][:], 1.0)
    p.tok("dve", t)
    t = nc.vector.memset(R["ones_col"][:], 1.0)
    p.tok("dve", t)
    t = nc.gpsimd.memset(R["ones_bf"], 1.0)
    tk = p.tok("pool", t)
    t = nc.gpsimd.memset(R["ones_bfc"][:], 1.0)
    tk = p.tok("pool", t)
    p.wait("pool", tk)
    t = nc.gpsimd.affine_select(R["ident"][:], R["ones_bf"], [[-1, 128]],
                                ALU.is_equal, 0.0, base=0, channel_multiplier=1)
    p.tok("pool", t)
    t = nc.gpsimd.memset(R["ones_f128"][:], 1.0)
    tk = p.tok("pool", t)
    p.wait("pool", tk)
    t = nc.gpsimd.affine_select(R["ident_f"][:], R["ones_f128"][:], [[-1, 128]],
                                ALU.is_equal, 0.0, base=0, channel_multiplier=1)
    p.tok("pool", t)
    p.barrier()


def bcast_row_to_col(p, R, src_ap_1x1, dst_col_128x1, scale=None):
    nc = p.nc
    ps = R["ps_small"]
    p.wait("pe", p.last("dve"), p.last("act"), p.last("pool"))
    t = nc.tensor.matmul(ps[:, 0:1], R["ones_f"][:, 0:128], src_ap_1x1,
                         start=True, stop=True)
    tk = p.tok("pe", t)
    p.wait("dve", tk)
    t = nc.vector.tensor_copy(dst_col_128x1, ps[:, 0:1])
    if scale is not None:
        tk2 = p.tok("dve", t)
        p.wait("dve", tk2)
        t = nc.vector.tensor_scalar_mul(dst_col_128x1, dst_col_128x1, scale)
    return p.tok("dve", t)


def vec_to_bcast(p, R, vec_dram, dst_f32_128xN, n, tmp_row=None):
    nc = p.nc
    row = R["row_f"]
    ps = R["ps_row"]
    out_toks = []
    for off in range(0, n, 512):
        w = min(512, n - off)
        p.wait("sp", p.last("pe"), p.last("dve"))
        tk = p.dma("sp", row[0:1, 0:w], vec_dram[off:off + w].unsqueeze(0))
        p.wait("pe", tk, p.last("dve"))
        t = nc.tensor.matmul(ps[:, 0:w], R["ones_f"][:, 0:128],
                             row[0:1, 0:w], start=True, stop=True)
        ptk = p.tok("pe", t)
        p.wait("dve", ptk)
        t = nc.vector.tensor_copy(dst_f32_128xN[:, off:off + w], ps[:, 0:w])
        out_toks.append(p.tok("dve", t))
    return out_toks[-1]


def vec_to_cols(p, R, vec_dram, dst_f32_128xC, n):
    nc = p.nc
    c = n // 128
    row = R["row_f"]
    p.wait("sp", p.last("pe"), p.last("dve"))
    tk = p.dma("sp", row[0:c, 0:128], vec_dram.rearrange("(c q) -> c q", q=128))
    ps = R["ps_row"]
    p.wait("pe", tk, p.last("dve"))
    t = nc.tensor.transpose(ps[:, 0:c], row[0:c, 0:128], R["ident_f"][0:c, 0:c])
    ptk = p.tok("pe", t)
    p.wait("dve", ptk)
    t = nc.vector.tensor_copy(dst_f32_128xC[:, 0:c], ps[:, 0:c])
    return p.tok("dve", t)


def emit_ln(p, R, consume, eps=1e-5, src=None):
    nc = p.nc
    xr = src if src is not None else R["x_res"]
    st = R["stats"]
    s1 = st[:, 0:NT]
    s2 = st[:, NT:2 * NT]
    a_c = st[:, 2 * NT:3 * NT]
    b_c = st[:, 3 * NT:4 * NT]
    red_toks = []
    sq_toks = []
    for i in range(NT):
        t = nc.vector.reduce_sum(s1[:, i:i + 1], xr[:, i, :], axis=AX.X)
        red_toks.append(p.tok("dve", t))
        p.wait("act", p.last("dve") if i == 0 else None)
        sq_scr = R["tmpsq"] if (i % 2 == 0) else R["tmpf"]
        if i >= 2:
            p.wait("act", sq_toks[i - 2])
        t = nc.scalar.activation(sq_scr[:], xr[:, i, :], AF.Square,
                                 accum_out=s2[:, i:i + 1])
        sq_toks.append(p.tok("act", t))
    p.wait("dve", red_toks[-1], sq_toks[-1])
    t = nc.vector.tensor_scalar_mul(st[:, 0:2 * NT], st[:, 0:2 * NT], 1.0 / D)
    tk = p.tok("dve", t)
    p.wait("dve", tk)
    t = nc.vector.tensor_tensor(a_c, s1, s1, ALU.mult)
    tk = p.tok("dve", t)
    p.wait("dve", tk)
    t = nc.vector.tensor_tensor(a_c, s2, a_c, ALU.subtract)
    tk = p.tok("dve", t)
    eap, etk = p.ccol(eps)
    p.wait("act", tk, etk)
    t = nc.scalar.activation(a_c, a_c, AF.Sqrt, bias=eap)
    tk = p.tok("act", t)
    p.wait("dve", tk)
    t = nc.vector.reciprocal(a_c, a_c)
    tk = p.tok("dve", t)
    p.wait("dve", tk)
    t = nc.vector.scalar_tensor_tensor(b_c, s1, -1.0, a_c, ALU.mult, ALU.mult)
    tk = p.tok("dve", t)
    p.wait("dve", tk)
    guard = [None, None]
    for i in range(NT):
        dst = R["tmpf"] if (i % 2 == 0) else R["tmpsq"]
        p.wait("dve", guard[i % 2])
        t = nc.vector.tensor_scalar(dst[:], xr[:, i, :], a_c[:, i:i + 1],
                                    b_c[:, i:i + 1], ALU.mult, ALU.add)
        tk = p.tok("dve", t)
        guard[i % 2] = consume(i, dst, tk)
    return a_c, b_c


def emit_tile_to_fm(p, R, i, src_bf_tile, tok, dst_fm, evac_eng="dve"):
    nc = p.nc
    ps = R["ps_tr"]
    p.wait("pe", tok, R["_fm_evac_tok"][0])
    for c in range(KC):
        t = nc.tensor.transpose(ps[:, c * 128:(c + 1) * 128],
                                src_bf_tile[:, c * 128:(c + 1) * 128],
                                R["ident"][:])
    tk = p.tok("pe", t)
    p.wait(evac_eng, tk)
    if evac_eng == "dve":
        t = nc.vector.tensor_copy(dst_fm[:, :, i * 128:(i + 1) * 128],
                                  ps[:].rearrange("p (c q) -> p c q", q=128))
    else:
        t = nc.scalar.copy(dst_fm[:, :, i * 128:(i + 1) * 128],
                           ps[:].rearrange("p (c q) -> p c q", q=128))
    R["_fm_evac_tok"][0] = p.tok(evac_eng, t)
    return R["_fm_evac_tok"][0]


def emit_ln_to_xnT(p, R, eps=1e-5):
    nc = p.nc
    R["_fm_evac_tok"] = [None]
    xnT = R["xnT"]

    def consume(i, xn_tile_f32, tok):
        bft = R["tmpbf_a"] if i % 2 == 0 else R["tmpbf_b"]
        p.wait("act", tok)
        t = nc.scalar.copy(bft[:], xn_tile_f32[:])
        ctk = p.tok("act", t)
        emit_tile_to_fm(p, R, i, bft[:], ctk, xnT[:],
                        evac_eng="dve" if i % 2 == 0 else "act")
        return ctk

    emit_ln(p, R, consume, eps=eps)
    return R["_fm_evac_tok"][0]


def load_w_chunks(p, dram_ap, dst_bf):
    k = dram_ap.shape[0]
    return p.dma("pool", dst_bf, dram_ap.rearrange("(c p) n -> p c n", p=128))


def emit_l0f(p, R, W):
    nc = p.nc
    S = 1.0 / 127.0
    ln_tok = emit_ln_to_xnT(p, R)
    xnT = R["xnT"]
    p.barrier()

    NHC = 24
    GRP = 4
    h_fm = R["A"].bitcast(BF16).rearrange("p (j t) -> p j t", t=T)
    wg = R["Wr"].bitcast(BF16)
    wg_sl = [wg[:, 0:1024].rearrange("p (c n) -> p c n", n=128),
             wg[:, 1024:2048].rearrange("p (c n) -> p c n", n=128)]
    wu_sl = [wg[:, 2048:3072].rearrange("p (c n) -> p c n", n=128),
             wg[:, 3072:4096].rearrange("p (c n) -> p c n", n=128)]
    wd_sl = [wg[:, 4096 + j * 1024: 4096 + (j + 1) * 1024] for j in range(4)]

    ps_g = [R["ps_a"], p.bps("ps_g2_l0f", [128, 512], F32)]
    ps_u = [R["ps_b"], p.bps("ps_u2_l0f", [128, 512], F32)]
    ps_y = [p.bps("ps_y_l0f", [128, 512], F32),
            p.bps("ps_y2_l0f", [128, 512], F32)]
    silu_t = R["tmpbf_a"]

    wtoks = {}
    mm_after = {}

    slot_use = {}

    def load_hc(hc, parity):
        p.wait("pool", slot_use.get(("gu", parity)))
        wtoks[("g", parity)] = p.dma("pool", wg_sl[parity], W["l0_wg4"][hc])
        wtoks[("u", parity)] = p.dma("pool", wu_sl[parity], W["l0_wu4"][hc])

    def load_wd(hc, j):
        p.wait("pool", slot_use.get(("d", j)))
        wtoks[("d", j)] = p.dma("pool", wd_sl[j],
                                W["l0_wdT"][hc * 128:(hc + 1) * 128, :])

    load_hc(0, 0)
    xres_tok = [None] * NT
    for g in range(6):
        hcs = [g * GRP + j for j in range(GRP)]
        for j, hc in enumerate(hcs):
            par = hc % 2
            if hc + 1 < NHC:
                load_hc(hc + 1, (hc + 1) % 2)
            load_wd(hc, j)
            for ts4 in range(4):
                q = ts4 % 2
                tsl = slice(ts4 * 512, (ts4 + 1) * 512)
                p.wait("pe", wtoks[("g", par)], ln_tok,
                       mm_after.get(("silu_rd", q)))
                for c in range(KC):
                    t = nc.tensor.matmul(ps_g[q][:], wg_sl[par][:, c, :],
                                         xnT[:, c, tsl], start=(c == 0),
                                         stop=(c == KC - 1))
                g_tok = p.tok("pe", t)
                p.wait("pe", wtoks[("u", par)], mm_after.get(("h_rd", q)))
                for c in range(KC):
                    t = nc.tensor.matmul(ps_u[q][:], wu_sl[par][:, c, :],
                                         xnT[:, c, tsl], start=(c == 0),
                                         stop=(c == KC - 1))
                u_tok = p.tok("pe", t)
                p.wait("act", g_tok, mm_after.get(("h_rd", q)),
                       mm_after.get(("silu_wr", q)))
                t = nc.scalar.activation(silu_t[:, q * 512:q * 512 + 512],
                                         ps_g[q][:], AF.Silu, scale=S)
                s_tok = p.tok("act", t)
                mm_after[("silu_rd", q)] = s_tok
                p.wait("dve", u_tok, s_tok)
                t = nc.vector.scalar_tensor_tensor(
                    h_fm[:, j, tsl], ps_u[q][:], S * S,
                    silu_t[:, q * 512:q * 512 + 512], ALU.mult, ALU.mult)
                mm_after[("h", j)] = p.tok("dve", t)
                mm_after[("h_rd", q)] = mm_after[("h", j)]
                mm_after[("silu_wr", q)] = mm_after[("h", j)]
            slot_use[("gu", par)] = p.last("pe")
        for i in range(NT):
            for half in range(2):
                q = half
                p.wait("pe", *[wtoks[("d", j)] for j in range(GRP)],
                       *[mm_after[("h", j)] for j in range(GRP)],
                       mm_after.get(("y_rd", q)))
                for j in range(GRP):
                    t = nc.tensor.matmul(
                        ps_y[q][:], h_fm[:, j, i * 128:(i + 1) * 128],
                        wd_sl[j][:, half * 512:(half + 1) * 512],
                        start=(j == 0), stop=(j == GRP - 1))
                y_tok = p.tok("pe", t)
                p.wait("dve", y_tok, xres_tok[i])
                t = nc.vector.scalar_tensor_tensor(
                    R["x_res"][:, i, half * 512:(half + 1) * 512], ps_y[q][:],
                    1.0, R["x_res"][:, i, half * 512:(half + 1) * 512],
                    ALU.mult, ALU.add)
                xres_tok[i] = p.tok("dve", t)
                mm_after[("y_rd", q)] = xres_tok[i]
        for j in range(GRP):
            slot_use[("d", j)] = p.last("pe")
    p.barrier()


def emit_l3a(p, R, W):
    nc = p.nc
    ln_tok = emit_ln_to_xnT(p, R)
    xnT = R["xnT"]
    p.barrier()

    comb = R["A"].bitcast(BF16).rearrange("p (c t) -> p c t", t=T)
    tail_bf = R["smallbf"]
    Bf = R["B"].bitcast(F32)
    si_f = Bf[0:8, 0:T]
    g_f = Bf[0:8, T:2 * T]
    d1_f = Bf[0:8, 2 * T:3 * T]
    states_f = Bf[0:8, 3 * T:4 * T]

    wsl = R["Wr"].bitcast(BF16)
    win_sl = [wsl[:, 0:1024].rearrange("p (c n) -> p c n", n=128),
              wsl[:, 1024:2048].rearrange("p (c n) -> p c n", n=128)]
    wg_sl = wsl[:, 2048:2048 + 9 * 8].rearrange("p (c n) -> p c n", n=8)
    wout_sl = [wsl[:, 4096 + j * 1024: 4096 + (j + 1) * 1024] for j in range(4)]

    bcols = R["stats"][:, 32:41]
    vec_to_cols(p, R, W["l3_b_in"][0:1024], bcols, 1024)
    row = R["row_f"]
    p.wait("sp", p.last("pe"), p.last("dve"))
    tk = p.dma("sp", row[0:1, 256:264], W["l3_b_in"][1024:1032].unsqueeze(0))
    p.wait("pe", tk, p.last("dve"))
    t = nc.tensor.transpose(R["ps_row"][0:8, 0:1], row[0:1, 256:264], R["ident_f"][0:1, 0:1])
    ptk = p.tok("pe", t)
    p.wait("dve", ptk)
    t = nc.vector.tensor_copy(bcols[0:8, 8:9], R["ps_row"][0:8, 0:1])
    btk = p.tok("dve", t)
    bg_col = R["stats"][:, 41:42]
    tk = p.dma("sp", row[0:1, 272:280], W["l3_b_g"].unsqueeze(0))
    p.wait("pe", tk, btk)
    t = nc.tensor.transpose(R["ps_row"][0:8, 1:2], row[0:1, 272:280], R["ident_f"][0:1, 0:1])
    ptk = p.tok("pe", t)
    p.wait("dve", ptk)
    t = nc.vector.tensor_copy(bg_col[0:8, :], R["ps_row"][0:8, 1:2])
    bgtk = p.tok("dve", t)
    wgt1 = p.dma("pool", wg_sl, W["l3_Wg"])
    wgt2 = wgt1

    ps_c2 = [R["ps_a"], R["ps_b"]]
    ps_g2 = R["ps_b"]

    wtok = [None, None]
    slot_use = [None, None]
    wtok[0] = p.dma("pool", win_sl[0], W["l3_Win"][0])
    comb_toks = []
    cgq = {}
    for oc in range(9):
        par = oc % 2
        m = 128 if oc < 8 else 8
        if oc + 1 < 9:
            nm = 128 if oc + 1 < 8 else 8
            p.wait("pool", slot_use[(oc + 1) % 2])
            wtok[(oc + 1) % 2] = p.dma(
                "pool", win_sl[(oc + 1) % 2][:, :, 0:nm],
                W["l3_Win"][oc + 1][:, :, 0:nm])
        for ts4 in range(4):
            q = ts4 % 2
            tsl = slice(ts4 * 512, (ts4 + 1) * 512)
            p.wait("pe", wtok[par], ln_tok, cgq.get(q))
            for c in range(KC):
                t = nc.tensor.matmul(ps_c2[q][0:m, :], win_sl[par][:, c, 0:m],
                                     xnT[:, c, tsl], start=(c == 0),
                                     stop=(c == KC - 1))
            mtok = p.tok("pe", t)
            slot_use[par] = mtok
            p.wait("dve", mtok, btk)
            if oc < 8:
                t = nc.vector.tensor_scalar_add(comb[:, oc, tsl], ps_c2[q][0:m, :],
                                                bcols[:, oc:oc + 1])
                cgq[q] = p.tok("dve", t)
                comb_toks.append(cgq[q])
            else:
                t = nc.vector.tensor_scalar_add(si_f[:, tsl], ps_c2[q][0:8, :],
                                                bcols[0:8, 8:9])
                ttk = p.tok("dve", t)
                cgq[q] = ttk
                comb_toks.append(ttk)
                p.wait("act", ttk)
                t = nc.scalar.copy(tail_bf[0:8, tsl], si_f[:, tsl])
                comb_toks.append(p.tok("act", t))
    for ts4 in range(4):
        tsl = slice(ts4 * 512, (ts4 + 1) * 512)
        p.wait("pe", wgt1, wgt2, comb_toks[-1], p.last("act"))
        for c in range(8):
            t = nc.tensor.matmul(ps_g2[0:8, 0:512], wg_sl[:, c, :],
                                 comb[:, c, tsl], start=(c == 0), stop=False)
        t = nc.tensor.matmul(ps_g2[0:8, 0:512], wg_sl[0:8, 8, :],
                             tail_bf[0:8, tsl], start=False, stop=True)
        mtok = p.tok("pe", t)
        p.wait("act", mtok, bgtk)
        t = nc.scalar.activation(g_f[:, tsl], ps_g2[0:8, 0:512], AF.Sigmoid,
                                 bias=bg_col[0:8, :])
        p.tok("act", t)
    gtk = p.last("act")
    p.wait("dve", gtk)
    t = nc.vector.tensor_tensor(d1_f, g_f, si_f, ALU.mult)
    tk = p.tok("dve", t)
    p.wait("dve", tk)
    t = nc.vector.tensor_tensor(d1_f, si_f, d1_f, ALU.subtract)
    tk = p.tok("dve", t)
    p.wait("dve", tk)
    t = nc.vector.tensor_tensor_scan(states_f, g_f, d1_f, 0.0, ALU.mult, ALU.add)
    tk = p.tok("dve", t)
    p.wait("dve", tk)
    t = nc.vector.tensor_copy(tail_bf[0:8, 0:T], states_f)
    states_tok = p.tok("dve", t)

    bout_tok = vec_to_bcast(p, R, W["l3_bout"], R["bb1"], 1024)
    xres_tok = [None] * NT
    for i in range(NT):
        p.wait("dve", bout_tok)
        t = nc.vector.tensor_tensor(R["x_res"][:, i, :], R["x_res"][:, i, :],
                                    R["bb1"][:], ALU.add)
        xres_tok[i] = p.tok("dve", t)
    guard = p.last("dve")
    for grp, chunks in enumerate([[0, 1, 2, 3], [4, 5, 6, 7], [8]]):
        wts = []
        for j, cch in enumerate(chunks):
            m = 128 if cch < 8 else 8
            wts.append(p.dma(
                "pool", wout_sl[j][0:m, :],
                W["l3_Wout"][cch * 128:cch * 128 + m, :]))
        ogq = {}
        for i in range(NT):
            til = slice(i * 128, (i + 1) * 128)
            for half in range(2):
                q = half
                osl = slice(half * 512, (half + 1) * 512)
                p.wait("pe", *wts, states_tok, comb_toks[-1], guard,
                       ogq.get(q))
                for j, cch in enumerate(chunks):
                    if cch < 8:
                        t = nc.tensor.matmul(ps_c2[q][:, 0:512], comb[:, cch, til],
                                             wout_sl[j][:, osl],
                                             start=(j == 0),
                                             stop=(j == len(chunks) - 1))
                    else:
                        t = nc.tensor.matmul(ps_c2[q][:, 0:512], tail_bf[0:8, til],
                                             wout_sl[j][0:8, osl],
                                             start=True, stop=True)
                mtok = p.tok("pe", t)
                p.wait("dve", mtok, xres_tok[i])
                t = nc.vector.scalar_tensor_tensor(
                    R["x_res"][:, i, osl], ps_c2[q][:, 0:512], 1.0,
                    R["x_res"][:, i, osl], ALU.mult, ALU.add)
                xres_tok[i] = p.tok("dve", t)
                ogq[q] = xres_tok[i]
                guard = xres_tok[i]
        p.barrier()
    p.barrier()


def emit_l3f(p, R, W, cc):
    nc = p.nc
    QB = 128.0
    BOUND = float(np.float32(QB - 1e-5))

    g_tok = vec_to_bcast(p, R, W["l3f_gamma"], R["bb1"], 1024)
    b_tok = vec_to_bcast(p, R, W["l3f_beta"], R["bb2"], 1024)

    xn2 = R["A"].bitcast(BF16).rearrange("p (i d) -> p i d", d=D)
    gmax = R["stats"][:, 48:49]
    t = nc.vector.memset(gmax, 0.0)
    gm_tok = p.tok("dve", t)

    st = R["stats"]
    s1 = st[:, 0:1]
    s2 = st[:, 1:2]
    ac = st[:, 2:3]
    bc = st[:, 3:4]
    tile_toks = [None] * NT
    for i in range(NT):
        xt = R["x_res"][:, i, :]
        p.wait("dve", p.last("act") if i else None, gm_tok)
        t = nc.vector.reduce_sum(s1, xt, axis=AX.X)
        r1 = p.tok("dve", t)
        p.wait("act", p.last("dve"))
        t = nc.scalar.activation(R["tmpsq"][:], xt, AF.Square, accum_out=s2)
        q1 = p.tok("act", t)

        def stats_chain(eps):
            p.wait("dve", p.last("act"))
            t = nc.vector.tensor_scalar_mul(st[:, 0:2], st[:, 0:2], 1.0 / D)
            tk = p.tok("dve", t)
            p.wait("dve", tk)
            t = nc.vector.tensor_tensor(ac, s1, s1, ALU.mult)
            tk = p.tok("dve", t)
            p.wait("dve", tk)
            t = nc.vector.tensor_tensor(ac, s2, ac, ALU.subtract)
            tk = p.tok("dve", t)
            eap, etk = p.ccol(eps)
            p.wait("act", tk, etk)
            t = nc.scalar.activation(ac, ac, AF.Sqrt, bias=eap)
            tk = p.tok("act", t)
            p.wait("dve", tk)
            t = nc.vector.reciprocal(ac, ac)
            tk = p.tok("dve", t)
            p.wait("dve", tk)
            t = nc.vector.scalar_tensor_tensor(bc, s1, -1.0, ac, ALU.mult, ALU.mult)
            return p.tok("dve", t)

        tk = stats_chain(1e-5)
        p.wait("dve", tk, r1, q1)
        t = nc.vector.tensor_scalar(R["tmpf"][:], xt, ac[:, 0:1], bc[:, 0:1],
                                    ALU.mult, ALU.add)
        ln1 = p.tok("dve", t)
        p.wait("dve", ln1)
        t = nc.vector.reduce_sum(s1, R["tmpf"][:], axis=AX.X)
        r2 = p.tok("dve", t)
        p.wait("act", ln1, p.last("dve"))
        t = nc.scalar.activation(R["tmpsq"][:], R["tmpf"][:], AF.Square,
                                 accum_out=s2)
        q2 = p.tok("act", t)
        tk = stats_chain(1e-3)
        p.wait("dve", tk, r2, q2)
        t = nc.vector.tensor_scalar(R["tmpf"][:], R["tmpf"][:], ac[:, 0:1],
                                    bc[:, 0:1], ALU.mult, ALU.add)
        u_tok = p.tok("dve", t)
        p.wait("dve", u_tok, g_tok, b_tok)
        t = nc.vector.tensor_tensor(R["tmpf"][:], R["tmpf"][:], R["bb1"][:], ALU.mult)
        tk = p.tok("dve", t)
        p.wait("dve", tk)
        t = nc.vector.tensor_tensor(xn2[:, i, :], R["tmpf"][:], R["bb2"][:], ALU.add)
        tile_toks[i] = p.tok("dve", t)
        p.wait("dve", tile_toks[i], gm_tok)
        t = nc.vector.reduce_max(s1, xn2[:, i, :], axis=AX.X,
                                 apply_absolute_value=True)
        tk = p.tok("dve", t)
        p.wait("dve", tk)
        t = nc.vector.tensor_tensor(gmax, gmax, s1, ALU.max)
        gm_tok = p.tok("dve", t)

    p.wait("pe", gm_tok)
    t = nc.tensor.transpose(R["ps_row"][0:1, 0:128],
                            gmax.rearrange("p one -> p (one)").unsqueeze(1)
                            if False else gmax, R["ident_f"][:])
    ptk = p.tok("pe", t)
    p.wait("dve", ptk)
    t = nc.vector.reduce_max(R["stats"][0:1, 49:50], R["ps_row"][0:1, 0:128],
                             axis=AX.X)
    tk = p.tok("dve", t)
    p.wait("sp", tk)
    cctk = p.dma("sp", cc["in"], R["stats"][0:1, 49:50])
    p.wait("pool", cctk)
    t = nc.gpsimd.collective_compute("AllReduce", ALU.max,
                                     [list(range(NCORES))],
                                     [cc["in"].opt()], [cc["out"].opt()])
    t.then_inc(p.sem["cc"], 1)
    p.wait("sp", ("cc", 1))
    cctk = p.dma("sp", R["stats"][0:1, 49:50], cc["out"])
    p.wait("dve", cctk)
    t = nc.vector.reciprocal(st[0:1, 50:51], st[0:1, 49:50])
    tk = p.tok("dve", t)
    p.wait("dve", tk)
    t = nc.vector.tensor_scalar_mul(st[0:1, 50:51], st[0:1, 50:51], QB)
    tk = p.tok("dve", t)
    s_tok = bcast_row_to_col(p, R, st[0:1, 50:51], st[:, 51:52])

    wreg = R["Wr"].bitcast(BF16)
    wq = wreg[:, 0:8192].rearrange("p (c n) -> p c n", n=1024)
    wcol = st[:, 52:60]
    for c in range(KC):
        p.wait("sp", p.last("dve"), p.last("act"))
        tk = p.dma("sp", R["tmpf"][:], W["l3f_w"][c * 128:(c + 1) * 128, :])
        p.wait("dve", tk)
        t = nc.vector.reduce_sum(wcol[:, c:c + 1], R["tmpf"][:], axis=AX.X,
                                 apply_absolute_value=True)
        p.tok("dve", t)
    tk = p.last("dve")
    p.wait("dve", tk)
    t = nc.vector.reduce_sum(wcol[:, 0:1], wcol[:, 0:8], axis=AX.X)
    tk = p.tok("dve", t)
    p.wait("pe", tk)
    t = nc.tensor.matmul(R["ps_row"][0:1, 0:1], R["ones_col"][:],
                         wcol[:, 0:1], start=True, stop=True)
    ptk = p.tok("pe", t)
    p.wait("dve", ptk)
    t = nc.vector.tensor_scalar(st[0:1, 60:61], R["ps_row"][0:1, 0:1],
                                1.0 / (D * D), 1e-5, ALU.mult, ALU.add)
    tk = p.tok("dve", t)
    p.wait("dve", tk)
    t = nc.vector.reciprocal(st[0:1, 61:62], st[0:1, 60:61])
    tk = p.tok("dve", t)
    rw_tok = bcast_row_to_col(p, R, st[0:1, 61:62], st[:, 62:63])
    p.wait("dve", rw_tok)
    t = nc.vector.tensor_tensor(st[0:1, 63:64], st[0:1, 49:50], st[0:1, 60:61],
                                ALU.mult)
    tk = p.tok("dve", t)
    p.wait("dve", tk)
    t = nc.vector.tensor_scalar_add(st[0:1, 64:65], st[0:1, 60:61], -1e-5)
    tk = p.tok("dve", t)
    p.wait("dve", tk)
    t = nc.vector.tensor_tensor(st[0:1, 63:64], st[0:1, 49:50], st[0:1, 64:65],
                                ALU.mult)
    tk = p.tok("dve", t)
    p.wait("dve", tk)
    t = nc.vector.tensor_scalar_mul(st[0:1, 63:64], st[0:1, 63:64], 1.0 / QB)
    tk = p.tok("dve", t)
    sc2_tok = bcast_row_to_col(p, R, st[0:1, 63:64], st[:, 65:66])

    for c in range(KC):
        p.wait("sp", p.last("dve"))
        tk = p.dma("sp", R["tmpf"][:], W["l3f_w"][c * 128:(c + 1) * 128, :])
        p.wait("dve", tk, rw_tok)
        t = nc.vector.tensor_scalar(R["tmpf"][:], R["tmpf"][:], st[:, 62:63],
                                    MAGIC, ALU.mult, ALU.add)
        ptk = p.tok("dve", t)
        p.wait("dve", ptk)
        t = nc.vector.tensor_scalar(R["tmpf"][:], R["tmpf"][:], MAGIC, 1.0,
                                    ALU.subtract, ALU.min)
        ptk = p.tok("dve", t)
        p.wait("dve", ptk)
        t = nc.vector.tensor_scalar(wq[:, c, :], R["tmpf"][:], -1.0, None,
                                    ALU.max)
        p.tok("dve", t)
    wq_tok = p.last("dve")

    xqT = R["B"].bitcast(BF16).rearrange("p (c t) -> p c t", t=T)
    R["_fm_evac_tok"] = [None]
    xq_tok = [None] * NT
    prev_p2 = p.last("dve")
    for i in range(NT):
        p.wait("dve", tile_toks[i], s_tok, prev_p2)
        t = nc.vector.tensor_scalar(R["tmpsq"][:], xn2[:, i, :], st[:, 51:52],
                                    BOUND, ALU.mult, ALU.min)
        tk = p.tok("dve", t)
        p.wait("dve", tk)
        t = nc.vector.tensor_scalar(xn2[:, i, :], R["tmpsq"][:], -BOUND, None,
                                    ALU.max)
        tk = p.tok("dve", t)
        prev_p2 = tk
        p.wait("act", tk)
        emit_tile_to_fm(p, R, i, xn2[:, i, :], tk, xqT[:],
                        evac_eng="dve" if i % 2 == 0 else "act")
        xq_tok[i] = R["_fm_evac_tok"][0]

    xres_tok = [None] * NT
    ps_yy = [R["ps_a"], R["ps_b"]]
    ygq = {}
    for i in range(NT):
        for half in range(2):
            q = half
            osl = slice(half * 512, (half + 1) * 512)
            p.wait("pe", wq_tok, xq_tok[-1], sc2_tok, ygq.get(q))
            for c in range(KC):
                t = nc.tensor.matmul(ps_yy[q][:], xqT[:, c, i * 128:(i + 1) * 128],
                                     wq[:, c, osl], start=(c == 0),
                                     stop=(c == KC - 1))
            mtok = p.tok("pe", t)
            p.wait("dve", mtok, xres_tok[i])
            t = nc.vector.scalar_tensor_tensor(
                R["x_res"][:, i, osl], ps_yy[q][:], st[:, 65:66],
                R["x_res"][:, i, osl], ALU.mult, ALU.add)
            xres_tok[i] = p.tok("dve", t)
            ygq[q] = xres_tok[i]
    p.barrier()


def emit_l1f(p, R, W):
    nc = p.nc
    ln_tok = emit_ln_to_xnT(p, R)
    xnT = R["xnT"]
    p.barrier()

    st = R["stats"]
    idxf_all = st[:, 0:NT]
    ps_l = R["ps_row"]
    wgv = R["Wr"].bitcast(BF16)
    wg_sb = R["kv_bf"][:, 0, :].rearrange("p (c n) -> p c n", n=8)
    w1_sb = wgv[:, 0:8192].rearrange("p (c n) -> p c n", n=1024)
    w2_bufs = [R["B"].bitcast(BF16)[:, 0:8192].rearrange("p (c n) -> p c n", n=1024),
               R["B"].bitcast(BF16)[:, 8192:16384].rearrange("p (c n) -> p c n", n=1024)]
    h_fm = R["A"].bitcast(BF16).rearrange("p (c t) -> p c t", t=T)

    wg_tok = p.dma("pool", wg_sb, W["l1_Wg"])
    bg_tok = vec_to_bcast(p, R, W["l1_bg"], R["bb1"][:, 0:8], 8)

    logit_sb = st[:, 40:48]
    top8 = st[:, 48:56]
    idx_u = R["idx_u"]
    idx_col = R["idx_c"]
    b2g_tok = [None] * NT
    for i in range(NT):
        til = slice(i * 128, (i + 1) * 128)
        p.wait("pe", wg_tok, ln_tok, bg_tok, p.last("dve") if i else None)
        for c in range(KC):
            t = nc.tensor.matmul(ps_l[:, 0:8], xnT[:, c, til], wg_sb[:, c, :],
                                 start=(c == 0), stop=(c == KC - 1))
        mtok = p.tok("pe", t)
        p.wait("dve", mtok, bg_tok)
        t = nc.vector.tensor_add(logit_sb, ps_l[:, 0:8], R["bb1"][:, 0:8])
        tk = p.tok("dve", t)
        p.wait("dve", tk)
        t = nc.vector.max(top8, logit_sb)
        tk = p.tok("dve", t)
        p.wait("dve", tk)
        t = nc.vector.max_index(idx_u, top8, logit_sb)
        tk = p.tok("dve", t)
        p.wait("dve", tk)
        t = nc.vector.tensor_copy(idxf_all[:, i:i + 1], idx_u[:, 0:1])
        p.tok("dve", t)
        t = nc.vector.tensor_copy(idx_col[:, i:i + 1], idx_u[:, 0:1])
        tk = p.tok("dve", t)
        p.wait("pool", tk)
        b2g_tok[i] = p.dma_indirect_add(R["x_res"][:, i, :], W["l1_b2"],
                                        idx_col[:, i:i + 1])
    idx_tok = p.last("dve")

    bcols = st[:, 16:24]
    mask_all = st[:, 24:40]
    ps_h = [R["ps_a"], p.bps("ps_h2_l1f", [128, 512], F32)]
    ps_y = [R["ps_b"], p.bps("ps_y2_l1f", [128, 512], F32)]
    w_use = [None, None]
    hguard = None
    sguard = None
    aguard = None
    w1_tok = p.dma("pool", w1_sb, W["l1_W1"][0])
    w2_toks = [p.dma("pool", w2_bufs[0], W["l1_W2"][0]), None]
    for e in range(8):
        w2_sb = w2_bufs[e % 2]
        w2_tok = w2_toks[e % 2]
        if e + 1 < 8:
            p.wait("pool", w_use[1])
            w2_toks[(e + 1) % 2] = p.dma("pool", w2_bufs[(e + 1) % 2],
                                         W["l1_W2"][e + 1])
        b1_tok = vec_to_cols(p, R, W["l1_b1"][e], bcols, 1024)
        p.wait("dve", idx_tok)
        t = nc.vector.tensor_single_scalar(mask_all, idxf_all, float(e),
                                           ALU.is_equal)
        m_tok = p.tok("dve", t)
        agq = [aguard, aguard]
        for ic in range(KC):
            for ts4 in range(4):
                q = ts4 % 2
                tsl = slice(ts4 * 512, (ts4 + 1) * 512)
                p.wait("pe", w1_tok, agq[q], hguard)
                for c in range(KC):
                    t = nc.tensor.matmul(
                        ps_h[q][:], w1_sb[:, c, ic * 128:(ic + 1) * 128],
                        xnT[:, c, tsl], start=(c == 0), stop=(c == KC - 1))
                mtok = p.tok("pe", t)
                p.wait("act", mtok, b1_tok, hguard)
                t = nc.scalar.activation(h_fm[:, ic, tsl], ps_h[q][:], AF.Gelu,
                                         bias=bcols[:, ic:ic + 1])
                agq[q] = p.tok("act", t)
        aguard = p.last("act")
        if e + 1 < 8:
            p.wait("pool", p.last("pe"))
            w1_tok = p.dma("pool", w1_sb, W["l1_W1"][e + 1])
        sgq = [sguard, sguard]
        for i in range(NT):
            til = slice(i * 128, (i + 1) * 128)
            for half in range(2):
                q = half
                osl = slice(half * 512, (half + 1) * 512)
                p.wait("pe", w2_tok, aguard, sgq[q])
                for ic in range(KC):
                    t = nc.tensor.matmul(ps_y[q][:], h_fm[:, ic, til],
                                         w2_sb[:, ic, osl], start=(ic == 0),
                                         stop=(ic == KC - 1))
                mtok = p.tok("pe", t)
                p.wait("dve", mtok, m_tok, b2g_tok[i])
                t = nc.vector.scalar_tensor_tensor(
                    R["x_res"][:, i, osl], ps_y[q][:], mask_all[:, i:i + 1],
                    R["x_res"][:, i, osl], ALU.mult, ALU.add)
                sgq[q] = p.tok("dve", t)
        sguard = p.last("dve")
        hguard = p.last("pe")
        w_use = [hguard, hguard]
    p.barrier()


L1A_STAGE = 5

def emit_l1a(p, R, W):
    nc = p.nc
    ln_tok = emit_ln_to_xnT(p, R)
    xnT = R["xnT"]
    p.barrier()
    st = R["stats"]

    q_fm = R["A"].bitcast(BF16).rearrange("p (c t) -> p c t", t=T)
    k_tm = R["B"].bitcast(BF16).rearrange("p (i d) -> p i d", d=D)
    wsb = R["Wr"].bitcast(BF16)[:, 0:8192].rearrange("p (c n) -> p c n", n=1024)
    ps_c = p.bps("ps_c_l1a", [128, 512], F32)
    ps_d = p.bps("ps_d_l1a", [128, 512], F32)
    ps_p = [ps_c, ps_d]

    bq_c = st[:, 0:8]
    bq1_c = st[:, 8:16]
    btk = vec_to_cols(p, R, W["l1_bq"], bq_c, 1024)
    p.wait("dve", btk)
    t = nc.vector.tensor_scalar_add(bq1_c, bq_c, 1.0)
    b1tk = p.tok("dve", t)
    wtok = p.dma("pool", wsb, W["l1_wqT"])
    dgq = {}
    dguard = None
    for oc in range(KC):
        for ts4 in range(4):
            q = ts4 % 2
            tsl = slice(ts4 * 512, (ts4 + 1) * 512)
            p.wait("pe", wtok, ln_tok, dgq.get(q))
            for c in range(KC):
                t = nc.tensor.matmul(ps_p[q][:], wsb[:, c, oc * 128:(oc + 1) * 128],
                                     xnT[:, c, tsl], start=(c == 0),
                                     stop=(c == KC - 1))
            mtok = p.tok("pe", t)
            p.wait("act", mtok, btk, dgq.get(q))
            t = nc.scalar.activation(R["tmpf"][:, q * 512:q * 512 + 512],
                                     ps_p[q][:], AF.Exp,
                                     bias=bq_c[:, oc:oc + 1])
            eguard = p.tok("act", t)
            p.wait("dve", eguard, b1tk)
            t = nc.vector.scalar_tensor_tensor(
                q_fm[:, oc, tsl], ps_p[q][:], bq1_c[:, oc:oc + 1],
                R["tmpf"][:, q * 512:q * 512 + 512], ALU.add, ALU.min)
            dguard = p.tok("dve", t)
            dgq[q] = dguard
    q_tok = dguard
    if L1A_STAGE <= 1:
        p.barrier()
        return

    bk_tok = vec_to_bcast(p, R, W["l1_bk"], R["bb1"], 1024)
    p.wait("dve", bk_tok)
    t = nc.vector.tensor_scalar_add(R["bb2"][:], R["bb1"][:], 1.0)
    bk1_tok = p.tok("dve", t)
    p.wait("pool", p.last("pe"))
    wtok = p.dma("pool", wsb, W["l1_wkT"])
    kgq = {}
    for i in range(NT):
        til = slice(i * 128, (i + 1) * 128)
        for half in range(2):
            q = half
            osl = slice(half * 512, (half + 1) * 512)
            p.wait("pe", wtok, dguard if i == 0 and half == 0 else None,
                   kgq.get(q))
            for c in range(KC):
                t = nc.tensor.matmul(ps_p[q][:], xnT[:, c, til],
                                     wsb[:, c, osl], start=(c == 0),
                                     stop=(c == KC - 1))
            mtok = p.tok("pe", t)
            p.wait("dve", mtok, bk1_tok, kgq.get(q))
            t = nc.vector.tensor_tensor(R["tmpf"][:, q * 512:q * 512 + 512],
                                        ps_p[q][:], R["bb1"][:, osl], ALU.add)
            dtk = p.tok("dve", t)
            p.wait("act", dtk)
            t = nc.scalar.activation(R["tmpsq"][:, q * 512:q * 512 + 512],
                                     R["tmpf"][:, q * 512:q * 512 + 512], AF.Exp)
            aguard = p.tok("act", t)
            p.wait("dve", aguard)
            t = nc.vector.scalar_tensor_tensor(
                k_tm[:, i, osl], R["tmpf"][:, q * 512:q * 512 + 512], 1.0,
                R["tmpsq"][:, q * 512:q * 512 + 512], ALU.add, ALU.min)
            dguard = p.tok("dve", t)
            kgq[q] = dguard
    k_tok = dguard
    if L1A_STAGE <= 2:
        p.barrier()
        return

    bv_tok = vec_to_bcast(p, R, W["l1_bv"], R["bb1"], 1024)
    p.wait("pool", p.last("pe"))
    wtok = p.dma("pool", wsb, W["l1_wvT"])
    ps_kv = [R["ps_a"], R["ps_b"]]
    ps_ks = R["ps_row"]
    vguard = [None, None]
    vgq = {}
    for i in range(NT):
        til = slice(i * 128, (i + 1) * 128)
        vb = R["tmpbf_a"] if i % 2 == 0 else R["tmpbf_b"]
        for half in range(2):
            q = half
            osl = slice(half * 512, (half + 1) * 512)
            p.wait("pe", wtok, k_tok, q_tok,
                   dguard if i == 0 and half == 0 else None,
                   vguard[i % 2], bv_tok, vgq.get(q))
            for c in range(KC):
                t = nc.tensor.matmul(ps_p[q][:], xnT[:, c, til],
                                     wsb[:, c, osl], start=(c == 0),
                                     stop=(c == KC - 1))
            mtok = p.tok("pe", t)
            p.wait("dve", mtok, bv_tok, vguard[i % 2])
            t = nc.vector.tensor_tensor(vb[:, osl], ps_p[q][:], R["bb1"][:, osl],
                                        ALU.add)
            dguard = p.tok("dve", t)
            vgq[q] = dguard
        p.wait("pe", dguard)
        for h in range(16):
            hb = ps_kv[h // 8]
            hsl = slice((h % 8) * 64, (h % 8) * 64 + 64)
            t = nc.tensor.matmul(hb[0:64, hsl], k_tm[:, i, h * 64:(h + 1) * 64],
                                 vb[:, h * 64:(h + 1) * 64],
                                 start=(i == 0 and h % 8 == 0),
                                 stop=(i == NT - 1 and h % 8 == 7))
            t = nc.tensor.matmul(ps_ks[0:64, h:h + 1],
                                 k_tm[:, i, h * 64:(h + 1) * 64],
                                 R["ones_bfc"][:],
                                 start=(i == 0 and h == 0),
                                 stop=(i == NT - 1 and h == 15))
        vguard[i % 2] = p.tok("pe", t)
    kv_tok = p.last("pe")

    kv_bf = R["kv_bf"]
    ks_bf = R["ks_bf"]
    p.wait("dve", kv_tok)
    for lohi in range(2):
        base = lohi * 64
        t = nc.vector.tensor_copy(
            kv_bf[base:base + 64, 0:8, :],
            ps_kv[0][0:64, :].rearrange("p (h d) -> p h d", d=64))
        p.tok("dve", t)
        t = nc.vector.tensor_copy(
            kv_bf[base:base + 64, 8:16, :],
            ps_kv[1][0:64, :].rearrange("p (h d) -> p h d", d=64))
        p.tok("dve", t)
        t = nc.vector.tensor_copy(ks_bf[base:base + 64, :], ps_ks[0:64, 0:16])
        p.tok("dve", t)
    kvbf_tok = p.last("dve")
    if L1A_STAGE <= 3:
        p.barrier()
        return

    attn_fm = R["B"].bitcast(BF16).rearrange("p (c t) -> p c t", t=T)
    ps_qk = [R["ps_a"], R["ps_b"]]
    ps_n = R["ps_row"]
    R["_fm_evac_tok"] = [None]
    nguard = None
    atoks = []
    for i in range(NT):
        til = slice(i * 128, (i + 1) * 128)
        p.wait("pe", kvbf_tok, nguard)
        for h in range(16):
            hb = (h % 2) * 64
            ch = h // 2
            bank = h % 2
            slot = h // 2
            t = nc.tensor.matmul(
                ps_qk[bank][:, slot * 64:slot * 64 + 64],
                q_fm[hb:hb + 64, ch, til], kv_bf[hb:hb + 64, h, :],
                start=(h < 2), stop=(h >= 14))
            nb = ps_n if bank == 0 else ps_c
            t = nc.tensor.matmul(
                nb[:, slot: slot + 1],
                q_fm[hb:hb + 64, ch, til],
                ks_bf[hb:hb + 64, h:h + 1], start=(h < 2), stop=(h >= 14))
        mtok = p.tok("pe", t)
        p.wait("dve", mtok)
        t = nc.vector.tensor_scalar_max(st[:, 32:40], ps_n[:, 0:8], 1e-6)
        p.tok("dve", t)
        t = nc.vector.tensor_scalar_max(st[:, 40:48], ps_c[:, 0:8], 1e-6)
        tk = p.tok("dve", t)
        p.wait("dve", tk)
        t = nc.vector.reciprocal(st[:, 32:48], st[:, 32:48])
        tk = p.tok("dve", t)
        p.wait("dve", tk)
        vb = R["tmpbf_a"] if i % 2 == 0 else R["tmpbf_b"]
        rb = st[:, 32:48].unsqueeze(2).to_broadcast([128, 16, 64])
        hview0 = ps_qk[0][:].rearrange("p (h d) -> p h d", d=64)
        hview1 = ps_qk[1][:].rearrange("p (h d) -> p h d", d=64)
        if L1A_STAGE >= 4.2:
            vbv = vb[:].rearrange("p (a b d) -> p a b d", b=2, d=64)
            t = nc.vector.tensor_tensor(vbv[:, :, 0, :], hview0,
                                        rb[:, 0:8], ALU.mult)
            p.tok("dve", t)
            t = nc.vector.tensor_tensor(vbv[:, :, 1, :], hview1,
                                        rb[:, 8:16], ALU.mult)
            nguard = p.tok("dve", t)
        else:
            nguard = p.last("dve")
        if L1A_STAGE >= 4.3:
            emit_tile_to_fm(p, R, i, vb[:], nguard, attn_fm[:],
                            evac_eng="dve" if i % 2 == 0 else "act")
            atoks.append(R["_fm_evac_tok"][0])

    if L1A_STAGE <= 4.9:
        p.barrier()
        return
    bo_tok = vec_to_bcast(p, R, W["l1_bo"], R["bb1"], 1024)
    p.wait("pool", p.last("pe"))
    wtok = p.dma("pool", wsb, W["l1_woT"])
    xres_tok = [None] * NT
    for i in range(NT):
        p.wait("dve", bo_tok)
        t = nc.vector.tensor_tensor(R["x_res"][:, i, :], R["x_res"][:, i, :],
                                    R["bb1"][:], ALU.add)
        xres_tok[i] = p.tok("dve", t)
    ygq = {}
    yguard = p.last("dve")
    for i in range(NT):
        til = slice(i * 128, (i + 1) * 128)
        for half in range(2):
            q = half
            osl = slice(half * 512, (half + 1) * 512)
            p.wait("pe", wtok, atoks[-1],
                   yguard if i == 0 and half == 0 else None, ygq.get(q))
            for c in range(KC):
                t = nc.tensor.matmul(ps_p[q][:], attn_fm[:, c, til],
                                     wsb[:, c, osl], start=(c == 0),
                                     stop=(c == KC - 1))
            mtok = p.tok("pe", t)
            p.wait("dve", mtok, xres_tok[i])
            t = nc.vector.scalar_tensor_tensor(
                R["x_res"][:, i, osl], ps_p[q][:], 1.0,
                R["x_res"][:, i, osl], ALU.mult, ALU.add)
            xres_tok[i] = p.tok("dve", t)
            ygq[q] = xres_tok[i]
    p.barrier()


def emit_wquant_to_dram(p, R, W, src_name, dst_dram, nrows, st_beta_col):
    nc = p.nc
    st = R["stats"]
    src = W[src_name]
    ncols = src.shape[1]
    nch = nrows // 128
    nhalf = ncols // 2048 if ncols >= 2048 else 1
    pw = min(2048, ncols)
    wrf = R["Wr"].bitcast(F32)
    slots = [wrf[:, 0:pw], wrf[:, 2048:2048 + pw]]
    outb = [R["A"].bitcast(BF16)[:, 0:pw], R["A"].bitcast(BF16)[:, 2048:2048 + pw]]
    wcol = st[:, 64:65]
    acc = st[:, 65:66]
    t = nc.vector.memset(acc, 0.0)
    atk = p.tok("dve", t)
    rd_guard = [None, None]
    pieces = [(c, hh) for c in range(nch) for hh in range(nhalf)]
    for k, (c, hh) in enumerate(pieces):
        sl = k % 2
        p.wait("sp", rd_guard[sl])
        tk = p.dma("sp", slots[sl], src[c * 128:(c + 1) * 128,
                                        hh * 2048:hh * 2048 + pw])
        p.wait("dve", tk, atk)
        t = nc.vector.reduce_sum(wcol, slots[sl], axis=AX.X,
                                 apply_absolute_value=True)
        tk = p.tok("dve", t)
        p.wait("dve", tk)
        t = nc.vector.tensor_tensor(acc, acc, wcol, ALU.add)
        atk = p.tok("dve", t)
        rd_guard[sl] = atk
    p.wait("pe", atk)
    t = nc.tensor.matmul(R["ps_small"][0:1, 0:1], R["ones_col"][:], acc,
                         start=True, stop=True)
    ptk = p.tok("pe", t)
    p.wait("dve", ptk)
    t = nc.vector.tensor_scalar(st[0:1, st_beta_col:st_beta_col + 1],
                                R["ps_small"][0:1, 0:1],
                                1.0 / (nrows * ncols), 1e-5, ALU.mult, ALU.max)
    tk = p.tok("dve", t)
    p.wait("dve", tk)
    t = nc.vector.reciprocal(st[0:1, 66:67], st[0:1, st_beta_col:st_beta_col + 1])
    tk = p.tok("dve", t)
    sc_tok = bcast_row_to_col(p, R, st[0:1, 66:67], st[:, 67:68])
    g1 = [None, None]
    g3 = [None, None]
    wq_guard = None
    for k, (c, hh) in enumerate(pieces):
        sl = k % 2
        e1, e2, e3 = ("dve", "dve", "dve")
        E = {"pool": nc.gpsimd, "dve": nc.vector}
        p.wait("sp", g3[sl], sc_tok)
        tk = p.dma("sp", slots[sl], src[c * 128:(c + 1) * 128,
                                        hh * 2048:hh * 2048 + pw])
        p.wait(e1, tk, sc_tok, g3[sl])
        t = E[e1].tensor_scalar(slots[sl], slots[sl], st[:, 67:68], MAGIC,
                                ALU.mult, ALU.add)
        t1 = p.tok(e1, t)
        p.wait(e2, t1)
        t = E[e2].tensor_scalar(slots[sl], slots[sl], MAGIC, 1.0,
                                ALU.subtract, ALU.min)
        t2 = p.tok(e2, t)
        p.wait(e3, t2, g1[sl])
        t = E[e3].tensor_scalar(outb[sl], slots[sl], -1.0, None, ALU.max)
        t3 = p.tok(e3, t)
        g3[sl] = t3
        p.wait("sp", t3)
        wq_guard = p.dma("sp", dst_dram[c * 128:(c + 1) * 128,
                                        hh * 2048:hh * 2048 + pw], outb[sl])
        g1[sl] = wq_guard
    return wq_guard


def emit_l2f(p, R, W, scr):
    nc = p.nc
    st = R["stats"]
    h_dram = scr["h_dram"]
    wqg_dram = scr["wqg_dram"]
    wqu_dram = scr["wqu_dram"]
    wqd_dram = scr["wqd_dram"]
    ps_c = p.bps("ps_c_l2f", [128, 512], F32)

    kT = R["B"].bitcast(BF16).rearrange("p (c t) -> p c t", t=T)
    rs_all = st[:, 16:32]
    s_all = st[:, 0:16]
    R["_fm_evac_tok"] = [None]

    def consume(i, xn, tok):
        scr2 = R["bb2"]
        kbf = R["tmpbf_a"] if i % 2 == 0 else R["tmpbf_b"]
        p.wait("dve", tok)
        t = nc.vector.reduce_max(st[:, 32:33], xn[:], axis=AX.X,
                                 apply_absolute_value=True)
        tk = p.tok("dve", t)
        p.wait("dve", tk)
        t = nc.vector.tensor_scalar(rs_all[:, i:i + 1], st[:, 32:33], 1e-5,
                                    1.0 / 127.0, ALU.max, ALU.mult)
        tk = p.tok("dve", t)
        p.wait("dve", tk)
        t = nc.vector.reciprocal(s_all[:, i:i + 1], rs_all[:, i:i + 1])
        tk = p.tok("dve", t)
        p.wait("dve", tk)
        t = nc.vector.tensor_scalar(scr2[:], xn[:], s_all[:, i:i + 1], MAGIC,
                                    ALU.mult, ALU.add)
        tk = p.tok("dve", t)
        p.wait("dve", tk)
        t = nc.vector.tensor_scalar(kbf[:], scr2[:], MAGIC, None, ALU.subtract)
        tk = p.tok("dve", t)
        p.wait("act", tk)
        emit_tile_to_fm(p, R, i, kbf[:], tk, kT[:],
                        evac_eng="dve" if i % 2 == 0 else "act")
        return R["_fm_evac_tok"][0]

    emit_ln(p, R, consume)
    ln_tok = R["_fm_evac_tok"][0]
    p.barrier()

    emit_wquant_to_dram(p, R, W, "l2_wgT", wqg_dram, 1024, 70)
    p.barrier()
    emit_wquant_to_dram(p, R, W, "l2_wuT", wqu_dram, 1024, 71)
    p.barrier()
    emit_wquant_to_dram(p, R, W, "l2_wdT", wqd_dram, 4096, 72)
    p.barrier()

    bgc = st[:, 33:34]
    buc = st[:, 34:35]
    t1 = bcast_row_to_col(p, R, st[0:1, 70:71], bgc)
    t2 = bcast_row_to_col(p, R, st[0:1, 71:72], buc)
    bg_rs = st[:, 36:52]
    bu_rs = st[:, 52:68]
    p.wait("dve", t1, t2)
    t = nc.vector.tensor_scalar(bg_rs, rs_all, bgc, None, ALU.mult)
    p.tok("dve", t)
    t = nc.vector.tensor_scalar(bu_rs, rs_all, buc, None, ALU.mult)
    sc_tok = p.tok("dve", t)
    amax2 = st[:, 0:16]
    t = nc.vector.memset(amax2, 0.0)
    am_tok = p.tok("dve", t)

    wsl = R["Wr"].bitcast(BF16)
    wq_g = wsl[:, 0:4096].rearrange("p (c n) -> p c n", n=512)
    wq_u = wsl[:, 4096:8192].rearrange("p (c n) -> p c n", n=512)
    ps_g = [R["ps_a"], p.bps("ps_g2_l2f", [128, 512], F32)]
    ps_u = [R["ps_b"], p.bps("ps_u2_l2f", [128, 512], F32)]
    agq = {}
    dgq = {}
    dguard = None
    aguard = None
    hguard = {}
    for hs in range(8):
        p.wait("sp", p.last("pe"))
        g_tok = p.dma("sp", wq_g, wqg_dram[:, hs * 512:(hs + 1) * 512]
                      .rearrange("(c p) n -> p c n", p=128))
        u_tok = p.dma("sp", wq_u, wqu_dram[:, hs * 512:(hs + 1) * 512]
                      .rearrange("(c p) n -> p c n", p=128))
        for i in range(NT):
            q = i % 2
            til = slice(i * 128, (i + 1) * 128)
            p.wait("pe", g_tok, ln_tok, agq.get(q))
            for c in range(KC):
                t = nc.tensor.matmul(ps_g[q][:], kT[:, c, til], wq_g[:, c, :],
                                     start=(c == 0), stop=(c == KC - 1))
            gm = p.tok("pe", t)
            p.wait("pe", u_tok, dgq.get(q))
            for c in range(KC):
                t = nc.tensor.matmul(ps_u[q][:], kT[:, c, til], wq_u[:, c, :],
                                     start=(c == 0), stop=(c == KC - 1))
            um = p.tok("pe", t)
            p.wait("act", gm, sc_tok, dgq.get(q))
            t = nc.scalar.activation(R["tmpf"][:, q * 512:q * 512 + 512],
                                     ps_g[q][:], AF.Silu,
                                     scale=bg_rs[:, i:i + 1])
            agq[q] = p.tok("act", t)
            hb = R["tmpbf_a"] if i % 2 == 0 else R["tmpbf_b"]
            p.wait("dve", um, agq[q], am_tok, hguard.get(i % 2))
            t = nc.vector.scalar_tensor_tensor(hb[:, 0:512], ps_u[q][:],
                                               bu_rs[:, i:i + 1],
                                               R["tmpf"][:, q * 512:q * 512 + 512],
                                               ALU.mult, ALU.mult)
            dguard = p.tok("dve", t)
            dgq[q] = dguard
            p.wait("dve", dguard)
            t = nc.vector.reduce_max(st[:, 32:33], hb[:, 0:512], axis=AX.X,
                                     apply_absolute_value=True)
            tk = p.tok("dve", t)
            p.wait("dve", tk)
            t = nc.vector.tensor_tensor(amax2[:, i:i + 1], amax2[:, i:i + 1],
                                        st[:, 32:33], ALU.max)
            am_tok = p.tok("dve", t)
            p.wait("sp", am_tok)
            hguard[i % 2] = p.dma(
                "sp", h_dram.rearrange("(i p) n -> i p n", p=128)[i][:, hs * 512:(hs + 1) * 512],
                hb[:, 0:512])
    p.barrier()

    bdc = st[:, 33:34]
    t1 = bcast_row_to_col(p, R, st[0:1, 72:73], bdc)
    s2_all = st[:, 16:32]
    bd_rs2 = st[:, 36:52]
    p.wait("dve", t1)
    t = nc.vector.tensor_scalar(s2_all, amax2, 1e-5, 1.0 / 127.0, ALU.max, ALU.mult)
    tk = p.tok("dve", t)
    p.wait("dve", tk)
    t = nc.vector.tensor_scalar(bd_rs2, s2_all, bdc, None, ALU.mult)
    sc2_tok = p.tok("dve", t)
    p.wait("dve", sc2_tok)
    t = nc.vector.reciprocal(s2_all, s2_all)
    s2_tok = p.tok("dve", t)

    k2T = R["A"].bitcast(BF16)[:, 0:8192].rearrange("p (c t) -> p c t", t=T)
    wq_d = wsl[:, 0:4096].rearrange("p (c n) -> p c n", n=1024)
    ps_cc = [ps_c, ps_g[1]]
    xres_tok = [None] * NT
    ygq = {}
    yguard = None
    trguard = None
    for hg in range(8):
        p.wait("sp", p.last("pe"))
        d_tok = p.dma("sp", wq_d, wqd_dram[hg * 512:(hg + 1) * 512, :]
                      .rearrange("(c p) n -> p c n", p=128))
        for i in range(NT):
            hb = R["tmpbf_a"] if i % 2 == 0 else R["tmpbf_b"]
            p.wait("sp", p.last("dve"), p.last("pe"))
            htk = p.dma("sp", hb[:, 0:512],
                        h_dram.rearrange("(i p) n -> i p n", p=128)[i][:, hg * 512:(hg + 1) * 512])
            p.wait("dve", htk, s2_tok, trguard)
            t = nc.vector.tensor_scalar(R["tmpf"][:, 0:512], hb[:, 0:512],
                                        s2_all[:, i:i + 1], MAGIC,
                                        ALU.mult, ALU.add)
            tk = p.tok("dve", t)
            p.wait("dve", tk)
            t = nc.vector.tensor_scalar(hb[:, 512:1024], R["tmpf"][:, 0:512],
                                        MAGIC, None, ALU.subtract)
            tk = p.tok("dve", t)
            p.wait("pe", tk, trguard)
            for j in range(4):
                t = nc.tensor.transpose(
                    R["ps_tr"][:, j * 128:(j + 1) * 128],
                    hb[:, 512 + j * 128:512 + (j + 1) * 128], R["ident"][:])
            ptk = p.tok("pe", t)
            eng = "dve" if i % 2 == 0 else "act"
            p.wait(eng, ptk)
            if eng == "dve":
                t = nc.vector.tensor_copy(
                    k2T[:, 0:4, i * 128:(i + 1) * 128],
                    R["ps_tr"][:, 0:512].rearrange("p (c q) -> p c q", q=128))
            else:
                t = nc.scalar.copy(
                    k2T[:, 0:4, i * 128:(i + 1) * 128],
                    R["ps_tr"][:, 0:512].rearrange("p (c q) -> p c q", q=128))
            trguard = p.tok(eng, t)
        for i in range(NT):
            til = slice(i * 128, (i + 1) * 128)
            for half in range(2):
                q = half
                osl = slice(half * 512, (half + 1) * 512)
                p.wait("pe", d_tok, trguard, ygq.get(q))
                for j in range(4):
                    t = nc.tensor.matmul(ps_cc[q][:], k2T[:, j, til],
                                         wq_d[:, j, osl], start=(j == 0),
                                         stop=(j == 3))
                mtok = p.tok("pe", t)
                p.wait("dve", mtok, xres_tok[i], sc2_tok)
                t = nc.vector.scalar_tensor_tensor(
                    R["x_res"][:, i, osl], ps_cc[q][:], bd_rs2[:, i:i + 1],
                    R["x_res"][:, i, osl], ALU.mult, ALU.add)
                xres_tok[i] = p.tok("dve", t)
                ygq[q] = xres_tok[i]
                yguard = xres_tok[i]
    p.barrier()


def emit_l0a(p, R, W, scr):
    nc = p.nc
    st = R["stats"]
    q_dram = scr["q_dram"]
    H = 16
    aux = R["aux"]
    auxf = aux.bitcast(F32)
    qselT = aux[:, 0:160]
    attnT = aux[:, 160:320]
    ao_sb = aux[:, 320:1344].rearrange("p (h d) -> p h d", d=64)
    t32i = auxf[0:32, 672:704]
    t32o = auxf[0:32, 704:736]
    wrf = R["Wr"].bitcast(F32)
    qs_byh = wrf[0:16, 0:2048]
    qs_scr = wrf[0:16, 2048:4096]
    iota2k = wrf[0:10, 0:2048]
    qs_all = R["bb2"][:, 0:256].rearrange("p (i h) -> p i h", h=16)

    ln_tok = emit_ln_to_xnT(p, R)
    xnT = R["xnT"]
    p.barrier()

    wsb = R["Wr"].bitcast(BF16)[:, 0:8192].rearrange("p (c n) -> p c n", n=1024)
    ps_a = R["ps_a"]
    ps_b = R["ps_b"]

    bq_tok = vec_to_bcast(p, R, W["l0_bq"], R["bb1"], 1024)
    p.wait("pool", p.last("pe"))
    wq_tok = p.dma("pool", wsb, W["l0_wqT"])
    eps9, eps9_tok = p.ccol(1e-9)
    dguard = None
    pguard = None
    qdma = [None] * NT
    for i in range(NT):
        til = slice(i * 128, (i + 1) * 128)
        ps = [ps_a, ps_b]
        p.wait("pe", wq_tok, ln_tok, dguard, p.last("act"))
        for half in range(2):
            osl = slice(half * 512, (half + 1) * 512)
            for c in range(KC):
                t = nc.tensor.matmul(ps[half][:], xnT[:, c, til],
                                     wsb[:, c, osl], start=(c == 0),
                                     stop=(c == KC - 1))
        mtok = p.tok("pe", t)
        p.wait("dve", mtok, bq_tok, pguard, dguard)
        t = nc.vector.tensor_tensor(R["tmpf"][:, 0:512], ps_a[:],
                                    R["bb1"][:, 0:512], ALU.add)
        p.tok("dve", t)
        t = nc.vector.tensor_tensor(R["tmpf"][:, 512:1024], ps_b[:],
                                    R["bb1"][:, 512:1024], ALU.add)
        qtk = p.tok("dve", t)
        p.wait("pool", qtk, qdma[i - 1] if i else None)
        t = nc.gpsimd.tensor_copy(R["tmpbf_a"][:], R["tmpf"][:])
        ctk = p.tok("pool", t)
        pguard = ctk
        p.wait("sp", ctk)
        qdma[i] = p.dma("sp", q_dram.rearrange("(i p) d -> i p d", p=128)[i],
                        R["tmpbf_a"][:])
        q3 = R["tmpf"][:].rearrange("p (h d) -> p h d", d=64)
        p.wait("dve", qtk)
        t = nc.vector.reduce_sum(st[:, 0:16], q3, axis=AX.X)
        s1tk = p.tok("dve", t)
        p.wait("act", qtk)
        t = nc.scalar.activation(R["tmpsq"][:], R["tmpf"][:], AF.Square)
        sqt = p.tok("act", t)
        p.wait("dve", sqt)
        t = nc.vector.reduce_sum(st[:, 16:32],
                                 R["tmpsq"][:].rearrange("p (h d) -> p h d", d=64),
                                 axis=AX.X)
        s2tk = p.tok("dve", t)
        p.wait("dve", s2tk)
        t = nc.vector.reduce_max(st[:, 32:48], q3, axis=AX.X)
        mtk = p.tok("dve", t)
        p.wait("dve", mtk)
        mb = st[:, 32:48].unsqueeze(2).to_broadcast([128, 16, 64])
        t = nc.vector.tensor_tensor(
            R["tmpsq"][:].rearrange("p (h d) -> p h d", d=64), q3, mb,
            ALU.subtract)
        dtk = p.tok("dve", t)
        p.wait("act", dtk, pguard)
        t = nc.scalar.activation(R["tmpf"][:], R["tmpsq"][:], AF.Exp)
        etk = p.tok("act", t)
        p.wait("dve", etk)
        t = nc.vector.reduce_sum(st[:, 48:64],
                                 R["tmpf"][:].rearrange("p (h d) -> p h d", d=64),
                                 axis=AX.X)
        stk = p.tok("dve", t)
        p.wait("dve", stk)
        t = nc.vector.reciprocal(st[:, 64:80], st[:, 48:64])
        rtk = p.tok("dve", t)
        p.wait("dve", rtk)
        rb = st[:, 64:80].unsqueeze(2).to_broadcast([128, 16, 64])
        t = nc.vector.tensor_tensor(
            R["tmpf"][:].rearrange("p (h d) -> p h d", d=64),
            R["tmpf"][:].rearrange("p (h d) -> p h d", d=64), rb, ALU.mult)
        ptk2 = p.tok("dve", t)
        p.wait("act", ptk2, eps9_tok)
        t = nc.scalar.activation(R["tmpsq"][:], R["tmpf"][:], AF.Ln,
                                 bias=eps9)
        ltk = p.tok("act", t)
        p.wait("dve", ltk)
        t = nc.vector.tensor_tensor(R["tmpsq"][:], R["tmpf"][:], R["tmpsq"][:],
                                    ALU.mult)
        ptk3 = p.tok("dve", t)
        p.wait("dve", ptk3)
        t = nc.vector.reduce_sum(st[:, 48:64],
                                 R["tmpsq"][:].rearrange("p (h d) -> p h d", d=64),
                                 axis=AX.X, negate=True)
        enttk = p.tok("dve", t)
        p.wait("act", s2tk)
        t = nc.scalar.activation(st[:, 64:80], st[:, 16:32], AF.Sqrt)
        l2tk = p.tok("act", t)
        p.wait("dve", s1tk, l2tk, enttk)
        t = nc.vector.tensor_tensor(qs_all[:, i, :], st[:, 0:16], st[:, 0:16],
                                    ALU.mult)
        tk = p.tok("dve", t)
        p.wait("dve", tk)
        t = nc.vector.tensor_scalar(qs_all[:, i, :], qs_all[:, i, :],
                                    -1.0 / 64.0, None, ALU.mult)
        tk = p.tok("dve", t)
        p.wait("dve", tk)
        t = nc.vector.tensor_tensor(qs_all[:, i, :], st[:, 16:32],
                                    qs_all[:, i, :], ALU.add)
        tk = p.tok("dve", t)
        p.wait("dve", tk)
        t = nc.vector.tensor_scalar(qs_all[:, i, :], qs_all[:, i, :],
                                    0.2 / 63.0, None, ALU.mult)
        tk = p.tok("dve", t)
        p.wait("dve", tk)
        t = nc.vector.scalar_tensor_tensor(st[:, 64:80], st[:, 64:80], 0.5,
                                           qs_all[:, i, :], ALU.mult, ALU.add)
        tk = p.tok("dve", t)
        p.wait("dve", tk)
        t = nc.vector.scalar_tensor_tensor(qs_all[:, i, :], st[:, 48:64], 0.3,
                                           st[:, 64:80], ALU.mult, ALU.add)
        dguard = p.tok("dve", t)

    K_fm = R["A"].bitcast(BF16).rearrange("p (c t) -> p c t", t=T)
    V_tm = R["B"].bitcast(BF16).rearrange("p (i d) -> p i d", d=D)
    bkc = st[:, 0:8]
    bk_tok = vec_to_cols(p, R, W["l0_bk"], bkc, 1024)
    p.wait("pool", p.last("pe"), dguard)
    wk_tok = p.dma("pool", wsb, W["l0_wkT"])
    kguard = None
    for oc in range(KC):
        for ts4 in range(4):
            tsl = slice(ts4 * 512, (ts4 + 1) * 512)
            p.wait("pe", wk_tok, kguard)
            for c in range(KC):
                t = nc.tensor.matmul(ps_a[:], wsb[:, c, oc * 128:(oc + 1) * 128],
                                     xnT[:, c, tsl], start=(c == 0),
                                     stop=(c == KC - 1))
            mtok = p.tok("pe", t)
            p.wait("dve", mtok, bk_tok)
            t = nc.vector.tensor_scalar_add(K_fm[:, oc, tsl], ps_a[:],
                                            bkc[:, oc:oc + 1])
            kguard = p.tok("dve", t)
    bv_tok = vec_to_bcast(p, R, W["l0_bv"], R["bb1"], 1024)
    p.wait("pool", p.last("pe"), kguard)
    wv_tok = p.dma("pool", wsb, W["l0_wvT"])
    vguard = None
    for i in range(NT):
        til = slice(i * 128, (i + 1) * 128)
        for half in range(2):
            osl = slice(half * 512, (half + 1) * 512)
            p.wait("pe", wv_tok, vguard, bv_tok)
            for c in range(KC):
                t = nc.tensor.matmul(ps_a[:], xnT[:, c, til],
                                     wsb[:, c, osl], start=(c == 0),
                                     stop=(c == KC - 1))
            mtok = p.tok("pe", t)
            p.wait("dve", mtok, bv_tok)
            t = nc.vector.tensor_tensor(V_tm[:, i, osl], ps_a[:],
                                        R["bb1"][:, osl], ALU.add)
            vguard = p.tok("dve", t)

    wr_guard = p.last("pe")
    tguard = None
    for i in range(NT):
        p.wait("pe", dguard, wr_guard, tguard)
        t = nc.tensor.transpose(R["ps_row"][0:16, 0:128], qs_all[:, i, :],
                                R["ident_f"][:])
        ptk = p.tok("pe", t)
        p.wait("dve", ptk)
        t = nc.vector.tensor_copy(qs_byh[:, i * 128:(i + 1) * 128],
                                  R["ps_row"][0:16, 0:128])
        tguard = p.tok("dve", t)
    top8 = st[0:16, 16:24]
    next8 = st[0:16, 24:32]
    p.wait("dve", tguard)
    t = nc.vector.max(top8, qs_byh)
    tk = p.tok("dve", t)
    p.wait("dve", tk)
    t = nc.vector.max_index(R["idx_u"][0:16, :], top8, qs_byh)
    tk = p.tok("dve", t)
    p.wait("dve", tk)
    t = nc.vector.tensor_copy(st[0:16, 32:40], R["idx_u"][0:16, :])
    tk1 = p.tok("dve", t)
    p.wait("dve", tk1)
    t = nc.vector.match_replace(qs_scr, top8, qs_byh, -1e30)
    tk = p.tok("dve", t)
    p.wait("dve", tk)
    t = nc.vector.max(next8, qs_scr)
    tk = p.tok("dve", t)
    p.wait("dve", tk)
    t = nc.vector.max_index(R["idx_u"][0:16, :], next8, qs_byh)
    tk = p.tok("dve", t)
    p.wait("dve", tk)
    t = nc.vector.tensor_copy(st[0:16, 40:42], R["idx_u"][0:16, 0:2])
    idxf_tok = p.tok("dve", t)
    t = nc.vector.reduce_sum(st[0:16, 44:45], qs_byh, axis=AX.X)
    s1tk = p.tok("dve", t)
    p.wait("act", tguard, idxf_tok)
    t = nc.scalar.activation(qs_scr, qs_byh, AF.Square,
                             accum_out=st[0:16, 45:46])
    s2tk = p.tok("act", t)
    p.wait("dve", s1tk, s2tk)
    t = nc.vector.tensor_scalar_mul(st[0:16, 46:47], st[0:16, 44:45], 1.0 / T)
    tk = p.tok("dve", t)
    p.wait("dve", tk)
    t = nc.vector.tensor_tensor(st[0:16, 47:48], st[0:16, 44:45],
                                st[0:16, 46:47], ALU.mult)
    tk = p.tok("dve", t)
    p.wait("dve", tk)
    t = nc.vector.tensor_tensor(st[0:16, 47:48], st[0:16, 45:46],
                                st[0:16, 47:48], ALU.subtract)
    tk = p.tok("dve", t)
    p.wait("dve", tk)
    t = nc.vector.tensor_scalar_mul(st[0:16, 47:48], st[0:16, 47:48],
                                    1.0 / (T - 1))
    tk = p.tok("dve", t)
    p.wait("act", tk)
    t = nc.scalar.activation(st[0:16, 47:48], st[0:16, 47:48], AF.Sqrt)
    tk = p.tok("act", t)
    p.wait("dve", tk)
    t = nc.vector.tensor_scalar_add(st[0:16, 46:47], st[0:16, 46:47], 1e-6)
    tk = p.tok("dve", t)
    p.wait("dve", tk)
    t = nc.vector.reciprocal(st[0:16, 46:47], st[0:16, 46:47])
    tk = p.tok("dve", t)
    p.wait("dve", tk)
    t = nc.vector.tensor_tensor(st[0:16, 47:48], st[0:16, 47:48],
                                st[0:16, 46:47], ALU.mult)
    tk = p.tok("dve", t)
    p.wait("dve", tk)
    t = nc.vector.tensor_scalar(st[0:16, 47:48], st[0:16, 47:48], 10.0, MAGIC,
                                ALU.mult, ALU.add)
    tk = p.tok("dve", t)
    p.wait("dve", tk)
    t = nc.vector.tensor_scalar(st[0:16, 47:48], st[0:16, 47:48], MAGIC, 10.0,
                                ALU.subtract, ALU.min)
    tk = p.tok("dve", t)
    p.wait("dve", tk)
    t = nc.vector.tensor_scalar_max(st[0:16, 47:48], st[0:16, 47:48], 3.0)
    ftok = p.tok("dve", t)
    p.wait("pe", ftok)
    t = nc.tensor.matmul(R["ps_small"][0:16, 0:1], R["ones_f"][0:1, 0:16],
                         st[0:1, 47:48], start=True, stop=True)
    ptk = p.tok("pe", t)
    p.wait("dve", ptk)
    t = nc.vector.tensor_copy(st[0:16, 46:47], R["ps_small"][0:16, 0:1])
    utok = p.tok("dve", t)
    t = nc.gpsimd.iota(st[0:16, 48:58], [[1, 10]], channel_multiplier=0,
                       allow_small_or_imprecise_dtypes=True)
    iotok = p.tok("pool", t)
    p.wait("dve", utok, iotok)
    t = nc.vector.tensor_single_scalar(st[0:16, 58:68], st[0:16, 48:58],
                                       st[0:16, 46:47], ALU.is_lt)
    mtok = p.tok("dve", t)
    p.wait("dve", mtok, idxf_tok, tk1)
    t = nc.vector.tensor_scalar_add(st[0:16, 68:78], st[0:16, 32:42], -2048.0)
    tk = p.tok("dve", t)
    p.wait("dve", tk)
    t = nc.vector.tensor_tensor(st[0:16, 68:78], st[0:16, 68:78],
                                st[0:16, 58:68], ALU.mult)
    tk = p.tok("dve", t)
    p.wait("dve", tk)
    t = nc.vector.tensor_scalar_add(st[0:16, 68:78], st[0:16, 68:78], 2048.0)
    sidx_tok = p.tok("dve", t)
    p.wait("dve", sidx_tok)
    t = nc.vector.memset(t32i, 2048.0)
    tk = p.tok("dve", t)
    p.wait("dve", tk)
    t = nc.vector.tensor_copy(t32i[0:16, 0:10], st[0:16, 68:78])
    tk = p.tok("dve", t)
    p.wait("dve", tk)
    t = nc.vector.transpose(t32o, t32i)
    tk = p.tok("dve", t)
    p.wait("dve", tk)
    t = nc.vector.tensor_copy(st[0:10, 48:64], t32o[0:10, 0:16])
    sidxT_tok = p.tok("dve", t)
    p.wait("dve", sidxT_tok)
    t = nc.vector.tensor_copy(t32i[0:16, 0:10], st[0:16, 32:42])
    tk = p.tok("dve", t)
    p.wait("dve", tk)
    t = nc.vector.transpose(t32o, t32i)
    tk = p.tok("dve", t)
    p.wait("dve", tk)
    t = nc.vector.tensor_copy(R["idx_c"][0:10, :], t32o[0:10, 0:16])
    gidx_tok = p.tok("dve", t)

    qsel_tok = None
    tr_tok = None
    for h in range(H):
        hb = (h % 2) * 64
        p.wait("pool", gidx_tok, tr_tok, *(qdma if h == 0 else []))
        qg = p.dma_gather_rows(R["smallbf"][0:10, 1024:2048], q_dram,
                               R["idx_c"][0:10, h:h + 1])
        p.wait("pe", qg, qsel_tok)
        t = nc.tensor.transpose(R["ps_tr"][0:64, 0:10],
                                R["smallbf"][0:10, 1024 + h * 64:1024 + (h + 1) * 64],
                                R["ident"][0:10, 0:10])
        tr_tok = p.tok("pe", t)
        p.wait("dve", tr_tok)
        t = nc.vector.tensor_copy(aux[hb:hb + 64, h * 10:(h + 1) * 10],
                                  R["ps_tr"][0:64, 0:10])
        qsel_tok = p.tok("dve", t)

    ps_sc = p.bps("ps_sc_l0a", [16, 2048], F32)
    att = R["smallbf"][0:10, 0:2048]
    aoguard = None
    attT_guard = None
    for h in range(H):
        hb = (h % 2) * 64
        ch = h // 2
        p.wait("pe", qsel_tok, p.last("dve"), p.last("act"))
        for n4 in range(4):
            nsl = slice(n4 * 512, (n4 + 1) * 512)
            t = nc.tensor.matmul(ps_sc[0:10, nsl],
                                 aux[hb:hb + 64, h * 10:(h + 1) * 10],
                                 K_fm[hb:hb + 64, ch, nsl],
                                 start=True, stop=True)
        sctok = p.tok("pe", t)
        p.wait("dve", sctok)
        t = nc.vector.reduce_max(st[0:10, 78:79], ps_sc[0:10, :], axis=AX.X,
                                 negate=True)
        tk = p.tok("dve", t)
        p.wait("dve", tk)
        t = nc.vector.tensor_scalar_mul(st[0:10, 78:79], st[0:10, 78:79], 0.125)
        tk = p.tok("dve", t)
        p.wait("act", tk, aoguard)
        t = nc.scalar.activation(att, ps_sc[0:10, :], AF.Exp,
                                 bias=st[0:10, 78:79], scale=0.125,
                                 accum_out=st[0:10, 79:80])
        etk = p.tok("act", t)
        p.wait("dve", etk)
        t = nc.vector.reciprocal(st[0:10, 78:79], st[0:10, 79:80])
        rtk = p.tok("dve", t)
        p.wait("pe", etk, attT_guard)
        for j in range(NT):
            t = nc.tensor.transpose(R["ps_tr"][:, j * 10:(j + 1) * 10],
                                    att[:, j * 128:(j + 1) * 128],
                                    R["ident"][0:10, 0:10])
        ttk = p.tok("pe", t)
        p.wait("dve", ttk)
        t = nc.vector.tensor_copy(aux[:, 160:320], R["ps_tr"][:, 0:160])
        atk = p.tok("dve", t)
        attT_guard = atk
        p.wait("pe", atk, aoguard)
        for j in range(NT):
            t = nc.tensor.matmul(R["ps_row"][0:10, 0:64],
                                 aux[:, 160 + j * 10:160 + (j + 1) * 10],
                                 V_tm[:, j, h * 64:(h + 1) * 64],
                                 start=(j == 0), stop=(j == NT - 1))
        vtk = p.tok("pe", t)
        p.wait("dve", vtk, rtk)
        t = nc.vector.tensor_scalar_mul(ao_sb[0:10, h, :],
                                        R["ps_row"][0:10, 0:64],
                                        st[0:10, 78:79])
        aoguard = p.tok("dve", t)

    p.wait("pool", p.last("dve"), p.last("pe"))
    t = nc.gpsimd.iota(iota2k, [[1, 2048]], channel_multiplier=0,
                       allow_small_or_imprecise_dtypes=True)
    io_tok = p.tok("pool", t)
    full_fm = R["xnT"]
    ohguard = None
    evguard = None
    for pair in range(8):
        for par in range(2):
            h = 2 * pair + par
            psx = ps_a if par == 0 else ps_b
            p.wait("dve", io_tok, sidxT_tok, ohguard, aoguard)
            t = nc.vector.tensor_single_scalar(att, iota2k,
                                               st[0:10, 48 + h:49 + h],
                                               ALU.is_equal)
            ohtok = p.tok("dve", t)
            for n4 in range(4):
                nsl = slice(n4 * 512, (n4 + 1) * 512)
                p.wait("pe", ohtok, evguard)
                t = nc.tensor.matmul(psx[0:64, :], ao_sb[0:10, h, :],
                                     att[:, nsl], start=True, stop=True)
                mtok = p.tok("pe", t)
                eng = "dve" if n4 % 2 == 0 else "act"
                p.wait(eng, mtok)
                if eng == "dve":
                    t = nc.vector.tensor_copy(
                        full_fm[par * 64:par * 64 + 64, pair, nsl], psx[0:64, :])
                else:
                    t = nc.scalar.copy(
                        full_fm[par * 64:par * 64 + 64, pair, nsl], psx[0:64, :])
                evguard = p.tok(eng, t)
            ohguard = evguard

    bo_tok = vec_to_bcast(p, R, W["l0_bo"], R["bb1"], 1024)
    p.wait("pool", p.last("pe"), p.last("dve"), p.last("act"))
    wo_tok = p.dma("pool", wsb, W["l0_woT"])
    xres_tok = [None] * NT
    for i in range(NT):
        p.wait("dve", bo_tok)
        t = nc.vector.tensor_tensor(R["x_res"][:, i, :], R["x_res"][:, i, :],
                                    R["bb1"][:], ALU.add)
        xres_tok[i] = p.tok("dve", t)
    yguard = p.last("dve")
    for i in range(NT):
        til = slice(i * 128, (i + 1) * 128)
        for half in range(2):
            osl = slice(half * 512, (half + 1) * 512)
            p.wait("pe", wo_tok, evguard, yguard)
            for c in range(KC):
                t = nc.tensor.matmul(ps_a[:], full_fm[:, c, til],
                                     wsb[:, c, osl], start=(c == 0),
                                     stop=(c == KC - 1))
            mtok = p.tok("pe", t)
            p.wait("dve", mtok, xres_tok[i])
            t = nc.vector.scalar_tensor_tensor(
                R["x_res"][:, i, osl], ps_a[:], 1.0,
                R["x_res"][:, i, osl], ALU.mult, ALU.add)
            xres_tok[i] = p.tok("dve", t)
            yguard = xres_tok[i]
    p.barrier()


def emit_l2a(p, R, W, scr):
    nc = p.nc
    st = R["stats"]
    R["cmask"] = R["Wr"].bitcast(F32)[:, 1536:1664]
    v_dram = scr["v_dram"]
    ln_tok = emit_ln_to_xnT(p, R)
    xnT = R["xnT"]
    p.barrier()

    Q_fm = R["A"].bitcast(BF16).rearrange("p (c t) -> p c t", t=T)
    K_fm = R["B"].bitcast(BF16).rearrange("p (c t) -> p c t", t=T)
    wsl = R["Wr"].bitcast(BF16)
    w_sl = [wsl[:, 0:1024].rearrange("p (c n) -> p c n", n=128),
            wsl[:, 1024:2048].rearrange("p (c n) -> p c n", n=128)]
    ps_a = R["ps_a"]
    ps_b = R["ps_b"]

    bc_tok = vec_to_cols(p, R, W["l2_battn"][0:1024], st[:, 0:8], 1024)
    bc_tok = vec_to_cols(p, R, W["l2_battn"][1024:2048], st[:, 8:16], 1024)
    bc_tok = vec_to_cols(p, R, W["l2_battn"][2048:3072], st[:, 16:24], 1024)
    bcols = st[:, 0:24]

    wtok = [None, None]
    slot_use = [None, None]
    wtok[0] = p.dma("pool", w_sl[0], W["l2_wattn"][0])
    vstrip = R["tmpsq"].bitcast(BF16).rearrange("p (j d) -> p j d", d=128)
    eguard = None
    vguard = None
    vs_tok = None
    for oc in range(24):
        par = oc % 2
        if oc + 1 < 24:
            p.wait("pool", slot_use[(oc + 1) % 2])
            wtok[(oc + 1) % 2] = p.dma(
                "pool", w_sl[(oc + 1) % 2], W["l2_wattn"][oc + 1])
        for ts4 in range(4):
            tsl = slice(ts4 * 512, (ts4 + 1) * 512)
            p.wait("pe", wtok[par], ln_tok, eguard)
            for c in range(KC):
                t = nc.tensor.matmul(ps_a[:], w_sl[par][:, c, :],
                                     xnT[:, c, tsl], start=(c == 0),
                                     stop=(c == KC - 1))
            mtok = p.tok("pe", t)
            slot_use[par] = mtok
            p.wait("dve", mtok, bc_tok)
            if oc < 8:
                t = nc.vector.tensor_scalar_add(Q_fm[:, oc, tsl], ps_a[:],
                                                bcols[:, oc:oc + 1])
                eguard = p.tok("dve", t)
            elif oc < 16:
                t = nc.vector.tensor_scalar_add(K_fm[:, oc - 8, tsl], ps_a[:],
                                                bcols[:, oc:oc + 1])
                eguard = p.tok("dve", t)
            else:
                p.wait("dve", vguard)
                vb = R["tmpbf_a"]
                t = nc.vector.tensor_scalar_add(vb[:, 0:512], ps_a[:],
                                                bcols[:, oc:oc + 1])
                eguard = p.tok("dve", t)
                p.wait("pe", eguard, vs_tok)
                for jj in range(4):
                    t = nc.tensor.transpose(
                        R["ps_tr"][:, jj * 128:(jj + 1) * 128],
                        vb[:, jj * 128:(jj + 1) * 128], R["ident"][:])
                ptk = p.tok("pe", t)
                p.wait("dve", ptk)
                t = nc.vector.tensor_copy(vstrip[:, ts4 * 4:ts4 * 4 + 4, :],
                                          R["ps_tr"][:, 0:512]
                                          .rearrange("p (j q) -> p j q", q=128))
                vguard = p.tok("dve", t)
        if oc >= 16:
            voc = oc - 16
            p.wait("sp", vguard)
            vs_tok = p.dma(
                "sp",
                v_dram.rearrange("(j p) d -> p j d", p=128)[:, :, voc * 128:(voc + 1) * 128],
                vstrip[:])
    p.barrier()

    t = nc.gpsimd.memset(R["tmpf"][:, 0:128], 0.0)
    tk = p.tok("pool", t)
    p.wait("pool", tk)
    t = nc.gpsimd.affine_select(R["cmask"], R["tmpf"][:, 0:128], [[-1, 128]],
                                ALU.is_ge, -1e30, base=0, channel_multiplier=1)
    cm_tok = p.tok("pool", t)

    bo_tok = vec_to_bcast(p, R, W["l2_bproj"], R["bb1"], 1024)
    xres_tok = [None] * NT
    for i in range(NT):
        p.wait("dve", bo_tok)
        t = nc.vector.tensor_tensor(R["x_res"][:, i, :], R["x_res"][:, i, :],
                                    R["bb1"][:], ALU.add)
        xres_tok[i] = p.tok("dve", t)

    ps_s = [p.bps("ps_s1_l2a", [128, 2048], F32)]
    ps_v = R["ps_row"]
    ps_y = ps_b
    atts = [R["tmpf"].bitcast(BF16), R["bb2"].bitcast(BF16)]
    apair = R["tmpsq"].bitcast(BF16).rearrange("p (q c) -> p q c", c=128)
    vpair = R["aux"][:, 0:2048].rearrange("p (j d) -> p j d", d=128)
    wp_sb = wsl[:, 2048:3072]
    yguard = p.last("dve")
    vp_tok = None
    wp_tok = None
    apguard = None

    def emit_TV(prev, nxt_stok):
        h, qt, inst, etk, rtk, apar = prev
        att = atts[apar]
        hb = (h % 2) * 64
        nj = qt + 1
        vtk = None
        atg = None
        for j0 in range(0, nj, 8):
            jn = min(8, nj - j0)
            abuf = R["tmpbf_a"] if (j0 // 8) % 2 == 0 else R["tmpbf_b"]
            p.wait("pe", etk, R["_l2a_atg"][(j0 // 8) % 2], vp_tok)
            for jj in range(jn):
                t = nc.tensor.transpose(
                    R["ps_tr"][:, jj * 128:(jj + 1) * 128],
                    att[:, (j0 + jj) * 128:(j0 + jj + 1) * 128],
                    R["ident"][:])
            ttk = p.tok("pe", t)
            p.wait("dve", ttk, R["_l2a_veva"][0] if j0 == 0 else None)
            t = nc.vector.tensor_copy(abuf[:, 0:jn * 128],
                                      R["ps_tr"][:, 0:jn * 128])
            atg = p.tok("dve", t)
            R["_l2a_atg"][(j0 // 8) % 2] = atg
            p.wait("pe", atg)
            for jj in range(jn):
                t = nc.tensor.matmul(
                    ps_v[:, 0:64], abuf[:, jj * 128:(jj + 1) * 128],
                    vpair[:, j0 + jj, hb:hb + 64],
                    start=(j0 + jj == 0), stop=(j0 + jj == nj - 1))
            vtk = p.tok("pe", t)
        R["_l2a_T"][apar] = p.last("pe")
        p.wait("dve", vtk, rtk)
        t = nc.vector.tensor_scalar_mul(apair[:, qt, hb:hb + 64],
                                        ps_v[:, 0:64], st[:, 30 + inst:31 + inst])
        R["_l2a_veva"][0] = p.tok("dve", t)
        return R["_l2a_veva"][0]

    R["_l2a_atg"] = [None, None]
    R["_l2a_veva"] = [None]
    R["_l2a_T"] = [None, None]
    exp_done = [None, None]
    prev = None
    for pair in range(8):
        if prev is not None:
            emit_TV(prev, None)
            prev = None
        p.wait("act", apguard, R["_l2a_veva"][0], p.last("pe"))
        vp_tok = p.dma("act", vpair,
                       v_dram.rearrange("(j p) d -> p j d", p=128)
                       [:, :, pair * 128:(pair + 1) * 128])
        p.wait("pool", p.last("pe"))
        wp_tok = p.dma("pool", wp_sb,
                       W["l2_wproj"][pair * 128:(pair + 1) * 128, :])
        for par in range(2):
            h = 2 * pair + par
            ch = pair
            hb = par * 64
            for qt in range(NT):
                k = par * NT + qt
                inst = 0
                til = slice(qt * 128, (qt + 1) * 128)
                L = (qt + 1) * 128
                p.wait("pe", ln_tok, exp_done[inst], apguard)
                nb = (L + 511) // 512
                for b in range(nb):
                    bw = min(512, L - b * 512)
                    t = nc.tensor.matmul(ps_s[inst][:, b * 512:b * 512 + bw],
                                         Q_fm[hb:hb + 64, ch, til],
                                         K_fm[hb:hb + 64, ch, b * 512:b * 512 + bw],
                                         start=True, stop=True)
                stok = p.tok("pe", t)
                if prev is not None:
                    emit_TV(prev, stok)
                p.wait("dve", stok, cm_tok)
                t = nc.vector.tensor_tensor(ps_s[inst][:, til], ps_s[inst][:, til],
                                            R["cmask"], ALU.add)
                tk = p.tok("dve", t)
                p.wait("dve", tk)
                t = nc.vector.reduce_max(st[:, 28 + inst:29 + inst],
                                         ps_s[inst][:, 0:L], axis=AX.X,
                                         negate=True)
                tk = p.tok("dve", t)
                p.wait("dve", tk)
                t = nc.vector.tensor_scalar_mul(st[:, 28 + inst:29 + inst],
                                                st[:, 28 + inst:29 + inst], 0.125)
                tk = p.tok("dve", t)
                apar = k % 2
                p.wait("act", tk, R["_l2a_T"][apar])
                t = nc.scalar.activation(atts[apar][:, 0:L], ps_s[inst][:, 0:L],
                                         AF.Exp,
                                         bias=st[:, 28 + inst:29 + inst],
                                         scale=0.125,
                                         accum_out=st[:, 26 + inst:27 + inst])
                etk = p.tok("act", t)
                exp_done[inst] = etk
                p.wait("dve", etk)
                t = nc.vector.reciprocal(st[:, 30 + inst:31 + inst],
                                         st[:, 26 + inst:27 + inst])
                rtk = p.tok("dve", t)
                prev = (h, qt, inst, etk, rtk, apar)
        emit_TV(prev, None)
        prev = None
        aguard = R["_l2a_veva"][0]
        for qt in range(NT):
            p.wait("pe", aguard, apguard)
            t = nc.tensor.transpose(R["ps_tr"][:, 0:128], apair[:, qt, :],
                                    R["ident"][:])
            ptk = p.tok("pe", t)
            p.wait("dve", ptk)
            t = nc.vector.tensor_copy(apair[:, qt, :], R["ps_tr"][:, 0:128])
            apguard = p.tok("dve", t)
        for qt in range(NT):
            for half in range(2):
                osl = slice(half * 512, (half + 1) * 512)
                p.wait("pe", wp_tok, apguard, yguard)
                t = nc.tensor.matmul(ps_y[:], apair[:, qt, :],
                                     wp_sb[:, osl], start=True, stop=True)
                mtok = p.tok("pe", t)
                p.wait("dve", mtok, xres_tok[qt])
                t = nc.vector.scalar_tensor_tensor(
                    R["x_res"][:, qt, osl], ps_y[:], 1.0,
                    R["x_res"][:, qt, osl], ALU.mult, ALU.add)
                xres_tok[qt] = p.tok("dve", t)
                yguard = xres_tok[qt]
        apguard = yguard
    p.barrier()


EMITTERS = {}


def build(blocks, weight_specs):
    nc = bass.Bass("TRN2", target_bir_lowering=False)
    x_in = nc.dram_tensor("x", [128, NT, D], F32, kind="ExternalInput").ap()
    y_out = nc.dram_tensor("y", [128, NT, D], F32, kind="ExternalOutput").ap()
    W = {}
    for name, (shape, dt) in weight_specs.items():
        W[name] = nc.dram_tensor(name, list(shape), dt, kind="ExternalInput").ap()
    cc = None
    if "l3f" in blocks:
        cc = {"in": nc.dram_tensor("cc_in", [1, 1], F32, kind="Internal").ap(),
              "out": nc.dram_tensor("cc_out", [1, 1], F32, kind="Internal").ap()}
    scr = {}
    if "l0a" in blocks:
        scr["q_dram"] = nc.dram_tensor("q_dram", [T, D], BF16, kind="Internal").ap()
    if "l2a" in blocks:
        scr["v_dram"] = nc.dram_tensor("v_dram", [T, D], BF16, kind="Internal").ap()
    if "l2f" in blocks:
        scr["h_dram"] = nc.dram_tensor("h_dram", [T, 4096], BF16, kind="Internal").ap()
        scr["wqg_dram"] = nc.dram_tensor("wqg_dram", [1024, 4096], BF16, kind="Internal").ap()
        scr["wqu_dram"] = nc.dram_tensor("wqu_dram", [1024, 4096], BF16, kind="Internal").ap()
        scr["wqd_dram"] = nc.dram_tensor("wqd_dram", [4096, 1024], BF16, kind="Internal").ap()

    with ExitStack() as ctx:
        p = Prog(nc, ctx)
        R = {}
        R["x_res"] = p.sb("x_res", [128, NT, D], F32)[:]
        R["xnT"] = p.sb("xnT", [128, KC, T], BF16)[:]
        R["A"] = p.sb("regA", [128, 8192], F32)[:]
        R["B"] = p.sb("regB", [128, 8192], F32)[:]
        R["Wr"] = p.sb("regW", [128, 4096], F32)[:]
        R["tmpf"] = p.sb("tmpf", [128, 1024], F32)[:]
        R["tmpsq"] = p.sb("tmpsq", [128, 1024], F32)[:]
        R["tmpbf_a"] = p.sb("tmpbf_a", [128, 1024], BF16)[:]
        R["tmpbf_b"] = p.sb("tmpbf_b", [128, 1024], BF16)[:]
        R["smallbf"] = p.sb("smallbf", [16, 2048], BF16)[:]
        R["stats"] = p.sb("stats", [128, 80], F32)[:]
        R["bb1"] = p.sb("bb1", [128, 1024], F32)[:]
        R["bb2"] = p.sb("bb2", [128, 1024], F32)[:]
        R["row_f"] = p.sb("row_f", [16, 512], F32)[:]
        R["idx_u"] = p.sb("idx_u", [128, 8], U32)[:]
        R["idx_c"] = p.sb("idx_c", [128, 16], U32)[:]
        R["aux"] = p.sb("aux", [128, 2048], BF16)[:]
        R["kv_bf"] = R["aux"][:, 0:1024].rearrange("p (h d) -> p h d", d=64)
        R["ks_bf"] = R["aux"][:, 1024:1040]
        R["ones_f"] = p.sb("ones_f", [1, 128], F32)[:]
        R["ones_col"] = p.sb("ones_col", [128, 1], F32)[:]
        R["ones_bf"] = R["tmpbf_a"][:, 0:128]
        R["ones_bfc"] = p.sb("ones_bfc", [128, 1], BF16)[:]
        R["ones_f128"] = R["tmpf"][:, 0:128]
        R["ident"] = p.sb("ident", [128, 128], BF16)[:]
        R["ident_f"] = p.sb("ident_f", [128, 128], F32)[:]
        R["ps_a"] = p.ps("ps_a", [128, 512], F32)[:]
        R["ps_b"] = p.ps("ps_b", [128, 512], F32)[:]
        R["ps_tr"] = p.ps("ps_tr", [128, 1024], BF16)[:]
        R["ps_row"] = p.ps("ps_row", [128, 512], F32)[:]
        R["ps_small"] = R["ps_row"][:, 496:512]
        R["_fm_evac_tok"] = [None]

        emit_common(p, R)
        tk = p.dma("sp", R["x_res"], x_in)
        p.wait("dve", tk)
        p.wait("act", tk)
        p.wait("pe", tk)
        p.wait("pool", tk)

        for b in blocks:
            with ExitStack() as bctx:
                p.bctx = bctx
                EMITTERS[b](p, R, W, cc=cc, scr=scr)
            p.barrier()

        tk = p.dma("sp", y_out, R["x_res"])
        p.wait("sp", tk)
    return nc


EMITTERS["l0f"] = lambda p, R, W, cc=None, scr=None: emit_l0f(p, R, W)
EMITTERS["l1f"] = lambda p, R, W, cc=None, scr=None: emit_l1f(p, R, W)
EMITTERS["l2f"] = lambda p, R, W, cc=None, scr=None: emit_l2f(p, R, W, scr)
EMITTERS["l0a"] = lambda p, R, W, cc=None, scr=None: emit_l0a(p, R, W, scr)
EMITTERS["l2a"] = lambda p, R, W, cc=None, scr=None: emit_l2a(p, R, W, scr)
EMITTERS["l1a"] = lambda p, R, W, cc=None, scr=None: emit_l1a(p, R, W)
EMITTERS["l3a"] = lambda p, R, W, cc=None, scr=None: emit_l3a(p, R, W)
EMITTERS["l3f"] = lambda p, R, W, cc=None, scr=None: emit_l3f(p, R, W, cc)


def _to32(a):
    return np.ascontiguousarray(a, dtype=np.float32)


def _pcn(WT):
    K, N = WT.shape
    return np.ascontiguousarray(WT.reshape(K // 128, 128, N).transpose(1, 0, 2))


def _tiles(WT, tn):
    K, N = WT.shape
    a = WT.reshape(K // 128, 128, N // tn, tn).transpose(2, 1, 0, 3)
    return np.ascontiguousarray(a)


def host_weights(inputs):
    I = inputs
    w = {}
    w["l0_wqT"] = (_pcn(_to32(I["L0_attn_Wq"].T)), F32)
    w["l0_wkT"] = (_pcn(_to32(I["L0_attn_Wk"].T)), F32)
    w["l0_wvT"] = (_pcn(_to32(I["L0_attn_Wv"].T)), F32)
    w["l0_woT"] = (_pcn(_to32(I["L0_attn_Wo"].T)), F32)
    for b in ["bq", "bk", "bv", "bo"]:
        w["l0_" + b] = (_to32(I["L0_attn_" + b]), F32)
    w["l0_wg4"] = (_tiles(np.ascontiguousarray(I["L0_ffn_w_gate"].T), 128), I32)
    w["l0_wu4"] = (_tiles(np.ascontiguousarray(I["L0_ffn_w_up"].T), 128), I32)
    w["l0_wdT"] = (np.ascontiguousarray(I["L0_ffn_w_down"].T), I32)
    w["l1_wqT"] = (_pcn(_to32(I["L1_attn_Wq"].T)), F32)
    w["l1_wkT"] = (_pcn(_to32(I["L1_attn_Wk"].T)), F32)
    w["l1_wvT"] = (_pcn(_to32(I["L1_attn_Wv"].T)), F32)
    w["l1_woT"] = (_pcn(_to32(I["L1_attn_Wo"].T)), F32)
    for b in ["bq", "bk", "bv", "bo"]:
        w["l1_" + b] = (_to32(I["L1_attn_" + b]), F32)
    w["l1_Wg"] = (_pcn(_to32(I["L1_ffn_Wg"])), F32)
    w["l1_bg"] = (_to32(I["L1_ffn_bg"]), F32)
    w["l1_W1"] = (np.stack([_pcn(_to32(I["L1_ffn_W1"][e])) for e in range(8)]), F32)
    w["l1_b1"] = (_to32(I["L1_ffn_b1"]), F32)
    w["l1_W2"] = (np.stack([_pcn(_to32(I["L1_ffn_W2"][e])) for e in range(8)]), F32)
    w["l1_b2"] = (_to32(I["L1_ffn_b2"]), F32)
    w["l2_wattn"] = (_tiles(_to32(I["L2_attn_w_attn"]), 128), F32)
    w["l2_battn"] = (_to32(I["L2_attn_b_attn"]), F32)
    w["l2_wproj"] = (_to32(I["L2_attn_w_proj"]), F32)
    w["l2_bproj"] = (_to32(I["L2_attn_b_proj"]), F32)
    w["l2_wgT"] = (_to32(I["L2_ffn_w_gate"].T), F32)
    w["l2_wuT"] = (_to32(I["L2_ffn_w_up"].T), F32)
    w["l2_wdT"] = (_to32(I["L2_ffn_w_down"].T), F32)
    win = _to32(I["L3_attn_W_in"])
    win_p = np.zeros((1024, 1152), np.float32)
    win_p[:, 0:1032] = win
    w["l3_Win"] = (_tiles(win_p, 128), F32)
    w["l3_b_in"] = (_to32(I["L3_attn_b_in"]), F32)
    wg3 = _to32(I["L3_attn_W_g"])
    wg3_p = np.zeros((1152, 8), np.float32)
    wg3_p[0:1032] = wg3
    w["l3_Wg"] = (_pcn(wg3_p), F32)
    w["l3_b_g"] = (_to32(I["L3_attn_b_g"]), F32)
    w["l3_Wout"] = (_to32(I["L3_attn_W_out"]), F32)
    w["l3_bout"] = (_to32(I["L3_attn_b_out"]), F32)
    w["l3f_w"] = (_to32(I["L3_ffn_weight"]), F32)
    w["l3f_gamma"] = (_to32(I["L3_ffn_ln_gamma"]), F32)
    w["l3f_beta"] = (_to32(I["L3_ffn_ln_beta"]), F32)
    return w


BLOCK_WEIGHTS = {
    "l0a": ["l0_wqT", "l0_wkT", "l0_wvT", "l0_woT", "l0_bq", "l0_bk", "l0_bv", "l0_bo"],
    "l0f": ["l0_wg4", "l0_wu4", "l0_wdT"],
    "l1a": ["l1_wqT", "l1_wkT", "l1_wvT", "l1_woT", "l1_bq", "l1_bk", "l1_bv", "l1_bo"],
    "l1f": ["l1_Wg", "l1_bg", "l1_W1", "l1_b1", "l1_W2", "l1_b2"],
    "l2a": ["l2_wattn", "l2_battn", "l2_wproj", "l2_bproj"],
    "l2f": ["l2_wgT", "l2_wuT", "l2_wdT"],
    "l3a": ["l3_Win", "l3_b_in", "l3_Wg", "l3_b_g", "l3_Wout", "l3_bout"],
    "l3f": ["l3f_w", "l3f_gamma", "l3f_beta"],
}

LAUNCH_GROUPS = [ALL_BLOCKS]


def run_groups(x_full, wdict, groups):
    x = np.ascontiguousarray(x_full, dtype=np.float32)
    for blocks in groups:
        names = []
        for b in blocks:
            names += BLOCK_WEIGHTS[b]
        spec = {n: (wdict[n][0].shape, wdict[n][1]) for n in names}
        nc = build(blocks, spec)
        in_maps = []
        for c in range(NCORES):
            xc = np.ascontiguousarray(
                x[c].reshape(NT, 128, D).transpose(1, 0, 2))
            m = {"x": xc}
            for n in names:
                m[n] = wdict[n][0]
            in_maps.append(m)
        res = run_bass_kernel_spmd(nc, in_maps, core_ids=list(range(NCORES)))
        x = np.stack([np.asarray(res.results[c]["y"])
                      .transpose(1, 0, 2).reshape(T, D) for c in range(NCORES)])
    return x


def kernel(**inputs):
    w = host_weights(inputs)
    x = np.ascontiguousarray(np.asarray(inputs["L0_attn_x"], dtype=np.float32))
    out = run_groups(x, w, LAUNCH_GROUPS)
    return out.astype(np.float32)


# version 10
# speedup vs baseline: 1.0337x; 1.0337x over previous
import numpy as np
from contextlib import ExitStack

import concourse.bass as bass
import concourse.mybir as mybir
from concourse.bass_utils import run_bass_kernel_spmd

F32 = mybir.dt.float32
BF16 = mybir.dt.bfloat16
I32 = mybir.dt.int32
U32 = mybir.dt.uint32
AF = mybir.ActivationFunctionType
ALU = mybir.AluOpType
AX = mybir.AxisListType
IOffs = bass.IndirectOffsetOnAxis

MAGIC = float(np.float32(1.5 * 2 ** 23))
T = 2048
D = 1024
NT = T // 128
KC = D // 128
NCORES = 8

ALL_BLOCKS = ["l0a", "l0f", "l1a", "l1f", "l2a", "l2f", "l3a", "l3f"]


class Prog:
    def __init__(self, nc, ctx):
        self.nc = nc
        self.ctx = ctx
        self.e = {"pe": nc.tensor, "dve": nc.vector, "act": nc.scalar,
                  "pool": nc.gpsimd, "sp": nc.sync}
        self.sem = {}
        self.cnt = {}
        self.waited = {}
        names = ["pe", "dve", "act", "pool", "sp", "cc"]
        for q in ["sp", "pool", "act"]:
            for i in range(6):
                names.append(f"dma_{q}{i}")
        for n in names:
            self.sem[n] = ctx.enter_context(nc.semaphore("s_" + n))
            self.cnt[n] = 0
        self.dma_rr = {"sp": 0, "pool": 0, "act": 0}

    def tok(self, name, instr, n=1):
        self.cnt[name] += n
        instr.then_inc(self.sem[name], n)
        return (name, self.cnt[name])

    def last(self, name):
        return (name, self.cnt[name])

    def dma(self, queue, out, in_, **kw):
        i = self.dma_rr[queue]
        self.dma_rr[queue] = (i + 1) % 6
        name = f"dma_{queue}{i}"
        self.wait(queue, (name, self.cnt[name]))
        ins = self.e[queue].dma_start(out, in_, **kw)
        return self.tok(name, ins, 16)

    def dma_indirect_add(self, out_ap, table_dram, idx_ap):
        i = self.dma_rr["pool"]
        self.dma_rr["pool"] = (i + 1) % 6
        name = f"dma_pool{i}"
        self.wait("pool", (name, self.cnt[name]))
        ins = self.nc.gpsimd.indirect_dma_start(
            out_ap, None, table_dram, IOffs(idx_ap, 0), compute_op=ALU.add)
        return self.tok(name, ins, 16)

    def dma_gather_rows(self, out_ap, table_dram, idx_ap, element_offset=0):
        i = self.dma_rr["pool"]
        self.dma_rr["pool"] = (i + 1) % 6
        name = f"dma_pool{i}"
        self.wait("pool", (name, self.cnt[name]))
        ins = self.nc.gpsimd.indirect_dma_start(
            out_ap, None, table_dram, IOffs(idx_ap, 0),
            element_offset=element_offset)
        return self.tok(name, ins, 16)

    def wait(self, eng, *toks):
        for t in toks:
            if t is None:
                continue
            sname, val = t
            if val <= 0:
                continue
            key = (eng, sname)
            if self.waited.get(key, 0) >= val:
                continue
            self.e[eng].wait_ge(self.sem[sname], val)
            self.waited[key] = val

    def barrier(self):
        toks = [self.last(n) for n in self.sem if n != "cc"]
        for eng in ["pe", "dve", "act", "pool", "sp"]:
            self.wait(eng, *toks)

    def sb(self, name, shape, dt):
        return self.ctx.enter_context(self.nc.sbuf_tensor(name, shape, dt))

    def ccol(self, value):
        if not hasattr(self, "_cpool"):
            self._cpool = self.sb("cpool", [128, 16], F32)[:]
            self._cvals = {}
        v = float(value)
        if v not in self._cvals:
            i = len(self._cvals)
            assert i < 16
            ap = self._cpool[:, i:i + 1]
            t = self.nc.gpsimd.memset(ap, v)
            tk = self.tok("pool", t)
            self._cvals[v] = (ap, tk)
        return self._cvals[v]

    def ps(self, name, shape, dt=F32):
        return self.ctx.enter_context(self.nc.psum_tensor(name, shape, dt))

    def bps(self, name, shape, dt=F32):
        return self.bctx.enter_context(self.nc.psum_tensor(name, shape, dt))[:]


def emit_common(p, R):
    nc = p.nc
    t = nc.vector.memset(R["ones_f"][:], 1.0)
    p.tok("dve", t)
    t = nc.vector.memset(R["ones_col"][:], 1.0)
    p.tok("dve", t)
    t = nc.gpsimd.memset(R["ones_bf"], 1.0)
    tk = p.tok("pool", t)
    t = nc.gpsimd.memset(R["ones_bfc"][:], 1.0)
    tk = p.tok("pool", t)
    p.wait("pool", tk)
    t = nc.gpsimd.affine_select(R["ident"][:], R["ones_bf"], [[-1, 128]],
                                ALU.is_equal, 0.0, base=0, channel_multiplier=1)
    p.tok("pool", t)
    t = nc.gpsimd.memset(R["ones_f128"][:], 1.0)
    tk = p.tok("pool", t)
    p.wait("pool", tk)
    t = nc.gpsimd.affine_select(R["ident_f"][:], R["ones_f128"][:], [[-1, 128]],
                                ALU.is_equal, 0.0, base=0, channel_multiplier=1)
    p.tok("pool", t)
    p.barrier()


def bcast_row_to_col(p, R, src_ap_1x1, dst_col_128x1, scale=None):
    nc = p.nc
    ps = R["ps_small"]
    p.wait("pe", p.last("dve"), p.last("act"), p.last("pool"))
    t = nc.tensor.matmul(ps[:, 0:1], R["ones_f"][:, 0:128], src_ap_1x1,
                         start=True, stop=True)
    tk = p.tok("pe", t)
    p.wait("dve", tk)
    t = nc.vector.tensor_copy(dst_col_128x1, ps[:, 0:1])
    if scale is not None:
        tk2 = p.tok("dve", t)
        p.wait("dve", tk2)
        t = nc.vector.tensor_scalar_mul(dst_col_128x1, dst_col_128x1, scale)
    return p.tok("dve", t)


def vec_to_bcast(p, R, vec_dram, dst_f32_128xN, n, tmp_row=None):
    nc = p.nc
    row = R["row_f"]
    ps = R["ps_row"]
    out_toks = []
    for off in range(0, n, 512):
        w = min(512, n - off)
        p.wait("sp", p.last("pe"), p.last("dve"))
        tk = p.dma("sp", row[0:1, 0:w], vec_dram[off:off + w].unsqueeze(0))
        p.wait("pe", tk, p.last("dve"))
        t = nc.tensor.matmul(ps[:, 0:w], R["ones_f"][:, 0:128],
                             row[0:1, 0:w], start=True, stop=True)
        ptk = p.tok("pe", t)
        p.wait("dve", ptk)
        t = nc.vector.tensor_copy(dst_f32_128xN[:, off:off + w], ps[:, 0:w])
        out_toks.append(p.tok("dve", t))
    return out_toks[-1]


def vec_to_cols(p, R, vec_dram, dst_f32_128xC, n):
    nc = p.nc
    c = n // 128
    row = R["row_f"]
    p.wait("sp", p.last("pe"), p.last("dve"))
    tk = p.dma("sp", row[0:c, 0:128], vec_dram.rearrange("(c q) -> c q", q=128))
    ps = R["ps_row"]
    p.wait("pe", tk, p.last("dve"))
    t = nc.tensor.transpose(ps[:, 0:c], row[0:c, 0:128], R["ident_f"][0:c, 0:c])
    ptk = p.tok("pe", t)
    p.wait("dve", ptk)
    t = nc.vector.tensor_copy(dst_f32_128xC[:, 0:c], ps[:, 0:c])
    return p.tok("dve", t)


def emit_ln(p, R, consume, eps=1e-5, src=None):
    nc = p.nc
    xr = src if src is not None else R["x_res"]
    st = R["stats"]
    s1 = st[:, 0:NT]
    s2 = st[:, NT:2 * NT]
    a_c = st[:, 2 * NT:3 * NT]
    b_c = st[:, 3 * NT:4 * NT]
    red_toks = []
    sq_toks = []
    for i in range(NT):
        t = nc.vector.reduce_sum(s1[:, i:i + 1], xr[:, i, :], axis=AX.X)
        red_toks.append(p.tok("dve", t))
        p.wait("act", p.last("dve") if i == 0 else None)
        sq_scr = R["tmpsq"] if (i % 2 == 0) else R["tmpf"]
        if i >= 2:
            p.wait("act", sq_toks[i - 2])
        t = nc.scalar.activation(sq_scr[:], xr[:, i, :], AF.Square,
                                 accum_out=s2[:, i:i + 1])
        sq_toks.append(p.tok("act", t))
    p.wait("dve", red_toks[-1], sq_toks[-1])
    t = nc.vector.tensor_scalar_mul(st[:, 0:2 * NT], st[:, 0:2 * NT], 1.0 / D)
    tk = p.tok("dve", t)
    p.wait("dve", tk)
    t = nc.vector.tensor_tensor(a_c, s1, s1, ALU.mult)
    tk = p.tok("dve", t)
    p.wait("dve", tk)
    t = nc.vector.tensor_tensor(a_c, s2, a_c, ALU.subtract)
    tk = p.tok("dve", t)
    eap, etk = p.ccol(eps)
    p.wait("act", tk, etk)
    t = nc.scalar.activation(a_c, a_c, AF.Sqrt, bias=eap)
    tk = p.tok("act", t)
    p.wait("dve", tk)
    t = nc.vector.reciprocal(a_c, a_c)
    tk = p.tok("dve", t)
    p.wait("dve", tk)
    t = nc.vector.scalar_tensor_tensor(b_c, s1, -1.0, a_c, ALU.mult, ALU.mult)
    tk = p.tok("dve", t)
    p.wait("dve", tk)
    guard = [None, None]
    for i in range(NT):
        dst = R["tmpf"] if (i % 2 == 0) else R["tmpsq"]
        p.wait("dve", guard[i % 2])
        t = nc.vector.tensor_scalar(dst[:], xr[:, i, :], a_c[:, i:i + 1],
                                    b_c[:, i:i + 1], ALU.mult, ALU.add)
        tk = p.tok("dve", t)
        guard[i % 2] = consume(i, dst, tk)
    return a_c, b_c


def emit_tile_to_fm(p, R, i, src_bf_tile, tok, dst_fm, evac_eng="dve"):
    nc = p.nc
    ps = R["ps_tr"]
    p.wait("pe", tok, R["_fm_evac_tok"][0])
    for c in range(KC):
        t = nc.tensor.transpose(ps[:, c * 128:(c + 1) * 128],
                                src_bf_tile[:, c * 128:(c + 1) * 128],
                                R["ident"][:])
    tk = p.tok("pe", t)
    p.wait(evac_eng, tk)
    if evac_eng == "dve":
        t = nc.vector.tensor_copy(dst_fm[:, :, i * 128:(i + 1) * 128],
                                  ps[:].rearrange("p (c q) -> p c q", q=128))
    else:
        t = nc.scalar.copy(dst_fm[:, :, i * 128:(i + 1) * 128],
                           ps[:].rearrange("p (c q) -> p c q", q=128))
    R["_fm_evac_tok"][0] = p.tok(evac_eng, t)
    return R["_fm_evac_tok"][0]


def emit_ln_to_xnT(p, R, eps=1e-5):
    nc = p.nc
    R["_fm_evac_tok"] = [None]
    xnT = R["xnT"]

    def consume(i, xn_tile_f32, tok):
        bft = R["tmpbf_a"] if i % 2 == 0 else R["tmpbf_b"]
        p.wait("act", tok)
        t = nc.scalar.copy(bft[:], xn_tile_f32[:])
        ctk = p.tok("act", t)
        emit_tile_to_fm(p, R, i, bft[:], ctk, xnT[:],
                        evac_eng="dve" if i % 2 == 0 else "act")
        return ctk

    emit_ln(p, R, consume, eps=eps)
    return R["_fm_evac_tok"][0]


def load_w_chunks(p, dram_ap, dst_bf):
    k = dram_ap.shape[0]
    return p.dma("pool", dst_bf, dram_ap.rearrange("(c p) n -> p c n", p=128))


def emit_l0f(p, R, W):
    nc = p.nc
    S = 1.0 / 127.0
    ln_tok = emit_ln_to_xnT(p, R)
    xnT = R["xnT"]
    p.barrier()

    NHC = 24
    GRP = 4
    h_fm = R["A"].bitcast(BF16).rearrange("p (j t) -> p j t", t=T)
    wg = R["Wr"].bitcast(BF16)
    wg_sl = [wg[:, 0:1024].rearrange("p (c n) -> p c n", n=128),
             wg[:, 1024:2048].rearrange("p (c n) -> p c n", n=128)]
    wu_sl = [wg[:, 2048:3072].rearrange("p (c n) -> p c n", n=128),
             wg[:, 3072:4096].rearrange("p (c n) -> p c n", n=128)]
    wd_sl = [wg[:, 4096 + j * 1024: 4096 + (j + 1) * 1024] for j in range(4)]

    ps_g = [R["ps_a"], p.bps("ps_g2_l0f", [128, 512], F32)]
    ps_u = [R["ps_b"], p.bps("ps_u2_l0f", [128, 512], F32)]
    ps_y = [p.bps("ps_y_l0f", [128, 512], F32),
            p.bps("ps_y2_l0f", [128, 512], F32)]
    silu_t = R["tmpbf_a"]

    wtoks = {}
    mm_after = {}

    slot_use = {}

    def load_hc(hc, parity):
        p.wait("pool", slot_use.get(("gu", parity)))
        wtoks[("g", parity)] = p.dma("pool", wg_sl[parity], W["l0_wg4"][hc])
        wtoks[("u", parity)] = p.dma("pool", wu_sl[parity], W["l0_wu4"][hc])

    def load_wd(hc, j):
        p.wait("pool", slot_use.get(("d", j)))
        wtoks[("d", j)] = p.dma("pool", wd_sl[j],
                                W["l0_wdT"][hc * 128:(hc + 1) * 128, :])

    load_hc(0, 0)
    xres_tok = [None] * NT
    for g in range(6):
        hcs = [g * GRP + j for j in range(GRP)]
        for j, hc in enumerate(hcs):
            par = hc % 2
            if hc + 1 < NHC:
                load_hc(hc + 1, (hc + 1) % 2)
            load_wd(hc, j)
            for ts4 in range(4):
                q = ts4 % 2
                tsl = slice(ts4 * 512, (ts4 + 1) * 512)
                p.wait("pe", wtoks[("g", par)], ln_tok,
                       mm_after.get(("silu_rd", q)))
                for c in range(KC):
                    t = nc.tensor.matmul(ps_g[q][:], wg_sl[par][:, c, :],
                                         xnT[:, c, tsl], start=(c == 0),
                                         stop=(c == KC - 1))
                g_tok = p.tok("pe", t)
                p.wait("pe", wtoks[("u", par)], mm_after.get(("h_rd", q)))
                for c in range(KC):
                    t = nc.tensor.matmul(ps_u[q][:], wu_sl[par][:, c, :],
                                         xnT[:, c, tsl], start=(c == 0),
                                         stop=(c == KC - 1))
                u_tok = p.tok("pe", t)
                p.wait("act", g_tok, mm_after.get(("h_rd", q)),
                       mm_after.get(("silu_wr", q)))
                t = nc.scalar.activation(silu_t[:, q * 512:q * 512 + 512],
                                         ps_g[q][:], AF.Silu, scale=S)
                s_tok = p.tok("act", t)
                mm_after[("silu_rd", q)] = s_tok
                p.wait("dve", u_tok, s_tok)
                t = nc.vector.scalar_tensor_tensor(
                    h_fm[:, j, tsl], ps_u[q][:], S * S,
                    silu_t[:, q * 512:q * 512 + 512], ALU.mult, ALU.mult)
                mm_after[("h", j)] = p.tok("dve", t)
                mm_after[("h_rd", q)] = mm_after[("h", j)]
                mm_after[("silu_wr", q)] = mm_after[("h", j)]
            slot_use[("gu", par)] = p.last("pe")
        for i in range(NT):
            for half in range(2):
                q = half
                p.wait("pe", *[wtoks[("d", j)] for j in range(GRP)],
                       *[mm_after[("h", j)] for j in range(GRP)],
                       mm_after.get(("y_rd", q)))
                for j in range(GRP):
                    t = nc.tensor.matmul(
                        ps_y[q][:], h_fm[:, j, i * 128:(i + 1) * 128],
                        wd_sl[j][:, half * 512:(half + 1) * 512],
                        start=(j == 0), stop=(j == GRP - 1))
                y_tok = p.tok("pe", t)
                p.wait("dve", y_tok, xres_tok[i])
                t = nc.vector.scalar_tensor_tensor(
                    R["x_res"][:, i, half * 512:(half + 1) * 512], ps_y[q][:],
                    1.0, R["x_res"][:, i, half * 512:(half + 1) * 512],
                    ALU.mult, ALU.add)
                xres_tok[i] = p.tok("dve", t)
                mm_after[("y_rd", q)] = xres_tok[i]
        for j in range(GRP):
            slot_use[("d", j)] = p.last("pe")
    p.barrier()


def emit_l3a(p, R, W):
    nc = p.nc
    ln_tok = emit_ln_to_xnT(p, R)
    xnT = R["xnT"]
    p.barrier()

    comb = R["A"].bitcast(BF16).rearrange("p (c t) -> p c t", t=T)
    tail_bf = R["smallbf"]
    Bf = R["B"].bitcast(F32)
    si_f = Bf[0:8, 0:T]
    g_f = Bf[0:8, T:2 * T]
    d1_f = Bf[0:8, 2 * T:3 * T]
    states_f = Bf[0:8, 3 * T:4 * T]

    wsl = R["Wr"].bitcast(BF16)
    win_sl = [wsl[:, 0:1024].rearrange("p (c n) -> p c n", n=128),
              wsl[:, 1024:2048].rearrange("p (c n) -> p c n", n=128)]
    wg_sl = wsl[:, 2048:2048 + 9 * 8].rearrange("p (c n) -> p c n", n=8)
    wout_sl = [wsl[:, 4096 + j * 1024: 4096 + (j + 1) * 1024] for j in range(4)]

    bcols = R["stats"][:, 32:41]
    vec_to_cols(p, R, W["l3_b_in"][0:1024], bcols, 1024)
    row = R["row_f"]
    p.wait("sp", p.last("pe"), p.last("dve"))
    tk = p.dma("sp", row[0:1, 256:264], W["l3_b_in"][1024:1032].unsqueeze(0))
    p.wait("pe", tk, p.last("dve"))
    t = nc.tensor.transpose(R["ps_row"][0:8, 0:1], row[0:1, 256:264], R["ident_f"][0:1, 0:1])
    ptk = p.tok("pe", t)
    p.wait("dve", ptk)
    t = nc.vector.tensor_copy(bcols[0:8, 8:9], R["ps_row"][0:8, 0:1])
    btk = p.tok("dve", t)
    bg_col = R["stats"][:, 41:42]
    tk = p.dma("sp", row[0:1, 272:280], W["l3_b_g"].unsqueeze(0))
    p.wait("pe", tk, btk)
    t = nc.tensor.transpose(R["ps_row"][0:8, 1:2], row[0:1, 272:280], R["ident_f"][0:1, 0:1])
    ptk = p.tok("pe", t)
    p.wait("dve", ptk)
    t = nc.vector.tensor_copy(bg_col[0:8, :], R["ps_row"][0:8, 1:2])
    bgtk = p.tok("dve", t)
    wgt1 = p.dma("pool", wg_sl, W["l3_Wg"])
    wgt2 = wgt1

    ps_c2 = [R["ps_a"], R["ps_b"]]
    ps_g2 = R["ps_b"]

    wtok = [None, None]
    slot_use = [None, None]
    wtok[0] = p.dma("pool", win_sl[0], W["l3_Win"][0])
    comb_toks = []
    cgq = {}
    for oc in range(9):
        par = oc % 2
        m = 128 if oc < 8 else 8
        if oc + 1 < 9:
            nm = 128 if oc + 1 < 8 else 8
            p.wait("pool", slot_use[(oc + 1) % 2])
            wtok[(oc + 1) % 2] = p.dma(
                "pool", win_sl[(oc + 1) % 2][:, :, 0:nm],
                W["l3_Win"][oc + 1][:, :, 0:nm])
        for ts4 in range(4):
            q = ts4 % 2
            tsl = slice(ts4 * 512, (ts4 + 1) * 512)
            p.wait("pe", wtok[par], ln_tok, cgq.get(q))
            for c in range(KC):
                t = nc.tensor.matmul(ps_c2[q][0:m, :], win_sl[par][:, c, 0:m],
                                     xnT[:, c, tsl], start=(c == 0),
                                     stop=(c == KC - 1))
            mtok = p.tok("pe", t)
            slot_use[par] = mtok
            p.wait("dve", mtok, btk)
            if oc < 8:
                t = nc.vector.tensor_scalar_add(comb[:, oc, tsl], ps_c2[q][0:m, :],
                                                bcols[:, oc:oc + 1])
                cgq[q] = p.tok("dve", t)
                comb_toks.append(cgq[q])
            else:
                t = nc.vector.tensor_scalar_add(si_f[:, tsl], ps_c2[q][0:8, :],
                                                bcols[0:8, 8:9])
                ttk = p.tok("dve", t)
                cgq[q] = ttk
                comb_toks.append(ttk)
                p.wait("act", ttk)
                t = nc.scalar.copy(tail_bf[0:8, tsl], si_f[:, tsl])
                comb_toks.append(p.tok("act", t))
    for ts4 in range(4):
        tsl = slice(ts4 * 512, (ts4 + 1) * 512)
        p.wait("pe", wgt1, wgt2, comb_toks[-1], p.last("act"))
        for c in range(8):
            t = nc.tensor.matmul(ps_g2[0:8, 0:512], wg_sl[:, c, :],
                                 comb[:, c, tsl], start=(c == 0), stop=False)
        t = nc.tensor.matmul(ps_g2[0:8, 0:512], wg_sl[0:8, 8, :],
                             tail_bf[0:8, tsl], start=False, stop=True)
        mtok = p.tok("pe", t)
        p.wait("act", mtok, bgtk)
        t = nc.scalar.activation(g_f[:, tsl], ps_g2[0:8, 0:512], AF.Sigmoid,
                                 bias=bg_col[0:8, :])
        p.tok("act", t)
    gtk = p.last("act")
    p.wait("dve", gtk)
    t = nc.vector.tensor_tensor(d1_f, g_f, si_f, ALU.mult)
    tk = p.tok("dve", t)
    p.wait("dve", tk)
    t = nc.vector.tensor_tensor(d1_f, si_f, d1_f, ALU.subtract)
    tk = p.tok("dve", t)
    p.wait("dve", tk)
    t = nc.vector.tensor_tensor_scan(states_f, g_f, d1_f, 0.0, ALU.mult, ALU.add)
    tk = p.tok("dve", t)
    p.wait("dve", tk)
    t = nc.vector.tensor_copy(tail_bf[0:8, 0:T], states_f)
    states_tok = p.tok("dve", t)

    bout_tok = vec_to_bcast(p, R, W["l3_bout"], R["bb1"], 1024)
    xres_tok = [None] * NT
    for i in range(NT):
        p.wait("dve", bout_tok)
        t = nc.vector.tensor_tensor(R["x_res"][:, i, :], R["x_res"][:, i, :],
                                    R["bb1"][:], ALU.add)
        xres_tok[i] = p.tok("dve", t)
    guard = p.last("dve")
    for grp, chunks in enumerate([[0, 1, 2, 3], [4, 5, 6, 7], [8]]):
        wts = []
        for j, cch in enumerate(chunks):
            m = 128 if cch < 8 else 8
            wts.append(p.dma(
                "pool", wout_sl[j][0:m, :],
                W["l3_Wout"][cch * 128:cch * 128 + m, :]))
        ogq = {}
        for i in range(NT):
            til = slice(i * 128, (i + 1) * 128)
            for half in range(2):
                q = half
                osl = slice(half * 512, (half + 1) * 512)
                p.wait("pe", *wts, states_tok, comb_toks[-1], guard,
                       ogq.get(q))
                for j, cch in enumerate(chunks):
                    if cch < 8:
                        t = nc.tensor.matmul(ps_c2[q][:, 0:512], comb[:, cch, til],
                                             wout_sl[j][:, osl],
                                             start=(j == 0),
                                             stop=(j == len(chunks) - 1))
                    else:
                        t = nc.tensor.matmul(ps_c2[q][:, 0:512], tail_bf[0:8, til],
                                             wout_sl[j][0:8, osl],
                                             start=True, stop=True)
                mtok = p.tok("pe", t)
                p.wait("dve", mtok, xres_tok[i])
                t = nc.vector.scalar_tensor_tensor(
                    R["x_res"][:, i, osl], ps_c2[q][:, 0:512], 1.0,
                    R["x_res"][:, i, osl], ALU.mult, ALU.add)
                xres_tok[i] = p.tok("dve", t)
                ogq[q] = xres_tok[i]
                guard = xres_tok[i]
        p.barrier()
    p.barrier()


def emit_l3f(p, R, W, cc):
    nc = p.nc
    QB = 128.0
    BOUND = float(np.float32(QB - 1e-5))

    g_tok = vec_to_bcast(p, R, W["l3f_gamma"], R["bb1"], 1024)
    b_tok = vec_to_bcast(p, R, W["l3f_beta"], R["bb2"], 1024)

    xn2 = R["A"].bitcast(BF16).rearrange("p (i d) -> p i d", d=D)
    gmax = R["stats"][:, 48:49]
    t = nc.vector.memset(gmax, 0.0)
    gm_tok = p.tok("dve", t)

    st = R["stats"]
    s1 = st[:, 0:1]
    s2 = st[:, 1:2]
    ac = st[:, 2:3]
    bc = st[:, 3:4]
    tile_toks = [None] * NT
    for i in range(NT):
        xt = R["x_res"][:, i, :]
        p.wait("dve", p.last("act") if i else None, gm_tok)
        t = nc.vector.reduce_sum(s1, xt, axis=AX.X)
        r1 = p.tok("dve", t)
        p.wait("act", p.last("dve"))
        t = nc.scalar.activation(R["tmpsq"][:], xt, AF.Square, accum_out=s2)
        q1 = p.tok("act", t)

        def stats_chain(eps):
            p.wait("dve", p.last("act"))
            t = nc.vector.tensor_scalar_mul(st[:, 0:2], st[:, 0:2], 1.0 / D)
            tk = p.tok("dve", t)
            p.wait("dve", tk)
            t = nc.vector.tensor_tensor(ac, s1, s1, ALU.mult)
            tk = p.tok("dve", t)
            p.wait("dve", tk)
            t = nc.vector.tensor_tensor(ac, s2, ac, ALU.subtract)
            tk = p.tok("dve", t)
            eap, etk = p.ccol(eps)
            p.wait("act", tk, etk)
            t = nc.scalar.activation(ac, ac, AF.Sqrt, bias=eap)
            tk = p.tok("act", t)
            p.wait("dve", tk)
            t = nc.vector.reciprocal(ac, ac)
            tk = p.tok("dve", t)
            p.wait("dve", tk)
            t = nc.vector.scalar_tensor_tensor(bc, s1, -1.0, ac, ALU.mult, ALU.mult)
            return p.tok("dve", t)

        tk = stats_chain(1e-5)
        p.wait("dve", tk, r1, q1)
        t = nc.vector.tensor_scalar(R["tmpf"][:], xt, ac[:, 0:1], bc[:, 0:1],
                                    ALU.mult, ALU.add)
        ln1 = p.tok("dve", t)
        p.wait("dve", ln1)
        t = nc.vector.reduce_sum(s1, R["tmpf"][:], axis=AX.X)
        r2 = p.tok("dve", t)
        p.wait("act", ln1, p.last("dve"))
        t = nc.scalar.activation(R["tmpsq"][:], R["tmpf"][:], AF.Square,
                                 accum_out=s2)
        q2 = p.tok("act", t)
        tk = stats_chain(1e-3)
        p.wait("dve", tk, r2, q2)
        t = nc.vector.tensor_scalar(R["tmpf"][:], R["tmpf"][:], ac[:, 0:1],
                                    bc[:, 0:1], ALU.mult, ALU.add)
        u_tok = p.tok("dve", t)
        p.wait("dve", u_tok, g_tok, b_tok)
        t = nc.vector.tensor_tensor(R["tmpf"][:], R["tmpf"][:], R["bb1"][:], ALU.mult)
        tk = p.tok("dve", t)
        p.wait("dve", tk)
        t = nc.vector.tensor_tensor(xn2[:, i, :], R["tmpf"][:], R["bb2"][:], ALU.add)
        tile_toks[i] = p.tok("dve", t)
        p.wait("dve", tile_toks[i], gm_tok)
        t = nc.vector.reduce_max(s1, xn2[:, i, :], axis=AX.X,
                                 apply_absolute_value=True)
        tk = p.tok("dve", t)
        p.wait("dve", tk)
        t = nc.vector.tensor_tensor(gmax, gmax, s1, ALU.max)
        gm_tok = p.tok("dve", t)

    p.wait("pe", gm_tok)
    t = nc.tensor.transpose(R["ps_row"][0:1, 0:128],
                            gmax.rearrange("p one -> p (one)").unsqueeze(1)
                            if False else gmax, R["ident_f"][:])
    ptk = p.tok("pe", t)
    p.wait("dve", ptk)
    t = nc.vector.reduce_max(R["stats"][0:1, 49:50], R["ps_row"][0:1, 0:128],
                             axis=AX.X)
    tk = p.tok("dve", t)
    p.wait("sp", tk)
    cctk = p.dma("sp", cc["in"], R["stats"][0:1, 49:50])
    p.wait("pool", cctk)
    t = nc.gpsimd.collective_compute("AllReduce", ALU.max,
                                     [list(range(NCORES))],
                                     [cc["in"].opt()], [cc["out"].opt()])
    t.then_inc(p.sem["cc"], 1)
    p.wait("sp", ("cc", 1))
    cctk = p.dma("sp", R["stats"][0:1, 49:50], cc["out"])
    p.wait("dve", cctk)
    t = nc.vector.reciprocal(st[0:1, 50:51], st[0:1, 49:50])
    tk = p.tok("dve", t)
    p.wait("dve", tk)
    t = nc.vector.tensor_scalar_mul(st[0:1, 50:51], st[0:1, 50:51], QB)
    tk = p.tok("dve", t)
    s_tok = bcast_row_to_col(p, R, st[0:1, 50:51], st[:, 51:52])

    wreg = R["Wr"].bitcast(BF16)
    wq = wreg[:, 0:8192].rearrange("p (c n) -> p c n", n=1024)
    wcol = st[:, 52:60]
    for c in range(KC):
        p.wait("sp", p.last("dve"), p.last("act"))
        tk = p.dma("sp", R["tmpf"][:], W["l3f_w"][c * 128:(c + 1) * 128, :])
        p.wait("dve", tk)
        t = nc.vector.reduce_sum(wcol[:, c:c + 1], R["tmpf"][:], axis=AX.X,
                                 apply_absolute_value=True)
        p.tok("dve", t)
    tk = p.last("dve")
    p.wait("dve", tk)
    t = nc.vector.reduce_sum(wcol[:, 0:1], wcol[:, 0:8], axis=AX.X)
    tk = p.tok("dve", t)
    p.wait("pe", tk)
    t = nc.tensor.matmul(R["ps_row"][0:1, 0:1], R["ones_col"][:],
                         wcol[:, 0:1], start=True, stop=True)
    ptk = p.tok("pe", t)
    p.wait("dve", ptk)
    t = nc.vector.tensor_scalar(st[0:1, 60:61], R["ps_row"][0:1, 0:1],
                                1.0 / (D * D), 1e-5, ALU.mult, ALU.add)
    tk = p.tok("dve", t)
    p.wait("dve", tk)
    t = nc.vector.reciprocal(st[0:1, 61:62], st[0:1, 60:61])
    tk = p.tok("dve", t)
    rw_tok = bcast_row_to_col(p, R, st[0:1, 61:62], st[:, 62:63])
    p.wait("dve", rw_tok)
    t = nc.vector.tensor_tensor(st[0:1, 63:64], st[0:1, 49:50], st[0:1, 60:61],
                                ALU.mult)
    tk = p.tok("dve", t)
    p.wait("dve", tk)
    t = nc.vector.tensor_scalar_add(st[0:1, 64:65], st[0:1, 60:61], -1e-5)
    tk = p.tok("dve", t)
    p.wait("dve", tk)
    t = nc.vector.tensor_tensor(st[0:1, 63:64], st[0:1, 49:50], st[0:1, 64:65],
                                ALU.mult)
    tk = p.tok("dve", t)
    p.wait("dve", tk)
    t = nc.vector.tensor_scalar_mul(st[0:1, 63:64], st[0:1, 63:64], 1.0 / QB)
    tk = p.tok("dve", t)
    sc2_tok = bcast_row_to_col(p, R, st[0:1, 63:64], st[:, 65:66])

    for c in range(KC):
        p.wait("sp", p.last("dve"))
        tk = p.dma("sp", R["tmpf"][:], W["l3f_w"][c * 128:(c + 1) * 128, :])
        p.wait("dve", tk, rw_tok)
        t = nc.vector.tensor_scalar(R["tmpf"][:], R["tmpf"][:], st[:, 62:63],
                                    MAGIC, ALU.mult, ALU.add)
        ptk = p.tok("dve", t)
        p.wait("dve", ptk)
        t = nc.vector.tensor_scalar(R["tmpf"][:], R["tmpf"][:], MAGIC, 1.0,
                                    ALU.subtract, ALU.min)
        ptk = p.tok("dve", t)
        p.wait("dve", ptk)
        t = nc.vector.tensor_scalar(wq[:, c, :], R["tmpf"][:], -1.0, None,
                                    ALU.max)
        p.tok("dve", t)
    wq_tok = p.last("dve")

    xqT = R["B"].bitcast(BF16).rearrange("p (c t) -> p c t", t=T)
    R["_fm_evac_tok"] = [None]
    xq_tok = [None] * NT
    prev_p2 = p.last("dve")
    for i in range(NT):
        p.wait("dve", tile_toks[i], s_tok, prev_p2)
        t = nc.vector.tensor_scalar(R["tmpsq"][:], xn2[:, i, :], st[:, 51:52],
                                    BOUND, ALU.mult, ALU.min)
        tk = p.tok("dve", t)
        p.wait("dve", tk)
        t = nc.vector.tensor_scalar(xn2[:, i, :], R["tmpsq"][:], -BOUND, None,
                                    ALU.max)
        tk = p.tok("dve", t)
        prev_p2 = tk
        p.wait("act", tk)
        emit_tile_to_fm(p, R, i, xn2[:, i, :], tk, xqT[:],
                        evac_eng="dve" if i % 2 == 0 else "act")
        xq_tok[i] = R["_fm_evac_tok"][0]

    xres_tok = [None] * NT
    ps_yy = [R["ps_a"], R["ps_b"]]
    ygq = {}
    for i in range(NT):
        for half in range(2):
            q = half
            osl = slice(half * 512, (half + 1) * 512)
            p.wait("pe", wq_tok, xq_tok[-1], sc2_tok, ygq.get(q))
            for c in range(KC):
                t = nc.tensor.matmul(ps_yy[q][:], xqT[:, c, i * 128:(i + 1) * 128],
                                     wq[:, c, osl], start=(c == 0),
                                     stop=(c == KC - 1))
            mtok = p.tok("pe", t)
            p.wait("dve", mtok, xres_tok[i])
            t = nc.vector.scalar_tensor_tensor(
                R["x_res"][:, i, osl], ps_yy[q][:], st[:, 65:66],
                R["x_res"][:, i, osl], ALU.mult, ALU.add)
            xres_tok[i] = p.tok("dve", t)
            ygq[q] = xres_tok[i]
    p.barrier()


def emit_l1f(p, R, W):
    nc = p.nc
    ln_tok = emit_ln_to_xnT(p, R)
    xnT = R["xnT"]
    p.barrier()

    st = R["stats"]
    idxf_all = st[:, 0:NT]
    ps_l = R["ps_row"]
    wgv = R["Wr"].bitcast(BF16)
    wg_sb = R["kv_bf"][:, 0, :].rearrange("p (c n) -> p c n", n=8)
    w1_sb = wgv[:, 0:8192].rearrange("p (c n) -> p c n", n=1024)
    w2_bufs = [R["B"].bitcast(BF16)[:, 0:8192].rearrange("p (c n) -> p c n", n=1024),
               R["B"].bitcast(BF16)[:, 8192:16384].rearrange("p (c n) -> p c n", n=1024)]
    h_fm = R["A"].bitcast(BF16).rearrange("p (c t) -> p c t", t=T)

    wg_tok = p.dma("pool", wg_sb, W["l1_Wg"])
    bg_tok = vec_to_bcast(p, R, W["l1_bg"], R["bb1"][:, 0:8], 8)

    logit_sb = st[:, 40:48]
    top8 = st[:, 48:56]
    idx_u = R["idx_u"]
    idx_col = R["idx_c"]
    b2g_tok = [None] * NT
    for i in range(NT):
        til = slice(i * 128, (i + 1) * 128)
        p.wait("pe", wg_tok, ln_tok, bg_tok, p.last("dve") if i else None)
        for c in range(KC):
            t = nc.tensor.matmul(ps_l[:, 0:8], xnT[:, c, til], wg_sb[:, c, :],
                                 start=(c == 0), stop=(c == KC - 1))
        mtok = p.tok("pe", t)
        p.wait("dve", mtok, bg_tok)
        t = nc.vector.tensor_add(logit_sb, ps_l[:, 0:8], R["bb1"][:, 0:8])
        tk = p.tok("dve", t)
        p.wait("dve", tk)
        t = nc.vector.max(top8, logit_sb)
        tk = p.tok("dve", t)
        p.wait("dve", tk)
        t = nc.vector.max_index(idx_u, top8, logit_sb)
        tk = p.tok("dve", t)
        p.wait("dve", tk)
        t = nc.vector.tensor_copy(idxf_all[:, i:i + 1], idx_u[:, 0:1])
        p.tok("dve", t)
        t = nc.vector.tensor_copy(idx_col[:, i:i + 1], idx_u[:, 0:1])
        tk = p.tok("dve", t)
        p.wait("pool", tk)
        b2g_tok[i] = p.dma_indirect_add(R["x_res"][:, i, :], W["l1_b2"],
                                        idx_col[:, i:i + 1])
    idx_tok = p.last("dve")

    bcols = st[:, 16:24]
    mask_all = st[:, 24:40]
    ps_h = [R["ps_a"], p.bps("ps_h2_l1f", [128, 512], F32)]
    ps_y = [R["ps_b"], p.bps("ps_y2_l1f", [128, 512], F32)]
    w_use = [None, None]
    hguard = None
    sguard = None
    aguard = None
    w1_tok = p.dma("pool", w1_sb, W["l1_W1"][0])
    w2_toks = [p.dma("pool", w2_bufs[0], W["l1_W2"][0]), None]
    for e in range(8):
        w2_sb = w2_bufs[e % 2]
        w2_tok = w2_toks[e % 2]
        if e + 1 < 8:
            p.wait("pool", w_use[1])
            w2_toks[(e + 1) % 2] = p.dma("pool", w2_bufs[(e + 1) % 2],
                                         W["l1_W2"][e + 1])
        b1_tok = vec_to_cols(p, R, W["l1_b1"][e], bcols, 1024)
        p.wait("dve", idx_tok)
        t = nc.vector.tensor_single_scalar(mask_all, idxf_all, float(e),
                                           ALU.is_equal)
        m_tok = p.tok("dve", t)
        agq = [aguard, aguard]
        for ic in range(KC):
            for ts4 in range(4):
                q = ts4 % 2
                tsl = slice(ts4 * 512, (ts4 + 1) * 512)
                p.wait("pe", w1_tok, agq[q], hguard)
                for c in range(KC):
                    t = nc.tensor.matmul(
                        ps_h[q][:], w1_sb[:, c, ic * 128:(ic + 1) * 128],
                        xnT[:, c, tsl], start=(c == 0), stop=(c == KC - 1))
                mtok = p.tok("pe", t)
                p.wait("act", mtok, b1_tok, hguard)
                t = nc.scalar.activation(h_fm[:, ic, tsl], ps_h[q][:], AF.Gelu,
                                         bias=bcols[:, ic:ic + 1])
                agq[q] = p.tok("act", t)
        aguard = p.last("act")
        if e + 1 < 8:
            p.wait("pool", p.last("pe"))
            w1_tok = p.dma("pool", w1_sb, W["l1_W1"][e + 1])
        sgq = [sguard, sguard]
        for i in range(NT):
            til = slice(i * 128, (i + 1) * 128)
            for half in range(2):
                q = half
                osl = slice(half * 512, (half + 1) * 512)
                p.wait("pe", w2_tok, aguard, sgq[q])
                for ic in range(KC):
                    t = nc.tensor.matmul(ps_y[q][:], h_fm[:, ic, til],
                                         w2_sb[:, ic, osl], start=(ic == 0),
                                         stop=(ic == KC - 1))
                mtok = p.tok("pe", t)
                p.wait("dve", mtok, m_tok, b2g_tok[i])
                t = nc.vector.scalar_tensor_tensor(
                    R["x_res"][:, i, osl], ps_y[q][:], mask_all[:, i:i + 1],
                    R["x_res"][:, i, osl], ALU.mult, ALU.add)
                sgq[q] = p.tok("dve", t)
        sguard = p.last("dve")
        hguard = p.last("pe")
        w_use = [hguard, hguard]
    p.barrier()


L1A_STAGE = 5

def emit_l1a(p, R, W):
    nc = p.nc
    ln_tok = emit_ln_to_xnT(p, R)
    xnT = R["xnT"]
    p.barrier()
    st = R["stats"]

    q_fm = R["A"].bitcast(BF16).rearrange("p (c t) -> p c t", t=T)
    k_tm = R["B"].bitcast(BF16).rearrange("p (i d) -> p i d", d=D)
    wsb = R["Wr"].bitcast(BF16)[:, 0:8192].rearrange("p (c n) -> p c n", n=1024)
    ps_c = p.bps("ps_c_l1a", [128, 512], F32)
    ps_d = p.bps("ps_d_l1a", [128, 512], F32)
    ps_p = [ps_c, ps_d]

    bq_c = st[:, 0:8]
    bq1_c = st[:, 8:16]
    btk = vec_to_cols(p, R, W["l1_bq"], bq_c, 1024)
    p.wait("dve", btk)
    t = nc.vector.tensor_scalar_add(bq1_c, bq_c, 1.0)
    b1tk = p.tok("dve", t)
    wtok = p.dma("pool", wsb, W["l1_wqT"])
    dgq = {}
    dguard = None
    for oc in range(KC):
        for ts4 in range(4):
            q = ts4 % 2
            tsl = slice(ts4 * 512, (ts4 + 1) * 512)
            p.wait("pe", wtok, ln_tok, dgq.get(q))
            for c in range(KC):
                t = nc.tensor.matmul(ps_p[q][:], wsb[:, c, oc * 128:(oc + 1) * 128],
                                     xnT[:, c, tsl], start=(c == 0),
                                     stop=(c == KC - 1))
            mtok = p.tok("pe", t)
            p.wait("act", mtok, btk, dgq.get(q))
            t = nc.scalar.activation(R["tmpf"][:, q * 512:q * 512 + 512],
                                     ps_p[q][:], AF.Exp,
                                     bias=bq_c[:, oc:oc + 1])
            eguard = p.tok("act", t)
            p.wait("dve", eguard, b1tk)
            t = nc.vector.scalar_tensor_tensor(
                q_fm[:, oc, tsl], ps_p[q][:], bq1_c[:, oc:oc + 1],
                R["tmpf"][:, q * 512:q * 512 + 512], ALU.add, ALU.min)
            dguard = p.tok("dve", t)
            dgq[q] = dguard
    q_tok = dguard
    if L1A_STAGE <= 1:
        p.barrier()
        return

    bk_tok = vec_to_bcast(p, R, W["l1_bk"], R["bb1"], 1024)
    p.wait("dve", bk_tok)
    t = nc.vector.tensor_scalar_add(R["bb2"][:], R["bb1"][:], 1.0)
    bk1_tok = p.tok("dve", t)
    p.wait("pool", p.last("pe"))
    wtok = p.dma("pool", wsb, W["l1_wkT"])
    kgq = {}
    for i in range(NT):
        til = slice(i * 128, (i + 1) * 128)
        for half in range(2):
            q = half
            osl = slice(half * 512, (half + 1) * 512)
            p.wait("pe", wtok, dguard if i == 0 and half == 0 else None,
                   kgq.get(q))
            for c in range(KC):
                t = nc.tensor.matmul(ps_p[q][:], xnT[:, c, til],
                                     wsb[:, c, osl], start=(c == 0),
                                     stop=(c == KC - 1))
            mtok = p.tok("pe", t)
            p.wait("dve", mtok, bk1_tok, kgq.get(q))
            t = nc.vector.tensor_tensor(R["tmpf"][:, q * 512:q * 512 + 512],
                                        ps_p[q][:], R["bb1"][:, osl], ALU.add)
            dtk = p.tok("dve", t)
            p.wait("act", dtk)
            t = nc.scalar.activation(R["tmpsq"][:, q * 512:q * 512 + 512],
                                     R["tmpf"][:, q * 512:q * 512 + 512], AF.Exp)
            aguard = p.tok("act", t)
            p.wait("dve", aguard)
            t = nc.vector.scalar_tensor_tensor(
                k_tm[:, i, osl], R["tmpf"][:, q * 512:q * 512 + 512], 1.0,
                R["tmpsq"][:, q * 512:q * 512 + 512], ALU.add, ALU.min)
            dguard = p.tok("dve", t)
            kgq[q] = dguard
    k_tok = dguard
    if L1A_STAGE <= 2:
        p.barrier()
        return

    bv_tok = vec_to_bcast(p, R, W["l1_bv"], R["bb1"], 1024)
    p.wait("pool", p.last("pe"))
    wtok = p.dma("pool", wsb, W["l1_wvT"])
    ps_kv = [R["ps_a"], R["ps_b"]]
    ps_ks = R["ps_row"]
    vguard = [None, None]
    vgq = {}
    for i in range(NT):
        til = slice(i * 128, (i + 1) * 128)
        vb = R["tmpbf_a"] if i % 2 == 0 else R["tmpbf_b"]
        for half in range(2):
            q = half
            osl = slice(half * 512, (half + 1) * 512)
            p.wait("pe", wtok, k_tok, q_tok,
                   dguard if i == 0 and half == 0 else None,
                   vguard[i % 2], bv_tok, vgq.get(q))
            for c in range(KC):
                t = nc.tensor.matmul(ps_p[q][:], xnT[:, c, til],
                                     wsb[:, c, osl], start=(c == 0),
                                     stop=(c == KC - 1))
            mtok = p.tok("pe", t)
            p.wait("dve", mtok, bv_tok, vguard[i % 2])
            t = nc.vector.tensor_tensor(vb[:, osl], ps_p[q][:], R["bb1"][:, osl],
                                        ALU.add)
            dguard = p.tok("dve", t)
            vgq[q] = dguard
        p.wait("pe", dguard)
        for h in range(16):
            hb = ps_kv[h // 8]
            hsl = slice((h % 8) * 64, (h % 8) * 64 + 64)
            t = nc.tensor.matmul(hb[0:64, hsl], k_tm[:, i, h * 64:(h + 1) * 64],
                                 vb[:, h * 64:(h + 1) * 64],
                                 start=(i == 0 and h % 8 == 0),
                                 stop=(i == NT - 1 and h % 8 == 7))
            t = nc.tensor.matmul(ps_ks[0:64, h:h + 1],
                                 k_tm[:, i, h * 64:(h + 1) * 64],
                                 R["ones_bfc"][:],
                                 start=(i == 0 and h == 0),
                                 stop=(i == NT - 1 and h == 15))
        vguard[i % 2] = p.tok("pe", t)
    kv_tok = p.last("pe")

    kv_bf = R["kv_bf"]
    ks_bf = R["ks_bf"]
    p.wait("dve", kv_tok)
    for lohi in range(2):
        base = lohi * 64
        t = nc.vector.tensor_copy(
            kv_bf[base:base + 64, 0:8, :],
            ps_kv[0][0:64, :].rearrange("p (h d) -> p h d", d=64))
        p.tok("dve", t)
        t = nc.vector.tensor_copy(
            kv_bf[base:base + 64, 8:16, :],
            ps_kv[1][0:64, :].rearrange("p (h d) -> p h d", d=64))
        p.tok("dve", t)
        t = nc.vector.tensor_copy(ks_bf[base:base + 64, :], ps_ks[0:64, 0:16])
        p.tok("dve", t)
    kvbf_tok = p.last("dve")
    if L1A_STAGE <= 3:
        p.barrier()
        return

    attn_fm = R["B"].bitcast(BF16).rearrange("p (c t) -> p c t", t=T)
    ps_qk = [R["ps_a"], R["ps_b"]]
    ps_n = R["ps_row"]
    R["_fm_evac_tok"] = [None]
    nguard = None
    atoks = []
    for i in range(NT):
        til = slice(i * 128, (i + 1) * 128)
        p.wait("pe", kvbf_tok, nguard)
        for h in range(16):
            hb = (h % 2) * 64
            ch = h // 2
            bank = h % 2
            slot = h // 2
            t = nc.tensor.matmul(
                ps_qk[bank][:, slot * 64:slot * 64 + 64],
                q_fm[hb:hb + 64, ch, til], kv_bf[hb:hb + 64, h, :],
                start=(h < 2), stop=(h >= 14))
            nb = ps_n if bank == 0 else ps_c
            t = nc.tensor.matmul(
                nb[:, slot: slot + 1],
                q_fm[hb:hb + 64, ch, til],
                ks_bf[hb:hb + 64, h:h + 1], start=(h < 2), stop=(h >= 14))
        mtok = p.tok("pe", t)
        p.wait("dve", mtok)
        t = nc.vector.tensor_scalar_max(st[:, 32:40], ps_n[:, 0:8], 1e-6)
        p.tok("dve", t)
        t = nc.vector.tensor_scalar_max(st[:, 40:48], ps_c[:, 0:8], 1e-6)
        tk = p.tok("dve", t)
        p.wait("dve", tk)
        t = nc.vector.reciprocal(st[:, 32:48], st[:, 32:48])
        tk = p.tok("dve", t)
        p.wait("dve", tk)
        vb = R["tmpbf_a"] if i % 2 == 0 else R["tmpbf_b"]
        rb = st[:, 32:48].unsqueeze(2).to_broadcast([128, 16, 64])
        hview0 = ps_qk[0][:].rearrange("p (h d) -> p h d", d=64)
        hview1 = ps_qk[1][:].rearrange("p (h d) -> p h d", d=64)
        if L1A_STAGE >= 4.2:
            vbv = vb[:].rearrange("p (a b d) -> p a b d", b=2, d=64)
            t = nc.vector.tensor_tensor(vbv[:, :, 0, :], hview0,
                                        rb[:, 0:8], ALU.mult)
            p.tok("dve", t)
            t = nc.vector.tensor_tensor(vbv[:, :, 1, :], hview1,
                                        rb[:, 8:16], ALU.mult)
            nguard = p.tok("dve", t)
        else:
            nguard = p.last("dve")
        if L1A_STAGE >= 4.3:
            emit_tile_to_fm(p, R, i, vb[:], nguard, attn_fm[:],
                            evac_eng="dve" if i % 2 == 0 else "act")
            atoks.append(R["_fm_evac_tok"][0])

    if L1A_STAGE <= 4.9:
        p.barrier()
        return
    bo_tok = vec_to_bcast(p, R, W["l1_bo"], R["bb1"], 1024)
    p.wait("pool", p.last("pe"))
    wtok = p.dma("pool", wsb, W["l1_woT"])
    xres_tok = [None] * NT
    for i in range(NT):
        p.wait("dve", bo_tok)
        t = nc.vector.tensor_tensor(R["x_res"][:, i, :], R["x_res"][:, i, :],
                                    R["bb1"][:], ALU.add)
        xres_tok[i] = p.tok("dve", t)
    ygq = {}
    yguard = p.last("dve")
    for i in range(NT):
        til = slice(i * 128, (i + 1) * 128)
        for half in range(2):
            q = half
            osl = slice(half * 512, (half + 1) * 512)
            p.wait("pe", wtok, atoks[-1],
                   yguard if i == 0 and half == 0 else None, ygq.get(q))
            for c in range(KC):
                t = nc.tensor.matmul(ps_p[q][:], attn_fm[:, c, til],
                                     wsb[:, c, osl], start=(c == 0),
                                     stop=(c == KC - 1))
            mtok = p.tok("pe", t)
            p.wait("dve", mtok, xres_tok[i])
            t = nc.vector.scalar_tensor_tensor(
                R["x_res"][:, i, osl], ps_p[q][:], 1.0,
                R["x_res"][:, i, osl], ALU.mult, ALU.add)
            xres_tok[i] = p.tok("dve", t)
            ygq[q] = xres_tok[i]
    p.barrier()


def emit_wquant_to_dram(p, R, W, src_name, dst_dram, nrows, st_beta_col):
    nc = p.nc
    st = R["stats"]
    src = W[src_name]
    ncols = src.shape[1]
    nch = nrows // 128
    nhalf = ncols // 2048 if ncols >= 2048 else 1
    pw = min(2048, ncols)
    wrf = R["Wr"].bitcast(F32)
    slots = [wrf[:, 0:pw], wrf[:, 2048:2048 + pw]]
    outb = [R["A"].bitcast(BF16)[:, 0:pw], R["A"].bitcast(BF16)[:, 2048:2048 + pw]]
    wcol = st[:, 64:65]
    acc = st[:, 65:66]
    t = nc.vector.memset(acc, 0.0)
    atk = p.tok("dve", t)
    rd_guard = [None, None]
    pieces = [(c, hh) for c in range(nch) for hh in range(nhalf)]
    for k, (c, hh) in enumerate(pieces):
        sl = k % 2
        qn = "sp" if sl == 0 else "act"
        p.wait(qn, rd_guard[sl])
        tk = p.dma(qn, slots[sl], src[c * 128:(c + 1) * 128,
                                      hh * 2048:hh * 2048 + pw])
        p.wait("dve", tk, atk)
        t = nc.vector.reduce_sum(wcol, slots[sl], axis=AX.X,
                                 apply_absolute_value=True)
        tk = p.tok("dve", t)
        p.wait("dve", tk)
        t = nc.vector.tensor_tensor(acc, acc, wcol, ALU.add)
        atk = p.tok("dve", t)
        rd_guard[sl] = atk
    p.wait("pe", atk)
    t = nc.tensor.matmul(R["ps_small"][0:1, 0:1], R["ones_col"][:], acc,
                         start=True, stop=True)
    ptk = p.tok("pe", t)
    p.wait("dve", ptk)
    t = nc.vector.tensor_scalar(st[0:1, st_beta_col:st_beta_col + 1],
                                R["ps_small"][0:1, 0:1],
                                1.0 / (nrows * ncols), 1e-5, ALU.mult, ALU.max)
    tk = p.tok("dve", t)
    p.wait("dve", tk)
    t = nc.vector.reciprocal(st[0:1, 66:67], st[0:1, st_beta_col:st_beta_col + 1])
    tk = p.tok("dve", t)
    sc_tok = bcast_row_to_col(p, R, st[0:1, 66:67], st[:, 67:68])
    g1 = [None, None]
    g3 = [None, None]
    wq_guard = None
    for k, (c, hh) in enumerate(pieces):
        sl = k % 2
        qn = "sp" if sl == 0 else "act"
        e1, e2, e3 = ("dve", "dve", "dve")
        E = {"pool": nc.gpsimd, "dve": nc.vector}
        p.wait(qn, g3[sl], sc_tok)
        tk = p.dma(qn, slots[sl], src[c * 128:(c + 1) * 128,
                                      hh * 2048:hh * 2048 + pw])
        p.wait(e1, tk, sc_tok, g3[sl])
        t = E[e1].tensor_scalar(slots[sl], slots[sl], st[:, 67:68], MAGIC,
                                ALU.mult, ALU.add)
        t1 = p.tok(e1, t)
        p.wait(e2, t1)
        t = E[e2].tensor_scalar(slots[sl], slots[sl], MAGIC, 1.0,
                                ALU.subtract, ALU.min)
        t2 = p.tok(e2, t)
        p.wait(e3, t2, g1[sl])
        t = E[e3].tensor_scalar(outb[sl], slots[sl], -1.0, None, ALU.max)
        t3 = p.tok(e3, t)
        g3[sl] = t3
        p.wait(qn, t3)
        wq_guard = p.dma(qn, dst_dram[c * 128:(c + 1) * 128,
                                      hh * 2048:hh * 2048 + pw], outb[sl])
        g1[sl] = wq_guard
    return wq_guard


def emit_l2f(p, R, W, scr):
    nc = p.nc
    st = R["stats"]
    h_dram = scr["h_dram"]
    wqg_dram = scr["wqg_dram"]
    wqu_dram = scr["wqu_dram"]
    wqd_dram = scr["wqd_dram"]
    ps_c = p.bps("ps_c_l2f", [128, 512], F32)

    kT = R["B"].bitcast(BF16).rearrange("p (c t) -> p c t", t=T)
    rs_all = st[:, 16:32]
    s_all = st[:, 0:16]
    R["_fm_evac_tok"] = [None]

    def consume(i, xn, tok):
        scr2 = R["bb2"]
        kbf = R["tmpbf_a"] if i % 2 == 0 else R["tmpbf_b"]
        p.wait("dve", tok)
        t = nc.vector.reduce_max(st[:, 32:33], xn[:], axis=AX.X,
                                 apply_absolute_value=True)
        tk = p.tok("dve", t)
        p.wait("dve", tk)
        t = nc.vector.tensor_scalar(rs_all[:, i:i + 1], st[:, 32:33], 1e-5,
                                    1.0 / 127.0, ALU.max, ALU.mult)
        tk = p.tok("dve", t)
        p.wait("dve", tk)
        t = nc.vector.reciprocal(s_all[:, i:i + 1], rs_all[:, i:i + 1])
        tk = p.tok("dve", t)
        p.wait("dve", tk)
        t = nc.vector.tensor_scalar(scr2[:], xn[:], s_all[:, i:i + 1], MAGIC,
                                    ALU.mult, ALU.add)
        tk = p.tok("dve", t)
        p.wait("dve", tk)
        t = nc.vector.tensor_scalar(kbf[:], scr2[:], MAGIC, None, ALU.subtract)
        tk = p.tok("dve", t)
        p.wait("act", tk)
        emit_tile_to_fm(p, R, i, kbf[:], tk, kT[:],
                        evac_eng="dve" if i % 2 == 0 else "act")
        return R["_fm_evac_tok"][0]

    emit_ln(p, R, consume)
    ln_tok = R["_fm_evac_tok"][0]
    p.barrier()

    emit_wquant_to_dram(p, R, W, "l2_wgT", wqg_dram, 1024, 70)
    p.barrier()
    emit_wquant_to_dram(p, R, W, "l2_wuT", wqu_dram, 1024, 71)
    p.barrier()
    emit_wquant_to_dram(p, R, W, "l2_wdT", wqd_dram, 4096, 72)
    p.barrier()

    bgc = st[:, 33:34]
    buc = st[:, 34:35]
    t1 = bcast_row_to_col(p, R, st[0:1, 70:71], bgc)
    t2 = bcast_row_to_col(p, R, st[0:1, 71:72], buc)
    bg_rs = st[:, 36:52]
    bu_rs = st[:, 52:68]
    p.wait("dve", t1, t2)
    t = nc.vector.tensor_scalar(bg_rs, rs_all, bgc, None, ALU.mult)
    p.tok("dve", t)
    t = nc.vector.tensor_scalar(bu_rs, rs_all, buc, None, ALU.mult)
    sc_tok = p.tok("dve", t)
    amax2 = st[:, 0:16]
    t = nc.vector.memset(amax2, 0.0)
    am_tok = p.tok("dve", t)

    wsl = R["Wr"].bitcast(BF16)
    wq_g = wsl[:, 0:4096].rearrange("p (c n) -> p c n", n=512)
    wq_u = wsl[:, 4096:8192].rearrange("p (c n) -> p c n", n=512)
    ps_g = [R["ps_a"], p.bps("ps_g2_l2f", [128, 512], F32)]
    ps_u = [R["ps_b"], p.bps("ps_u2_l2f", [128, 512], F32)]
    agq = {}
    dgq = {}
    dguard = None
    aguard = None
    hguard = {}
    for hs in range(8):
        p.wait("sp", p.last("pe"))
        g_tok = p.dma("sp", wq_g, wqg_dram[:, hs * 512:(hs + 1) * 512]
                      .rearrange("(c p) n -> p c n", p=128))
        u_tok = p.dma("sp", wq_u, wqu_dram[:, hs * 512:(hs + 1) * 512]
                      .rearrange("(c p) n -> p c n", p=128))
        for i in range(NT):
            q = i % 2
            til = slice(i * 128, (i + 1) * 128)
            p.wait("pe", g_tok, ln_tok, agq.get(q))
            for c in range(KC):
                t = nc.tensor.matmul(ps_g[q][:], kT[:, c, til], wq_g[:, c, :],
                                     start=(c == 0), stop=(c == KC - 1))
            gm = p.tok("pe", t)
            p.wait("pe", u_tok, dgq.get(q))
            for c in range(KC):
                t = nc.tensor.matmul(ps_u[q][:], kT[:, c, til], wq_u[:, c, :],
                                     start=(c == 0), stop=(c == KC - 1))
            um = p.tok("pe", t)
            p.wait("act", gm, sc_tok, dgq.get(q))
            t = nc.scalar.activation(R["tmpf"][:, q * 512:q * 512 + 512],
                                     ps_g[q][:], AF.Silu,
                                     scale=bg_rs[:, i:i + 1])
            agq[q] = p.tok("act", t)
            hb = R["tmpbf_a"] if i % 2 == 0 else R["tmpbf_b"]
            p.wait("dve", um, agq[q], am_tok, hguard.get(i % 2))
            t = nc.vector.scalar_tensor_tensor(hb[:, 0:512], ps_u[q][:],
                                               bu_rs[:, i:i + 1],
                                               R["tmpf"][:, q * 512:q * 512 + 512],
                                               ALU.mult, ALU.mult)
            dguard = p.tok("dve", t)
            dgq[q] = dguard
            p.wait("dve", dguard)
            t = nc.vector.reduce_max(st[:, 32:33], hb[:, 0:512], axis=AX.X,
                                     apply_absolute_value=True)
            tk = p.tok("dve", t)
            p.wait("dve", tk)
            t = nc.vector.tensor_tensor(amax2[:, i:i + 1], amax2[:, i:i + 1],
                                        st[:, 32:33], ALU.max)
            am_tok = p.tok("dve", t)
            p.wait("sp", am_tok)
            hguard[i % 2] = p.dma(
                "sp", h_dram.rearrange("(i p) n -> i p n", p=128)[i][:, hs * 512:(hs + 1) * 512],
                hb[:, 0:512])
    p.barrier()

    bdc = st[:, 33:34]
    t1 = bcast_row_to_col(p, R, st[0:1, 72:73], bdc)
    s2_all = st[:, 16:32]
    bd_rs2 = st[:, 36:52]
    p.wait("dve", t1)
    t = nc.vector.tensor_scalar(s2_all, amax2, 1e-5, 1.0 / 127.0, ALU.max, ALU.mult)
    tk = p.tok("dve", t)
    p.wait("dve", tk)
    t = nc.vector.tensor_scalar(bd_rs2, s2_all, bdc, None, ALU.mult)
    sc2_tok = p.tok("dve", t)
    p.wait("dve", sc2_tok)
    t = nc.vector.reciprocal(s2_all, s2_all)
    s2_tok = p.tok("dve", t)

    k2T = R["A"].bitcast(BF16)[:, 0:8192].rearrange("p (c t) -> p c t", t=T)
    wq_d = wsl[:, 0:4096].rearrange("p (c n) -> p c n", n=1024)
    ps_cc = [ps_c, ps_g[1]]
    xres_tok = [None] * NT
    ygq = {}
    yguard = None
    trguard = None
    for hg in range(8):
        p.wait("sp", p.last("pe"))
        d_tok = p.dma("sp", wq_d, wqd_dram[hg * 512:(hg + 1) * 512, :]
                      .rearrange("(c p) n -> p c n", p=128))
        for i in range(NT):
            hb = R["tmpbf_a"] if i % 2 == 0 else R["tmpbf_b"]
            p.wait("sp", p.last("dve"), p.last("pe"))
            htk = p.dma("sp", hb[:, 0:512],
                        h_dram.rearrange("(i p) n -> i p n", p=128)[i][:, hg * 512:(hg + 1) * 512])
            p.wait("dve", htk, s2_tok, trguard)
            t = nc.vector.tensor_scalar(R["tmpf"][:, 0:512], hb[:, 0:512],
                                        s2_all[:, i:i + 1], MAGIC,
                                        ALU.mult, ALU.add)
            tk = p.tok("dve", t)
            p.wait("dve", tk)
            t = nc.vector.tensor_scalar(hb[:, 512:1024], R["tmpf"][:, 0:512],
                                        MAGIC, None, ALU.subtract)
            tk = p.tok("dve", t)
            p.wait("pe", tk, trguard)
            for j in range(4):
                t = nc.tensor.transpose(
                    R["ps_tr"][:, j * 128:(j + 1) * 128],
                    hb[:, 512 + j * 128:512 + (j + 1) * 128], R["ident"][:])
            ptk = p.tok("pe", t)
            eng = "dve" if i % 2 == 0 else "act"
            p.wait(eng, ptk)
            if eng == "dve":
                t = nc.vector.tensor_copy(
                    k2T[:, 0:4, i * 128:(i + 1) * 128],
                    R["ps_tr"][:, 0:512].rearrange("p (c q) -> p c q", q=128))
            else:
                t = nc.scalar.copy(
                    k2T[:, 0:4, i * 128:(i + 1) * 128],
                    R["ps_tr"][:, 0:512].rearrange("p (c q) -> p c q", q=128))
            trguard = p.tok(eng, t)
        for i in range(NT):
            til = slice(i * 128, (i + 1) * 128)
            for half in range(2):
                q = half
                osl = slice(half * 512, (half + 1) * 512)
                p.wait("pe", d_tok, trguard, ygq.get(q))
                for j in range(4):
                    t = nc.tensor.matmul(ps_cc[q][:], k2T[:, j, til],
                                         wq_d[:, j, osl], start=(j == 0),
                                         stop=(j == 3))
                mtok = p.tok("pe", t)
                p.wait("dve", mtok, xres_tok[i], sc2_tok)
                t = nc.vector.scalar_tensor_tensor(
                    R["x_res"][:, i, osl], ps_cc[q][:], bd_rs2[:, i:i + 1],
                    R["x_res"][:, i, osl], ALU.mult, ALU.add)
                xres_tok[i] = p.tok("dve", t)
                ygq[q] = xres_tok[i]
                yguard = xres_tok[i]
    p.barrier()


def emit_l0a(p, R, W, scr):
    nc = p.nc
    st = R["stats"]
    q_dram = scr["q_dram"]
    H = 16
    aux = R["aux"]
    auxf = aux.bitcast(F32)
    qselT = aux[:, 0:160]
    attnT = aux[:, 160:320]
    ao_sb = aux[:, 320:1344].rearrange("p (h d) -> p h d", d=64)
    t32i = auxf[0:32, 672:704]
    t32o = auxf[0:32, 704:736]
    wrf = R["Wr"].bitcast(F32)
    qs_byh = wrf[0:16, 0:2048]
    qs_scr = wrf[0:16, 2048:4096]
    iota2k = wrf[0:10, 0:2048]
    qs_all = R["bb2"][:, 0:256].rearrange("p (i h) -> p i h", h=16)

    ln_tok = emit_ln_to_xnT(p, R)
    xnT = R["xnT"]
    p.barrier()

    wsb = R["Wr"].bitcast(BF16)[:, 0:8192].rearrange("p (c n) -> p c n", n=1024)
    ps_a = R["ps_a"]
    ps_b = R["ps_b"]

    bq_tok = vec_to_bcast(p, R, W["l0_bq"], R["bb1"], 1024)
    p.wait("pool", p.last("pe"))
    wq_tok = p.dma("pool", wsb, W["l0_wqT"])
    eps9, eps9_tok = p.ccol(1e-9)
    dguard = [None, None]
    pguard = [None, None]
    Af = R["A"].bitcast(F32)
    ps_big = p.bps("ps_big_l0a", [128, 2048], F32)
    qdma = [None] * NT
    for i in range(NT):
        qq = i % 2
        til = slice(i * 128, (i + 1) * 128)
        ps = [ps_a, ps_b] if qq == 0 else [ps_big[:, 0:512], ps_big[:, 512:1024]]
        QF = R["tmpf"][:] if qq == 0 else Af[:, 0:1024]
        QS = R["tmpsq"][:] if qq == 0 else Af[:, 1024:2048]
        stq = R["stats"] if qq == 0 else R["bb2"][:, 256:336]
        cbf = R["tmpbf_a"] if qq == 0 else R["tmpbf_b"]
        p.wait("pe", wq_tok, ln_tok, dguard[qq], p.last("act") if i < 2 else None)
        for half in range(2):
            osl = slice(half * 512, (half + 1) * 512)
            for c in range(KC):
                t = nc.tensor.matmul(ps[half][:], xnT[:, c, til],
                                     wsb[:, c, osl], start=(c == 0),
                                     stop=(c == KC - 1))
        mtok = p.tok("pe", t)
        p.wait("dve", mtok, bq_tok, pguard[qq], dguard[qq])
        t = nc.vector.tensor_tensor(QF[:, 0:512], ps[0][:],
                                    R["bb1"][:, 0:512], ALU.add)
        p.tok("dve", t)
        t = nc.vector.tensor_tensor(QF[:, 512:1024], ps[1][:],
                                    R["bb1"][:, 512:1024], ALU.add)
        qtk = p.tok("dve", t)
        p.wait("pool", qtk, qdma[i - 2] if i >= 2 else None)
        t = nc.gpsimd.tensor_copy(cbf[:], QF[:])
        ctk = p.tok("pool", t)
        pguard[qq] = ctk
        p.wait("sp", ctk)
        qdma[i] = p.dma("sp", q_dram.rearrange("(i p) d -> i p d", p=128)[i],
                        cbf[:])
        q3 = QF[:].rearrange("p (h d) -> p h d", d=64)
        p.wait("dve", qtk)
        t = nc.vector.reduce_sum(stq[:, 0:16], q3, axis=AX.X)
        s1tk = p.tok("dve", t)
        p.wait("act", qtk)
        t = nc.scalar.activation(QS[:], QF[:], AF.Square)
        sqt = p.tok("act", t)
        p.wait("dve", sqt)
        t = nc.vector.reduce_sum(stq[:, 16:32],
                                 QS[:].rearrange("p (h d) -> p h d", d=64),
                                 axis=AX.X)
        s2tk = p.tok("dve", t)
        p.wait("dve", s2tk)
        t = nc.vector.reduce_max(stq[:, 32:48], q3, axis=AX.X)
        mtk = p.tok("dve", t)
        p.wait("dve", mtk)
        mb = stq[:, 32:48].unsqueeze(2).to_broadcast([128, 16, 64])
        t = nc.vector.tensor_tensor(
            QS[:].rearrange("p (h d) -> p h d", d=64), q3, mb,
            ALU.subtract)
        dtk = p.tok("dve", t)
        p.wait("act", dtk, pguard[qq])
        t = nc.scalar.activation(QF[:], QS[:], AF.Exp)
        etk = p.tok("act", t)
        p.wait("dve", etk)
        t = nc.vector.reduce_sum(stq[:, 48:64],
                                 QF[:].rearrange("p (h d) -> p h d", d=64),
                                 axis=AX.X)
        stk = p.tok("dve", t)
        p.wait("dve", stk)
        t = nc.vector.reciprocal(stq[:, 64:80], stq[:, 48:64])
        rtk = p.tok("dve", t)
        p.wait("dve", rtk)
        rb = stq[:, 64:80].unsqueeze(2).to_broadcast([128, 16, 64])
        t = nc.vector.tensor_tensor(
            QF[:].rearrange("p (h d) -> p h d", d=64),
            QF[:].rearrange("p (h d) -> p h d", d=64), rb, ALU.mult)
        ptk2 = p.tok("dve", t)
        p.wait("act", ptk2, eps9_tok)
        t = nc.scalar.activation(QS[:], QF[:], AF.Ln,
                                 bias=eps9)
        ltk = p.tok("act", t)
        p.wait("dve", ltk)
        t = nc.vector.tensor_tensor(QS[:], QF[:], QS[:],
                                    ALU.mult)
        ptk3 = p.tok("dve", t)
        p.wait("dve", ptk3)
        t = nc.vector.reduce_sum(stq[:, 48:64],
                                 QS[:].rearrange("p (h d) -> p h d", d=64),
                                 axis=AX.X, negate=True)
        enttk = p.tok("dve", t)
        p.wait("act", s2tk)
        t = nc.scalar.activation(stq[:, 64:80], stq[:, 16:32], AF.Sqrt)
        l2tk = p.tok("act", t)
        p.wait("dve", s1tk, l2tk, enttk)
        t = nc.vector.tensor_tensor(qs_all[:, i, :], stq[:, 0:16], stq[:, 0:16],
                                    ALU.mult)
        tk = p.tok("dve", t)
        p.wait("dve", tk)
        t = nc.vector.tensor_scalar(qs_all[:, i, :], qs_all[:, i, :],
                                    -1.0 / 64.0, None, ALU.mult)
        tk = p.tok("dve", t)
        p.wait("dve", tk)
        t = nc.vector.tensor_tensor(qs_all[:, i, :], stq[:, 16:32],
                                    qs_all[:, i, :], ALU.add)
        tk = p.tok("dve", t)
        p.wait("dve", tk)
        t = nc.vector.tensor_scalar(qs_all[:, i, :], qs_all[:, i, :],
                                    0.2 / 63.0, None, ALU.mult)
        tk = p.tok("dve", t)
        p.wait("dve", tk)
        t = nc.vector.scalar_tensor_tensor(stq[:, 64:80], stq[:, 64:80], 0.5,
                                           qs_all[:, i, :], ALU.mult, ALU.add)
        tk = p.tok("dve", t)
        p.wait("dve", tk)
        t = nc.vector.scalar_tensor_tensor(qs_all[:, i, :], stq[:, 48:64], 0.3,
                                           stq[:, 64:80], ALU.mult, ALU.add)
        dguard[qq] = p.tok("dve", t)

    K_fm = R["A"].bitcast(BF16).rearrange("p (c t) -> p c t", t=T)
    V_tm = R["B"].bitcast(BF16).rearrange("p (i d) -> p i d", d=D)
    bkc = st[:, 0:8]
    bk_tok = vec_to_cols(p, R, W["l0_bk"], bkc, 1024)
    p.wait("pool", p.last("pe"), dguard[0], dguard[1])
    wk_tok = p.dma("pool", wsb, W["l0_wkT"])
    kguard = None
    for oc in range(KC):
        for ts4 in range(4):
            tsl = slice(ts4 * 512, (ts4 + 1) * 512)
            p.wait("pe", wk_tok, kguard)
            for c in range(KC):
                t = nc.tensor.matmul(ps_a[:], wsb[:, c, oc * 128:(oc + 1) * 128],
                                     xnT[:, c, tsl], start=(c == 0),
                                     stop=(c == KC - 1))
            mtok = p.tok("pe", t)
            p.wait("dve", mtok, bk_tok)
            t = nc.vector.tensor_scalar_add(K_fm[:, oc, tsl], ps_a[:],
                                            bkc[:, oc:oc + 1])
            kguard = p.tok("dve", t)
    bv_tok = vec_to_bcast(p, R, W["l0_bv"], R["bb1"], 1024)
    p.wait("pool", p.last("pe"), kguard)
    wv_tok = p.dma("pool", wsb, W["l0_wvT"])
    vguard = None
    for i in range(NT):
        til = slice(i * 128, (i + 1) * 128)
        for half in range(2):
            osl = slice(half * 512, (half + 1) * 512)
            p.wait("pe", wv_tok, vguard, bv_tok)
            for c in range(KC):
                t = nc.tensor.matmul(ps_a[:], xnT[:, c, til],
                                     wsb[:, c, osl], start=(c == 0),
                                     stop=(c == KC - 1))
            mtok = p.tok("pe", t)
            p.wait("dve", mtok, bv_tok)
            t = nc.vector.tensor_tensor(V_tm[:, i, osl], ps_a[:],
                                        R["bb1"][:, osl], ALU.add)
            vguard = p.tok("dve", t)

    wr_guard = p.last("pe")
    tguard = None
    for i in range(NT):
        p.wait("pe", dguard[0], dguard[1], wr_guard, tguard)
        t = nc.tensor.transpose(R["ps_row"][0:16, 0:128], qs_all[:, i, :],
                                R["ident_f"][:])
        ptk = p.tok("pe", t)
        p.wait("dve", ptk)
        t = nc.vector.tensor_copy(qs_byh[:, i * 128:(i + 1) * 128],
                                  R["ps_row"][0:16, 0:128])
        tguard = p.tok("dve", t)
    top8 = st[0:16, 16:24]
    next8 = st[0:16, 24:32]
    p.wait("dve", tguard)
    t = nc.vector.max(top8, qs_byh)
    tk = p.tok("dve", t)
    p.wait("dve", tk)
    t = nc.vector.max_index(R["idx_u"][0:16, :], top8, qs_byh)
    tk = p.tok("dve", t)
    p.wait("dve", tk)
    t = nc.vector.tensor_copy(st[0:16, 32:40], R["idx_u"][0:16, :])
    tk1 = p.tok("dve", t)
    p.wait("dve", tk1)
    t = nc.vector.match_replace(qs_scr, top8, qs_byh, -1e30)
    tk = p.tok("dve", t)
    p.wait("dve", tk)
    t = nc.vector.max(next8, qs_scr)
    tk = p.tok("dve", t)
    p.wait("dve", tk)
    t = nc.vector.max_index(R["idx_u"][0:16, :], next8, qs_byh)
    tk = p.tok("dve", t)
    p.wait("dve", tk)
    t = nc.vector.tensor_copy(st[0:16, 40:42], R["idx_u"][0:16, 0:2])
    idxf_tok = p.tok("dve", t)
    t = nc.vector.reduce_sum(st[0:16, 44:45], qs_byh, axis=AX.X)
    s1tk = p.tok("dve", t)
    p.wait("act", tguard, idxf_tok)
    t = nc.scalar.activation(qs_scr, qs_byh, AF.Square,
                             accum_out=st[0:16, 45:46])
    s2tk = p.tok("act", t)
    p.wait("dve", s1tk, s2tk)
    t = nc.vector.tensor_scalar_mul(st[0:16, 46:47], st[0:16, 44:45], 1.0 / T)
    tk = p.tok("dve", t)
    p.wait("dve", tk)
    t = nc.vector.tensor_tensor(st[0:16, 47:48], st[0:16, 44:45],
                                st[0:16, 46:47], ALU.mult)
    tk = p.tok("dve", t)
    p.wait("dve", tk)
    t = nc.vector.tensor_tensor(st[0:16, 47:48], st[0:16, 45:46],
                                st[0:16, 47:48], ALU.subtract)
    tk = p.tok("dve", t)
    p.wait("dve", tk)
    t = nc.vector.tensor_scalar_mul(st[0:16, 47:48], st[0:16, 47:48],
                                    1.0 / (T - 1))
    tk = p.tok("dve", t)
    p.wait("act", tk)
    t = nc.scalar.activation(st[0:16, 47:48], st[0:16, 47:48], AF.Sqrt)
    tk = p.tok("act", t)
    p.wait("dve", tk)
    t = nc.vector.tensor_scalar_add(st[0:16, 46:47], st[0:16, 46:47], 1e-6)
    tk = p.tok("dve", t)
    p.wait("dve", tk)
    t = nc.vector.reciprocal(st[0:16, 46:47], st[0:16, 46:47])
    tk = p.tok("dve", t)
    p.wait("dve", tk)
    t = nc.vector.tensor_tensor(st[0:16, 47:48], st[0:16, 47:48],
                                st[0:16, 46:47], ALU.mult)
    tk = p.tok("dve", t)
    p.wait("dve", tk)
    t = nc.vector.tensor_scalar(st[0:16, 47:48], st[0:16, 47:48], 10.0, MAGIC,
                                ALU.mult, ALU.add)
    tk = p.tok("dve", t)
    p.wait("dve", tk)
    t = nc.vector.tensor_scalar(st[0:16, 47:48], st[0:16, 47:48], MAGIC, 10.0,
                                ALU.subtract, ALU.min)
    tk = p.tok("dve", t)
    p.wait("dve", tk)
    t = nc.vector.tensor_scalar_max(st[0:16, 47:48], st[0:16, 47:48], 3.0)
    ftok = p.tok("dve", t)
    p.wait("pe", ftok)
    t = nc.tensor.matmul(R["ps_small"][0:16, 0:1], R["ones_f"][0:1, 0:16],
                         st[0:1, 47:48], start=True, stop=True)
    ptk = p.tok("pe", t)
    p.wait("dve", ptk)
    t = nc.vector.tensor_copy(st[0:16, 46:47], R["ps_small"][0:16, 0:1])
    utok = p.tok("dve", t)
    t = nc.gpsimd.iota(st[0:16, 48:58], [[1, 10]], channel_multiplier=0,
                       allow_small_or_imprecise_dtypes=True)
    iotok = p.tok("pool", t)
    p.wait("dve", utok, iotok)
    t = nc.vector.tensor_single_scalar(st[0:16, 58:68], st[0:16, 48:58],
                                       st[0:16, 46:47], ALU.is_lt)
    mtok = p.tok("dve", t)
    p.wait("dve", mtok, idxf_tok, tk1)
    t = nc.vector.tensor_scalar_add(st[0:16, 68:78], st[0:16, 32:42], -2048.0)
    tk = p.tok("dve", t)
    p.wait("dve", tk)
    t = nc.vector.tensor_tensor(st[0:16, 68:78], st[0:16, 68:78],
                                st[0:16, 58:68], ALU.mult)
    tk = p.tok("dve", t)
    p.wait("dve", tk)
    t = nc.vector.tensor_scalar_add(st[0:16, 68:78], st[0:16, 68:78], 2048.0)
    sidx_tok = p.tok("dve", t)
    p.wait("dve", sidx_tok)
    t = nc.vector.memset(t32i, 2048.0)
    tk = p.tok("dve", t)
    p.wait("dve", tk)
    t = nc.vector.tensor_copy(t32i[0:16, 0:10], st[0:16, 68:78])
    tk = p.tok("dve", t)
    p.wait("dve", tk)
    t = nc.vector.transpose(t32o, t32i)
    tk = p.tok("dve", t)
    p.wait("dve", tk)
    t = nc.vector.tensor_copy(st[0:10, 48:64], t32o[0:10, 0:16])
    sidxT_tok = p.tok("dve", t)
    p.wait("dve", sidxT_tok)
    t = nc.vector.tensor_copy(t32i[0:16, 0:10], st[0:16, 32:42])
    tk = p.tok("dve", t)
    p.wait("dve", tk)
    t = nc.vector.transpose(t32o, t32i)
    tk = p.tok("dve", t)
    p.wait("dve", tk)
    t = nc.vector.tensor_copy(R["idx_c"][0:10, :], t32o[0:10, 0:16])
    gidx_tok = p.tok("dve", t)

    qsel_tok = None
    tr_tok = None
    for h in range(H):
        hb = (h % 2) * 64
        p.wait("pool", gidx_tok, tr_tok, *(qdma if h == 0 else []))
        qg = p.dma_gather_rows(R["smallbf"][0:10, 1024:2048], q_dram,
                               R["idx_c"][0:10, h:h + 1])
        p.wait("pe", qg, qsel_tok)
        t = nc.tensor.transpose(R["ps_tr"][0:64, 0:10],
                                R["smallbf"][0:10, 1024 + h * 64:1024 + (h + 1) * 64],
                                R["ident"][0:10, 0:10])
        tr_tok = p.tok("pe", t)
        p.wait("dve", tr_tok)
        t = nc.vector.tensor_copy(aux[hb:hb + 64, h * 10:(h + 1) * 10],
                                  R["ps_tr"][0:64, 0:10])
        qsel_tok = p.tok("dve", t)

    ps_sc = ps_big
    att = R["smallbf"][0:10, 0:2048]
    aoguard = None
    attT_guard = None
    for h in range(H):
        hb = (h % 2) * 64
        ch = h // 2
        p.wait("pe", qsel_tok, p.last("dve"), p.last("act"))
        for n4 in range(4):
            nsl = slice(n4 * 512, (n4 + 1) * 512)
            t = nc.tensor.matmul(ps_sc[0:10, nsl],
                                 aux[hb:hb + 64, h * 10:(h + 1) * 10],
                                 K_fm[hb:hb + 64, ch, nsl],
                                 start=True, stop=True)
        sctok = p.tok("pe", t)
        p.wait("dve", sctok)
        t = nc.vector.reduce_max(st[0:10, 78:79], ps_sc[0:10, :], axis=AX.X,
                                 negate=True)
        tk = p.tok("dve", t)
        p.wait("dve", tk)
        t = nc.vector.tensor_scalar_mul(st[0:10, 78:79], st[0:10, 78:79], 0.125)
        tk = p.tok("dve", t)
        p.wait("act", tk, aoguard)
        t = nc.scalar.activation(att, ps_sc[0:10, :], AF.Exp,
                                 bias=st[0:10, 78:79], scale=0.125,
                                 accum_out=st[0:10, 79:80])
        etk = p.tok("act", t)
        p.wait("dve", etk)
        t = nc.vector.reciprocal(st[0:10, 78:79], st[0:10, 79:80])
        rtk = p.tok("dve", t)
        p.wait("pe", etk, attT_guard)
        for j in range(NT):
            t = nc.tensor.transpose(R["ps_tr"][:, j * 10:(j + 1) * 10],
                                    att[:, j * 128:(j + 1) * 128],
                                    R["ident"][0:10, 0:10])
        ttk = p.tok("pe", t)
        p.wait("dve", ttk)
        t = nc.vector.tensor_copy(aux[:, 160:320], R["ps_tr"][:, 0:160])
        atk = p.tok("dve", t)
        attT_guard = atk
        p.wait("pe", atk, aoguard)
        for j in range(NT):
            t = nc.tensor.matmul(R["ps_row"][0:10, 0:64],
                                 aux[:, 160 + j * 10:160 + (j + 1) * 10],
                                 V_tm[:, j, h * 64:(h + 1) * 64],
                                 start=(j == 0), stop=(j == NT - 1))
        vtk = p.tok("pe", t)
        p.wait("dve", vtk, rtk)
        t = nc.vector.tensor_scalar_mul(ao_sb[0:10, h, :],
                                        R["ps_row"][0:10, 0:64],
                                        st[0:10, 78:79])
        aoguard = p.tok("dve", t)

    p.wait("pool", p.last("dve"), p.last("pe"))
    t = nc.gpsimd.iota(iota2k, [[1, 2048]], channel_multiplier=0,
                       allow_small_or_imprecise_dtypes=True)
    io_tok = p.tok("pool", t)
    full_fm = R["xnT"]
    ohguard = None
    evguard = None
    for pair in range(8):
        for par in range(2):
            h = 2 * pair + par
            psx = ps_a if par == 0 else ps_b
            p.wait("dve", io_tok, sidxT_tok, ohguard, aoguard)
            t = nc.vector.tensor_single_scalar(att, iota2k,
                                               st[0:10, 48 + h:49 + h],
                                               ALU.is_equal)
            ohtok = p.tok("dve", t)
            for n4 in range(4):
                nsl = slice(n4 * 512, (n4 + 1) * 512)
                p.wait("pe", ohtok, evguard)
                t = nc.tensor.matmul(psx[0:64, :], ao_sb[0:10, h, :],
                                     att[:, nsl], start=True, stop=True)
                mtok = p.tok("pe", t)
                eng = "dve" if n4 % 2 == 0 else "act"
                p.wait(eng, mtok)
                if eng == "dve":
                    t = nc.vector.tensor_copy(
                        full_fm[par * 64:par * 64 + 64, pair, nsl], psx[0:64, :])
                else:
                    t = nc.scalar.copy(
                        full_fm[par * 64:par * 64 + 64, pair, nsl], psx[0:64, :])
                evguard = p.tok(eng, t)
            ohguard = evguard

    bo_tok = vec_to_bcast(p, R, W["l0_bo"], R["bb1"], 1024)
    p.wait("pool", p.last("pe"), p.last("dve"), p.last("act"))
    wo_tok = p.dma("pool", wsb, W["l0_woT"])
    xres_tok = [None] * NT
    for i in range(NT):
        p.wait("dve", bo_tok)
        t = nc.vector.tensor_tensor(R["x_res"][:, i, :], R["x_res"][:, i, :],
                                    R["bb1"][:], ALU.add)
        xres_tok[i] = p.tok("dve", t)
    yguard = p.last("dve")
    for i in range(NT):
        til = slice(i * 128, (i + 1) * 128)
        for half in range(2):
            osl = slice(half * 512, (half + 1) * 512)
            p.wait("pe", wo_tok, evguard, yguard)
            for c in range(KC):
                t = nc.tensor.matmul(ps_a[:], full_fm[:, c, til],
                                     wsb[:, c, osl], start=(c == 0),
                                     stop=(c == KC - 1))
            mtok = p.tok("pe", t)
            p.wait("dve", mtok, xres_tok[i])
            t = nc.vector.scalar_tensor_tensor(
                R["x_res"][:, i, osl], ps_a[:], 1.0,
                R["x_res"][:, i, osl], ALU.mult, ALU.add)
            xres_tok[i] = p.tok("dve", t)
            yguard = xres_tok[i]
    p.barrier()


def emit_l2a(p, R, W, scr):
    nc = p.nc
    st = R["stats"]
    R["cmask"] = R["Wr"].bitcast(F32)[:, 1536:1664]
    v_dram = scr["v_dram"]
    ln_tok = emit_ln_to_xnT(p, R)
    xnT = R["xnT"]
    p.barrier()

    Q_fm = R["A"].bitcast(BF16).rearrange("p (c t) -> p c t", t=T)
    K_fm = R["B"].bitcast(BF16).rearrange("p (c t) -> p c t", t=T)
    wsl = R["Wr"].bitcast(BF16)
    w_sl = [wsl[:, 0:1024].rearrange("p (c n) -> p c n", n=128),
            wsl[:, 1024:2048].rearrange("p (c n) -> p c n", n=128)]
    ps_a = R["ps_a"]
    ps_b = R["ps_b"]

    bc_tok = vec_to_cols(p, R, W["l2_battn"][0:1024], st[:, 0:8], 1024)
    bc_tok = vec_to_cols(p, R, W["l2_battn"][1024:2048], st[:, 8:16], 1024)
    bc_tok = vec_to_cols(p, R, W["l2_battn"][2048:3072], st[:, 16:24], 1024)
    bcols = st[:, 0:24]

    wtok = [None, None]
    slot_use = [None, None]
    wtok[0] = p.dma("pool", w_sl[0], W["l2_wattn"][0])
    vstrip = R["tmpsq"].bitcast(BF16).rearrange("p (j d) -> p j d", d=128)
    eguard = None
    vguard = None
    vs_tok = None
    for oc in range(24):
        par = oc % 2
        if oc + 1 < 24:
            p.wait("pool", slot_use[(oc + 1) % 2])
            wtok[(oc + 1) % 2] = p.dma(
                "pool", w_sl[(oc + 1) % 2], W["l2_wattn"][oc + 1])
        for ts4 in range(4):
            tsl = slice(ts4 * 512, (ts4 + 1) * 512)
            p.wait("pe", wtok[par], ln_tok, eguard)
            for c in range(KC):
                t = nc.tensor.matmul(ps_a[:], w_sl[par][:, c, :],
                                     xnT[:, c, tsl], start=(c == 0),
                                     stop=(c == KC - 1))
            mtok = p.tok("pe", t)
            slot_use[par] = mtok
            p.wait("dve", mtok, bc_tok)
            if oc < 8:
                t = nc.vector.tensor_scalar_add(Q_fm[:, oc, tsl], ps_a[:],
                                                bcols[:, oc:oc + 1])
                eguard = p.tok("dve", t)
            elif oc < 16:
                t = nc.vector.tensor_scalar_add(K_fm[:, oc - 8, tsl], ps_a[:],
                                                bcols[:, oc:oc + 1])
                eguard = p.tok("dve", t)
            else:
                p.wait("dve", vguard)
                vb = R["tmpbf_a"]
                t = nc.vector.tensor_scalar_add(vb[:, 0:512], ps_a[:],
                                                bcols[:, oc:oc + 1])
                eguard = p.tok("dve", t)
                p.wait("pe", eguard, vs_tok)
                for jj in range(4):
                    t = nc.tensor.transpose(
                        R["ps_tr"][:, jj * 128:(jj + 1) * 128],
                        vb[:, jj * 128:(jj + 1) * 128], R["ident"][:])
                ptk = p.tok("pe", t)
                p.wait("dve", ptk)
                t = nc.vector.tensor_copy(vstrip[:, ts4 * 4:ts4 * 4 + 4, :],
                                          R["ps_tr"][:, 0:512]
                                          .rearrange("p (j q) -> p j q", q=128))
                vguard = p.tok("dve", t)
        if oc >= 16:
            voc = oc - 16
            p.wait("sp", vguard)
            vs_tok = p.dma(
                "sp",
                v_dram.rearrange("(j p) d -> p j d", p=128)[:, :, voc * 128:(voc + 1) * 128],
                vstrip[:])
    p.barrier()

    t = nc.gpsimd.memset(R["tmpf"][:, 0:128], 0.0)
    tk = p.tok("pool", t)
    p.wait("pool", tk)
    t = nc.gpsimd.affine_select(R["cmask"], R["tmpf"][:, 0:128], [[-1, 128]],
                                ALU.is_ge, -1e30, base=0, channel_multiplier=1)
    cm_tok = p.tok("pool", t)

    bo_tok = vec_to_bcast(p, R, W["l2_bproj"], R["bb1"], 1024)
    xres_tok = [None] * NT
    for i in range(NT):
        p.wait("dve", bo_tok)
        t = nc.vector.tensor_tensor(R["x_res"][:, i, :], R["x_res"][:, i, :],
                                    R["bb1"][:], ALU.add)
        xres_tok[i] = p.tok("dve", t)

    ps_s = [p.bps("ps_s1_l2a", [128, 2048], F32)]
    ps_v = R["ps_row"]
    ps_y = ps_b
    atts = [R["tmpf"].bitcast(BF16), R["bb2"].bitcast(BF16)]
    apair = R["tmpsq"].bitcast(BF16).rearrange("p (q c) -> p q c", c=128)
    vpair = R["aux"][:, 0:2048].rearrange("p (j d) -> p j d", d=128)
    wp_sb = wsl[:, 2048:3072]
    yguard = p.last("dve")
    vp_tok = None
    wp_tok = None
    apguard = None

    def emit_TV(prev, nxt_stok):
        h, qt, inst, etk, rtk, apar = prev
        att = atts[apar]
        hb = (h % 2) * 64
        nj = qt + 1
        vtk = None
        atg = None
        for j0 in range(0, nj, 8):
            jn = min(8, nj - j0)
            abuf = R["tmpbf_a"] if (j0 // 8) % 2 == 0 else R["tmpbf_b"]
            p.wait("pe", etk, R["_l2a_atg"][(j0 // 8) % 2], vp_tok)
            for jj in range(jn):
                t = nc.tensor.transpose(
                    R["ps_tr"][:, jj * 128:(jj + 1) * 128],
                    att[:, (j0 + jj) * 128:(j0 + jj + 1) * 128],
                    R["ident"][:])
            ttk = p.tok("pe", t)
            p.wait("dve", ttk, R["_l2a_veva"][0] if j0 == 0 else None)
            t = nc.vector.tensor_copy(abuf[:, 0:jn * 128],
                                      R["ps_tr"][:, 0:jn * 128])
            atg = p.tok("dve", t)
            R["_l2a_atg"][(j0 // 8) % 2] = atg
            p.wait("pe", atg)
            for jj in range(jn):
                t = nc.tensor.matmul(
                    ps_v[:, 0:64], abuf[:, jj * 128:(jj + 1) * 128],
                    vpair[:, j0 + jj, hb:hb + 64],
                    start=(j0 + jj == 0), stop=(j0 + jj == nj - 1))
            vtk = p.tok("pe", t)
        R["_l2a_T"][apar] = p.last("pe")
        p.wait("dve", vtk, rtk)
        t = nc.vector.tensor_scalar_mul(apair[:, qt, hb:hb + 64],
                                        ps_v[:, 0:64], st[:, 30 + inst:31 + inst])
        R["_l2a_veva"][0] = p.tok("dve", t)
        return R["_l2a_veva"][0]

    R["_l2a_atg"] = [None, None]
    R["_l2a_veva"] = [None]
    R["_l2a_T"] = [None, None]
    exp_done = [None, None]
    prev = None
    for pair in range(8):
        if prev is not None:
            emit_TV(prev, None)
            prev = None
        p.wait("act", apguard, R["_l2a_veva"][0], p.last("pe"))
        vp_tok = p.dma("act", vpair,
                       v_dram.rearrange("(j p) d -> p j d", p=128)
                       [:, :, pair * 128:(pair + 1) * 128])
        p.wait("pool", p.last("pe"))
        wp_tok = p.dma("pool", wp_sb,
                       W["l2_wproj"][pair * 128:(pair + 1) * 128, :])
        for par in range(2):
            h = 2 * pair + par
            ch = pair
            hb = par * 64
            for qt in range(NT):
                k = par * NT + qt
                inst = 0
                til = slice(qt * 128, (qt + 1) * 128)
                L = (qt + 1) * 128
                p.wait("pe", ln_tok, exp_done[inst], apguard)
                nb = (L + 511) // 512
                b_diag = (qt * 128) // 512
                order = [b_diag] + [b for b in range(nb) if b != b_diag]
                stoks = {}
                for b in order:
                    bw = min(512, L - b * 512)
                    t = nc.tensor.matmul(ps_s[inst][:, b * 512:b * 512 + bw],
                                         Q_fm[hb:hb + 64, ch, til],
                                         K_fm[hb:hb + 64, ch, b * 512:b * 512 + bw],
                                         start=True, stop=True)
                    stoks[b] = p.tok("pe", t)
                stok = p.last("pe")
                if prev is not None:
                    emit_TV(prev, stok)
                p.wait("dve", stoks[b_diag], cm_tok)
                t = nc.vector.tensor_tensor(ps_s[inst][:, til], ps_s[inst][:, til],
                                            R["cmask"], ALU.add)
                mstk = p.tok("dve", t)
                p.wait("dve", stok, mstk)
                t = nc.vector.reduce_max(st[:, 28 + inst:29 + inst],
                                         ps_s[inst][:, 0:L],
                                         axis=AX.X, negate=True)
                tk = p.tok("dve", t)
                p.wait("dve", tk)
                t = nc.vector.tensor_scalar_mul(st[:, 28 + inst:29 + inst],
                                                st[:, 28 + inst:29 + inst], 0.125)
                tk = p.tok("dve", t)
                apar = k % 2
                p.wait("act", tk, R["_l2a_T"][apar])
                t = nc.scalar.activation(atts[apar][:, 0:L], ps_s[inst][:, 0:L],
                                         AF.Exp,
                                         bias=st[:, 28 + inst:29 + inst],
                                         scale=0.125,
                                         accum_out=st[:, 26 + inst:27 + inst])
                etk = p.tok("act", t)
                exp_done[inst] = etk
                p.wait("dve", etk)
                t = nc.vector.reciprocal(st[:, 30 + inst:31 + inst],
                                         st[:, 26 + inst:27 + inst])
                rtk = p.tok("dve", t)
                prev = (h, qt, inst, etk, rtk, apar)
        emit_TV(prev, None)
        prev = None
        aguard = R["_l2a_veva"][0]
        for qt in range(NT):
            p.wait("pe", aguard, apguard)
            t = nc.tensor.transpose(R["ps_tr"][:, 0:128], apair[:, qt, :],
                                    R["ident"][:])
            ptk = p.tok("pe", t)
            p.wait("dve", ptk)
            t = nc.vector.tensor_copy(apair[:, qt, :], R["ps_tr"][:, 0:128])
            apguard = p.tok("dve", t)
        for qt in range(NT):
            for half in range(2):
                osl = slice(half * 512, (half + 1) * 512)
                p.wait("pe", wp_tok, apguard, yguard)
                t = nc.tensor.matmul(ps_y[:], apair[:, qt, :],
                                     wp_sb[:, osl], start=True, stop=True)
                mtok = p.tok("pe", t)
                p.wait("dve", mtok, xres_tok[qt])
                t = nc.vector.scalar_tensor_tensor(
                    R["x_res"][:, qt, osl], ps_y[:], 1.0,
                    R["x_res"][:, qt, osl], ALU.mult, ALU.add)
                xres_tok[qt] = p.tok("dve", t)
                yguard = xres_tok[qt]
        apguard = yguard
    p.barrier()


EMITTERS = {}


def build(blocks, weight_specs):
    nc = bass.Bass("TRN2", target_bir_lowering=False)
    x_in = nc.dram_tensor("x", [128, NT, D], F32, kind="ExternalInput").ap()
    y_out = nc.dram_tensor("y", [128, NT, D], F32, kind="ExternalOutput").ap()
    W = {}
    for name, (shape, dt) in weight_specs.items():
        W[name] = nc.dram_tensor(name, list(shape), dt, kind="ExternalInput").ap()
    cc = None
    if "l3f" in blocks:
        cc = {"in": nc.dram_tensor("cc_in", [1, 1], F32, kind="Internal").ap(),
              "out": nc.dram_tensor("cc_out", [1, 1], F32, kind="Internal").ap()}
    scr = {}
    if "l0a" in blocks:
        scr["q_dram"] = nc.dram_tensor("q_dram", [T, D], BF16, kind="Internal").ap()
    if "l2a" in blocks:
        scr["v_dram"] = nc.dram_tensor("v_dram", [T, D], BF16, kind="Internal").ap()
    if "l2f" in blocks:
        scr["h_dram"] = nc.dram_tensor("h_dram", [T, 4096], BF16, kind="Internal").ap()
        scr["wqg_dram"] = nc.dram_tensor("wqg_dram", [1024, 4096], BF16, kind="Internal").ap()
        scr["wqu_dram"] = nc.dram_tensor("wqu_dram", [1024, 4096], BF16, kind="Internal").ap()
        scr["wqd_dram"] = nc.dram_tensor("wqd_dram", [4096, 1024], BF16, kind="Internal").ap()

    with ExitStack() as ctx:
        p = Prog(nc, ctx)
        R = {}
        R["x_res"] = p.sb("x_res", [128, NT, D], F32)[:]
        R["xnT"] = p.sb("xnT", [128, KC, T], BF16)[:]
        R["A"] = p.sb("regA", [128, 8192], F32)[:]
        R["B"] = p.sb("regB", [128, 8192], F32)[:]
        R["Wr"] = p.sb("regW", [128, 4096], F32)[:]
        R["tmpf"] = p.sb("tmpf", [128, 1024], F32)[:]
        R["tmpsq"] = p.sb("tmpsq", [128, 1024], F32)[:]
        R["tmpbf_a"] = p.sb("tmpbf_a", [128, 1024], BF16)[:]
        R["tmpbf_b"] = p.sb("tmpbf_b", [128, 1024], BF16)[:]
        R["smallbf"] = p.sb("smallbf", [16, 2048], BF16)[:]
        R["stats"] = p.sb("stats", [128, 80], F32)[:]
        R["bb1"] = p.sb("bb1", [128, 1024], F32)[:]
        R["bb2"] = p.sb("bb2", [128, 1024], F32)[:]
        R["row_f"] = p.sb("row_f", [16, 512], F32)[:]
        R["idx_u"] = p.sb("idx_u", [128, 8], U32)[:]
        R["idx_c"] = p.sb("idx_c", [128, 16], U32)[:]
        R["aux"] = p.sb("aux", [128, 2048], BF16)[:]
        R["kv_bf"] = R["aux"][:, 0:1024].rearrange("p (h d) -> p h d", d=64)
        R["ks_bf"] = R["aux"][:, 1024:1040]
        R["ones_f"] = p.sb("ones_f", [1, 128], F32)[:]
        R["ones_col"] = p.sb("ones_col", [128, 1], F32)[:]
        R["ones_bf"] = R["tmpbf_a"][:, 0:128]
        R["ones_bfc"] = p.sb("ones_bfc", [128, 1], BF16)[:]
        R["ones_f128"] = R["tmpf"][:, 0:128]
        R["ident"] = p.sb("ident", [128, 128], BF16)[:]
        R["ident_f"] = p.sb("ident_f", [128, 128], F32)[:]
        R["ps_a"] = p.ps("ps_a", [128, 512], F32)[:]
        R["ps_b"] = p.ps("ps_b", [128, 512], F32)[:]
        R["ps_tr"] = p.ps("ps_tr", [128, 1024], BF16)[:]
        R["ps_row"] = p.ps("ps_row", [128, 512], F32)[:]
        R["ps_small"] = R["ps_row"][:, 496:512]
        R["_fm_evac_tok"] = [None]

        emit_common(p, R)
        tk = p.dma("sp", R["x_res"], x_in)
        p.wait("dve", tk)
        p.wait("act", tk)
        p.wait("pe", tk)
        p.wait("pool", tk)

        for b in blocks:
            with ExitStack() as bctx:
                p.bctx = bctx
                EMITTERS[b](p, R, W, cc=cc, scr=scr)
            p.barrier()

        tk = p.dma("sp", y_out, R["x_res"])
        p.wait("sp", tk)
    return nc


EMITTERS["l0f"] = lambda p, R, W, cc=None, scr=None: emit_l0f(p, R, W)
EMITTERS["l1f"] = lambda p, R, W, cc=None, scr=None: emit_l1f(p, R, W)
EMITTERS["l2f"] = lambda p, R, W, cc=None, scr=None: emit_l2f(p, R, W, scr)
EMITTERS["l0a"] = lambda p, R, W, cc=None, scr=None: emit_l0a(p, R, W, scr)
EMITTERS["l2a"] = lambda p, R, W, cc=None, scr=None: emit_l2a(p, R, W, scr)
EMITTERS["l1a"] = lambda p, R, W, cc=None, scr=None: emit_l1a(p, R, W)
EMITTERS["l3a"] = lambda p, R, W, cc=None, scr=None: emit_l3a(p, R, W)
EMITTERS["l3f"] = lambda p, R, W, cc=None, scr=None: emit_l3f(p, R, W, cc)


def _to32(a):
    return np.ascontiguousarray(a, dtype=np.float32)


def _pcn(WT):
    K, N = WT.shape
    return np.ascontiguousarray(WT.reshape(K // 128, 128, N).transpose(1, 0, 2))


def _tiles(WT, tn):
    K, N = WT.shape
    a = WT.reshape(K // 128, 128, N // tn, tn).transpose(2, 1, 0, 3)
    return np.ascontiguousarray(a)


def host_weights(inputs):
    I = inputs
    w = {}
    w["l0_wqT"] = (_pcn(_to32(I["L0_attn_Wq"].T)), F32)
    w["l0_wkT"] = (_pcn(_to32(I["L0_attn_Wk"].T)), F32)
    w["l0_wvT"] = (_pcn(_to32(I["L0_attn_Wv"].T)), F32)
    w["l0_woT"] = (_pcn(_to32(I["L0_attn_Wo"].T)), F32)
    for b in ["bq", "bk", "bv", "bo"]:
        w["l0_" + b] = (_to32(I["L0_attn_" + b]), F32)
    w["l0_wg4"] = (_tiles(np.ascontiguousarray(I["L0_ffn_w_gate"].T), 128), I32)
    w["l0_wu4"] = (_tiles(np.ascontiguousarray(I["L0_ffn_w_up"].T), 128), I32)
    w["l0_wdT"] = (np.ascontiguousarray(I["L0_ffn_w_down"].T), I32)
    w["l1_wqT"] = (_pcn(_to32(I["L1_attn_Wq"].T)), F32)
    w["l1_wkT"] = (_pcn(_to32(I["L1_attn_Wk"].T)), F32)
    w["l1_wvT"] = (_pcn(_to32(I["L1_attn_Wv"].T)), F32)
    w["l1_woT"] = (_pcn(_to32(I["L1_attn_Wo"].T)), F32)
    for b in ["bq", "bk", "bv", "bo"]:
        w["l1_" + b] = (_to32(I["L1_attn_" + b]), F32)
    w["l1_Wg"] = (_pcn(_to32(I["L1_ffn_Wg"])), F32)
    w["l1_bg"] = (_to32(I["L1_ffn_bg"]), F32)
    w["l1_W1"] = (np.stack([_pcn(_to32(I["L1_ffn_W1"][e])) for e in range(8)]), F32)
    w["l1_b1"] = (_to32(I["L1_ffn_b1"]), F32)
    w["l1_W2"] = (np.stack([_pcn(_to32(I["L1_ffn_W2"][e])) for e in range(8)]), F32)
    w["l1_b2"] = (_to32(I["L1_ffn_b2"]), F32)
    w["l2_wattn"] = (_tiles(_to32(I["L2_attn_w_attn"]), 128), F32)
    w["l2_battn"] = (_to32(I["L2_attn_b_attn"]), F32)
    w["l2_wproj"] = (_to32(I["L2_attn_w_proj"]), F32)
    w["l2_bproj"] = (_to32(I["L2_attn_b_proj"]), F32)
    w["l2_wgT"] = (_to32(I["L2_ffn_w_gate"].T), F32)
    w["l2_wuT"] = (_to32(I["L2_ffn_w_up"].T), F32)
    w["l2_wdT"] = (_to32(I["L2_ffn_w_down"].T), F32)
    win = _to32(I["L3_attn_W_in"])
    win_p = np.zeros((1024, 1152), np.float32)
    win_p[:, 0:1032] = win
    w["l3_Win"] = (_tiles(win_p, 128), F32)
    w["l3_b_in"] = (_to32(I["L3_attn_b_in"]), F32)
    wg3 = _to32(I["L3_attn_W_g"])
    wg3_p = np.zeros((1152, 8), np.float32)
    wg3_p[0:1032] = wg3
    w["l3_Wg"] = (_pcn(wg3_p), F32)
    w["l3_b_g"] = (_to32(I["L3_attn_b_g"]), F32)
    w["l3_Wout"] = (_to32(I["L3_attn_W_out"]), F32)
    w["l3_bout"] = (_to32(I["L3_attn_b_out"]), F32)
    w["l3f_w"] = (_to32(I["L3_ffn_weight"]), F32)
    w["l3f_gamma"] = (_to32(I["L3_ffn_ln_gamma"]), F32)
    w["l3f_beta"] = (_to32(I["L3_ffn_ln_beta"]), F32)
    return w


BLOCK_WEIGHTS = {
    "l0a": ["l0_wqT", "l0_wkT", "l0_wvT", "l0_woT", "l0_bq", "l0_bk", "l0_bv", "l0_bo"],
    "l0f": ["l0_wg4", "l0_wu4", "l0_wdT"],
    "l1a": ["l1_wqT", "l1_wkT", "l1_wvT", "l1_woT", "l1_bq", "l1_bk", "l1_bv", "l1_bo"],
    "l1f": ["l1_Wg", "l1_bg", "l1_W1", "l1_b1", "l1_W2", "l1_b2"],
    "l2a": ["l2_wattn", "l2_battn", "l2_wproj", "l2_bproj"],
    "l2f": ["l2_wgT", "l2_wuT", "l2_wdT"],
    "l3a": ["l3_Win", "l3_b_in", "l3_Wg", "l3_b_g", "l3_Wout", "l3_bout"],
    "l3f": ["l3f_w", "l3f_gamma", "l3f_beta"],
}

LAUNCH_GROUPS = [ALL_BLOCKS]


def run_groups(x_full, wdict, groups):
    x = np.ascontiguousarray(x_full, dtype=np.float32)
    for blocks in groups:
        names = []
        for b in blocks:
            names += BLOCK_WEIGHTS[b]
        spec = {n: (wdict[n][0].shape, wdict[n][1]) for n in names}
        nc = build(blocks, spec)
        in_maps = []
        for c in range(NCORES):
            xc = np.ascontiguousarray(
                x[c].reshape(NT, 128, D).transpose(1, 0, 2))
            m = {"x": xc}
            for n in names:
                m[n] = wdict[n][0]
            in_maps.append(m)
        res = run_bass_kernel_spmd(nc, in_maps, core_ids=list(range(NCORES)))
        x = np.stack([np.asarray(res.results[c]["y"])
                      .transpose(1, 0, 2).reshape(T, D) for c in range(NCORES)])
    return x


def kernel(**inputs):
    w = host_weights(inputs)
    x = np.ascontiguousarray(np.asarray(inputs["L0_attn_x"], dtype=np.float32))
    out = run_groups(x, w, LAUNCH_GROUPS)
    return out.astype(np.float32)


# version 11
# speedup vs baseline: 1.0580x; 1.0234x over previous
import numpy as np
from contextlib import ExitStack

import concourse.bass as bass
import concourse.mybir as mybir
from concourse.bass_utils import run_bass_kernel_spmd

F32 = mybir.dt.float32
BF16 = mybir.dt.bfloat16
I32 = mybir.dt.int32
U32 = mybir.dt.uint32
AF = mybir.ActivationFunctionType
ALU = mybir.AluOpType
AX = mybir.AxisListType
IOffs = bass.IndirectOffsetOnAxis

MAGIC = float(np.float32(1.5 * 2 ** 23))
T = 2048
D = 1024
NT = T // 128
KC = D // 128
NCORES = 8

ALL_BLOCKS = ["l0a", "l0f", "l1a", "l1f", "l2a", "l2f", "l3a", "l3f"]


class Prog:
    def __init__(self, nc, ctx):
        self.nc = nc
        self.ctx = ctx
        self.e = {"pe": nc.tensor, "dve": nc.vector, "act": nc.scalar,
                  "pool": nc.gpsimd, "sp": nc.sync}
        self.sem = {}
        self.cnt = {}
        self.waited = {}
        names = ["pe", "dve", "act", "pool", "sp", "cc"]
        for q in ["sp", "pool", "act"]:
            for i in range(6):
                names.append(f"dma_{q}{i}")
        for n in names:
            self.sem[n] = ctx.enter_context(nc.semaphore("s_" + n))
            self.cnt[n] = 0
        self.dma_rr = {"sp": 0, "pool": 0, "act": 0}

    def tok(self, name, instr, n=1):
        self.cnt[name] += n
        instr.then_inc(self.sem[name], n)
        return (name, self.cnt[name])

    def last(self, name):
        return (name, self.cnt[name])

    def dma(self, queue, out, in_, **kw):
        i = self.dma_rr[queue]
        self.dma_rr[queue] = (i + 1) % 6
        name = f"dma_{queue}{i}"
        self.wait(queue, (name, self.cnt[name]))
        ins = self.e[queue].dma_start(out, in_, **kw)
        return self.tok(name, ins, 16)

    def dma_indirect_add(self, out_ap, table_dram, idx_ap):
        i = self.dma_rr["pool"]
        self.dma_rr["pool"] = (i + 1) % 6
        name = f"dma_pool{i}"
        self.wait("pool", (name, self.cnt[name]))
        ins = self.nc.gpsimd.indirect_dma_start(
            out_ap, None, table_dram, IOffs(idx_ap, 0), compute_op=ALU.add)
        return self.tok(name, ins, 16)

    def dma_gather_rows(self, out_ap, table_dram, idx_ap, element_offset=0):
        i = self.dma_rr["pool"]
        self.dma_rr["pool"] = (i + 1) % 6
        name = f"dma_pool{i}"
        self.wait("pool", (name, self.cnt[name]))
        ins = self.nc.gpsimd.indirect_dma_start(
            out_ap, None, table_dram, IOffs(idx_ap, 0),
            element_offset=element_offset)
        return self.tok(name, ins, 16)

    def wait(self, eng, *toks):
        for t in toks:
            if t is None:
                continue
            sname, val = t
            if val <= 0:
                continue
            key = (eng, sname)
            if self.waited.get(key, 0) >= val:
                continue
            self.e[eng].wait_ge(self.sem[sname], val)
            self.waited[key] = val

    def barrier(self):
        toks = [self.last(n) for n in self.sem if n != "cc"]
        for eng in ["pe", "dve", "act", "pool", "sp"]:
            self.wait(eng, *toks)

    def sb(self, name, shape, dt):
        return self.ctx.enter_context(self.nc.sbuf_tensor(name, shape, dt))

    def ccol(self, value):
        if not hasattr(self, "_cpool"):
            self._cpool = self.sb("cpool", [128, 16], F32)[:]
            self._cvals = {}
        v = float(value)
        if v not in self._cvals:
            i = len(self._cvals)
            assert i < 16
            ap = self._cpool[:, i:i + 1]
            t = self.nc.gpsimd.memset(ap, v)
            tk = self.tok("pool", t)
            self._cvals[v] = (ap, tk)
        return self._cvals[v]

    def ps(self, name, shape, dt=F32):
        return self.ctx.enter_context(self.nc.psum_tensor(name, shape, dt))

    def bps(self, name, shape, dt=F32):
        return self.bctx.enter_context(self.nc.psum_tensor(name, shape, dt))[:]


def emit_common(p, R):
    nc = p.nc
    t = nc.vector.memset(R["ones_f"][:], 1.0)
    p.tok("dve", t)
    t = nc.vector.memset(R["ones_col"][:], 1.0)
    p.tok("dve", t)
    t = nc.gpsimd.memset(R["ones_bf"], 1.0)
    tk = p.tok("pool", t)
    t = nc.gpsimd.memset(R["ones_bfc"][:], 1.0)
    tk = p.tok("pool", t)
    p.wait("pool", tk)
    t = nc.gpsimd.affine_select(R["ident"][:], R["ones_bf"], [[-1, 128]],
                                ALU.is_equal, 0.0, base=0, channel_multiplier=1)
    p.tok("pool", t)
    t = nc.gpsimd.memset(R["ones_f128"][:], 1.0)
    tk = p.tok("pool", t)
    p.wait("pool", tk)
    t = nc.gpsimd.affine_select(R["ident_f"][:], R["ones_f128"][:], [[-1, 128]],
                                ALU.is_equal, 0.0, base=0, channel_multiplier=1)
    p.tok("pool", t)
    p.barrier()


def bcast_row_to_col(p, R, src_ap_1x1, dst_col_128x1, scale=None):
    nc = p.nc
    ps = R["ps_small"]
    p.wait("pe", p.last("dve"), p.last("act"), p.last("pool"))
    t = nc.tensor.matmul(ps[:, 0:1], R["ones_f"][:, 0:128], src_ap_1x1,
                         start=True, stop=True)
    tk = p.tok("pe", t)
    p.wait("dve", tk)
    t = nc.vector.tensor_copy(dst_col_128x1, ps[:, 0:1])
    if scale is not None:
        tk2 = p.tok("dve", t)
        p.wait("dve", tk2)
        t = nc.vector.tensor_scalar_mul(dst_col_128x1, dst_col_128x1, scale)
    return p.tok("dve", t)


def vec_to_bcast(p, R, vec_dram, dst_f32_128xN, n, tmp_row=None):
    nc = p.nc
    row = R["row_f"]
    ps = R["ps_row"]
    out_toks = []
    for off in range(0, n, 512):
        w = min(512, n - off)
        p.wait("sp", p.last("pe"), p.last("dve"))
        tk = p.dma("sp", row[0:1, 0:w], vec_dram[off:off + w].unsqueeze(0))
        p.wait("pe", tk, p.last("dve"))
        t = nc.tensor.matmul(ps[:, 0:w], R["ones_f"][:, 0:128],
                             row[0:1, 0:w], start=True, stop=True)
        ptk = p.tok("pe", t)
        p.wait("dve", ptk)
        t = nc.vector.tensor_copy(dst_f32_128xN[:, off:off + w], ps[:, 0:w])
        out_toks.append(p.tok("dve", t))
    return out_toks[-1]


def vec_to_cols(p, R, vec_dram, dst_f32_128xC, n):
    nc = p.nc
    c = n // 128
    row = R["row_f"]
    p.wait("sp", p.last("pe"), p.last("dve"))
    tk = p.dma("sp", row[0:c, 0:128], vec_dram.rearrange("(c q) -> c q", q=128))
    ps = R["ps_row"]
    p.wait("pe", tk, p.last("dve"))
    t = nc.tensor.transpose(ps[:, 0:c], row[0:c, 0:128], R["ident_f"][0:c, 0:c])
    ptk = p.tok("pe", t)
    p.wait("dve", ptk)
    t = nc.vector.tensor_copy(dst_f32_128xC[:, 0:c], ps[:, 0:c])
    return p.tok("dve", t)


def emit_ln(p, R, consume, eps=1e-5, src=None):
    nc = p.nc
    xr = src if src is not None else R["x_res"]
    st = R["stats"]
    s1 = st[:, 0:NT]
    s2 = st[:, NT:2 * NT]
    a_c = st[:, 2 * NT:3 * NT]
    b_c = st[:, 3 * NT:4 * NT]
    red_toks = []
    sq_toks = []
    for i in range(NT):
        t = nc.vector.reduce_sum(s1[:, i:i + 1], xr[:, i, :], axis=AX.X)
        red_toks.append(p.tok("dve", t))
        p.wait("act", p.last("dve") if i == 0 else None)
        sq_scr = R["tmpsq"] if (i % 2 == 0) else R["tmpf"]
        if i >= 2:
            p.wait("act", sq_toks[i - 2])
        t = nc.scalar.activation(sq_scr[:], xr[:, i, :], AF.Square,
                                 accum_out=s2[:, i:i + 1])
        sq_toks.append(p.tok("act", t))
    p.wait("dve", red_toks[-1], sq_toks[-1])
    t = nc.vector.tensor_scalar_mul(st[:, 0:2 * NT], st[:, 0:2 * NT], 1.0 / D)
    tk = p.tok("dve", t)
    p.wait("dve", tk)
    t = nc.vector.tensor_tensor(a_c, s1, s1, ALU.mult)
    tk = p.tok("dve", t)
    p.wait("dve", tk)
    t = nc.vector.tensor_tensor(a_c, s2, a_c, ALU.subtract)
    tk = p.tok("dve", t)
    eap, etk = p.ccol(eps)
    p.wait("act", tk, etk)
    t = nc.scalar.activation(a_c, a_c, AF.Sqrt, bias=eap)
    tk = p.tok("act", t)
    p.wait("dve", tk)
    t = nc.vector.reciprocal(a_c, a_c)
    tk = p.tok("dve", t)
    p.wait("dve", tk)
    t = nc.vector.scalar_tensor_tensor(b_c, s1, -1.0, a_c, ALU.mult, ALU.mult)
    tk = p.tok("dve", t)
    p.wait("dve", tk)
    guard = [None, None]
    for i in range(NT):
        dst = R["tmpf"] if (i % 2 == 0) else R["tmpsq"]
        p.wait("dve", guard[i % 2])
        t = nc.vector.tensor_scalar(dst[:], xr[:, i, :], a_c[:, i:i + 1],
                                    b_c[:, i:i + 1], ALU.mult, ALU.add)
        tk = p.tok("dve", t)
        guard[i % 2] = consume(i, dst, tk)
    return a_c, b_c


def emit_tile_to_fm(p, R, i, src_bf_tile, tok, dst_fm, evac_eng="dve"):
    nc = p.nc
    ps = R["ps_tr"]
    p.wait("pe", tok, R["_fm_evac_tok"][0])
    for c in range(KC):
        t = nc.tensor.transpose(ps[:, c * 128:(c + 1) * 128],
                                src_bf_tile[:, c * 128:(c + 1) * 128],
                                R["ident"][:])
    tk = p.tok("pe", t)
    p.wait(evac_eng, tk)
    if evac_eng == "dve":
        t = nc.vector.tensor_copy(dst_fm[:, :, i * 128:(i + 1) * 128],
                                  ps[:].rearrange("p (c q) -> p c q", q=128))
    else:
        t = nc.scalar.copy(dst_fm[:, :, i * 128:(i + 1) * 128],
                           ps[:].rearrange("p (c q) -> p c q", q=128))
    R["_fm_evac_tok"][0] = p.tok(evac_eng, t)
    return R["_fm_evac_tok"][0]


def emit_ln_to_xnT(p, R, eps=1e-5):
    nc = p.nc
    R["_fm_evac_tok"] = [None]
    xnT = R["xnT"]

    def consume(i, xn_tile_f32, tok):
        bft = R["tmpbf_a"] if i % 2 == 0 else R["tmpbf_b"]
        p.wait("act", tok)
        t = nc.scalar.copy(bft[:], xn_tile_f32[:])
        ctk = p.tok("act", t)
        emit_tile_to_fm(p, R, i, bft[:], ctk, xnT[:],
                        evac_eng="dve" if i % 2 == 0 else "act")
        return ctk

    emit_ln(p, R, consume, eps=eps)
    return R["_fm_evac_tok"][0]


def load_w_chunks(p, dram_ap, dst_bf):
    k = dram_ap.shape[0]
    return p.dma("pool", dst_bf, dram_ap.rearrange("(c p) n -> p c n", p=128))


def emit_l0f(p, R, W):
    nc = p.nc
    S = 1.0 / 127.0
    ln_tok = emit_ln_to_xnT(p, R)
    xnT = R["xnT"]
    p.barrier()

    NHC = 24
    GRP = 4
    h_fm = R["A"].bitcast(BF16).rearrange("p (j t) -> p j t", t=T)
    wg = R["Wr"].bitcast(BF16)
    wg_sl = [wg[:, 0:1024].rearrange("p (c n) -> p c n", n=128),
             wg[:, 1024:2048].rearrange("p (c n) -> p c n", n=128)]
    wu_sl = [wg[:, 2048:3072].rearrange("p (c n) -> p c n", n=128),
             wg[:, 3072:4096].rearrange("p (c n) -> p c n", n=128)]
    wd_sl = [wg[:, 4096 + j * 1024: 4096 + (j + 1) * 1024] for j in range(4)]

    ps_g = [R["ps_a"], p.bps("ps_g2_l0f", [128, 512], F32)]
    ps_u = [R["ps_b"], p.bps("ps_u2_l0f", [128, 512], F32)]
    ps_y = [p.bps("ps_y_l0f", [128, 512], F32),
            p.bps("ps_y2_l0f", [128, 512], F32)]
    silu_t = R["tmpbf_a"]

    wtoks = {}
    mm_after = {}

    slot_use = {}

    def load_hc(hc, parity):
        p.wait("pool", slot_use.get(("gu", parity)))
        wtoks[("g", parity)] = p.dma("pool", wg_sl[parity], W["l0_wg4"][hc])
        wtoks[("u", parity)] = p.dma("pool", wu_sl[parity], W["l0_wu4"][hc])

    def load_wd(hc, j):
        p.wait("pool", slot_use.get(("d", j)))
        wtoks[("d", j)] = p.dma("pool", wd_sl[j],
                                W["l0_wdT"][hc * 128:(hc + 1) * 128, :])

    load_hc(0, 0)
    xres_tok = [None] * NT
    for g in range(6):
        hcs = [g * GRP + j for j in range(GRP)]
        for j, hc in enumerate(hcs):
            par = hc % 2
            if hc + 1 < NHC:
                load_hc(hc + 1, (hc + 1) % 2)
            load_wd(hc, j)
            for ts4 in range(4):
                q = ts4 % 2
                tsl = slice(ts4 * 512, (ts4 + 1) * 512)
                p.wait("pe", wtoks[("g", par)], ln_tok,
                       mm_after.get(("silu_rd", q)))
                for c in range(KC):
                    t = nc.tensor.matmul(ps_g[q][:], wg_sl[par][:, c, :],
                                         xnT[:, c, tsl], start=(c == 0),
                                         stop=(c == KC - 1))
                g_tok = p.tok("pe", t)
                p.wait("pe", wtoks[("u", par)], mm_after.get(("h_rd", q)))
                for c in range(KC):
                    t = nc.tensor.matmul(ps_u[q][:], wu_sl[par][:, c, :],
                                         xnT[:, c, tsl], start=(c == 0),
                                         stop=(c == KC - 1))
                u_tok = p.tok("pe", t)
                p.wait("act", g_tok, mm_after.get(("h_rd", q)),
                       mm_after.get(("silu_wr", q)))
                t = nc.scalar.activation(silu_t[:, q * 512:q * 512 + 512],
                                         ps_g[q][:], AF.Silu, scale=S)
                s_tok = p.tok("act", t)
                mm_after[("silu_rd", q)] = s_tok
                p.wait("dve", u_tok, s_tok)
                t = nc.vector.scalar_tensor_tensor(
                    h_fm[:, j, tsl], ps_u[q][:], S * S,
                    silu_t[:, q * 512:q * 512 + 512], ALU.mult, ALU.mult)
                mm_after[("h", j)] = p.tok("dve", t)
                mm_after[("h_rd", q)] = mm_after[("h", j)]
                mm_after[("silu_wr", q)] = mm_after[("h", j)]
            slot_use[("gu", par)] = p.last("pe")
        for i in range(NT):
            for half in range(2):
                q = half
                p.wait("pe", *[wtoks[("d", j)] for j in range(GRP)],
                       *[mm_after[("h", j)] for j in range(GRP)],
                       mm_after.get(("y_rd", q)))
                for j in range(GRP):
                    t = nc.tensor.matmul(
                        ps_y[q][:], h_fm[:, j, i * 128:(i + 1) * 128],
                        wd_sl[j][:, half * 512:(half + 1) * 512],
                        start=(j == 0), stop=(j == GRP - 1))
                y_tok = p.tok("pe", t)
                p.wait("dve", y_tok, xres_tok[i])
                t = nc.vector.scalar_tensor_tensor(
                    R["x_res"][:, i, half * 512:(half + 1) * 512], ps_y[q][:],
                    1.0, R["x_res"][:, i, half * 512:(half + 1) * 512],
                    ALU.mult, ALU.add)
                xres_tok[i] = p.tok("dve", t)
                mm_after[("y_rd", q)] = xres_tok[i]
        for j in range(GRP):
            slot_use[("d", j)] = p.last("pe")
    p.barrier()


def emit_l3a(p, R, W):
    nc = p.nc
    ln_tok = emit_ln_to_xnT(p, R)
    xnT = R["xnT"]
    p.barrier()

    comb = R["A"].bitcast(BF16).rearrange("p (c t) -> p c t", t=T)
    tail_bf = R["smallbf"]
    Bf = R["B"].bitcast(F32)
    si_f = Bf[0:8, 0:T]
    g_f = Bf[0:8, T:2 * T]
    d1_f = Bf[0:8, 2 * T:3 * T]
    states_f = Bf[0:8, 3 * T:4 * T]

    wsl = R["Wr"].bitcast(BF16)
    win_sl = [wsl[:, 0:1024].rearrange("p (c n) -> p c n", n=128),
              wsl[:, 1024:2048].rearrange("p (c n) -> p c n", n=128)]
    wg_sl = wsl[:, 2048:2048 + 9 * 8].rearrange("p (c n) -> p c n", n=8)
    wout_sl = [wsl[:, 4096 + j * 1024: 4096 + (j + 1) * 1024] for j in range(4)]

    bcols = R["stats"][:, 32:41]
    vec_to_cols(p, R, W["l3_b_in"][0:1024], bcols, 1024)
    row = R["row_f"]
    p.wait("sp", p.last("pe"), p.last("dve"))
    tk = p.dma("sp", row[0:1, 256:264], W["l3_b_in"][1024:1032].unsqueeze(0))
    p.wait("pe", tk, p.last("dve"))
    t = nc.tensor.transpose(R["ps_row"][0:8, 0:1], row[0:1, 256:264], R["ident_f"][0:1, 0:1])
    ptk = p.tok("pe", t)
    p.wait("dve", ptk)
    t = nc.vector.tensor_copy(bcols[0:8, 8:9], R["ps_row"][0:8, 0:1])
    btk = p.tok("dve", t)
    bg_col = R["stats"][:, 41:42]
    tk = p.dma("sp", row[0:1, 272:280], W["l3_b_g"].unsqueeze(0))
    p.wait("pe", tk, btk)
    t = nc.tensor.transpose(R["ps_row"][0:8, 1:2], row[0:1, 272:280], R["ident_f"][0:1, 0:1])
    ptk = p.tok("pe", t)
    p.wait("dve", ptk)
    t = nc.vector.tensor_copy(bg_col[0:8, :], R["ps_row"][0:8, 1:2])
    bgtk = p.tok("dve", t)
    wgt1 = p.dma("pool", wg_sl, W["l3_Wg"])
    wgt2 = wgt1

    ps_c2 = [R["ps_a"], R["ps_b"]]
    ps_g2 = R["ps_b"]

    wtok = [None, None]
    slot_use = [None, None]
    wtok[0] = p.dma("pool", win_sl[0], W["l3_Win"][0])
    comb_toks = []
    cgq = {}
    for oc in range(9):
        par = oc % 2
        m = 128 if oc < 8 else 8
        if oc + 1 < 9:
            nm = 128 if oc + 1 < 8 else 8
            p.wait("pool", slot_use[(oc + 1) % 2])
            wtok[(oc + 1) % 2] = p.dma(
                "pool", win_sl[(oc + 1) % 2][:, :, 0:nm],
                W["l3_Win"][oc + 1][:, :, 0:nm])
        for ts4 in range(4):
            q = ts4 % 2
            tsl = slice(ts4 * 512, (ts4 + 1) * 512)
            p.wait("pe", wtok[par], ln_tok, cgq.get(q))
            for c in range(KC):
                t = nc.tensor.matmul(ps_c2[q][0:m, :], win_sl[par][:, c, 0:m],
                                     xnT[:, c, tsl], start=(c == 0),
                                     stop=(c == KC - 1))
            mtok = p.tok("pe", t)
            slot_use[par] = mtok
            p.wait("dve", mtok, btk)
            if oc < 8:
                t = nc.vector.tensor_scalar_add(comb[:, oc, tsl], ps_c2[q][0:m, :],
                                                bcols[:, oc:oc + 1])
                cgq[q] = p.tok("dve", t)
                comb_toks.append(cgq[q])
            else:
                t = nc.vector.tensor_scalar_add(si_f[:, tsl], ps_c2[q][0:8, :],
                                                bcols[0:8, 8:9])
                ttk = p.tok("dve", t)
                cgq[q] = ttk
                comb_toks.append(ttk)
                p.wait("act", ttk)
                t = nc.scalar.copy(tail_bf[0:8, tsl], si_f[:, tsl])
                comb_toks.append(p.tok("act", t))
    for ts4 in range(4):
        tsl = slice(ts4 * 512, (ts4 + 1) * 512)
        p.wait("pe", wgt1, wgt2, comb_toks[-1], p.last("act"))
        for c in range(8):
            t = nc.tensor.matmul(ps_g2[0:8, 0:512], wg_sl[:, c, :],
                                 comb[:, c, tsl], start=(c == 0), stop=False)
        t = nc.tensor.matmul(ps_g2[0:8, 0:512], wg_sl[0:8, 8, :],
                             tail_bf[0:8, tsl], start=False, stop=True)
        mtok = p.tok("pe", t)
        p.wait("act", mtok, bgtk)
        t = nc.scalar.activation(g_f[:, tsl], ps_g2[0:8, 0:512], AF.Sigmoid,
                                 bias=bg_col[0:8, :])
        p.tok("act", t)
    gtk = p.last("act")
    p.wait("dve", gtk)
    t = nc.vector.tensor_tensor(d1_f, g_f, si_f, ALU.mult)
    tk = p.tok("dve", t)
    p.wait("dve", tk)
    t = nc.vector.tensor_tensor(d1_f, si_f, d1_f, ALU.subtract)
    tk = p.tok("dve", t)
    p.wait("dve", tk)
    t = nc.vector.tensor_tensor_scan(states_f, g_f, d1_f, 0.0, ALU.mult, ALU.add)
    tk = p.tok("dve", t)
    p.wait("dve", tk)
    t = nc.vector.tensor_copy(tail_bf[0:8, 0:T], states_f)
    states_tok = p.tok("dve", t)

    bout_tok = vec_to_bcast(p, R, W["l3_bout"], R["bb1"], 1024)
    xres_tok = [None] * NT
    for i in range(NT):
        p.wait("dve", bout_tok)
        t = nc.vector.tensor_tensor(R["x_res"][:, i, :], R["x_res"][:, i, :],
                                    R["bb1"][:], ALU.add)
        xres_tok[i] = p.tok("dve", t)
    guard = p.last("dve")
    for grp, chunks in enumerate([[0, 1, 2, 3], [4, 5, 6, 7], [8]]):
        wts = []
        for j, cch in enumerate(chunks):
            m = 128 if cch < 8 else 8
            wts.append(p.dma(
                "pool", wout_sl[j][0:m, :],
                W["l3_Wout"][cch * 128:cch * 128 + m, :]))
        ogq = {}
        for i in range(NT):
            til = slice(i * 128, (i + 1) * 128)
            for half in range(2):
                q = half
                osl = slice(half * 512, (half + 1) * 512)
                p.wait("pe", *wts, states_tok, comb_toks[-1], guard,
                       ogq.get(q))
                for j, cch in enumerate(chunks):
                    if cch < 8:
                        t = nc.tensor.matmul(ps_c2[q][:, 0:512], comb[:, cch, til],
                                             wout_sl[j][:, osl],
                                             start=(j == 0),
                                             stop=(j == len(chunks) - 1))
                    else:
                        t = nc.tensor.matmul(ps_c2[q][:, 0:512], tail_bf[0:8, til],
                                             wout_sl[j][0:8, osl],
                                             start=True, stop=True)
                mtok = p.tok("pe", t)
                p.wait("dve", mtok, xres_tok[i])
                t = nc.vector.scalar_tensor_tensor(
                    R["x_res"][:, i, osl], ps_c2[q][:, 0:512], 1.0,
                    R["x_res"][:, i, osl], ALU.mult, ALU.add)
                xres_tok[i] = p.tok("dve", t)
                ogq[q] = xres_tok[i]
                guard = xres_tok[i]
        p.barrier()
    p.barrier()


def emit_l3f(p, R, W, cc):
    nc = p.nc
    QB = 128.0
    BOUND = float(np.float32(QB - 1e-5))

    g_tok = vec_to_bcast(p, R, W["l3f_gamma"], R["bb1"], 1024)
    b_tok = vec_to_bcast(p, R, W["l3f_beta"], R["bb2"], 1024)

    xn2 = R["A"].bitcast(BF16).rearrange("p (i d) -> p i d", d=D)
    gmax = R["stats"][:, 48:49]
    t = nc.vector.memset(gmax, 0.0)
    gm_tok = p.tok("dve", t)

    st = R["stats"]
    s1 = st[:, 0:1]
    s2 = st[:, 1:2]
    ac = st[:, 2:3]
    bc = st[:, 3:4]
    tile_toks = [None] * NT
    for i in range(NT):
        xt = R["x_res"][:, i, :]
        p.wait("dve", p.last("act") if i else None, gm_tok)
        t = nc.vector.reduce_sum(s1, xt, axis=AX.X)
        r1 = p.tok("dve", t)
        p.wait("act", p.last("dve"))
        t = nc.scalar.activation(R["tmpsq"][:], xt, AF.Square, accum_out=s2)
        q1 = p.tok("act", t)

        def stats_chain(eps):
            p.wait("dve", p.last("act"))
            t = nc.vector.tensor_scalar_mul(st[:, 0:2], st[:, 0:2], 1.0 / D)
            tk = p.tok("dve", t)
            p.wait("dve", tk)
            t = nc.vector.tensor_tensor(ac, s1, s1, ALU.mult)
            tk = p.tok("dve", t)
            p.wait("dve", tk)
            t = nc.vector.tensor_tensor(ac, s2, ac, ALU.subtract)
            tk = p.tok("dve", t)
            eap, etk = p.ccol(eps)
            p.wait("act", tk, etk)
            t = nc.scalar.activation(ac, ac, AF.Sqrt, bias=eap)
            tk = p.tok("act", t)
            p.wait("dve", tk)
            t = nc.vector.reciprocal(ac, ac)
            tk = p.tok("dve", t)
            p.wait("dve", tk)
            t = nc.vector.scalar_tensor_tensor(bc, s1, -1.0, ac, ALU.mult, ALU.mult)
            return p.tok("dve", t)

        tk = stats_chain(1e-5)
        p.wait("dve", tk, r1, q1)
        t = nc.vector.tensor_scalar(R["tmpf"][:], xt, ac[:, 0:1], bc[:, 0:1],
                                    ALU.mult, ALU.add)
        ln1 = p.tok("dve", t)
        p.wait("dve", ln1)
        t = nc.vector.reduce_sum(s1, R["tmpf"][:], axis=AX.X)
        r2 = p.tok("dve", t)
        p.wait("act", ln1, p.last("dve"))
        t = nc.scalar.activation(R["tmpsq"][:], R["tmpf"][:], AF.Square,
                                 accum_out=s2)
        q2 = p.tok("act", t)
        tk = stats_chain(1e-3)
        p.wait("dve", tk, r2, q2)
        t = nc.vector.tensor_scalar(R["tmpf"][:], R["tmpf"][:], ac[:, 0:1],
                                    bc[:, 0:1], ALU.mult, ALU.add)
        u_tok = p.tok("dve", t)
        p.wait("dve", u_tok, g_tok, b_tok)
        t = nc.vector.tensor_tensor(R["tmpf"][:], R["tmpf"][:], R["bb1"][:], ALU.mult)
        tk = p.tok("dve", t)
        p.wait("dve", tk)
        t = nc.vector.tensor_tensor(xn2[:, i, :], R["tmpf"][:], R["bb2"][:], ALU.add)
        tile_toks[i] = p.tok("dve", t)
        p.wait("dve", tile_toks[i], gm_tok)
        t = nc.vector.reduce_max(s1, xn2[:, i, :], axis=AX.X,
                                 apply_absolute_value=True)
        tk = p.tok("dve", t)
        p.wait("dve", tk)
        t = nc.vector.tensor_tensor(gmax, gmax, s1, ALU.max)
        gm_tok = p.tok("dve", t)

    p.wait("pe", gm_tok)
    t = nc.tensor.transpose(R["ps_row"][0:1, 0:128],
                            gmax.rearrange("p one -> p (one)").unsqueeze(1)
                            if False else gmax, R["ident_f"][:])
    ptk = p.tok("pe", t)
    p.wait("dve", ptk)
    t = nc.vector.reduce_max(R["stats"][0:1, 49:50], R["ps_row"][0:1, 0:128],
                             axis=AX.X)
    tk = p.tok("dve", t)
    p.wait("sp", tk)
    cctk = p.dma("sp", cc["in"], R["stats"][0:1, 49:50])
    p.wait("pool", cctk)
    t = nc.gpsimd.collective_compute("AllReduce", ALU.max,
                                     [list(range(NCORES))],
                                     [cc["in"].opt()], [cc["out"].opt()])
    t.then_inc(p.sem["cc"], 1)
    p.wait("sp", ("cc", 1))
    cctk = p.dma("sp", R["stats"][0:1, 49:50], cc["out"])
    p.wait("dve", cctk)
    t = nc.vector.reciprocal(st[0:1, 50:51], st[0:1, 49:50])
    tk = p.tok("dve", t)
    p.wait("dve", tk)
    t = nc.vector.tensor_scalar_mul(st[0:1, 50:51], st[0:1, 50:51], QB)
    tk = p.tok("dve", t)
    s_tok = bcast_row_to_col(p, R, st[0:1, 50:51], st[:, 51:52])

    wreg = R["Wr"].bitcast(BF16)
    wq = wreg[:, 0:8192].rearrange("p (c n) -> p c n", n=1024)
    wcol = st[:, 52:60]
    for c in range(KC):
        p.wait("sp", p.last("dve"), p.last("act"))
        tk = p.dma("sp", R["tmpf"][:], W["l3f_w"][c * 128:(c + 1) * 128, :])
        p.wait("dve", tk)
        t = nc.vector.reduce_sum(wcol[:, c:c + 1], R["tmpf"][:], axis=AX.X,
                                 apply_absolute_value=True)
        p.tok("dve", t)
    tk = p.last("dve")
    p.wait("dve", tk)
    t = nc.vector.reduce_sum(wcol[:, 0:1], wcol[:, 0:8], axis=AX.X)
    tk = p.tok("dve", t)
    p.wait("pe", tk)
    t = nc.tensor.matmul(R["ps_row"][0:1, 0:1], R["ones_col"][:],
                         wcol[:, 0:1], start=True, stop=True)
    ptk = p.tok("pe", t)
    p.wait("dve", ptk)
    t = nc.vector.tensor_scalar(st[0:1, 60:61], R["ps_row"][0:1, 0:1],
                                1.0 / (D * D), 1e-5, ALU.mult, ALU.add)
    tk = p.tok("dve", t)
    p.wait("dve", tk)
    t = nc.vector.reciprocal(st[0:1, 61:62], st[0:1, 60:61])
    tk = p.tok("dve", t)
    rw_tok = bcast_row_to_col(p, R, st[0:1, 61:62], st[:, 62:63])
    p.wait("dve", rw_tok)
    t = nc.vector.tensor_tensor(st[0:1, 63:64], st[0:1, 49:50], st[0:1, 60:61],
                                ALU.mult)
    tk = p.tok("dve", t)
    p.wait("dve", tk)
    t = nc.vector.tensor_scalar_add(st[0:1, 64:65], st[0:1, 60:61], -1e-5)
    tk = p.tok("dve", t)
    p.wait("dve", tk)
    t = nc.vector.tensor_tensor(st[0:1, 63:64], st[0:1, 49:50], st[0:1, 64:65],
                                ALU.mult)
    tk = p.tok("dve", t)
    p.wait("dve", tk)
    t = nc.vector.tensor_scalar_mul(st[0:1, 63:64], st[0:1, 63:64], 1.0 / QB)
    tk = p.tok("dve", t)
    sc2_tok = bcast_row_to_col(p, R, st[0:1, 63:64], st[:, 65:66])

    for c in range(KC):
        p.wait("sp", p.last("dve"))
        tk = p.dma("sp", R["tmpf"][:], W["l3f_w"][c * 128:(c + 1) * 128, :])
        p.wait("dve", tk, rw_tok)
        t = nc.vector.tensor_scalar(R["tmpf"][:], R["tmpf"][:], st[:, 62:63],
                                    MAGIC, ALU.mult, ALU.add)
        ptk = p.tok("dve", t)
        p.wait("dve", ptk)
        t = nc.vector.tensor_scalar(R["tmpf"][:], R["tmpf"][:], MAGIC, 1.0,
                                    ALU.subtract, ALU.min)
        ptk = p.tok("dve", t)
        p.wait("dve", ptk)
        t = nc.vector.tensor_scalar(wq[:, c, :], R["tmpf"][:], -1.0, None,
                                    ALU.max)
        p.tok("dve", t)
    wq_tok = p.last("dve")

    xqT = R["B"].bitcast(BF16).rearrange("p (c t) -> p c t", t=T)
    R["_fm_evac_tok"] = [None]
    xq_tok = [None] * NT
    prev_p2 = p.last("dve")
    for i in range(NT):
        p.wait("dve", tile_toks[i], s_tok, prev_p2)
        t = nc.vector.tensor_scalar(R["tmpsq"][:], xn2[:, i, :], st[:, 51:52],
                                    BOUND, ALU.mult, ALU.min)
        tk = p.tok("dve", t)
        p.wait("dve", tk)
        t = nc.vector.tensor_scalar(xn2[:, i, :], R["tmpsq"][:], -BOUND, None,
                                    ALU.max)
        tk = p.tok("dve", t)
        prev_p2 = tk
        p.wait("act", tk)
        emit_tile_to_fm(p, R, i, xn2[:, i, :], tk, xqT[:],
                        evac_eng="dve" if i % 2 == 0 else "act")
        xq_tok[i] = R["_fm_evac_tok"][0]

    xres_tok = [None] * NT
    ps_yy = [R["ps_a"], R["ps_b"]]
    ygq = {}
    for i in range(NT):
        for half in range(2):
            q = half
            osl = slice(half * 512, (half + 1) * 512)
            p.wait("pe", wq_tok, xq_tok[-1], sc2_tok, ygq.get(q))
            for c in range(KC):
                t = nc.tensor.matmul(ps_yy[q][:], xqT[:, c, i * 128:(i + 1) * 128],
                                     wq[:, c, osl], start=(c == 0),
                                     stop=(c == KC - 1))
            mtok = p.tok("pe", t)
            p.wait("dve", mtok, xres_tok[i])
            t = nc.vector.scalar_tensor_tensor(
                R["x_res"][:, i, osl], ps_yy[q][:], st[:, 65:66],
                R["x_res"][:, i, osl], ALU.mult, ALU.add)
            xres_tok[i] = p.tok("dve", t)
            ygq[q] = xres_tok[i]
    p.barrier()


def emit_l1f(p, R, W):
    nc = p.nc
    ln_tok = emit_ln_to_xnT(p, R)
    xnT = R["xnT"]
    p.barrier()

    st = R["stats"]
    idxf_all = st[:, 0:NT]
    ps_l = R["ps_row"]
    wgv = R["Wr"].bitcast(BF16)
    wg_sb = R["kv_bf"][:, 0, :].rearrange("p (c n) -> p c n", n=8)
    w1_sb = wgv[:, 0:8192].rearrange("p (c n) -> p c n", n=1024)
    w2_bufs = [R["B"].bitcast(BF16)[:, 0:8192].rearrange("p (c n) -> p c n", n=1024),
               R["B"].bitcast(BF16)[:, 8192:16384].rearrange("p (c n) -> p c n", n=1024)]
    h_fm = R["A"].bitcast(BF16).rearrange("p (c t) -> p c t", t=T)

    wg_tok = p.dma("pool", wg_sb, W["l1_Wg"])
    bg_tok = vec_to_bcast(p, R, W["l1_bg"], R["bb1"][:, 0:8], 8)

    logit_sb = st[:, 40:48]
    top8 = st[:, 48:56]
    idx_u = R["idx_u"]
    idx_col = R["idx_c"]
    b2g_tok = [None] * NT
    for i in range(NT):
        til = slice(i * 128, (i + 1) * 128)
        p.wait("pe", wg_tok, ln_tok, bg_tok, p.last("dve") if i else None)
        for c in range(KC):
            t = nc.tensor.matmul(ps_l[:, 0:8], xnT[:, c, til], wg_sb[:, c, :],
                                 start=(c == 0), stop=(c == KC - 1))
        mtok = p.tok("pe", t)
        p.wait("dve", mtok, bg_tok)
        t = nc.vector.tensor_add(logit_sb, ps_l[:, 0:8], R["bb1"][:, 0:8])
        tk = p.tok("dve", t)
        p.wait("dve", tk)
        t = nc.vector.max(top8, logit_sb)
        tk = p.tok("dve", t)
        p.wait("dve", tk)
        t = nc.vector.max_index(idx_u, top8, logit_sb)
        tk = p.tok("dve", t)
        p.wait("dve", tk)
        t = nc.vector.tensor_copy(idxf_all[:, i:i + 1], idx_u[:, 0:1])
        p.tok("dve", t)
        t = nc.vector.tensor_copy(idx_col[:, i:i + 1], idx_u[:, 0:1])
        tk = p.tok("dve", t)
        p.wait("pool", tk)
        b2g_tok[i] = p.dma_indirect_add(R["x_res"][:, i, :], W["l1_b2"],
                                        idx_col[:, i:i + 1])
    idx_tok = p.last("dve")

    bcols = st[:, 16:24]
    mask_all = st[:, 24:40]
    ps_h = [R["ps_a"], p.bps("ps_h2_l1f", [128, 512], F32)]
    ps_y = [R["ps_b"], p.bps("ps_y2_l1f", [128, 512], F32)]
    w_use = [None, None]
    hguard = None
    sguard = None
    aguard = None
    w1_tok = p.dma("pool", w1_sb, W["l1_W1"][0])
    w2_toks = [p.dma("pool", w2_bufs[0], W["l1_W2"][0]), None]
    for e in range(8):
        w2_sb = w2_bufs[e % 2]
        w2_tok = w2_toks[e % 2]
        if e + 1 < 8:
            p.wait("pool", w_use[1])
            w2_toks[(e + 1) % 2] = p.dma("pool", w2_bufs[(e + 1) % 2],
                                         W["l1_W2"][e + 1])
        b1_tok = vec_to_cols(p, R, W["l1_b1"][e], bcols, 1024)
        p.wait("dve", idx_tok)
        t = nc.vector.tensor_single_scalar(mask_all, idxf_all, float(e),
                                           ALU.is_equal)
        m_tok = p.tok("dve", t)
        agq = [aguard, aguard]
        for ic in range(KC):
            for ts4 in range(4):
                q = ts4 % 2
                tsl = slice(ts4 * 512, (ts4 + 1) * 512)
                p.wait("pe", w1_tok, agq[q], hguard)
                for c in range(KC):
                    t = nc.tensor.matmul(
                        ps_h[q][:], w1_sb[:, c, ic * 128:(ic + 1) * 128],
                        xnT[:, c, tsl], start=(c == 0), stop=(c == KC - 1))
                mtok = p.tok("pe", t)
                p.wait("act", mtok, b1_tok, hguard)
                t = nc.scalar.activation(h_fm[:, ic, tsl], ps_h[q][:], AF.Gelu,
                                         bias=bcols[:, ic:ic + 1])
                agq[q] = p.tok("act", t)
        aguard = p.last("act")
        if e + 1 < 8:
            p.wait("pool", p.last("pe"))
            w1_tok = p.dma("pool", w1_sb, W["l1_W1"][e + 1])
        sgq = [sguard, sguard]
        for i in range(NT):
            til = slice(i * 128, (i + 1) * 128)
            for half in range(2):
                q = half
                osl = slice(half * 512, (half + 1) * 512)
                p.wait("pe", w2_tok, aguard, sgq[q])
                for ic in range(KC):
                    t = nc.tensor.matmul(ps_y[q][:], h_fm[:, ic, til],
                                         w2_sb[:, ic, osl], start=(ic == 0),
                                         stop=(ic == KC - 1))
                mtok = p.tok("pe", t)
                p.wait("dve", mtok, m_tok, b2g_tok[i])
                t = nc.vector.scalar_tensor_tensor(
                    R["x_res"][:, i, osl], ps_y[q][:], mask_all[:, i:i + 1],
                    R["x_res"][:, i, osl], ALU.mult, ALU.add)
                sgq[q] = p.tok("dve", t)
        sguard = p.last("dve")
        hguard = p.last("pe")
        w_use = [hguard, hguard]
    p.barrier()


L1A_STAGE = 5

def emit_l1a(p, R, W):
    nc = p.nc
    ln_tok = emit_ln_to_xnT(p, R)
    xnT = R["xnT"]
    p.barrier()
    st = R["stats"]

    q_fm = R["A"].bitcast(BF16).rearrange("p (c t) -> p c t", t=T)
    k_tm = R["B"].bitcast(BF16).rearrange("p (i d) -> p i d", d=D)
    wsb = R["Wr"].bitcast(BF16)[:, 0:8192].rearrange("p (c n) -> p c n", n=1024)
    ps_c = p.bps("ps_c_l1a", [128, 512], F32)
    ps_d = p.bps("ps_d_l1a", [128, 512], F32)
    ps_p = [ps_c, ps_d]

    bq_c = st[:, 0:8]
    bq1_c = st[:, 8:16]
    btk = vec_to_cols(p, R, W["l1_bq"], bq_c, 1024)
    p.wait("dve", btk)
    t = nc.vector.tensor_scalar_add(bq1_c, bq_c, 1.0)
    b1tk = p.tok("dve", t)
    wtok = p.dma("pool", wsb, W["l1_wqT"])
    dgq = {}
    dguard = None
    for oc in range(KC):
        for ts4 in range(4):
            q = ts4 % 2
            tsl = slice(ts4 * 512, (ts4 + 1) * 512)
            p.wait("pe", wtok, ln_tok, dgq.get(q))
            for c in range(KC):
                t = nc.tensor.matmul(ps_p[q][:], wsb[:, c, oc * 128:(oc + 1) * 128],
                                     xnT[:, c, tsl], start=(c == 0),
                                     stop=(c == KC - 1))
            mtok = p.tok("pe", t)
            p.wait("act", mtok, btk, dgq.get(q))
            t = nc.scalar.activation(R["tmpf"][:, q * 512:q * 512 + 512],
                                     ps_p[q][:], AF.Exp,
                                     bias=bq_c[:, oc:oc + 1])
            eguard = p.tok("act", t)
            p.wait("dve", eguard, b1tk)
            t = nc.vector.scalar_tensor_tensor(
                q_fm[:, oc, tsl], ps_p[q][:], bq1_c[:, oc:oc + 1],
                R["tmpf"][:, q * 512:q * 512 + 512], ALU.add, ALU.min)
            dguard = p.tok("dve", t)
            dgq[q] = dguard
    q_tok = dguard
    if L1A_STAGE <= 1:
        p.barrier()
        return

    bk_tok = vec_to_bcast(p, R, W["l1_bk"], R["bb1"], 1024)
    p.wait("dve", bk_tok)
    t = nc.vector.tensor_scalar_add(R["bb2"][:], R["bb1"][:], 1.0)
    bk1_tok = p.tok("dve", t)
    p.wait("pool", p.last("pe"))
    wtok = p.dma("pool", wsb, W["l1_wkT"])
    kgq = {}
    for i in range(NT):
        til = slice(i * 128, (i + 1) * 128)
        for half in range(2):
            q = half
            osl = slice(half * 512, (half + 1) * 512)
            p.wait("pe", wtok, dguard if i == 0 and half == 0 else None,
                   kgq.get(q))
            for c in range(KC):
                t = nc.tensor.matmul(ps_p[q][:], xnT[:, c, til],
                                     wsb[:, c, osl], start=(c == 0),
                                     stop=(c == KC - 1))
            mtok = p.tok("pe", t)
            p.wait("dve", mtok, bk1_tok, kgq.get(q))
            t = nc.vector.tensor_tensor(R["tmpf"][:, q * 512:q * 512 + 512],
                                        ps_p[q][:], R["bb1"][:, osl], ALU.add)
            dtk = p.tok("dve", t)
            p.wait("act", dtk)
            t = nc.scalar.activation(R["tmpsq"][:, q * 512:q * 512 + 512],
                                     R["tmpf"][:, q * 512:q * 512 + 512], AF.Exp)
            aguard = p.tok("act", t)
            p.wait("dve", aguard)
            t = nc.vector.scalar_tensor_tensor(
                k_tm[:, i, osl], R["tmpf"][:, q * 512:q * 512 + 512], 1.0,
                R["tmpsq"][:, q * 512:q * 512 + 512], ALU.add, ALU.min)
            dguard = p.tok("dve", t)
            kgq[q] = dguard
    k_tok = dguard
    if L1A_STAGE <= 2:
        p.barrier()
        return

    bv_tok = vec_to_bcast(p, R, W["l1_bv"], R["bb1"], 1024)
    p.wait("pool", p.last("pe"))
    wtok = p.dma("pool", wsb, W["l1_wvT"])
    ps_kv = [R["ps_a"], R["ps_b"]]
    ps_ks = R["ps_row"]
    vguard = [None, None]
    vgq = {}
    for i in range(NT):
        til = slice(i * 128, (i + 1) * 128)
        vb = R["tmpbf_a"] if i % 2 == 0 else R["tmpbf_b"]
        for half in range(2):
            q = half
            osl = slice(half * 512, (half + 1) * 512)
            p.wait("pe", wtok, k_tok, q_tok,
                   dguard if i == 0 and half == 0 else None,
                   vguard[i % 2], bv_tok, vgq.get(q))
            for c in range(KC):
                t = nc.tensor.matmul(ps_p[q][:], xnT[:, c, til],
                                     wsb[:, c, osl], start=(c == 0),
                                     stop=(c == KC - 1))
            mtok = p.tok("pe", t)
            p.wait("dve", mtok, bv_tok, vguard[i % 2])
            t = nc.vector.tensor_tensor(vb[:, osl], ps_p[q][:], R["bb1"][:, osl],
                                        ALU.add)
            dguard = p.tok("dve", t)
            vgq[q] = dguard
        p.wait("pe", dguard)
        for h in range(16):
            hb = ps_kv[h // 8]
            hsl = slice((h % 8) * 64, (h % 8) * 64 + 64)
            t = nc.tensor.matmul(hb[0:64, hsl], k_tm[:, i, h * 64:(h + 1) * 64],
                                 vb[:, h * 64:(h + 1) * 64],
                                 start=(i == 0 and h % 8 == 0),
                                 stop=(i == NT - 1 and h % 8 == 7))
            t = nc.tensor.matmul(ps_ks[0:64, h:h + 1],
                                 k_tm[:, i, h * 64:(h + 1) * 64],
                                 R["ones_bfc"][:],
                                 start=(i == 0 and h == 0),
                                 stop=(i == NT - 1 and h == 15))
        vguard[i % 2] = p.tok("pe", t)
    kv_tok = p.last("pe")

    kv_bf = R["kv_bf"]
    ks_bf = R["ks_bf"]
    p.wait("dve", kv_tok)
    for lohi in range(2):
        base = lohi * 64
        t = nc.vector.tensor_copy(
            kv_bf[base:base + 64, 0:8, :],
            ps_kv[0][0:64, :].rearrange("p (h d) -> p h d", d=64))
        p.tok("dve", t)
        t = nc.vector.tensor_copy(
            kv_bf[base:base + 64, 8:16, :],
            ps_kv[1][0:64, :].rearrange("p (h d) -> p h d", d=64))
        p.tok("dve", t)
        t = nc.vector.tensor_copy(ks_bf[base:base + 64, :], ps_ks[0:64, 0:16])
        p.tok("dve", t)
    kvbf_tok = p.last("dve")
    if L1A_STAGE <= 3:
        p.barrier()
        return

    attn_fm = R["B"].bitcast(BF16).rearrange("p (c t) -> p c t", t=T)
    ps_qk = [R["ps_a"], R["ps_b"]]
    ps_n = R["ps_row"]
    R["_fm_evac_tok"] = [None]
    nguard = None
    atoks = []
    for i in range(NT):
        til = slice(i * 128, (i + 1) * 128)
        p.wait("pe", kvbf_tok, nguard)
        for h in range(16):
            hb = (h % 2) * 64
            ch = h // 2
            bank = h % 2
            slot = h // 2
            t = nc.tensor.matmul(
                ps_qk[bank][:, slot * 64:slot * 64 + 64],
                q_fm[hb:hb + 64, ch, til], kv_bf[hb:hb + 64, h, :],
                start=(h < 2), stop=(h >= 14))
            nb = ps_n if bank == 0 else ps_c
            t = nc.tensor.matmul(
                nb[:, slot: slot + 1],
                q_fm[hb:hb + 64, ch, til],
                ks_bf[hb:hb + 64, h:h + 1], start=(h < 2), stop=(h >= 14))
        mtok = p.tok("pe", t)
        p.wait("dve", mtok)
        t = nc.vector.tensor_scalar_max(st[:, 32:40], ps_n[:, 0:8], 1e-6)
        p.tok("dve", t)
        t = nc.vector.tensor_scalar_max(st[:, 40:48], ps_c[:, 0:8], 1e-6)
        tk = p.tok("dve", t)
        p.wait("dve", tk)
        t = nc.vector.reciprocal(st[:, 32:48], st[:, 32:48])
        tk = p.tok("dve", t)
        p.wait("dve", tk)
        vb = R["tmpbf_a"] if i % 2 == 0 else R["tmpbf_b"]
        rb = st[:, 32:48].unsqueeze(2).to_broadcast([128, 16, 64])
        hview0 = ps_qk[0][:].rearrange("p (h d) -> p h d", d=64)
        hview1 = ps_qk[1][:].rearrange("p (h d) -> p h d", d=64)
        if L1A_STAGE >= 4.2:
            vbv = vb[:].rearrange("p (a b d) -> p a b d", b=2, d=64)
            t = nc.vector.tensor_tensor(vbv[:, :, 0, :], hview0,
                                        rb[:, 0:8], ALU.mult)
            p.tok("dve", t)
            t = nc.vector.tensor_tensor(vbv[:, :, 1, :], hview1,
                                        rb[:, 8:16], ALU.mult)
            nguard = p.tok("dve", t)
        else:
            nguard = p.last("dve")
        if L1A_STAGE >= 4.3:
            emit_tile_to_fm(p, R, i, vb[:], nguard, attn_fm[:],
                            evac_eng="dve" if i % 2 == 0 else "act")
            atoks.append(R["_fm_evac_tok"][0])

    if L1A_STAGE <= 4.9:
        p.barrier()
        return
    bo_tok = vec_to_bcast(p, R, W["l1_bo"], R["bb1"], 1024)
    p.wait("pool", p.last("pe"))
    wtok = p.dma("pool", wsb, W["l1_woT"])
    xres_tok = [None] * NT
    for i in range(NT):
        p.wait("dve", bo_tok)
        t = nc.vector.tensor_tensor(R["x_res"][:, i, :], R["x_res"][:, i, :],
                                    R["bb1"][:], ALU.add)
        xres_tok[i] = p.tok("dve", t)
    ygq = {}
    yguard = p.last("dve")
    for i in range(NT):
        til = slice(i * 128, (i + 1) * 128)
        for half in range(2):
            q = half
            osl = slice(half * 512, (half + 1) * 512)
            p.wait("pe", wtok, atoks[-1],
                   yguard if i == 0 and half == 0 else None, ygq.get(q))
            for c in range(KC):
                t = nc.tensor.matmul(ps_p[q][:], attn_fm[:, c, til],
                                     wsb[:, c, osl], start=(c == 0),
                                     stop=(c == KC - 1))
            mtok = p.tok("pe", t)
            p.wait("dve", mtok, xres_tok[i])
            t = nc.vector.scalar_tensor_tensor(
                R["x_res"][:, i, osl], ps_p[q][:], 1.0,
                R["x_res"][:, i, osl], ALU.mult, ALU.add)
            xres_tok[i] = p.tok("dve", t)
            ygq[q] = xres_tok[i]
    p.barrier()


def emit_wquant_to_dram(p, R, W, src_name, dst_dram, nrows, st_beta_col):
    nc = p.nc
    st = R["stats"]
    src = W[src_name]
    ncols = src.shape[1]
    nch = nrows // 128
    nhalf = ncols // 2048 if ncols >= 2048 else 1
    pw = min(2048, ncols)
    wrf = R["Wr"].bitcast(F32)
    slots = [wrf[:, 0:pw], wrf[:, 2048:2048 + pw]]
    outb = [R["A"].bitcast(BF16)[:, 0:pw], R["A"].bitcast(BF16)[:, 2048:2048 + pw]]
    wcol = st[:, 64:65]
    acc = st[:, 65:66]
    t = nc.vector.memset(acc, 0.0)
    atk = p.tok("dve", t)
    rd_guard = [None, None]
    pieces = [(c, hh) for c in range(nch) for hh in range(nhalf)]
    for k, (c, hh) in enumerate(pieces):
        sl = k % 2
        qn = "sp" if sl == 0 else "act"
        p.wait(qn, rd_guard[sl])
        tk = p.dma(qn, slots[sl], src[c * 128:(c + 1) * 128,
                                      hh * 2048:hh * 2048 + pw])
        p.wait("dve", tk, atk)
        t = nc.vector.reduce_sum(wcol, slots[sl], axis=AX.X,
                                 apply_absolute_value=True)
        tk = p.tok("dve", t)
        p.wait("dve", tk)
        t = nc.vector.tensor_tensor(acc, acc, wcol, ALU.add)
        atk = p.tok("dve", t)
        rd_guard[sl] = atk
    p.wait("pe", atk)
    t = nc.tensor.matmul(R["ps_small"][0:1, 0:1], R["ones_col"][:], acc,
                         start=True, stop=True)
    ptk = p.tok("pe", t)
    p.wait("dve", ptk)
    t = nc.vector.tensor_scalar(st[0:1, st_beta_col:st_beta_col + 1],
                                R["ps_small"][0:1, 0:1],
                                1.0 / (nrows * ncols), 1e-5, ALU.mult, ALU.max)
    tk = p.tok("dve", t)
    p.wait("dve", tk)
    t = nc.vector.reciprocal(st[0:1, 66:67], st[0:1, st_beta_col:st_beta_col + 1])
    tk = p.tok("dve", t)
    sc_tok = bcast_row_to_col(p, R, st[0:1, 66:67], st[:, 67:68])
    g1 = [None, None]
    g3 = [None, None]
    wq_guard = None
    for k, (c, hh) in enumerate(pieces):
        sl = k % 2
        qn = "sp" if sl == 0 else "act"
        e1, e2, e3 = ("dve", "dve", "dve")
        E = {"pool": nc.gpsimd, "dve": nc.vector}
        p.wait(qn, g3[sl], sc_tok)
        tk = p.dma(qn, slots[sl], src[c * 128:(c + 1) * 128,
                                      hh * 2048:hh * 2048 + pw])
        p.wait(e1, tk, sc_tok, g3[sl])
        t = E[e1].tensor_scalar(slots[sl], slots[sl], st[:, 67:68], MAGIC,
                                ALU.mult, ALU.add)
        t1 = p.tok(e1, t)
        p.wait(e2, t1)
        t = E[e2].tensor_scalar(slots[sl], slots[sl], MAGIC, 1.0,
                                ALU.subtract, ALU.min)
        t2 = p.tok(e2, t)
        p.wait(e3, t2, g1[sl])
        t = E[e3].tensor_scalar(outb[sl], slots[sl], -1.0, None, ALU.max)
        t3 = p.tok(e3, t)
        g3[sl] = t3
        p.wait(qn, t3)
        wq_guard = p.dma(qn, dst_dram[c * 128:(c + 1) * 128,
                                      hh * 2048:hh * 2048 + pw], outb[sl])
        g1[sl] = wq_guard
    return wq_guard


def emit_l2f(p, R, W, scr):
    nc = p.nc
    st = R["stats"]
    h_dram = scr["h_dram"]
    wqg_dram = scr["wqg_dram"]
    wqu_dram = scr["wqu_dram"]
    wqd_dram = scr["wqd_dram"]
    ps_c = p.bps("ps_c_l2f", [128, 512], F32)

    kT = R["B"].bitcast(BF16).rearrange("p (c t) -> p c t", t=T)
    rs_all = st[:, 16:32]
    s_all = st[:, 0:16]
    R["_fm_evac_tok"] = [None]

    def consume(i, xn, tok):
        scr2 = R["bb2"]
        kbf = R["tmpbf_a"] if i % 2 == 0 else R["tmpbf_b"]
        p.wait("dve", tok)
        t = nc.vector.reduce_max(st[:, 32:33], xn[:], axis=AX.X,
                                 apply_absolute_value=True)
        tk = p.tok("dve", t)
        p.wait("dve", tk)
        t = nc.vector.tensor_scalar(rs_all[:, i:i + 1], st[:, 32:33], 1e-5,
                                    1.0 / 127.0, ALU.max, ALU.mult)
        tk = p.tok("dve", t)
        p.wait("dve", tk)
        t = nc.vector.reciprocal(s_all[:, i:i + 1], rs_all[:, i:i + 1])
        tk = p.tok("dve", t)
        p.wait("dve", tk)
        t = nc.vector.tensor_scalar(scr2[:], xn[:], s_all[:, i:i + 1], MAGIC,
                                    ALU.mult, ALU.add)
        tk = p.tok("dve", t)
        p.wait("dve", tk)
        t = nc.vector.tensor_scalar(kbf[:], scr2[:], MAGIC, None, ALU.subtract)
        tk = p.tok("dve", t)
        p.wait("act", tk)
        emit_tile_to_fm(p, R, i, kbf[:], tk, kT[:],
                        evac_eng="dve" if i % 2 == 0 else "act")
        return R["_fm_evac_tok"][0]

    emit_ln(p, R, consume)
    ln_tok = R["_fm_evac_tok"][0]
    p.barrier()

    emit_wquant_to_dram(p, R, W, "l2_wgT", wqg_dram, 1024, 70)
    p.barrier()
    emit_wquant_to_dram(p, R, W, "l2_wuT", wqu_dram, 1024, 71)
    p.barrier()
    emit_wquant_to_dram(p, R, W, "l2_wdT", wqd_dram, 4096, 72)
    p.barrier()

    bgc = st[:, 33:34]
    buc = st[:, 34:35]
    t1 = bcast_row_to_col(p, R, st[0:1, 70:71], bgc)
    t2 = bcast_row_to_col(p, R, st[0:1, 71:72], buc)
    bg_rs = st[:, 36:52]
    bu_rs = st[:, 52:68]
    p.wait("dve", t1, t2)
    t = nc.vector.tensor_scalar(bg_rs, rs_all, bgc, None, ALU.mult)
    p.tok("dve", t)
    t = nc.vector.tensor_scalar(bu_rs, rs_all, buc, None, ALU.mult)
    sc_tok = p.tok("dve", t)
    amax2 = st[:, 0:16]
    t = nc.vector.memset(amax2, 0.0)
    am_tok = p.tok("dve", t)

    wsl = R["Wr"].bitcast(BF16)
    wq_g = wsl[:, 0:4096].rearrange("p (c n) -> p c n", n=512)
    wq_u = wsl[:, 4096:8192].rearrange("p (c n) -> p c n", n=512)
    ps_g = [R["ps_a"], p.bps("ps_g2_l2f", [128, 512], F32)]
    ps_u = [R["ps_b"], p.bps("ps_u2_l2f", [128, 512], F32)]
    agq = {}
    dgq = {}
    dguard = None
    aguard = None
    hguard = {}
    for hs in range(8):
        p.wait("sp", p.last("pe"))
        g_tok = p.dma("sp", wq_g, wqg_dram[:, hs * 512:(hs + 1) * 512]
                      .rearrange("(c p) n -> p c n", p=128))
        u_tok = p.dma("sp", wq_u, wqu_dram[:, hs * 512:(hs + 1) * 512]
                      .rearrange("(c p) n -> p c n", p=128))
        for i in range(NT):
            q = i % 2
            til = slice(i * 128, (i + 1) * 128)
            p.wait("pe", g_tok, ln_tok, agq.get(q))
            for c in range(KC):
                t = nc.tensor.matmul(ps_g[q][:], kT[:, c, til], wq_g[:, c, :],
                                     start=(c == 0), stop=(c == KC - 1))
            gm = p.tok("pe", t)
            p.wait("pe", u_tok, dgq.get(q))
            for c in range(KC):
                t = nc.tensor.matmul(ps_u[q][:], kT[:, c, til], wq_u[:, c, :],
                                     start=(c == 0), stop=(c == KC - 1))
            um = p.tok("pe", t)
            p.wait("act", gm, sc_tok, dgq.get(q))
            t = nc.scalar.activation(R["tmpf"][:, q * 512:q * 512 + 512],
                                     ps_g[q][:], AF.Silu,
                                     scale=bg_rs[:, i:i + 1])
            agq[q] = p.tok("act", t)
            hb = R["tmpbf_a"] if i % 2 == 0 else R["tmpbf_b"]
            p.wait("dve", um, agq[q], am_tok, hguard.get(i % 2))
            t = nc.vector.scalar_tensor_tensor(hb[:, 0:512], ps_u[q][:],
                                               bu_rs[:, i:i + 1],
                                               R["tmpf"][:, q * 512:q * 512 + 512],
                                               ALU.mult, ALU.mult)
            dguard = p.tok("dve", t)
            dgq[q] = dguard
            p.wait("dve", dguard)
            t = nc.vector.reduce_max(st[:, 32:33], hb[:, 0:512], axis=AX.X,
                                     apply_absolute_value=True)
            tk = p.tok("dve", t)
            p.wait("dve", tk)
            t = nc.vector.tensor_tensor(amax2[:, i:i + 1], amax2[:, i:i + 1],
                                        st[:, 32:33], ALU.max)
            am_tok = p.tok("dve", t)
            p.wait("sp", am_tok)
            hguard[i % 2] = p.dma(
                "sp", h_dram.rearrange("(i p) n -> i p n", p=128)[i][:, hs * 512:(hs + 1) * 512],
                hb[:, 0:512])
    p.barrier()

    bdc = st[:, 33:34]
    t1 = bcast_row_to_col(p, R, st[0:1, 72:73], bdc)
    s2_all = st[:, 16:32]
    bd_rs2 = st[:, 36:52]
    p.wait("dve", t1)
    t = nc.vector.tensor_scalar(s2_all, amax2, 1e-5, 1.0 / 127.0, ALU.max, ALU.mult)
    tk = p.tok("dve", t)
    p.wait("dve", tk)
    t = nc.vector.tensor_scalar(bd_rs2, s2_all, bdc, None, ALU.mult)
    sc2_tok = p.tok("dve", t)
    p.wait("dve", sc2_tok)
    t = nc.vector.reciprocal(s2_all, s2_all)
    s2_tok = p.tok("dve", t)

    k2T = R["A"].bitcast(BF16)[:, 0:8192].rearrange("p (c t) -> p c t", t=T)
    wq_d = wsl[:, 0:4096].rearrange("p (c n) -> p c n", n=1024)
    ps_cc = [ps_c, ps_g[1]]
    xres_tok = [None] * NT
    ygq = {}
    yguard = None
    trguard = None
    for hg in range(8):
        p.wait("sp", p.last("pe"))
        d_tok = p.dma("sp", wq_d, wqd_dram[hg * 512:(hg + 1) * 512, :]
                      .rearrange("(c p) n -> p c n", p=128))
        for i in range(NT):
            hb = R["tmpbf_a"] if i % 2 == 0 else R["tmpbf_b"]
            p.wait("sp", p.last("dve"), p.last("pe"))
            htk = p.dma("sp", hb[:, 0:512],
                        h_dram.rearrange("(i p) n -> i p n", p=128)[i][:, hg * 512:(hg + 1) * 512])
            p.wait("dve", htk, s2_tok, trguard)
            t = nc.vector.tensor_scalar(R["tmpf"][:, 0:512], hb[:, 0:512],
                                        s2_all[:, i:i + 1], MAGIC,
                                        ALU.mult, ALU.add)
            tk = p.tok("dve", t)
            p.wait("dve", tk)
            t = nc.vector.tensor_scalar(hb[:, 512:1024], R["tmpf"][:, 0:512],
                                        MAGIC, None, ALU.subtract)
            tk = p.tok("dve", t)
            p.wait("pe", tk, trguard)
            for j in range(4):
                t = nc.tensor.transpose(
                    R["ps_tr"][:, j * 128:(j + 1) * 128],
                    hb[:, 512 + j * 128:512 + (j + 1) * 128], R["ident"][:])
            ptk = p.tok("pe", t)
            eng = "dve" if i % 2 == 0 else "act"
            p.wait(eng, ptk)
            if eng == "dve":
                t = nc.vector.tensor_copy(
                    k2T[:, 0:4, i * 128:(i + 1) * 128],
                    R["ps_tr"][:, 0:512].rearrange("p (c q) -> p c q", q=128))
            else:
                t = nc.scalar.copy(
                    k2T[:, 0:4, i * 128:(i + 1) * 128],
                    R["ps_tr"][:, 0:512].rearrange("p (c q) -> p c q", q=128))
            trguard = p.tok(eng, t)
        for i in range(NT):
            til = slice(i * 128, (i + 1) * 128)
            for half in range(2):
                q = half
                osl = slice(half * 512, (half + 1) * 512)
                p.wait("pe", d_tok, trguard, ygq.get(q))
                for j in range(4):
                    t = nc.tensor.matmul(ps_cc[q][:], k2T[:, j, til],
                                         wq_d[:, j, osl], start=(j == 0),
                                         stop=(j == 3))
                mtok = p.tok("pe", t)
                p.wait("dve", mtok, xres_tok[i], sc2_tok)
                t = nc.vector.scalar_tensor_tensor(
                    R["x_res"][:, i, osl], ps_cc[q][:], bd_rs2[:, i:i + 1],
                    R["x_res"][:, i, osl], ALU.mult, ALU.add)
                xres_tok[i] = p.tok("dve", t)
                ygq[q] = xres_tok[i]
                yguard = xres_tok[i]
    p.barrier()


def emit_l0a(p, R, W, scr):
    nc = p.nc
    st = R["stats"]
    q_dram = scr["q_dram"]
    H = 16
    aux = R["aux"]
    auxf = aux.bitcast(F32)
    qselT = aux[:, 0:160]
    attnT = aux[:, 160:320]
    ao_sb = aux[:, 320:1344].rearrange("p (h d) -> p h d", d=64)
    t32i = auxf[0:32, 672:704]
    t32o = auxf[0:32, 704:736]
    wrf = R["Wr"].bitcast(F32)
    qs_byh = wrf[0:16, 0:2048]
    qs_scr = wrf[0:16, 2048:4096]
    iota2k = wrf[0:10, 0:2048]
    qs_all = R["bb2"][:, 0:256].rearrange("p (i h) -> p i h", h=16)

    ln_tok = emit_ln_to_xnT(p, R)
    xnT = R["xnT"]
    p.barrier()

    wsb = R["Wr"].bitcast(BF16)[:, 0:8192].rearrange("p (c n) -> p c n", n=1024)
    ps_a = R["ps_a"]
    ps_b = R["ps_b"]

    bq_tok = vec_to_bcast(p, R, W["l0_bq"], R["bb1"], 1024)
    p.wait("pool", p.last("pe"))
    wq_tok = p.dma("pool", wsb, W["l0_wqT"])
    eps9, eps9_tok = p.ccol(1e-9)
    dguard = [None, None]
    pguard = [None, None]
    Af = R["A"].bitcast(F32)
    ps_big = p.bps("ps_big_l0a", [128, 2048], F32)
    qdma = [None] * NT
    for i in range(NT):
        qq = i % 2
        til = slice(i * 128, (i + 1) * 128)
        ps = [ps_a, ps_b] if qq == 0 else [ps_big[:, 0:512], ps_big[:, 512:1024]]
        QF = R["tmpf"][:] if qq == 0 else Af[:, 0:1024]
        QS = R["tmpsq"][:] if qq == 0 else Af[:, 1024:2048]
        stq = R["stats"] if qq == 0 else R["bb2"][:, 256:336]
        cbf = R["tmpbf_a"] if qq == 0 else R["tmpbf_b"]
        p.wait("pe", wq_tok, ln_tok, dguard[qq], p.last("act") if i < 2 else None)
        for half in range(2):
            osl = slice(half * 512, (half + 1) * 512)
            for c in range(KC):
                t = nc.tensor.matmul(ps[half][:], xnT[:, c, til],
                                     wsb[:, c, osl], start=(c == 0),
                                     stop=(c == KC - 1))
        mtok = p.tok("pe", t)
        p.wait("dve", mtok, bq_tok, pguard[qq], dguard[qq])
        t = nc.vector.tensor_tensor(QF[:, 0:512], ps[0][:],
                                    R["bb1"][:, 0:512], ALU.add)
        p.tok("dve", t)
        t = nc.vector.tensor_tensor(QF[:, 512:1024], ps[1][:],
                                    R["bb1"][:, 512:1024], ALU.add)
        qtk = p.tok("dve", t)
        p.wait("pool", qtk, qdma[i - 2] if i >= 2 else None)
        t = nc.gpsimd.tensor_copy(cbf[:], QF[:])
        ctk = p.tok("pool", t)
        pguard[qq] = ctk
        p.wait("sp", ctk)
        qdma[i] = p.dma("sp", q_dram.rearrange("(i p) d -> i p d", p=128)[i],
                        cbf[:])
        q3 = QF[:].rearrange("p (h d) -> p h d", d=64)
        p.wait("dve", qtk)
        t = nc.vector.reduce_sum(stq[:, 0:16], q3, axis=AX.X)
        s1tk = p.tok("dve", t)
        p.wait("act", qtk)
        t = nc.scalar.activation(QS[:], QF[:], AF.Square)
        sqt = p.tok("act", t)
        p.wait("dve", sqt)
        t = nc.vector.reduce_sum(stq[:, 16:32],
                                 QS[:].rearrange("p (h d) -> p h d", d=64),
                                 axis=AX.X)
        s2tk = p.tok("dve", t)
        p.wait("dve", s2tk)
        t = nc.vector.reduce_max(stq[:, 32:48], q3, axis=AX.X)
        mtk = p.tok("dve", t)
        p.wait("dve", mtk)
        mb = stq[:, 32:48].unsqueeze(2).to_broadcast([128, 16, 64])
        t = nc.vector.tensor_tensor(
            QS[:].rearrange("p (h d) -> p h d", d=64), q3, mb,
            ALU.subtract)
        dtk = p.tok("dve", t)
        p.wait("act", dtk, pguard[qq])
        t = nc.scalar.activation(QF[:], QS[:], AF.Exp)
        etk = p.tok("act", t)
        p.wait("dve", etk)
        t = nc.vector.reduce_sum(stq[:, 48:64],
                                 QF[:].rearrange("p (h d) -> p h d", d=64),
                                 axis=AX.X)
        stk = p.tok("dve", t)
        p.wait("dve", stk)
        t = nc.vector.reciprocal(stq[:, 64:80], stq[:, 48:64])
        rtk = p.tok("dve", t)
        p.wait("dve", rtk)
        rb = stq[:, 64:80].unsqueeze(2).to_broadcast([128, 16, 64])
        t = nc.vector.tensor_tensor(
            QF[:].rearrange("p (h d) -> p h d", d=64),
            QF[:].rearrange("p (h d) -> p h d", d=64), rb, ALU.mult)
        ptk2 = p.tok("dve", t)
        p.wait("act", ptk2, eps9_tok)
        t = nc.scalar.activation(QS[:], QF[:], AF.Ln,
                                 bias=eps9)
        ltk = p.tok("act", t)
        p.wait("dve", ltk)
        t = nc.vector.tensor_tensor(QS[:], QF[:], QS[:],
                                    ALU.mult)
        ptk3 = p.tok("dve", t)
        p.wait("dve", ptk3)
        t = nc.vector.reduce_sum(stq[:, 48:64],
                                 QS[:].rearrange("p (h d) -> p h d", d=64),
                                 axis=AX.X, negate=True)
        enttk = p.tok("dve", t)
        p.wait("act", s2tk)
        t = nc.scalar.activation(stq[:, 64:80], stq[:, 16:32], AF.Sqrt)
        l2tk = p.tok("act", t)
        p.wait("dve", s1tk, l2tk, enttk)
        t = nc.vector.tensor_tensor(qs_all[:, i, :], stq[:, 0:16], stq[:, 0:16],
                                    ALU.mult)
        tk = p.tok("dve", t)
        p.wait("dve", tk)
        t = nc.vector.tensor_scalar(qs_all[:, i, :], qs_all[:, i, :],
                                    -1.0 / 64.0, None, ALU.mult)
        tk = p.tok("dve", t)
        p.wait("dve", tk)
        t = nc.vector.tensor_tensor(qs_all[:, i, :], stq[:, 16:32],
                                    qs_all[:, i, :], ALU.add)
        tk = p.tok("dve", t)
        p.wait("dve", tk)
        t = nc.vector.tensor_scalar(qs_all[:, i, :], qs_all[:, i, :],
                                    0.2 / 63.0, None, ALU.mult)
        tk = p.tok("dve", t)
        p.wait("dve", tk)
        t = nc.vector.scalar_tensor_tensor(stq[:, 64:80], stq[:, 64:80], 0.5,
                                           qs_all[:, i, :], ALU.mult, ALU.add)
        tk = p.tok("dve", t)
        p.wait("dve", tk)
        t = nc.vector.scalar_tensor_tensor(qs_all[:, i, :], stq[:, 48:64], 0.3,
                                           stq[:, 64:80], ALU.mult, ALU.add)
        dguard[qq] = p.tok("dve", t)

    K_fm = R["A"].bitcast(BF16).rearrange("p (c t) -> p c t", t=T)
    V_tm = R["B"].bitcast(BF16).rearrange("p (i d) -> p i d", d=D)
    bkc = st[:, 0:8]
    bk_tok = vec_to_cols(p, R, W["l0_bk"], bkc, 1024)
    p.wait("pool", p.last("pe"), dguard[0], dguard[1])
    wk_tok = p.dma("pool", wsb, W["l0_wkT"])
    kguard = None
    for oc in range(KC):
        for ts4 in range(4):
            tsl = slice(ts4 * 512, (ts4 + 1) * 512)
            p.wait("pe", wk_tok, kguard)
            for c in range(KC):
                t = nc.tensor.matmul(ps_a[:], wsb[:, c, oc * 128:(oc + 1) * 128],
                                     xnT[:, c, tsl], start=(c == 0),
                                     stop=(c == KC - 1))
            mtok = p.tok("pe", t)
            p.wait("dve", mtok, bk_tok)
            t = nc.vector.tensor_scalar_add(K_fm[:, oc, tsl], ps_a[:],
                                            bkc[:, oc:oc + 1])
            kguard = p.tok("dve", t)
    bv_tok = vec_to_bcast(p, R, W["l0_bv"], R["bb1"], 1024)
    p.wait("pool", p.last("pe"), kguard)
    wv_tok = p.dma("pool", wsb, W["l0_wvT"])
    vguard = None
    for i in range(NT):
        til = slice(i * 128, (i + 1) * 128)
        for half in range(2):
            osl = slice(half * 512, (half + 1) * 512)
            p.wait("pe", wv_tok, vguard, bv_tok)
            for c in range(KC):
                t = nc.tensor.matmul(ps_a[:], xnT[:, c, til],
                                     wsb[:, c, osl], start=(c == 0),
                                     stop=(c == KC - 1))
            mtok = p.tok("pe", t)
            p.wait("dve", mtok, bv_tok)
            t = nc.vector.tensor_tensor(V_tm[:, i, osl], ps_a[:],
                                        R["bb1"][:, osl], ALU.add)
            vguard = p.tok("dve", t)

    wr_guard = p.last("pe")
    tguard = None
    for i in range(NT):
        p.wait("pe", dguard[0], dguard[1], wr_guard, tguard)
        t = nc.tensor.transpose(R["ps_row"][0:16, 0:128], qs_all[:, i, :],
                                R["ident_f"][:])
        ptk = p.tok("pe", t)
        p.wait("dve", ptk)
        t = nc.vector.tensor_copy(qs_byh[:, i * 128:(i + 1) * 128],
                                  R["ps_row"][0:16, 0:128])
        tguard = p.tok("dve", t)
    top8 = st[0:16, 16:24]
    next8 = st[0:16, 24:32]
    p.wait("dve", tguard)
    t = nc.vector.max(top8, qs_byh)
    tk = p.tok("dve", t)
    p.wait("dve", tk)
    t = nc.vector.max_index(R["idx_u"][0:16, :], top8, qs_byh)
    tk = p.tok("dve", t)
    p.wait("dve", tk)
    t = nc.vector.tensor_copy(st[0:16, 32:40], R["idx_u"][0:16, :])
    tk1 = p.tok("dve", t)
    p.wait("dve", tk1)
    t = nc.vector.match_replace(qs_scr, top8, qs_byh, -1e30)
    tk = p.tok("dve", t)
    p.wait("dve", tk)
    t = nc.vector.max(next8, qs_scr)
    tk = p.tok("dve", t)
    p.wait("dve", tk)
    t = nc.vector.max_index(R["idx_u"][0:16, :], next8, qs_byh)
    tk = p.tok("dve", t)
    p.wait("dve", tk)
    t = nc.vector.tensor_copy(st[0:16, 40:42], R["idx_u"][0:16, 0:2])
    idxf_tok = p.tok("dve", t)
    t = nc.vector.reduce_sum(st[0:16, 44:45], qs_byh, axis=AX.X)
    s1tk = p.tok("dve", t)
    p.wait("act", tguard, idxf_tok)
    t = nc.scalar.activation(qs_scr, qs_byh, AF.Square,
                             accum_out=st[0:16, 45:46])
    s2tk = p.tok("act", t)
    p.wait("dve", s1tk, s2tk)
    t = nc.vector.tensor_scalar_mul(st[0:16, 46:47], st[0:16, 44:45], 1.0 / T)
    tk = p.tok("dve", t)
    p.wait("dve", tk)
    t = nc.vector.tensor_tensor(st[0:16, 47:48], st[0:16, 44:45],
                                st[0:16, 46:47], ALU.mult)
    tk = p.tok("dve", t)
    p.wait("dve", tk)
    t = nc.vector.tensor_tensor(st[0:16, 47:48], st[0:16, 45:46],
                                st[0:16, 47:48], ALU.subtract)
    tk = p.tok("dve", t)
    p.wait("dve", tk)
    t = nc.vector.tensor_scalar_mul(st[0:16, 47:48], st[0:16, 47:48],
                                    1.0 / (T - 1))
    tk = p.tok("dve", t)
    p.wait("act", tk)
    t = nc.scalar.activation(st[0:16, 47:48], st[0:16, 47:48], AF.Sqrt)
    tk = p.tok("act", t)
    p.wait("dve", tk)
    t = nc.vector.tensor_scalar_add(st[0:16, 46:47], st[0:16, 46:47], 1e-6)
    tk = p.tok("dve", t)
    p.wait("dve", tk)
    t = nc.vector.reciprocal(st[0:16, 46:47], st[0:16, 46:47])
    tk = p.tok("dve", t)
    p.wait("dve", tk)
    t = nc.vector.tensor_tensor(st[0:16, 47:48], st[0:16, 47:48],
                                st[0:16, 46:47], ALU.mult)
    tk = p.tok("dve", t)
    p.wait("dve", tk)
    t = nc.vector.tensor_scalar(st[0:16, 47:48], st[0:16, 47:48], 10.0, MAGIC,
                                ALU.mult, ALU.add)
    tk = p.tok("dve", t)
    p.wait("dve", tk)
    t = nc.vector.tensor_scalar(st[0:16, 47:48], st[0:16, 47:48], MAGIC, 10.0,
                                ALU.subtract, ALU.min)
    tk = p.tok("dve", t)
    p.wait("dve", tk)
    t = nc.vector.tensor_scalar_max(st[0:16, 47:48], st[0:16, 47:48], 3.0)
    ftok = p.tok("dve", t)
    p.wait("pe", ftok)
    t = nc.tensor.matmul(R["ps_small"][0:16, 0:1], R["ones_f"][0:1, 0:16],
                         st[0:1, 47:48], start=True, stop=True)
    ptk = p.tok("pe", t)
    p.wait("dve", ptk)
    t = nc.vector.tensor_copy(st[0:16, 46:47], R["ps_small"][0:16, 0:1])
    utok = p.tok("dve", t)
    t = nc.gpsimd.iota(st[0:16, 48:58], [[1, 10]], channel_multiplier=0,
                       allow_small_or_imprecise_dtypes=True)
    iotok = p.tok("pool", t)
    p.wait("dve", utok, iotok)
    t = nc.vector.tensor_single_scalar(st[0:16, 58:68], st[0:16, 48:58],
                                       st[0:16, 46:47], ALU.is_lt)
    mtok = p.tok("dve", t)
    p.wait("dve", mtok, idxf_tok, tk1)
    t = nc.vector.tensor_scalar_add(st[0:16, 68:78], st[0:16, 32:42], -2048.0)
    tk = p.tok("dve", t)
    p.wait("dve", tk)
    t = nc.vector.tensor_tensor(st[0:16, 68:78], st[0:16, 68:78],
                                st[0:16, 58:68], ALU.mult)
    tk = p.tok("dve", t)
    p.wait("dve", tk)
    t = nc.vector.tensor_scalar_add(st[0:16, 68:78], st[0:16, 68:78], 2048.0)
    sidx_tok = p.tok("dve", t)
    p.wait("dve", sidx_tok)
    t = nc.vector.memset(t32i, 2048.0)
    tk = p.tok("dve", t)
    p.wait("dve", tk)
    t = nc.vector.tensor_copy(t32i[0:16, 0:10], st[0:16, 68:78])
    tk = p.tok("dve", t)
    p.wait("dve", tk)
    t = nc.vector.transpose(t32o, t32i)
    tk = p.tok("dve", t)
    p.wait("dve", tk)
    t = nc.vector.tensor_copy(st[0:10, 48:64], t32o[0:10, 0:16])
    sidxT_tok = p.tok("dve", t)
    p.wait("dve", sidxT_tok)
    t = nc.vector.tensor_copy(t32i[0:16, 0:10], st[0:16, 32:42])
    tk = p.tok("dve", t)
    p.wait("dve", tk)
    t = nc.vector.transpose(t32o, t32i)
    tk = p.tok("dve", t)
    p.wait("dve", tk)
    t = nc.vector.tensor_copy(R["idx_c"][0:10, :], t32o[0:10, 0:16])
    gidx_tok = p.tok("dve", t)

    qsel_tok = None
    tr_tok = None
    for h in range(H):
        hb = (h % 2) * 64
        p.wait("pool", gidx_tok, tr_tok, *(qdma if h == 0 else []))
        qg = p.dma_gather_rows(R["smallbf"][0:10, 1024:2048], q_dram,
                               R["idx_c"][0:10, h:h + 1])
        p.wait("pe", qg, qsel_tok)
        t = nc.tensor.transpose(R["ps_tr"][0:64, 0:10],
                                R["smallbf"][0:10, 1024 + h * 64:1024 + (h + 1) * 64],
                                R["ident"][0:10, 0:10])
        tr_tok = p.tok("pe", t)
        p.wait("dve", tr_tok)
        t = nc.vector.tensor_copy(aux[hb:hb + 64, h * 10:(h + 1) * 10],
                                  R["ps_tr"][0:64, 0:10])
        qsel_tok = p.tok("dve", t)

    ps_sc = ps_big
    att = R["smallbf"][0:10, 0:2048]
    aoguard = None
    attT_guard = None
    for h in range(H):
        hb = (h % 2) * 64
        ch = h // 2
        p.wait("pe", qsel_tok, p.last("dve"), p.last("act"))
        for n4 in range(4):
            nsl = slice(n4 * 512, (n4 + 1) * 512)
            t = nc.tensor.matmul(ps_sc[0:10, nsl],
                                 aux[hb:hb + 64, h * 10:(h + 1) * 10],
                                 K_fm[hb:hb + 64, ch, nsl],
                                 start=True, stop=True)
        sctok = p.tok("pe", t)
        p.wait("dve", sctok)
        t = nc.vector.reduce_max(st[0:10, 78:79], ps_sc[0:10, :], axis=AX.X,
                                 negate=True)
        tk = p.tok("dve", t)
        p.wait("dve", tk)
        t = nc.vector.tensor_scalar_mul(st[0:10, 78:79], st[0:10, 78:79], 0.125)
        tk = p.tok("dve", t)
        p.wait("act", tk, aoguard)
        t = nc.scalar.activation(att, ps_sc[0:10, :], AF.Exp,
                                 bias=st[0:10, 78:79], scale=0.125,
                                 accum_out=st[0:10, 79:80])
        etk = p.tok("act", t)
        p.wait("dve", etk)
        t = nc.vector.reciprocal(st[0:10, 78:79], st[0:10, 79:80])
        rtk = p.tok("dve", t)
        p.wait("pe", etk, attT_guard)
        for j in range(NT):
            t = nc.tensor.transpose(R["ps_tr"][:, j * 10:(j + 1) * 10],
                                    att[:, j * 128:(j + 1) * 128],
                                    R["ident"][0:10, 0:10])
        ttk = p.tok("pe", t)
        p.wait("dve", ttk)
        t = nc.vector.tensor_copy(aux[:, 160:320], R["ps_tr"][:, 0:160])
        atk = p.tok("dve", t)
        attT_guard = atk
        p.wait("pe", atk, aoguard)
        for j in range(NT):
            t = nc.tensor.matmul(R["ps_row"][0:10, 0:64],
                                 aux[:, 160 + j * 10:160 + (j + 1) * 10],
                                 V_tm[:, j, h * 64:(h + 1) * 64],
                                 start=(j == 0), stop=(j == NT - 1))
        vtk = p.tok("pe", t)
        p.wait("dve", vtk, rtk)
        t = nc.vector.tensor_scalar_mul(ao_sb[0:10, h, :],
                                        R["ps_row"][0:10, 0:64],
                                        st[0:10, 78:79])
        aoguard = p.tok("dve", t)

    p.wait("pool", p.last("dve"), p.last("pe"))
    t = nc.gpsimd.iota(iota2k, [[1, 2048]], channel_multiplier=0,
                       allow_small_or_imprecise_dtypes=True)
    io_tok = p.tok("pool", t)
    full_fm = R["xnT"]
    ohguard = None
    evguard = None
    for pair in range(8):
        for par in range(2):
            h = 2 * pair + par
            psx = ps_a if par == 0 else ps_b
            p.wait("dve", io_tok, sidxT_tok, ohguard, aoguard)
            t = nc.vector.tensor_single_scalar(att, iota2k,
                                               st[0:10, 48 + h:49 + h],
                                               ALU.is_equal)
            ohtok = p.tok("dve", t)
            for n4 in range(4):
                nsl = slice(n4 * 512, (n4 + 1) * 512)
                p.wait("pe", ohtok, evguard)
                t = nc.tensor.matmul(psx[0:64, :], ao_sb[0:10, h, :],
                                     att[:, nsl], start=True, stop=True)
                mtok = p.tok("pe", t)
                eng = "dve" if n4 % 2 == 0 else "act"
                p.wait(eng, mtok)
                if eng == "dve":
                    t = nc.vector.tensor_copy(
                        full_fm[par * 64:par * 64 + 64, pair, nsl], psx[0:64, :])
                else:
                    t = nc.scalar.copy(
                        full_fm[par * 64:par * 64 + 64, pair, nsl], psx[0:64, :])
                evguard = p.tok(eng, t)
            ohguard = evguard

    bo_tok = vec_to_bcast(p, R, W["l0_bo"], R["bb1"], 1024)
    p.wait("pool", p.last("pe"), p.last("dve"), p.last("act"))
    wo_tok = p.dma("pool", wsb, W["l0_woT"])
    xres_tok = [None] * NT
    for i in range(NT):
        p.wait("dve", bo_tok)
        t = nc.vector.tensor_tensor(R["x_res"][:, i, :], R["x_res"][:, i, :],
                                    R["bb1"][:], ALU.add)
        xres_tok[i] = p.tok("dve", t)
    yguard = p.last("dve")
    for i in range(NT):
        til = slice(i * 128, (i + 1) * 128)
        for half in range(2):
            osl = slice(half * 512, (half + 1) * 512)
            p.wait("pe", wo_tok, evguard, yguard)
            for c in range(KC):
                t = nc.tensor.matmul(ps_a[:], full_fm[:, c, til],
                                     wsb[:, c, osl], start=(c == 0),
                                     stop=(c == KC - 1))
            mtok = p.tok("pe", t)
            p.wait("dve", mtok, xres_tok[i])
            t = nc.vector.scalar_tensor_tensor(
                R["x_res"][:, i, osl], ps_a[:], 1.0,
                R["x_res"][:, i, osl], ALU.mult, ALU.add)
            xres_tok[i] = p.tok("dve", t)
            yguard = xres_tok[i]
    p.barrier()


def emit_l2a(p, R, W, scr):
    nc = p.nc
    st = R["stats"]
    R["cmask"] = R["Wr"].bitcast(F32)[:, 1536:1664]
    v_dram = scr["v_dram"]
    ln_tok = emit_ln_to_xnT(p, R)
    xnT = R["xnT"]
    p.barrier()

    Q_fm = R["A"].bitcast(BF16).rearrange("p (c t) -> p c t", t=T)
    K_fm = R["B"].bitcast(BF16).rearrange("p (c t) -> p c t", t=T)
    wsl = R["Wr"].bitcast(BF16)
    w_sl = [wsl[:, 0:1024].rearrange("p (c n) -> p c n", n=128),
            wsl[:, 1024:2048].rearrange("p (c n) -> p c n", n=128)]
    ps_a = R["ps_a"]
    ps_b = R["ps_b"]

    bc_tok = vec_to_cols(p, R, W["l2_battn"][0:1024], st[:, 0:8], 1024)
    bc_tok = vec_to_cols(p, R, W["l2_battn"][1024:2048], st[:, 8:16], 1024)
    bc_tok = vec_to_cols(p, R, W["l2_battn"][2048:3072], st[:, 16:24], 1024)
    bcols = st[:, 0:24]

    wtok = [None, None]
    slot_use = [None, None]
    wtok[0] = p.dma("pool", w_sl[0], W["l2_wattn"][0])
    vstrip = R["tmpsq"].bitcast(BF16).rearrange("p (j d) -> p j d", d=128)
    eguard = None
    vguard = None
    vs_tok = None
    for oc in range(24):
        par = oc % 2
        if oc + 1 < 24:
            p.wait("pool", slot_use[(oc + 1) % 2])
            wtok[(oc + 1) % 2] = p.dma(
                "pool", w_sl[(oc + 1) % 2], W["l2_wattn"][oc + 1])
        for ts4 in range(4):
            tsl = slice(ts4 * 512, (ts4 + 1) * 512)
            p.wait("pe", wtok[par], ln_tok, eguard)
            for c in range(KC):
                t = nc.tensor.matmul(ps_a[:], w_sl[par][:, c, :],
                                     xnT[:, c, tsl], start=(c == 0),
                                     stop=(c == KC - 1))
            mtok = p.tok("pe", t)
            slot_use[par] = mtok
            p.wait("dve", mtok, bc_tok)
            if oc < 8:
                t = nc.vector.tensor_scalar_add(Q_fm[:, oc, tsl], ps_a[:],
                                                bcols[:, oc:oc + 1])
                eguard = p.tok("dve", t)
            elif oc < 16:
                t = nc.vector.tensor_scalar_add(K_fm[:, oc - 8, tsl], ps_a[:],
                                                bcols[:, oc:oc + 1])
                eguard = p.tok("dve", t)
            else:
                p.wait("dve", vguard)
                vb = R["tmpbf_a"]
                t = nc.vector.tensor_scalar_add(vb[:, 0:512], ps_a[:],
                                                bcols[:, oc:oc + 1])
                eguard = p.tok("dve", t)
                p.wait("pe", eguard, vs_tok)
                for jj in range(4):
                    t = nc.tensor.transpose(
                        R["ps_tr"][:, jj * 128:(jj + 1) * 128],
                        vb[:, jj * 128:(jj + 1) * 128], R["ident"][:])
                ptk = p.tok("pe", t)
                p.wait("dve", ptk)
                t = nc.vector.tensor_copy(vstrip[:, ts4 * 4:ts4 * 4 + 4, :],
                                          R["ps_tr"][:, 0:512]
                                          .rearrange("p (j q) -> p j q", q=128))
                vguard = p.tok("dve", t)
        if oc >= 16:
            voc = oc - 16
            p.wait("sp", vguard)
            vs_tok = p.dma(
                "sp",
                v_dram.rearrange("(j p) d -> p j d", p=128)[:, :, voc * 128:(voc + 1) * 128],
                vstrip[:])
    p.barrier()

    t = nc.gpsimd.memset(R["tmpf"][:, 0:128], 0.0)
    tk = p.tok("pool", t)
    p.wait("pool", tk)
    t = nc.gpsimd.affine_select(R["cmask"], R["tmpf"][:, 0:128], [[-1, 128]],
                                ALU.is_ge, -1e30, base=0, channel_multiplier=1)
    cm_tok = p.tok("pool", t)

    bo_tok = vec_to_bcast(p, R, W["l2_bproj"], R["bb1"], 1024)
    xres_tok = [None] * NT
    for i in range(NT):
        p.wait("dve", bo_tok)
        t = nc.vector.tensor_tensor(R["x_res"][:, i, :], R["x_res"][:, i, :],
                                    R["bb1"][:], ALU.add)
        xres_tok[i] = p.tok("dve", t)

    ps_s = [p.bps("ps_s1_l2a", [128, 2048], F32)]
    ps_v = R["ps_row"]
    ps_y = ps_b
    atts = [R["tmpf"].bitcast(BF16), R["bb2"].bitcast(BF16)]
    apair = R["tmpsq"].bitcast(BF16).rearrange("p (q c) -> p q c", c=128)
    vpair = R["aux"][:, 0:2048].rearrange("p (j d) -> p j d", d=128)
    wp_sb = wsl[:, 2048:3072]
    yguard = p.last("dve")
    vp_tok = None
    wp_tok = None
    apguard = None

    def emit_TV(prev, nxt_stok):
        h, qt, inst, etk, rtk, apar = prev
        att = atts[apar]
        hb = (h % 2) * 64
        nj = qt + 1
        vtk = None
        atg = None
        for j0 in range(0, nj, 8):
            jn = min(8, nj - j0)
            abuf = R["tmpbf_a"] if (j0 // 8) % 2 == 0 else R["tmpbf_b"]
            p.wait("pe", etk, R["_l2a_atg"][(j0 // 8) % 2], vp_tok)
            for jj in range(jn):
                t = nc.tensor.transpose(
                    R["ps_tr"][:, jj * 128:(jj + 1) * 128],
                    att[:, (j0 + jj) * 128:(j0 + jj + 1) * 128],
                    R["ident"][:])
            ttk = p.tok("pe", t)
            p.wait("dve", ttk, R["_l2a_veva"][0] if j0 == 0 else None)
            t = nc.vector.tensor_copy(abuf[:, 0:jn * 128],
                                      R["ps_tr"][:, 0:jn * 128])
            atg = p.tok("dve", t)
            R["_l2a_atg"][(j0 // 8) % 2] = atg
            p.wait("pe", atg)
            for jj in range(jn):
                t = nc.tensor.matmul(
                    ps_v[:, 0:64], abuf[:, jj * 128:(jj + 1) * 128],
                    vpair[:, j0 + jj, hb:hb + 64],
                    start=(j0 + jj == 0), stop=(j0 + jj == nj - 1))
            vtk = p.tok("pe", t)
        R["_l2a_T"][apar] = p.last("pe")
        p.wait("dve", vtk, rtk)
        t = nc.vector.tensor_scalar_mul(apair[:, qt, hb:hb + 64],
                                        ps_v[:, 0:64], st[:, 30 + inst:31 + inst])
        R["_l2a_veva"][0] = p.tok("dve", t)
        return R["_l2a_veva"][0]

    R["_l2a_atg"] = [None, None]
    R["_l2a_veva"] = [None]
    R["_l2a_T"] = [None, None]
    exp_done = [None, None]
    prev = None
    for pair in range(8):
        if prev is not None:
            emit_TV(prev, None)
            prev = None
        p.wait("act", apguard, R["_l2a_veva"][0], p.last("pe"))
        vp_tok = p.dma("act", vpair,
                       v_dram.rearrange("(j p) d -> p j d", p=128)
                       [:, :, pair * 128:(pair + 1) * 128])
        p.wait("pool", p.last("pe"))
        wp_tok = p.dma("pool", wp_sb,
                       W["l2_wproj"][pair * 128:(pair + 1) * 128, :])
        for par in range(2):
            h = 2 * pair + par
            ch = pair
            hb = par * 64
            for qt in range(NT):
                k = par * NT + qt
                inst = 0
                til = slice(qt * 128, (qt + 1) * 128)
                L = (qt + 1) * 128
                p.wait("pe", ln_tok, exp_done[inst], apguard)
                nb = (L + 511) // 512
                b_diag = (qt * 128) // 512
                order = [b_diag] + [b for b in range(nb) if b != b_diag]
                stoks = {}
                for b in order:
                    bw = min(512, L - b * 512)
                    t = nc.tensor.matmul(ps_s[inst][:, b * 512:b * 512 + bw],
                                         Q_fm[hb:hb + 64, ch, til],
                                         K_fm[hb:hb + 64, ch, b * 512:b * 512 + bw],
                                         start=True, stop=True)
                    stoks[b] = p.tok("pe", t)
                stok = p.last("pe")
                if prev is not None:
                    emit_TV(prev, stok)
                p.wait("dve", stoks[b_diag], cm_tok)
                t = nc.vector.tensor_tensor(ps_s[inst][:, til], ps_s[inst][:, til],
                                            R["cmask"], ALU.add)
                mstk = p.tok("dve", t)
                p.wait("dve", stok, mstk)
                t = nc.vector.reduce_max(st[:, 28 + inst:29 + inst],
                                         ps_s[inst][:, 0:L],
                                         axis=AX.X, negate=True)
                tk = p.tok("dve", t)
                p.wait("dve", tk)
                t = nc.vector.tensor_scalar_mul(st[:, 28 + inst:29 + inst],
                                                st[:, 28 + inst:29 + inst], 0.125)
                tk = p.tok("dve", t)
                apar = k % 2
                p.wait("act", tk, R["_l2a_T"][apar])
                t = nc.scalar.activation(atts[apar][:, 0:L], ps_s[inst][:, 0:L],
                                         AF.Exp,
                                         bias=st[:, 28 + inst:29 + inst],
                                         scale=0.125,
                                         accum_out=st[:, 26 + inst:27 + inst])
                etk = p.tok("act", t)
                exp_done[inst] = etk
                p.wait("dve", etk)
                t = nc.vector.reciprocal(st[:, 30 + inst:31 + inst],
                                         st[:, 26 + inst:27 + inst])
                rtk = p.tok("dve", t)
                prev = (h, qt, inst, etk, rtk, apar)
        emit_TV(prev, None)
        prev = None
        aguard = R["_l2a_veva"][0]
        for qt in range(NT):
            p.wait("pe", aguard, apguard)
            t = nc.tensor.transpose(R["ps_tr"][:, 0:128], apair[:, qt, :],
                                    R["ident"][:])
            ptk = p.tok("pe", t)
            p.wait("dve", ptk)
            t = nc.vector.tensor_copy(apair[:, qt, :], R["ps_tr"][:, 0:128])
            apguard = p.tok("dve", t)
        ypp = [ps_y, ps_a]
        ygq = {}
        for qt in range(NT):
            for half in range(2):
                q2 = half
                osl = slice(half * 512, (half + 1) * 512)
                p.wait("pe", wp_tok, apguard, ygq.get(q2))
                t = nc.tensor.matmul(ypp[q2][:], apair[:, qt, :],
                                     wp_sb[:, osl], start=True, stop=True)
                mtok = p.tok("pe", t)
                p.wait("dve", mtok, xres_tok[qt])
                t = nc.vector.scalar_tensor_tensor(
                    R["x_res"][:, qt, osl], ypp[q2][:], 1.0,
                    R["x_res"][:, qt, osl], ALU.mult, ALU.add)
                xres_tok[qt] = p.tok("dve", t)
                ygq[q2] = xres_tok[qt]
                yguard = xres_tok[qt]
        apguard = yguard
    p.barrier()


EMITTERS = {}


def build(blocks, weight_specs):
    nc = bass.Bass("TRN2", target_bir_lowering=False)
    x_in = nc.dram_tensor("x", [128, NT, D], F32, kind="ExternalInput").ap()
    y_out = nc.dram_tensor("y", [128, NT, D], F32, kind="ExternalOutput").ap()
    W = {}
    for name, (shape, dt) in weight_specs.items():
        W[name] = nc.dram_tensor(name, list(shape), dt, kind="ExternalInput").ap()
    cc = None
    if "l3f" in blocks:
        cc = {"in": nc.dram_tensor("cc_in", [1, 1], F32, kind="Internal").ap(),
              "out": nc.dram_tensor("cc_out", [1, 1], F32, kind="Internal").ap()}
    scr = {}
    if "l0a" in blocks:
        scr["q_dram"] = nc.dram_tensor("q_dram", [T, D], BF16, kind="Internal").ap()
    if "l2a" in blocks:
        scr["v_dram"] = nc.dram_tensor("v_dram", [T, D], BF16, kind="Internal").ap()
    if "l2f" in blocks:
        scr["h_dram"] = nc.dram_tensor("h_dram", [T, 4096], BF16, kind="Internal").ap()
        scr["wqg_dram"] = nc.dram_tensor("wqg_dram", [1024, 4096], BF16, kind="Internal").ap()
        scr["wqu_dram"] = nc.dram_tensor("wqu_dram", [1024, 4096], BF16, kind="Internal").ap()
        scr["wqd_dram"] = nc.dram_tensor("wqd_dram", [4096, 1024], BF16, kind="Internal").ap()

    with ExitStack() as ctx:
        p = Prog(nc, ctx)
        R = {}
        R["x_res"] = p.sb("x_res", [128, NT, D], F32)[:]
        R["xnT"] = p.sb("xnT", [128, KC, T], BF16)[:]
        R["A"] = p.sb("regA", [128, 8192], F32)[:]
        R["B"] = p.sb("regB", [128, 8192], F32)[:]
        R["Wr"] = p.sb("regW", [128, 4096], F32)[:]
        R["tmpf"] = p.sb("tmpf", [128, 1024], F32)[:]
        R["tmpsq"] = p.sb("tmpsq", [128, 1024], F32)[:]
        R["tmpbf_a"] = p.sb("tmpbf_a", [128, 1024], BF16)[:]
        R["tmpbf_b"] = p.sb("tmpbf_b", [128, 1024], BF16)[:]
        R["smallbf"] = p.sb("smallbf", [16, 2048], BF16)[:]
        R["stats"] = p.sb("stats", [128, 80], F32)[:]
        R["bb1"] = p.sb("bb1", [128, 1024], F32)[:]
        R["bb2"] = p.sb("bb2", [128, 1024], F32)[:]
        R["row_f"] = p.sb("row_f", [16, 512], F32)[:]
        R["idx_u"] = p.sb("idx_u", [128, 8], U32)[:]
        R["idx_c"] = p.sb("idx_c", [128, 16], U32)[:]
        R["aux"] = p.sb("aux", [128, 2048], BF16)[:]
        R["kv_bf"] = R["aux"][:, 0:1024].rearrange("p (h d) -> p h d", d=64)
        R["ks_bf"] = R["aux"][:, 1024:1040]
        R["ones_f"] = p.sb("ones_f", [1, 128], F32)[:]
        R["ones_col"] = p.sb("ones_col", [128, 1], F32)[:]
        R["ones_bf"] = R["tmpbf_a"][:, 0:128]
        R["ones_bfc"] = p.sb("ones_bfc", [128, 1], BF16)[:]
        R["ones_f128"] = R["tmpf"][:, 0:128]
        R["ident"] = p.sb("ident", [128, 128], BF16)[:]
        R["ident_f"] = p.sb("ident_f", [128, 128], F32)[:]
        R["ps_a"] = p.ps("ps_a", [128, 512], F32)[:]
        R["ps_b"] = p.ps("ps_b", [128, 512], F32)[:]
        R["ps_tr"] = p.ps("ps_tr", [128, 1024], BF16)[:]
        R["ps_row"] = p.ps("ps_row", [128, 512], F32)[:]
        R["ps_small"] = R["ps_row"][:, 496:512]
        R["_fm_evac_tok"] = [None]

        emit_common(p, R)
        tk = p.dma("sp", R["x_res"], x_in)
        p.wait("dve", tk)
        p.wait("act", tk)
        p.wait("pe", tk)
        p.wait("pool", tk)

        for b in blocks:
            with ExitStack() as bctx:
                p.bctx = bctx
                EMITTERS[b](p, R, W, cc=cc, scr=scr)
            p.barrier()

        tk = p.dma("sp", y_out, R["x_res"])
        p.wait("sp", tk)
    return nc


EMITTERS["l0f"] = lambda p, R, W, cc=None, scr=None: emit_l0f(p, R, W)
EMITTERS["l1f"] = lambda p, R, W, cc=None, scr=None: emit_l1f(p, R, W)
EMITTERS["l2f"] = lambda p, R, W, cc=None, scr=None: emit_l2f(p, R, W, scr)
EMITTERS["l0a"] = lambda p, R, W, cc=None, scr=None: emit_l0a(p, R, W, scr)
EMITTERS["l2a"] = lambda p, R, W, cc=None, scr=None: emit_l2a(p, R, W, scr)
EMITTERS["l1a"] = lambda p, R, W, cc=None, scr=None: emit_l1a(p, R, W)
EMITTERS["l3a"] = lambda p, R, W, cc=None, scr=None: emit_l3a(p, R, W)
EMITTERS["l3f"] = lambda p, R, W, cc=None, scr=None: emit_l3f(p, R, W, cc)


def _to32(a):
    return np.ascontiguousarray(a, dtype=np.float32)


def _pcn(WT):
    K, N = WT.shape
    return np.ascontiguousarray(WT.reshape(K // 128, 128, N).transpose(1, 0, 2))


def _tiles(WT, tn):
    K, N = WT.shape
    a = WT.reshape(K // 128, 128, N // tn, tn).transpose(2, 1, 0, 3)
    return np.ascontiguousarray(a)


def host_weights(inputs):
    I = inputs
    w = {}
    w["l0_wqT"] = (_pcn(_to32(I["L0_attn_Wq"].T)), F32)
    w["l0_wkT"] = (_pcn(_to32(I["L0_attn_Wk"].T)), F32)
    w["l0_wvT"] = (_pcn(_to32(I["L0_attn_Wv"].T)), F32)
    w["l0_woT"] = (_pcn(_to32(I["L0_attn_Wo"].T)), F32)
    for b in ["bq", "bk", "bv", "bo"]:
        w["l0_" + b] = (_to32(I["L0_attn_" + b]), F32)
    w["l0_wg4"] = (_tiles(np.ascontiguousarray(I["L0_ffn_w_gate"].T), 128), I32)
    w["l0_wu4"] = (_tiles(np.ascontiguousarray(I["L0_ffn_w_up"].T), 128), I32)
    w["l0_wdT"] = (np.ascontiguousarray(I["L0_ffn_w_down"].T), I32)
    w["l1_wqT"] = (_pcn(_to32(I["L1_attn_Wq"].T)), F32)
    w["l1_wkT"] = (_pcn(_to32(I["L1_attn_Wk"].T)), F32)
    w["l1_wvT"] = (_pcn(_to32(I["L1_attn_Wv"].T)), F32)
    w["l1_woT"] = (_pcn(_to32(I["L1_attn_Wo"].T)), F32)
    for b in ["bq", "bk", "bv", "bo"]:
        w["l1_" + b] = (_to32(I["L1_attn_" + b]), F32)
    w["l1_Wg"] = (_pcn(_to32(I["L1_ffn_Wg"])), F32)
    w["l1_bg"] = (_to32(I["L1_ffn_bg"]), F32)
    w["l1_W1"] = (np.stack([_pcn(_to32(I["L1_ffn_W1"][e])) for e in range(8)]), F32)
    w["l1_b1"] = (_to32(I["L1_ffn_b1"]), F32)
    w["l1_W2"] = (np.stack([_pcn(_to32(I["L1_ffn_W2"][e])) for e in range(8)]), F32)
    w["l1_b2"] = (_to32(I["L1_ffn_b2"]), F32)
    w["l2_wattn"] = (_tiles(_to32(I["L2_attn_w_attn"]), 128), F32)
    w["l2_battn"] = (_to32(I["L2_attn_b_attn"]), F32)
    w["l2_wproj"] = (_to32(I["L2_attn_w_proj"]), F32)
    w["l2_bproj"] = (_to32(I["L2_attn_b_proj"]), F32)
    w["l2_wgT"] = (_to32(I["L2_ffn_w_gate"].T), F32)
    w["l2_wuT"] = (_to32(I["L2_ffn_w_up"].T), F32)
    w["l2_wdT"] = (_to32(I["L2_ffn_w_down"].T), F32)
    win = _to32(I["L3_attn_W_in"])
    win_p = np.zeros((1024, 1152), np.float32)
    win_p[:, 0:1032] = win
    w["l3_Win"] = (_tiles(win_p, 128), F32)
    w["l3_b_in"] = (_to32(I["L3_attn_b_in"]), F32)
    wg3 = _to32(I["L3_attn_W_g"])
    wg3_p = np.zeros((1152, 8), np.float32)
    wg3_p[0:1032] = wg3
    w["l3_Wg"] = (_pcn(wg3_p), F32)
    w["l3_b_g"] = (_to32(I["L3_attn_b_g"]), F32)
    w["l3_Wout"] = (_to32(I["L3_attn_W_out"]), F32)
    w["l3_bout"] = (_to32(I["L3_attn_b_out"]), F32)
    w["l3f_w"] = (_to32(I["L3_ffn_weight"]), F32)
    w["l3f_gamma"] = (_to32(I["L3_ffn_ln_gamma"]), F32)
    w["l3f_beta"] = (_to32(I["L3_ffn_ln_beta"]), F32)
    return w


BLOCK_WEIGHTS = {
    "l0a": ["l0_wqT", "l0_wkT", "l0_wvT", "l0_woT", "l0_bq", "l0_bk", "l0_bv", "l0_bo"],
    "l0f": ["l0_wg4", "l0_wu4", "l0_wdT"],
    "l1a": ["l1_wqT", "l1_wkT", "l1_wvT", "l1_woT", "l1_bq", "l1_bk", "l1_bv", "l1_bo"],
    "l1f": ["l1_Wg", "l1_bg", "l1_W1", "l1_b1", "l1_W2", "l1_b2"],
    "l2a": ["l2_wattn", "l2_battn", "l2_wproj", "l2_bproj"],
    "l2f": ["l2_wgT", "l2_wuT", "l2_wdT"],
    "l3a": ["l3_Win", "l3_b_in", "l3_Wg", "l3_b_g", "l3_Wout", "l3_bout"],
    "l3f": ["l3f_w", "l3f_gamma", "l3f_beta"],
}

LAUNCH_GROUPS = [ALL_BLOCKS]


def run_groups(x_full, wdict, groups):
    x = np.ascontiguousarray(x_full, dtype=np.float32)
    for blocks in groups:
        names = []
        for b in blocks:
            names += BLOCK_WEIGHTS[b]
        spec = {n: (wdict[n][0].shape, wdict[n][1]) for n in names}
        nc = build(blocks, spec)
        in_maps = []
        for c in range(NCORES):
            xc = np.ascontiguousarray(
                x[c].reshape(NT, 128, D).transpose(1, 0, 2))
            m = {"x": xc}
            for n in names:
                m[n] = wdict[n][0]
            in_maps.append(m)
        res = run_bass_kernel_spmd(nc, in_maps, core_ids=list(range(NCORES)))
        x = np.stack([np.asarray(res.results[c]["y"])
                      .transpose(1, 0, 2).reshape(T, D) for c in range(NCORES)])
    return x


def kernel(**inputs):
    w = host_weights(inputs)
    x = np.ascontiguousarray(np.asarray(inputs["L0_attn_x"], dtype=np.float32))
    out = run_groups(x, w, LAUNCH_GROUPS)
    return out.astype(np.float32)


# version 12 (final)
# speedup vs baseline: 1.0606x; 1.0025x over previous
import numpy as np
from contextlib import ExitStack

import concourse.bass as bass
import concourse.mybir as mybir
from concourse.bass_utils import run_bass_kernel_spmd

F32 = mybir.dt.float32
BF16 = mybir.dt.bfloat16
I32 = mybir.dt.int32
U32 = mybir.dt.uint32
AF = mybir.ActivationFunctionType
ALU = mybir.AluOpType
AX = mybir.AxisListType
IOffs = bass.IndirectOffsetOnAxis

MAGIC = float(np.float32(1.5 * 2 ** 23))
T = 2048
D = 1024
NT = T // 128
KC = D // 128
NCORES = 8

ALL_BLOCKS = ["l0a", "l0f", "l1a", "l1f", "l2a", "l2f", "l3a", "l3f"]


class Prog:
    def __init__(self, nc, ctx):
        self.nc = nc
        self.ctx = ctx
        self.e = {"pe": nc.tensor, "dve": nc.vector, "act": nc.scalar,
                  "pool": nc.gpsimd, "sp": nc.sync}
        self.sem = {}
        self.cnt = {}
        self.waited = {}
        names = ["pe", "dve", "act", "pool", "sp", "cc"]
        for q in ["sp", "pool", "act"]:
            for i in range(6):
                names.append(f"dma_{q}{i}")
        for n in names:
            self.sem[n] = ctx.enter_context(nc.semaphore("s_" + n))
            self.cnt[n] = 0
        self.dma_rr = {"sp": 0, "pool": 0, "act": 0}

    def tok(self, name, instr, n=1):
        self.cnt[name] += n
        instr.then_inc(self.sem[name], n)
        return (name, self.cnt[name])

    def last(self, name):
        return (name, self.cnt[name])

    def dma(self, queue, out, in_, **kw):
        i = self.dma_rr[queue]
        self.dma_rr[queue] = (i + 1) % 6
        name = f"dma_{queue}{i}"
        self.wait(queue, (name, self.cnt[name]))
        ins = self.e[queue].dma_start(out, in_, **kw)
        return self.tok(name, ins, 16)

    def dma_indirect_add(self, out_ap, table_dram, idx_ap):
        i = self.dma_rr["pool"]
        self.dma_rr["pool"] = (i + 1) % 6
        name = f"dma_pool{i}"
        self.wait("pool", (name, self.cnt[name]))
        ins = self.nc.gpsimd.indirect_dma_start(
            out_ap, None, table_dram, IOffs(idx_ap, 0), compute_op=ALU.add)
        return self.tok(name, ins, 16)

    def dma_gather_rows(self, out_ap, table_dram, idx_ap, element_offset=0):
        i = self.dma_rr["pool"]
        self.dma_rr["pool"] = (i + 1) % 6
        name = f"dma_pool{i}"
        self.wait("pool", (name, self.cnt[name]))
        ins = self.nc.gpsimd.indirect_dma_start(
            out_ap, None, table_dram, IOffs(idx_ap, 0),
            element_offset=element_offset)
        return self.tok(name, ins, 16)

    def wait(self, eng, *toks):
        for t in toks:
            if t is None:
                continue
            sname, val = t
            if val <= 0:
                continue
            key = (eng, sname)
            if self.waited.get(key, 0) >= val:
                continue
            self.e[eng].wait_ge(self.sem[sname], val)
            self.waited[key] = val

    def barrier(self):
        toks = [self.last(n) for n in self.sem if n != "cc"]
        for eng in ["pe", "dve", "act", "pool", "sp"]:
            self.wait(eng, *toks)

    def sb(self, name, shape, dt):
        return self.ctx.enter_context(self.nc.sbuf_tensor(name, shape, dt))

    def ccol(self, value):
        if not hasattr(self, "_cpool"):
            self._cpool = self.sb("cpool", [128, 16], F32)[:]
            self._cvals = {}
        v = float(value)
        if v not in self._cvals:
            i = len(self._cvals)
            assert i < 16
            ap = self._cpool[:, i:i + 1]
            t = self.nc.gpsimd.memset(ap, v)
            tk = self.tok("pool", t)
            self._cvals[v] = (ap, tk)
        return self._cvals[v]

    def ps(self, name, shape, dt=F32):
        return self.ctx.enter_context(self.nc.psum_tensor(name, shape, dt))

    def bps(self, name, shape, dt=F32):
        return self.bctx.enter_context(self.nc.psum_tensor(name, shape, dt))[:]


def emit_common(p, R):
    nc = p.nc
    t = nc.vector.memset(R["ones_f"][:], 1.0)
    p.tok("dve", t)
    t = nc.vector.memset(R["ones_col"][:], 1.0)
    p.tok("dve", t)
    t = nc.gpsimd.memset(R["ones_bf"], 1.0)
    tk = p.tok("pool", t)
    t = nc.gpsimd.memset(R["ones_bfc"][:], 1.0)
    tk = p.tok("pool", t)
    p.wait("pool", tk)
    t = nc.gpsimd.affine_select(R["ident"][:], R["ones_bf"], [[-1, 128]],
                                ALU.is_equal, 0.0, base=0, channel_multiplier=1)
    p.tok("pool", t)
    t = nc.gpsimd.memset(R["ones_f128"][:], 1.0)
    tk = p.tok("pool", t)
    p.wait("pool", tk)
    t = nc.gpsimd.affine_select(R["ident_f"][:], R["ones_f128"][:], [[-1, 128]],
                                ALU.is_equal, 0.0, base=0, channel_multiplier=1)
    p.tok("pool", t)
    p.barrier()


def bcast_row_to_col(p, R, src_ap_1x1, dst_col_128x1, scale=None):
    nc = p.nc
    ps = R["ps_small"]
    p.wait("pe", p.last("dve"), p.last("act"), p.last("pool"))
    t = nc.tensor.matmul(ps[:, 0:1], R["ones_f"][:, 0:128], src_ap_1x1,
                         start=True, stop=True)
    tk = p.tok("pe", t)
    p.wait("dve", tk)
    t = nc.vector.tensor_copy(dst_col_128x1, ps[:, 0:1])
    if scale is not None:
        tk2 = p.tok("dve", t)
        p.wait("dve", tk2)
        t = nc.vector.tensor_scalar_mul(dst_col_128x1, dst_col_128x1, scale)
    return p.tok("dve", t)


def vec_to_bcast(p, R, vec_dram, dst_f32_128xN, n, tmp_row=None):
    nc = p.nc
    row = R["row_f"]
    ps = R["ps_row"]
    out_toks = []
    for off in range(0, n, 512):
        w = min(512, n - off)
        p.wait("sp", p.last("pe"), p.last("dve"))
        tk = p.dma("sp", row[0:1, 0:w], vec_dram[off:off + w].unsqueeze(0))
        p.wait("pe", tk, p.last("dve"))
        t = nc.tensor.matmul(ps[:, 0:w], R["ones_f"][:, 0:128],
                             row[0:1, 0:w], start=True, stop=True)
        ptk = p.tok("pe", t)
        p.wait("dve", ptk)
        t = nc.vector.tensor_copy(dst_f32_128xN[:, off:off + w], ps[:, 0:w])
        out_toks.append(p.tok("dve", t))
    return out_toks[-1]


def vec_to_cols(p, R, vec_dram, dst_f32_128xC, n):
    nc = p.nc
    c = n // 128
    row = R["row_f"]
    p.wait("sp", p.last("pe"), p.last("dve"))
    tk = p.dma("sp", row[0:c, 0:128], vec_dram.rearrange("(c q) -> c q", q=128))
    ps = R["ps_row"]
    p.wait("pe", tk, p.last("dve"))
    t = nc.tensor.transpose(ps[:, 0:c], row[0:c, 0:128], R["ident_f"][0:c, 0:c])
    ptk = p.tok("pe", t)
    p.wait("dve", ptk)
    t = nc.vector.tensor_copy(dst_f32_128xC[:, 0:c], ps[:, 0:c])
    return p.tok("dve", t)


def emit_ln(p, R, consume, eps=1e-5, src=None):
    nc = p.nc
    xr = src if src is not None else R["x_res"]
    st = R["stats"]
    s1 = st[:, 0:NT]
    s2 = st[:, NT:2 * NT]
    a_c = st[:, 2 * NT:3 * NT]
    b_c = st[:, 3 * NT:4 * NT]
    red_toks = []
    sq_toks = []
    for i in range(NT):
        t = nc.vector.reduce_sum(s1[:, i:i + 1], xr[:, i, :], axis=AX.X)
        red_toks.append(p.tok("dve", t))
        p.wait("act", p.last("dve") if i == 0 else None)
        sq_scr = R["tmpsq"] if (i % 2 == 0) else R["tmpf"]
        if i >= 2:
            p.wait("act", sq_toks[i - 2])
        t = nc.scalar.activation(sq_scr[:], xr[:, i, :], AF.Square,
                                 accum_out=s2[:, i:i + 1])
        sq_toks.append(p.tok("act", t))
    p.wait("dve", red_toks[-1], sq_toks[-1])
    t = nc.vector.tensor_scalar_mul(st[:, 0:2 * NT], st[:, 0:2 * NT], 1.0 / D)
    tk = p.tok("dve", t)
    p.wait("dve", tk)
    t = nc.vector.tensor_tensor(a_c, s1, s1, ALU.mult)
    tk = p.tok("dve", t)
    p.wait("dve", tk)
    t = nc.vector.tensor_tensor(a_c, s2, a_c, ALU.subtract)
    tk = p.tok("dve", t)
    eap, etk = p.ccol(eps)
    p.wait("act", tk, etk)
    t = nc.scalar.activation(a_c, a_c, AF.Sqrt, bias=eap)
    tk = p.tok("act", t)
    p.wait("dve", tk)
    t = nc.vector.reciprocal(a_c, a_c)
    tk = p.tok("dve", t)
    p.wait("dve", tk)
    t = nc.vector.scalar_tensor_tensor(b_c, s1, -1.0, a_c, ALU.mult, ALU.mult)
    tk = p.tok("dve", t)
    p.wait("dve", tk)
    guard = [None, None]
    for i in range(NT):
        dst = R["tmpf"] if (i % 2 == 0) else R["tmpsq"]
        p.wait("dve", guard[i % 2])
        t = nc.vector.tensor_scalar(dst[:], xr[:, i, :], a_c[:, i:i + 1],
                                    b_c[:, i:i + 1], ALU.mult, ALU.add)
        tk = p.tok("dve", t)
        guard[i % 2] = consume(i, dst, tk)
    return a_c, b_c


def emit_tile_to_fm(p, R, i, src_bf_tile, tok, dst_fm, evac_eng="dve"):
    nc = p.nc
    ps = R["ps_tr"]
    p.wait("pe", tok, R["_fm_evac_tok"][0])
    for c in range(KC):
        t = nc.tensor.transpose(ps[:, c * 128:(c + 1) * 128],
                                src_bf_tile[:, c * 128:(c + 1) * 128],
                                R["ident"][:])
    tk = p.tok("pe", t)
    p.wait(evac_eng, tk)
    if evac_eng == "dve":
        t = nc.vector.tensor_copy(dst_fm[:, :, i * 128:(i + 1) * 128],
                                  ps[:].rearrange("p (c q) -> p c q", q=128))
    else:
        t = nc.scalar.copy(dst_fm[:, :, i * 128:(i + 1) * 128],
                           ps[:].rearrange("p (c q) -> p c q", q=128))
    R["_fm_evac_tok"][0] = p.tok(evac_eng, t)
    return R["_fm_evac_tok"][0]


def emit_ln_to_xnT(p, R, eps=1e-5):
    nc = p.nc
    R["_fm_evac_tok"] = [None]
    xnT = R["xnT"]

    def consume(i, xn_tile_f32, tok):
        bft = R["tmpbf_a"] if i % 2 == 0 else R["tmpbf_b"]
        p.wait("act", tok)
        t = nc.scalar.copy(bft[:], xn_tile_f32[:])
        ctk = p.tok("act", t)
        emit_tile_to_fm(p, R, i, bft[:], ctk, xnT[:],
                        evac_eng="dve" if i % 2 == 0 else "act")
        return ctk

    emit_ln(p, R, consume, eps=eps)
    return R["_fm_evac_tok"][0]


def load_w_chunks(p, dram_ap, dst_bf):
    k = dram_ap.shape[0]
    return p.dma("pool", dst_bf, dram_ap.rearrange("(c p) n -> p c n", p=128))


def emit_l0f(p, R, W):
    nc = p.nc
    S = 1.0 / 127.0
    ln_tok = emit_ln_to_xnT(p, R)
    xnT = R["xnT"]
    p.barrier()

    NHC = 24
    GRP = 4
    h_fm = R["A"].bitcast(BF16).rearrange("p (j t) -> p j t", t=T)
    wg = R["Wr"].bitcast(BF16)
    wg_sl = [wg[:, 0:1024].rearrange("p (c n) -> p c n", n=128),
             wg[:, 1024:2048].rearrange("p (c n) -> p c n", n=128)]
    wu_sl = [wg[:, 2048:3072].rearrange("p (c n) -> p c n", n=128),
             wg[:, 3072:4096].rearrange("p (c n) -> p c n", n=128)]
    wd_sl = [wg[:, 4096 + j * 1024: 4096 + (j + 1) * 1024] for j in range(4)]

    ps_g = [R["ps_a"], p.bps("ps_g2_l0f", [128, 512], F32)]
    ps_u = [R["ps_b"], p.bps("ps_u2_l0f", [128, 512], F32)]
    ps_y = [p.bps("ps_y_l0f", [128, 512], F32),
            p.bps("ps_y2_l0f", [128, 512], F32)]
    silu_t = R["tmpbf_a"]

    wtoks = {}
    mm_after = {}

    slot_use = {}

    def load_hc(hc, parity):
        p.wait("pool", slot_use.get(("gu", parity)))
        wtoks[("g", parity)] = p.dma("pool", wg_sl[parity], W["l0_wg4"][hc])
        wtoks[("u", parity)] = p.dma("pool", wu_sl[parity], W["l0_wu4"][hc])

    def load_wd(hc, j):
        p.wait("pool", slot_use.get(("d", j)))
        wtoks[("d", j)] = p.dma("pool", wd_sl[j],
                                W["l0_wdT"][hc * 128:(hc + 1) * 128, :])

    load_hc(0, 0)
    xres_tok = [None] * NT
    for g in range(6):
        hcs = [g * GRP + j for j in range(GRP)]
        for j, hc in enumerate(hcs):
            par = hc % 2
            if hc + 1 < NHC:
                load_hc(hc + 1, (hc + 1) % 2)
            load_wd(hc, j)
            for ts4 in range(4):
                q = ts4 % 2
                tsl = slice(ts4 * 512, (ts4 + 1) * 512)
                p.wait("pe", wtoks[("g", par)], ln_tok,
                       mm_after.get(("silu_rd", q)))
                for c in range(KC):
                    t = nc.tensor.matmul(ps_g[q][:], wg_sl[par][:, c, :],
                                         xnT[:, c, tsl], start=(c == 0),
                                         stop=(c == KC - 1))
                g_tok = p.tok("pe", t)
                p.wait("pe", wtoks[("u", par)], mm_after.get(("h_rd", q)))
                for c in range(KC):
                    t = nc.tensor.matmul(ps_u[q][:], wu_sl[par][:, c, :],
                                         xnT[:, c, tsl], start=(c == 0),
                                         stop=(c == KC - 1))
                u_tok = p.tok("pe", t)
                p.wait("act", g_tok, mm_after.get(("h_rd", q)),
                       mm_after.get(("silu_wr", q)))
                t = nc.scalar.activation(silu_t[:, q * 512:q * 512 + 512],
                                         ps_g[q][:], AF.Silu, scale=S)
                s_tok = p.tok("act", t)
                mm_after[("silu_rd", q)] = s_tok
                p.wait("dve", u_tok, s_tok)
                t = nc.vector.scalar_tensor_tensor(
                    h_fm[:, j, tsl], ps_u[q][:], S * S,
                    silu_t[:, q * 512:q * 512 + 512], ALU.mult, ALU.mult)
                mm_after[("h", j)] = p.tok("dve", t)
                mm_after[("h_rd", q)] = mm_after[("h", j)]
                mm_after[("silu_wr", q)] = mm_after[("h", j)]
            slot_use[("gu", par)] = p.last("pe")
        for i in range(NT):
            for half in range(2):
                q = half
                p.wait("pe", *[wtoks[("d", j)] for j in range(GRP)],
                       *[mm_after[("h", j)] for j in range(GRP)],
                       mm_after.get(("y_rd", q)))
                for j in range(GRP):
                    t = nc.tensor.matmul(
                        ps_y[q][:], h_fm[:, j, i * 128:(i + 1) * 128],
                        wd_sl[j][:, half * 512:(half + 1) * 512],
                        start=(j == 0), stop=(j == GRP - 1))
                y_tok = p.tok("pe", t)
                p.wait("dve", y_tok, xres_tok[i])
                t = nc.vector.scalar_tensor_tensor(
                    R["x_res"][:, i, half * 512:(half + 1) * 512], ps_y[q][:],
                    1.0, R["x_res"][:, i, half * 512:(half + 1) * 512],
                    ALU.mult, ALU.add)
                xres_tok[i] = p.tok("dve", t)
                mm_after[("y_rd", q)] = xres_tok[i]
        for j in range(GRP):
            slot_use[("d", j)] = p.last("pe")
    p.barrier()


def emit_l3a(p, R, W):
    nc = p.nc
    ln_tok = emit_ln_to_xnT(p, R)
    xnT = R["xnT"]
    p.barrier()

    comb = R["A"].bitcast(BF16).rearrange("p (c t) -> p c t", t=T)
    tail_bf = R["smallbf"]
    Bf = R["B"].bitcast(F32)
    si_f = Bf[0:8, 0:T]
    g_f = Bf[0:8, T:2 * T]
    d1_f = Bf[0:8, 2 * T:3 * T]
    states_f = Bf[0:8, 3 * T:4 * T]

    wsl = R["Wr"].bitcast(BF16)
    win_sl = [wsl[:, 0:1024].rearrange("p (c n) -> p c n", n=128),
              wsl[:, 1024:2048].rearrange("p (c n) -> p c n", n=128)]
    wg_sl = wsl[:, 2048:2048 + 9 * 8].rearrange("p (c n) -> p c n", n=8)
    wout_sl = [wsl[:, 4096 + j * 1024: 4096 + (j + 1) * 1024] for j in range(4)]

    bcols = R["stats"][:, 32:41]
    vec_to_cols(p, R, W["l3_b_in"][0:1024], bcols, 1024)
    row = R["row_f"]
    p.wait("sp", p.last("pe"), p.last("dve"))
    tk = p.dma("sp", row[0:1, 256:264], W["l3_b_in"][1024:1032].unsqueeze(0))
    p.wait("pe", tk, p.last("dve"))
    t = nc.tensor.transpose(R["ps_row"][0:8, 0:1], row[0:1, 256:264], R["ident_f"][0:1, 0:1])
    ptk = p.tok("pe", t)
    p.wait("dve", ptk)
    t = nc.vector.tensor_copy(bcols[0:8, 8:9], R["ps_row"][0:8, 0:1])
    btk = p.tok("dve", t)
    bg_col = R["stats"][:, 41:42]
    tk = p.dma("sp", row[0:1, 272:280], W["l3_b_g"].unsqueeze(0))
    p.wait("pe", tk, btk)
    t = nc.tensor.transpose(R["ps_row"][0:8, 1:2], row[0:1, 272:280], R["ident_f"][0:1, 0:1])
    ptk = p.tok("pe", t)
    p.wait("dve", ptk)
    t = nc.vector.tensor_copy(bg_col[0:8, :], R["ps_row"][0:8, 1:2])
    bgtk = p.tok("dve", t)
    wgt1 = p.dma("pool", wg_sl, W["l3_Wg"])
    wgt2 = wgt1

    ps_c2 = [R["ps_a"], R["ps_b"]]
    ps_g2 = R["ps_b"]

    wtok = [None, None]
    slot_use = [None, None]
    wtok[0] = p.dma("pool", win_sl[0], W["l3_Win"][0])
    comb_toks = []
    cgq = {}
    for oc in range(9):
        par = oc % 2
        m = 128 if oc < 8 else 8
        if oc + 1 < 9:
            nm = 128 if oc + 1 < 8 else 8
            p.wait("pool", slot_use[(oc + 1) % 2])
            wtok[(oc + 1) % 2] = p.dma(
                "pool", win_sl[(oc + 1) % 2][:, :, 0:nm],
                W["l3_Win"][oc + 1][:, :, 0:nm])
        for ts4 in range(4):
            q = ts4 % 2
            tsl = slice(ts4 * 512, (ts4 + 1) * 512)
            p.wait("pe", wtok[par], ln_tok, cgq.get(q))
            for c in range(KC):
                t = nc.tensor.matmul(ps_c2[q][0:m, :], win_sl[par][:, c, 0:m],
                                     xnT[:, c, tsl], start=(c == 0),
                                     stop=(c == KC - 1))
            mtok = p.tok("pe", t)
            slot_use[par] = mtok
            p.wait("dve", mtok, btk)
            if oc < 8:
                t = nc.vector.tensor_scalar_add(comb[:, oc, tsl], ps_c2[q][0:m, :],
                                                bcols[:, oc:oc + 1])
                cgq[q] = p.tok("dve", t)
                comb_toks.append(cgq[q])
            else:
                t = nc.vector.tensor_scalar_add(si_f[:, tsl], ps_c2[q][0:8, :],
                                                bcols[0:8, 8:9])
                ttk = p.tok("dve", t)
                cgq[q] = ttk
                comb_toks.append(ttk)
                p.wait("act", ttk)
                t = nc.scalar.copy(tail_bf[0:8, tsl], si_f[:, tsl])
                comb_toks.append(p.tok("act", t))
    for ts4 in range(4):
        tsl = slice(ts4 * 512, (ts4 + 1) * 512)
        p.wait("pe", wgt1, wgt2, comb_toks[-1], p.last("act"))
        for c in range(8):
            t = nc.tensor.matmul(ps_g2[0:8, 0:512], wg_sl[:, c, :],
                                 comb[:, c, tsl], start=(c == 0), stop=False)
        t = nc.tensor.matmul(ps_g2[0:8, 0:512], wg_sl[0:8, 8, :],
                             tail_bf[0:8, tsl], start=False, stop=True)
        mtok = p.tok("pe", t)
        p.wait("act", mtok, bgtk)
        t = nc.scalar.activation(g_f[:, tsl], ps_g2[0:8, 0:512], AF.Sigmoid,
                                 bias=bg_col[0:8, :])
        p.tok("act", t)
    gtk = p.last("act")
    p.wait("dve", gtk)
    t = nc.vector.tensor_tensor(d1_f, g_f, si_f, ALU.mult)
    tk = p.tok("dve", t)
    p.wait("dve", tk)
    t = nc.vector.tensor_tensor(d1_f, si_f, d1_f, ALU.subtract)
    tk = p.tok("dve", t)
    p.wait("dve", tk)
    t = nc.vector.tensor_tensor_scan(states_f, g_f, d1_f, 0.0, ALU.mult, ALU.add)
    tk = p.tok("dve", t)
    p.wait("dve", tk)
    t = nc.vector.tensor_copy(tail_bf[0:8, 0:T], states_f)
    states_tok = p.tok("dve", t)

    bout_tok = vec_to_bcast(p, R, W["l3_bout"], R["bb1"], 1024)
    xres_tok = [None] * NT
    for i in range(NT):
        p.wait("dve", bout_tok)
        t = nc.vector.tensor_tensor(R["x_res"][:, i, :], R["x_res"][:, i, :],
                                    R["bb1"][:], ALU.add)
        xres_tok[i] = p.tok("dve", t)
    guard = p.last("dve")
    for grp, chunks in enumerate([[0, 1, 2, 3], [4, 5, 6, 7], [8]]):
        wts = []
        for j, cch in enumerate(chunks):
            m = 128 if cch < 8 else 8
            wts.append(p.dma(
                "pool", wout_sl[j][0:m, :],
                W["l3_Wout"][cch * 128:cch * 128 + m, :]))
        ogq = {}
        for i in range(NT):
            til = slice(i * 128, (i + 1) * 128)
            for half in range(2):
                q = half
                osl = slice(half * 512, (half + 1) * 512)
                p.wait("pe", *wts, states_tok, comb_toks[-1], guard,
                       ogq.get(q))
                for j, cch in enumerate(chunks):
                    if cch < 8:
                        t = nc.tensor.matmul(ps_c2[q][:, 0:512], comb[:, cch, til],
                                             wout_sl[j][:, osl],
                                             start=(j == 0),
                                             stop=(j == len(chunks) - 1))
                    else:
                        t = nc.tensor.matmul(ps_c2[q][:, 0:512], tail_bf[0:8, til],
                                             wout_sl[j][0:8, osl],
                                             start=True, stop=True)
                mtok = p.tok("pe", t)
                p.wait("dve", mtok, xres_tok[i])
                t = nc.vector.scalar_tensor_tensor(
                    R["x_res"][:, i, osl], ps_c2[q][:, 0:512], 1.0,
                    R["x_res"][:, i, osl], ALU.mult, ALU.add)
                xres_tok[i] = p.tok("dve", t)
                ogq[q] = xres_tok[i]
                guard = xres_tok[i]
        p.barrier()
    p.barrier()


def emit_l3f(p, R, W, cc):
    nc = p.nc
    QB = 128.0
    BOUND = float(np.float32(QB - 1e-5))

    g_tok = vec_to_bcast(p, R, W["l3f_gamma"], R["bb1"], 1024)
    b_tok = vec_to_bcast(p, R, W["l3f_beta"], R["bb2"], 1024)

    xn2 = R["A"].bitcast(BF16).rearrange("p (i d) -> p i d", d=D)
    gmax = R["stats"][:, 48:49]
    t = nc.vector.memset(gmax, 0.0)
    gm_tok = p.tok("dve", t)

    st = R["stats"]
    s1 = st[:, 0:1]
    s2 = st[:, 1:2]
    ac = st[:, 2:3]
    bc = st[:, 3:4]
    tile_toks = [None] * NT
    for i in range(NT):
        xt = R["x_res"][:, i, :]
        p.wait("dve", p.last("act") if i else None, gm_tok)
        t = nc.vector.reduce_sum(s1, xt, axis=AX.X)
        r1 = p.tok("dve", t)
        p.wait("act", p.last("dve"))
        t = nc.scalar.activation(R["tmpsq"][:], xt, AF.Square, accum_out=s2)
        q1 = p.tok("act", t)

        def stats_chain(eps):
            p.wait("dve", p.last("act"))
            t = nc.vector.tensor_scalar_mul(st[:, 0:2], st[:, 0:2], 1.0 / D)
            tk = p.tok("dve", t)
            p.wait("dve", tk)
            t = nc.vector.tensor_tensor(ac, s1, s1, ALU.mult)
            tk = p.tok("dve", t)
            p.wait("dve", tk)
            t = nc.vector.tensor_tensor(ac, s2, ac, ALU.subtract)
            tk = p.tok("dve", t)
            eap, etk = p.ccol(eps)
            p.wait("act", tk, etk)
            t = nc.scalar.activation(ac, ac, AF.Sqrt, bias=eap)
            tk = p.tok("act", t)
            p.wait("dve", tk)
            t = nc.vector.reciprocal(ac, ac)
            tk = p.tok("dve", t)
            p.wait("dve", tk)
            t = nc.vector.scalar_tensor_tensor(bc, s1, -1.0, ac, ALU.mult, ALU.mult)
            return p.tok("dve", t)

        tk = stats_chain(1e-5)
        p.wait("dve", tk, r1, q1)
        t = nc.vector.tensor_scalar(R["tmpf"][:], xt, ac[:, 0:1], bc[:, 0:1],
                                    ALU.mult, ALU.add)
        ln1 = p.tok("dve", t)
        p.wait("dve", ln1)
        t = nc.vector.reduce_sum(s1, R["tmpf"][:], axis=AX.X)
        r2 = p.tok("dve", t)
        p.wait("act", ln1, p.last("dve"))
        t = nc.scalar.activation(R["tmpsq"][:], R["tmpf"][:], AF.Square,
                                 accum_out=s2)
        q2 = p.tok("act", t)
        tk = stats_chain(1e-3)
        p.wait("dve", tk, r2, q2)
        t = nc.vector.tensor_scalar(R["tmpf"][:], R["tmpf"][:], ac[:, 0:1],
                                    bc[:, 0:1], ALU.mult, ALU.add)
        u_tok = p.tok("dve", t)
        p.wait("dve", u_tok, g_tok, b_tok)
        t = nc.vector.tensor_tensor(R["tmpf"][:], R["tmpf"][:], R["bb1"][:], ALU.mult)
        tk = p.tok("dve", t)
        p.wait("dve", tk)
        t = nc.vector.tensor_tensor(xn2[:, i, :], R["tmpf"][:], R["bb2"][:], ALU.add)
        tile_toks[i] = p.tok("dve", t)
        p.wait("dve", tile_toks[i], gm_tok)
        t = nc.vector.reduce_max(s1, xn2[:, i, :], axis=AX.X,
                                 apply_absolute_value=True)
        tk = p.tok("dve", t)
        p.wait("dve", tk)
        t = nc.vector.tensor_tensor(gmax, gmax, s1, ALU.max)
        gm_tok = p.tok("dve", t)

    p.wait("pe", gm_tok)
    t = nc.tensor.transpose(R["ps_row"][0:1, 0:128],
                            gmax.rearrange("p one -> p (one)").unsqueeze(1)
                            if False else gmax, R["ident_f"][:])
    ptk = p.tok("pe", t)
    p.wait("dve", ptk)
    t = nc.vector.reduce_max(R["stats"][0:1, 49:50], R["ps_row"][0:1, 0:128],
                             axis=AX.X)
    tk = p.tok("dve", t)
    p.wait("sp", tk)
    cctk = p.dma("sp", cc["in"], R["stats"][0:1, 49:50])
    p.wait("pool", cctk)
    t = nc.gpsimd.collective_compute("AllReduce", ALU.max,
                                     [list(range(NCORES))],
                                     [cc["in"].opt()], [cc["out"].opt()])
    t.then_inc(p.sem["cc"], 1)
    p.wait("sp", ("cc", 1))
    cctk = p.dma("sp", R["stats"][0:1, 49:50], cc["out"])
    p.wait("dve", cctk)
    t = nc.vector.reciprocal(st[0:1, 50:51], st[0:1, 49:50])
    tk = p.tok("dve", t)
    p.wait("dve", tk)
    t = nc.vector.tensor_scalar_mul(st[0:1, 50:51], st[0:1, 50:51], QB)
    tk = p.tok("dve", t)
    s_tok = bcast_row_to_col(p, R, st[0:1, 50:51], st[:, 51:52])

    wreg = R["Wr"].bitcast(BF16)
    wq = wreg[:, 0:8192].rearrange("p (c n) -> p c n", n=1024)
    wcol = st[:, 52:60]
    for c in range(KC):
        p.wait("sp", p.last("dve"), p.last("act"))
        tk = p.dma("sp", R["tmpf"][:], W["l3f_w"][c * 128:(c + 1) * 128, :])
        p.wait("dve", tk)
        t = nc.vector.reduce_sum(wcol[:, c:c + 1], R["tmpf"][:], axis=AX.X,
                                 apply_absolute_value=True)
        p.tok("dve", t)
    tk = p.last("dve")
    p.wait("dve", tk)
    t = nc.vector.reduce_sum(wcol[:, 0:1], wcol[:, 0:8], axis=AX.X)
    tk = p.tok("dve", t)
    p.wait("pe", tk)
    t = nc.tensor.matmul(R["ps_row"][0:1, 0:1], R["ones_col"][:],
                         wcol[:, 0:1], start=True, stop=True)
    ptk = p.tok("pe", t)
    p.wait("dve", ptk)
    t = nc.vector.tensor_scalar(st[0:1, 60:61], R["ps_row"][0:1, 0:1],
                                1.0 / (D * D), 1e-5, ALU.mult, ALU.add)
    tk = p.tok("dve", t)
    p.wait("dve", tk)
    t = nc.vector.reciprocal(st[0:1, 61:62], st[0:1, 60:61])
    tk = p.tok("dve", t)
    rw_tok = bcast_row_to_col(p, R, st[0:1, 61:62], st[:, 62:63])
    p.wait("dve", rw_tok)
    t = nc.vector.tensor_tensor(st[0:1, 63:64], st[0:1, 49:50], st[0:1, 60:61],
                                ALU.mult)
    tk = p.tok("dve", t)
    p.wait("dve", tk)
    t = nc.vector.tensor_scalar_add(st[0:1, 64:65], st[0:1, 60:61], -1e-5)
    tk = p.tok("dve", t)
    p.wait("dve", tk)
    t = nc.vector.tensor_tensor(st[0:1, 63:64], st[0:1, 49:50], st[0:1, 64:65],
                                ALU.mult)
    tk = p.tok("dve", t)
    p.wait("dve", tk)
    t = nc.vector.tensor_scalar_mul(st[0:1, 63:64], st[0:1, 63:64], 1.0 / QB)
    tk = p.tok("dve", t)
    sc2_tok = bcast_row_to_col(p, R, st[0:1, 63:64], st[:, 65:66])

    for c in range(KC):
        p.wait("sp", p.last("dve"))
        tk = p.dma("sp", R["tmpf"][:], W["l3f_w"][c * 128:(c + 1) * 128, :])
        p.wait("dve", tk, rw_tok)
        t = nc.vector.tensor_scalar(R["tmpf"][:], R["tmpf"][:], st[:, 62:63],
                                    MAGIC, ALU.mult, ALU.add)
        ptk = p.tok("dve", t)
        p.wait("dve", ptk)
        t = nc.vector.tensor_scalar(R["tmpf"][:], R["tmpf"][:], MAGIC, 1.0,
                                    ALU.subtract, ALU.min)
        ptk = p.tok("dve", t)
        p.wait("dve", ptk)
        t = nc.vector.tensor_scalar(wq[:, c, :], R["tmpf"][:], -1.0, None,
                                    ALU.max)
        p.tok("dve", t)
    wq_tok = p.last("dve")

    xqT = R["B"].bitcast(BF16).rearrange("p (c t) -> p c t", t=T)
    R["_fm_evac_tok"] = [None]
    xq_tok = [None] * NT
    prev_p2 = p.last("dve")
    for i in range(NT):
        p.wait("dve", tile_toks[i], s_tok, prev_p2)
        t = nc.vector.tensor_scalar(R["tmpsq"][:], xn2[:, i, :], st[:, 51:52],
                                    BOUND, ALU.mult, ALU.min)
        tk = p.tok("dve", t)
        p.wait("dve", tk)
        t = nc.vector.tensor_scalar(xn2[:, i, :], R["tmpsq"][:], -BOUND, None,
                                    ALU.max)
        tk = p.tok("dve", t)
        prev_p2 = tk
        p.wait("act", tk)
        emit_tile_to_fm(p, R, i, xn2[:, i, :], tk, xqT[:],
                        evac_eng="dve" if i % 2 == 0 else "act")
        xq_tok[i] = R["_fm_evac_tok"][0]

    xres_tok = [None] * NT
    ps_yy = [R["ps_a"], R["ps_b"]]
    ygq = {}
    for i in range(NT):
        for half in range(2):
            q = half
            osl = slice(half * 512, (half + 1) * 512)
            p.wait("pe", wq_tok, xq_tok[-1], sc2_tok, ygq.get(q))
            for c in range(KC):
                t = nc.tensor.matmul(ps_yy[q][:], xqT[:, c, i * 128:(i + 1) * 128],
                                     wq[:, c, osl], start=(c == 0),
                                     stop=(c == KC - 1))
            mtok = p.tok("pe", t)
            p.wait("dve", mtok, xres_tok[i])
            t = nc.vector.scalar_tensor_tensor(
                R["x_res"][:, i, osl], ps_yy[q][:], st[:, 65:66],
                R["x_res"][:, i, osl], ALU.mult, ALU.add)
            xres_tok[i] = p.tok("dve", t)
            ygq[q] = xres_tok[i]
    p.barrier()


def emit_l1f(p, R, W):
    nc = p.nc
    ln_tok = emit_ln_to_xnT(p, R)
    xnT = R["xnT"]
    p.barrier()

    st = R["stats"]
    idxf_all = st[:, 0:NT]
    ps_l = R["ps_row"]
    wgv = R["Wr"].bitcast(BF16)
    wg_sb = R["kv_bf"][:, 0, :].rearrange("p (c n) -> p c n", n=8)
    w1_sb = wgv[:, 0:8192].rearrange("p (c n) -> p c n", n=1024)
    w2_bufs = [R["B"].bitcast(BF16)[:, 0:8192].rearrange("p (c n) -> p c n", n=1024),
               R["B"].bitcast(BF16)[:, 8192:16384].rearrange("p (c n) -> p c n", n=1024)]
    h_fm = R["A"].bitcast(BF16).rearrange("p (c t) -> p c t", t=T)

    wg_tok = p.dma("pool", wg_sb, W["l1_Wg"])
    bg_tok = vec_to_bcast(p, R, W["l1_bg"], R["bb1"][:, 0:8], 8)

    logit_sb = st[:, 40:48]
    top8 = st[:, 48:56]
    idx_u = R["idx_u"]
    idx_col = R["idx_c"]
    b2g_tok = [None] * NT
    for i in range(NT):
        til = slice(i * 128, (i + 1) * 128)
        p.wait("pe", wg_tok, ln_tok, bg_tok, p.last("dve") if i else None)
        for c in range(KC):
            t = nc.tensor.matmul(ps_l[:, 0:8], xnT[:, c, til], wg_sb[:, c, :],
                                 start=(c == 0), stop=(c == KC - 1))
        mtok = p.tok("pe", t)
        p.wait("dve", mtok, bg_tok)
        t = nc.vector.tensor_add(logit_sb, ps_l[:, 0:8], R["bb1"][:, 0:8])
        tk = p.tok("dve", t)
        p.wait("dve", tk)
        t = nc.vector.max(top8, logit_sb)
        tk = p.tok("dve", t)
        p.wait("dve", tk)
        t = nc.vector.max_index(idx_u, top8, logit_sb)
        tk = p.tok("dve", t)
        p.wait("dve", tk)
        t = nc.vector.tensor_copy(idxf_all[:, i:i + 1], idx_u[:, 0:1])
        p.tok("dve", t)
        t = nc.vector.tensor_copy(idx_col[:, i:i + 1], idx_u[:, 0:1])
        tk = p.tok("dve", t)
        p.wait("pool", tk)
        b2g_tok[i] = p.dma_indirect_add(R["x_res"][:, i, :], W["l1_b2"],
                                        idx_col[:, i:i + 1])
    idx_tok = p.last("dve")

    bcols = st[:, 16:24]
    mask_all = st[:, 24:40]
    ps_h = [R["ps_a"], p.bps("ps_h2_l1f", [128, 512], F32)]
    ps_y = [R["ps_b"], p.bps("ps_y2_l1f", [128, 512], F32)]
    w_use = [None, None]
    hguard = None
    sguard = None
    aguard = None
    w1_tok = p.dma("pool", w1_sb, W["l1_W1"][0])
    w2_toks = [p.dma("pool", w2_bufs[0], W["l1_W2"][0]), None]
    for e in range(8):
        w2_sb = w2_bufs[e % 2]
        w2_tok = w2_toks[e % 2]
        if e + 1 < 8:
            p.wait("pool", w_use[1])
            w2_toks[(e + 1) % 2] = p.dma("pool", w2_bufs[(e + 1) % 2],
                                         W["l1_W2"][e + 1])
        b1_tok = vec_to_cols(p, R, W["l1_b1"][e], bcols, 1024)
        p.wait("dve", idx_tok)
        t = nc.vector.tensor_single_scalar(mask_all, idxf_all, float(e),
                                           ALU.is_equal)
        m_tok = p.tok("dve", t)
        agq = [aguard, aguard]
        for ic in range(KC):
            for ts4 in range(4):
                q = ts4 % 2
                tsl = slice(ts4 * 512, (ts4 + 1) * 512)
                p.wait("pe", w1_tok, agq[q], hguard)
                for c in range(KC):
                    t = nc.tensor.matmul(
                        ps_h[q][:], w1_sb[:, c, ic * 128:(ic + 1) * 128],
                        xnT[:, c, tsl], start=(c == 0), stop=(c == KC - 1))
                mtok = p.tok("pe", t)
                p.wait("act", mtok, b1_tok, hguard)
                t = nc.scalar.activation(h_fm[:, ic, tsl], ps_h[q][:], AF.Gelu,
                                         bias=bcols[:, ic:ic + 1])
                agq[q] = p.tok("act", t)
        aguard = p.last("act")
        if e + 1 < 8:
            p.wait("pool", p.last("pe"))
            w1_tok = p.dma("pool", w1_sb, W["l1_W1"][e + 1])
        sgq = [sguard, sguard]
        for i in range(NT):
            til = slice(i * 128, (i + 1) * 128)
            for half in range(2):
                q = half
                osl = slice(half * 512, (half + 1) * 512)
                p.wait("pe", w2_tok, aguard, sgq[q])
                for ic in range(KC):
                    t = nc.tensor.matmul(ps_y[q][:], h_fm[:, ic, til],
                                         w2_sb[:, ic, osl], start=(ic == 0),
                                         stop=(ic == KC - 1))
                mtok = p.tok("pe", t)
                p.wait("dve", mtok, m_tok, b2g_tok[i])
                t = nc.vector.scalar_tensor_tensor(
                    R["x_res"][:, i, osl], ps_y[q][:], mask_all[:, i:i + 1],
                    R["x_res"][:, i, osl], ALU.mult, ALU.add)
                sgq[q] = p.tok("dve", t)
        sguard = p.last("dve")
        hguard = p.last("pe")
        w_use = [hguard, hguard]
    p.barrier()


L1A_STAGE = 5

def emit_l1a(p, R, W):
    nc = p.nc
    ln_tok = emit_ln_to_xnT(p, R)
    xnT = R["xnT"]
    p.barrier()
    st = R["stats"]

    q_fm = R["A"].bitcast(BF16).rearrange("p (c t) -> p c t", t=T)
    k_tm = R["B"].bitcast(BF16).rearrange("p (i d) -> p i d", d=D)
    wsb = R["Wr"].bitcast(BF16)[:, 0:8192].rearrange("p (c n) -> p c n", n=1024)
    ps_c = p.bps("ps_c_l1a", [128, 512], F32)
    ps_d = p.bps("ps_d_l1a", [128, 512], F32)
    ps_p = [ps_c, ps_d]

    bq_c = st[:, 0:8]
    bq1_c = st[:, 8:16]
    btk = vec_to_cols(p, R, W["l1_bq"], bq_c, 1024)
    p.wait("dve", btk)
    t = nc.vector.tensor_scalar_add(bq1_c, bq_c, 1.0)
    b1tk = p.tok("dve", t)
    wtok = p.dma("pool", wsb, W["l1_wqT"])
    dgq = {}
    dguard = None
    for oc in range(KC):
        for ts4 in range(4):
            q = ts4 % 2
            tsl = slice(ts4 * 512, (ts4 + 1) * 512)
            p.wait("pe", wtok, ln_tok, dgq.get(q))
            for c in range(KC):
                t = nc.tensor.matmul(ps_p[q][:], wsb[:, c, oc * 128:(oc + 1) * 128],
                                     xnT[:, c, tsl], start=(c == 0),
                                     stop=(c == KC - 1))
            mtok = p.tok("pe", t)
            p.wait("act", mtok, btk, dgq.get(q))
            t = nc.scalar.activation(R["tmpf"][:, q * 512:q * 512 + 512],
                                     ps_p[q][:], AF.Exp,
                                     bias=bq_c[:, oc:oc + 1])
            eguard = p.tok("act", t)
            p.wait("dve", eguard, b1tk)
            t = nc.vector.scalar_tensor_tensor(
                q_fm[:, oc, tsl], ps_p[q][:], bq1_c[:, oc:oc + 1],
                R["tmpf"][:, q * 512:q * 512 + 512], ALU.add, ALU.min)
            dguard = p.tok("dve", t)
            dgq[q] = dguard
    q_tok = dguard
    if L1A_STAGE <= 1:
        p.barrier()
        return

    bk_tok = vec_to_bcast(p, R, W["l1_bk"], R["bb1"], 1024)
    p.wait("dve", bk_tok)
    t = nc.vector.tensor_scalar_add(R["bb2"][:], R["bb1"][:], 1.0)
    bk1_tok = p.tok("dve", t)
    p.wait("pool", p.last("pe"))
    wtok = p.dma("pool", wsb, W["l1_wkT"])
    kgq = {}
    for i in range(NT):
        til = slice(i * 128, (i + 1) * 128)
        for half in range(2):
            q = half
            osl = slice(half * 512, (half + 1) * 512)
            p.wait("pe", wtok, dguard if i == 0 and half == 0 else None,
                   kgq.get(q))
            for c in range(KC):
                t = nc.tensor.matmul(ps_p[q][:], xnT[:, c, til],
                                     wsb[:, c, osl], start=(c == 0),
                                     stop=(c == KC - 1))
            mtok = p.tok("pe", t)
            p.wait("dve", mtok, bk1_tok, kgq.get(q))
            t = nc.vector.tensor_tensor(R["tmpf"][:, q * 512:q * 512 + 512],
                                        ps_p[q][:], R["bb1"][:, osl], ALU.add)
            dtk = p.tok("dve", t)
            p.wait("act", dtk)
            t = nc.scalar.activation(R["tmpsq"][:, q * 512:q * 512 + 512],
                                     R["tmpf"][:, q * 512:q * 512 + 512], AF.Exp)
            aguard = p.tok("act", t)
            p.wait("dve", aguard)
            t = nc.vector.scalar_tensor_tensor(
                k_tm[:, i, osl], R["tmpf"][:, q * 512:q * 512 + 512], 1.0,
                R["tmpsq"][:, q * 512:q * 512 + 512], ALU.add, ALU.min)
            dguard = p.tok("dve", t)
            kgq[q] = dguard
    k_tok = dguard
    if L1A_STAGE <= 2:
        p.barrier()
        return

    bv_tok = vec_to_bcast(p, R, W["l1_bv"], R["bb1"], 1024)
    p.wait("pool", p.last("pe"))
    wtok = p.dma("pool", wsb, W["l1_wvT"])
    ps_kv = [R["ps_a"], R["ps_b"]]
    ps_ks = R["ps_row"]
    vguard = [None, None]
    vgq = {}
    for i in range(NT):
        til = slice(i * 128, (i + 1) * 128)
        vb = R["tmpbf_a"] if i % 2 == 0 else R["tmpbf_b"]
        for half in range(2):
            q = half
            osl = slice(half * 512, (half + 1) * 512)
            p.wait("pe", wtok, k_tok, q_tok,
                   dguard if i == 0 and half == 0 else None,
                   vguard[i % 2], bv_tok, vgq.get(q))
            for c in range(KC):
                t = nc.tensor.matmul(ps_p[q][:], xnT[:, c, til],
                                     wsb[:, c, osl], start=(c == 0),
                                     stop=(c == KC - 1))
            mtok = p.tok("pe", t)
            p.wait("dve", mtok, bv_tok, vguard[i % 2])
            t = nc.vector.tensor_tensor(vb[:, osl], ps_p[q][:], R["bb1"][:, osl],
                                        ALU.add)
            dguard = p.tok("dve", t)
            vgq[q] = dguard
        p.wait("pe", dguard)
        for h in range(16):
            hb = ps_kv[h // 8]
            hsl = slice((h % 8) * 64, (h % 8) * 64 + 64)
            t = nc.tensor.matmul(hb[0:64, hsl], k_tm[:, i, h * 64:(h + 1) * 64],
                                 vb[:, h * 64:(h + 1) * 64],
                                 start=(i == 0 and h % 8 == 0),
                                 stop=(i == NT - 1 and h % 8 == 7))
            t = nc.tensor.matmul(ps_ks[0:64, h:h + 1],
                                 k_tm[:, i, h * 64:(h + 1) * 64],
                                 R["ones_bfc"][:],
                                 start=(i == 0 and h == 0),
                                 stop=(i == NT - 1 and h == 15))
        vguard[i % 2] = p.tok("pe", t)
    kv_tok = p.last("pe")

    kv_bf = R["kv_bf"]
    ks_bf = R["ks_bf"]
    p.wait("dve", kv_tok)
    for lohi in range(2):
        base = lohi * 64
        t = nc.vector.tensor_copy(
            kv_bf[base:base + 64, 0:8, :],
            ps_kv[0][0:64, :].rearrange("p (h d) -> p h d", d=64))
        p.tok("dve", t)
        t = nc.vector.tensor_copy(
            kv_bf[base:base + 64, 8:16, :],
            ps_kv[1][0:64, :].rearrange("p (h d) -> p h d", d=64))
        p.tok("dve", t)
        t = nc.vector.tensor_copy(ks_bf[base:base + 64, :], ps_ks[0:64, 0:16])
        p.tok("dve", t)
    kvbf_tok = p.last("dve")
    if L1A_STAGE <= 3:
        p.barrier()
        return

    attn_fm = R["B"].bitcast(BF16).rearrange("p (c t) -> p c t", t=T)
    ps_qk = [R["ps_a"], R["ps_b"]]
    ps_n = R["ps_row"]
    R["_fm_evac_tok"] = [None]
    nguard = None
    atoks = []
    for i in range(NT):
        til = slice(i * 128, (i + 1) * 128)
        p.wait("pe", kvbf_tok, nguard)
        for h in range(16):
            hb = (h % 2) * 64
            ch = h // 2
            bank = h % 2
            slot = h // 2
            t = nc.tensor.matmul(
                ps_qk[bank][:, slot * 64:slot * 64 + 64],
                q_fm[hb:hb + 64, ch, til], kv_bf[hb:hb + 64, h, :],
                start=(h < 2), stop=(h >= 14))
            nb = ps_n if bank == 0 else ps_c
            t = nc.tensor.matmul(
                nb[:, slot: slot + 1],
                q_fm[hb:hb + 64, ch, til],
                ks_bf[hb:hb + 64, h:h + 1], start=(h < 2), stop=(h >= 14))
        mtok = p.tok("pe", t)
        p.wait("dve", mtok)
        t = nc.vector.tensor_scalar_max(st[:, 32:40], ps_n[:, 0:8], 1e-6)
        p.tok("dve", t)
        t = nc.vector.tensor_scalar_max(st[:, 40:48], ps_c[:, 0:8], 1e-6)
        tk = p.tok("dve", t)
        p.wait("dve", tk)
        t = nc.vector.reciprocal(st[:, 32:48], st[:, 32:48])
        tk = p.tok("dve", t)
        p.wait("dve", tk)
        vb = R["tmpbf_a"] if i % 2 == 0 else R["tmpbf_b"]
        rb = st[:, 32:48].unsqueeze(2).to_broadcast([128, 16, 64])
        hview0 = ps_qk[0][:].rearrange("p (h d) -> p h d", d=64)
        hview1 = ps_qk[1][:].rearrange("p (h d) -> p h d", d=64)
        if L1A_STAGE >= 4.2:
            vbv = vb[:].rearrange("p (a b d) -> p a b d", b=2, d=64)
            t = nc.vector.tensor_tensor(vbv[:, :, 0, :], hview0,
                                        rb[:, 0:8], ALU.mult)
            p.tok("dve", t)
            t = nc.vector.tensor_tensor(vbv[:, :, 1, :], hview1,
                                        rb[:, 8:16], ALU.mult)
            nguard = p.tok("dve", t)
        else:
            nguard = p.last("dve")
        if L1A_STAGE >= 4.3:
            emit_tile_to_fm(p, R, i, vb[:], nguard, attn_fm[:],
                            evac_eng="dve" if i % 2 == 0 else "act")
            atoks.append(R["_fm_evac_tok"][0])

    if L1A_STAGE <= 4.9:
        p.barrier()
        return
    bo_tok = vec_to_bcast(p, R, W["l1_bo"], R["bb1"], 1024)
    p.wait("pool", p.last("pe"))
    wtok = p.dma("pool", wsb, W["l1_woT"])
    xres_tok = [None] * NT
    for i in range(NT):
        p.wait("dve", bo_tok)
        t = nc.vector.tensor_tensor(R["x_res"][:, i, :], R["x_res"][:, i, :],
                                    R["bb1"][:], ALU.add)
        xres_tok[i] = p.tok("dve", t)
    ygq = {}
    yguard = p.last("dve")
    for i in range(NT):
        til = slice(i * 128, (i + 1) * 128)
        for half in range(2):
            q = half
            osl = slice(half * 512, (half + 1) * 512)
            p.wait("pe", wtok, atoks[-1],
                   yguard if i == 0 and half == 0 else None, ygq.get(q))
            for c in range(KC):
                t = nc.tensor.matmul(ps_p[q][:], attn_fm[:, c, til],
                                     wsb[:, c, osl], start=(c == 0),
                                     stop=(c == KC - 1))
            mtok = p.tok("pe", t)
            p.wait("dve", mtok, xres_tok[i])
            t = nc.vector.scalar_tensor_tensor(
                R["x_res"][:, i, osl], ps_p[q][:], 1.0,
                R["x_res"][:, i, osl], ALU.mult, ALU.add)
            xres_tok[i] = p.tok("dve", t)
            ygq[q] = xres_tok[i]
    p.barrier()


def emit_wquant_to_dram(p, R, W, src_name, dst_dram, nrows, st_beta_col):
    nc = p.nc
    st = R["stats"]
    src = W[src_name]
    ncols = src.shape[1]
    nch = nrows // 128
    nhalf = ncols // 2048 if ncols >= 2048 else 1
    pw = min(2048, ncols)
    wrf = R["Wr"].bitcast(F32)
    slots = [wrf[:, 0:pw], wrf[:, 2048:2048 + pw]]
    outb = [R["A"].bitcast(BF16)[:, 0:pw], R["A"].bitcast(BF16)[:, 2048:2048 + pw]]
    wcol = st[:, 64:65]
    acc = st[:, 65:66]
    t = nc.vector.memset(acc, 0.0)
    atk = p.tok("dve", t)
    rd_guard = [None, None]
    pieces = [(c, hh) for c in range(nch) for hh in range(nhalf)]
    for k, (c, hh) in enumerate(pieces):
        sl = k % 2
        qn = "sp" if sl == 0 else "act"
        p.wait(qn, rd_guard[sl])
        tk = p.dma(qn, slots[sl], src[c * 128:(c + 1) * 128,
                                      hh * 2048:hh * 2048 + pw])
        p.wait("dve", tk, atk)
        t = nc.vector.reduce_sum(wcol, slots[sl], axis=AX.X,
                                 apply_absolute_value=True)
        tk = p.tok("dve", t)
        p.wait("dve", tk)
        t = nc.vector.tensor_tensor(acc, acc, wcol, ALU.add)
        atk = p.tok("dve", t)
        rd_guard[sl] = atk
    p.wait("pe", atk)
    t = nc.tensor.matmul(R["ps_small"][0:1, 0:1], R["ones_col"][:], acc,
                         start=True, stop=True)
    ptk = p.tok("pe", t)
    p.wait("dve", ptk)
    t = nc.vector.tensor_scalar(st[0:1, st_beta_col:st_beta_col + 1],
                                R["ps_small"][0:1, 0:1],
                                1.0 / (nrows * ncols), 1e-5, ALU.mult, ALU.max)
    tk = p.tok("dve", t)
    p.wait("dve", tk)
    t = nc.vector.reciprocal(st[0:1, 66:67], st[0:1, st_beta_col:st_beta_col + 1])
    tk = p.tok("dve", t)
    sc_tok = bcast_row_to_col(p, R, st[0:1, 66:67], st[:, 67:68])
    g1 = [None, None]
    g3 = [None, None]
    wq_guard = None
    for k, (c, hh) in enumerate(pieces):
        sl = k % 2
        qn = "sp" if sl == 0 else "act"
        e1, e2, e3 = ("dve", "dve", "dve")
        E = {"pool": nc.gpsimd, "dve": nc.vector}
        p.wait(qn, g3[sl], sc_tok)
        tk = p.dma(qn, slots[sl], src[c * 128:(c + 1) * 128,
                                      hh * 2048:hh * 2048 + pw])
        p.wait(e1, tk, sc_tok, g3[sl])
        t = E[e1].tensor_scalar(slots[sl], slots[sl], st[:, 67:68], MAGIC,
                                ALU.mult, ALU.add)
        t1 = p.tok(e1, t)
        p.wait(e2, t1)
        t = E[e2].tensor_scalar(slots[sl], slots[sl], MAGIC, 1.0,
                                ALU.subtract, ALU.min)
        t2 = p.tok(e2, t)
        p.wait(e3, t2, g1[sl])
        t = E[e3].tensor_scalar(outb[sl], slots[sl], -1.0, None, ALU.max)
        t3 = p.tok(e3, t)
        g3[sl] = t3
        p.wait(qn, t3)
        wq_guard = p.dma(qn, dst_dram[c * 128:(c + 1) * 128,
                                      hh * 2048:hh * 2048 + pw], outb[sl])
        g1[sl] = wq_guard
    return wq_guard


def emit_l2f(p, R, W, scr):
    nc = p.nc
    st = R["stats"]
    h_dram = scr["h_dram"]
    wqg_dram = scr["wqg_dram"]
    wqu_dram = scr["wqu_dram"]
    wqd_dram = scr["wqd_dram"]
    ps_c = p.bps("ps_c_l2f", [128, 512], F32)

    kT = R["B"].bitcast(BF16).rearrange("p (c t) -> p c t", t=T)
    rs_all = st[:, 16:32]
    s_all = st[:, 0:16]
    R["_fm_evac_tok"] = [None]

    def consume(i, xn, tok):
        scr2 = R["bb2"]
        kbf = R["tmpbf_a"] if i % 2 == 0 else R["tmpbf_b"]
        p.wait("dve", tok)
        t = nc.vector.reduce_max(st[:, 32:33], xn[:], axis=AX.X,
                                 apply_absolute_value=True)
        tk = p.tok("dve", t)
        p.wait("dve", tk)
        t = nc.vector.tensor_scalar(rs_all[:, i:i + 1], st[:, 32:33], 1e-5,
                                    1.0 / 127.0, ALU.max, ALU.mult)
        tk = p.tok("dve", t)
        p.wait("dve", tk)
        t = nc.vector.reciprocal(s_all[:, i:i + 1], rs_all[:, i:i + 1])
        tk = p.tok("dve", t)
        p.wait("dve", tk)
        t = nc.vector.tensor_scalar(scr2[:], xn[:], s_all[:, i:i + 1], MAGIC,
                                    ALU.mult, ALU.add)
        tk = p.tok("dve", t)
        p.wait("dve", tk)
        t = nc.vector.tensor_scalar(kbf[:], scr2[:], MAGIC, None, ALU.subtract)
        tk = p.tok("dve", t)
        p.wait("act", tk)
        emit_tile_to_fm(p, R, i, kbf[:], tk, kT[:],
                        evac_eng="dve" if i % 2 == 0 else "act")
        return R["_fm_evac_tok"][0]

    emit_ln(p, R, consume)
    ln_tok = R["_fm_evac_tok"][0]
    p.barrier()

    emit_wquant_to_dram(p, R, W, "l2_wgT", wqg_dram, 1024, 70)
    p.barrier()
    emit_wquant_to_dram(p, R, W, "l2_wuT", wqu_dram, 1024, 71)
    p.barrier()
    emit_wquant_to_dram(p, R, W, "l2_wdT", wqd_dram, 4096, 72)
    p.barrier()

    bgc = st[:, 33:34]
    buc = st[:, 34:35]
    t1 = bcast_row_to_col(p, R, st[0:1, 70:71], bgc)
    t2 = bcast_row_to_col(p, R, st[0:1, 71:72], buc)
    bg_rs = st[:, 36:52]
    bu_rs = st[:, 52:68]
    p.wait("dve", t1, t2)
    t = nc.vector.tensor_scalar(bg_rs, rs_all, bgc, None, ALU.mult)
    p.tok("dve", t)
    t = nc.vector.tensor_scalar(bu_rs, rs_all, buc, None, ALU.mult)
    sc_tok = p.tok("dve", t)
    amax2 = st[:, 0:16]
    t = nc.vector.memset(amax2, 0.0)
    am_tok = p.tok("dve", t)

    wsl = R["Wr"].bitcast(BF16)
    wq_g = wsl[:, 0:4096].rearrange("p (c n) -> p c n", n=512)
    wq_u = wsl[:, 4096:8192].rearrange("p (c n) -> p c n", n=512)
    ps_g = [R["ps_a"], p.bps("ps_g2_l2f", [128, 512], F32)]
    ps_u = [R["ps_b"], p.bps("ps_u2_l2f", [128, 512], F32)]
    agq = {}
    dgq = {}
    dguard = None
    aguard = None
    hguard = {}
    for hs in range(8):
        p.wait("sp", p.last("pe"))
        g_tok = p.dma("sp", wq_g, wqg_dram[:, hs * 512:(hs + 1) * 512]
                      .rearrange("(c p) n -> p c n", p=128))
        u_tok = p.dma("sp", wq_u, wqu_dram[:, hs * 512:(hs + 1) * 512]
                      .rearrange("(c p) n -> p c n", p=128))
        for i in range(NT):
            q = i % 2
            til = slice(i * 128, (i + 1) * 128)
            p.wait("pe", g_tok, ln_tok, agq.get(q))
            for c in range(KC):
                t = nc.tensor.matmul(ps_g[q][:], kT[:, c, til], wq_g[:, c, :],
                                     start=(c == 0), stop=(c == KC - 1))
            gm = p.tok("pe", t)
            p.wait("pe", u_tok, dgq.get(q))
            for c in range(KC):
                t = nc.tensor.matmul(ps_u[q][:], kT[:, c, til], wq_u[:, c, :],
                                     start=(c == 0), stop=(c == KC - 1))
            um = p.tok("pe", t)
            p.wait("act", gm, sc_tok, dgq.get(q))
            t = nc.scalar.activation(R["tmpf"][:, q * 512:q * 512 + 512],
                                     ps_g[q][:], AF.Silu,
                                     scale=bg_rs[:, i:i + 1])
            agq[q] = p.tok("act", t)
            hb = R["tmpbf_a"] if i % 2 == 0 else R["tmpbf_b"]
            p.wait("dve", um, agq[q], am_tok, hguard.get(i % 2))
            t = nc.vector.scalar_tensor_tensor(hb[:, 0:512], ps_u[q][:],
                                               bu_rs[:, i:i + 1],
                                               R["tmpf"][:, q * 512:q * 512 + 512],
                                               ALU.mult, ALU.mult)
            dguard = p.tok("dve", t)
            dgq[q] = dguard
            p.wait("dve", dguard)
            t = nc.vector.reduce_max(st[:, 32:33], hb[:, 0:512], axis=AX.X,
                                     apply_absolute_value=True)
            tk = p.tok("dve", t)
            p.wait("dve", tk)
            t = nc.vector.tensor_tensor(amax2[:, i:i + 1], amax2[:, i:i + 1],
                                        st[:, 32:33], ALU.max)
            am_tok = p.tok("dve", t)
            p.wait("sp", am_tok)
            hguard[i % 2] = p.dma(
                "sp", h_dram.rearrange("(i p) n -> i p n", p=128)[i][:, hs * 512:(hs + 1) * 512],
                hb[:, 0:512])
    p.barrier()

    bdc = st[:, 33:34]
    t1 = bcast_row_to_col(p, R, st[0:1, 72:73], bdc)
    s2_all = st[:, 16:32]
    bd_rs2 = st[:, 36:52]
    p.wait("dve", t1)
    t = nc.vector.tensor_scalar(s2_all, amax2, 1e-5, 1.0 / 127.0, ALU.max, ALU.mult)
    tk = p.tok("dve", t)
    p.wait("dve", tk)
    t = nc.vector.tensor_scalar(bd_rs2, s2_all, bdc, None, ALU.mult)
    sc2_tok = p.tok("dve", t)
    p.wait("dve", sc2_tok)
    t = nc.vector.reciprocal(s2_all, s2_all)
    s2_tok = p.tok("dve", t)

    k2T = R["A"].bitcast(BF16)[:, 0:8192].rearrange("p (c t) -> p c t", t=T)
    wq_d = wsl[:, 0:4096].rearrange("p (c n) -> p c n", n=1024)
    ps_cc = [ps_c, ps_g[1]]
    xres_tok = [None] * NT
    ygq = {}
    yguard = None
    trguard = None
    for hg in range(8):
        p.wait("sp", p.last("pe"))
        d_tok = p.dma("sp", wq_d, wqd_dram[hg * 512:(hg + 1) * 512, :]
                      .rearrange("(c p) n -> p c n", p=128))
        for i in range(NT):
            hb = R["tmpbf_a"] if i % 2 == 0 else R["tmpbf_b"]
            p.wait("sp", p.last("dve"), p.last("pe"))
            htk = p.dma("sp", hb[:, 0:512],
                        h_dram.rearrange("(i p) n -> i p n", p=128)[i][:, hg * 512:(hg + 1) * 512])
            p.wait("dve", htk, s2_tok, trguard)
            t = nc.vector.tensor_scalar(R["tmpf"][:, 0:512], hb[:, 0:512],
                                        s2_all[:, i:i + 1], MAGIC,
                                        ALU.mult, ALU.add)
            tk = p.tok("dve", t)
            p.wait("dve", tk)
            t = nc.vector.tensor_scalar(hb[:, 512:1024], R["tmpf"][:, 0:512],
                                        MAGIC, None, ALU.subtract)
            tk = p.tok("dve", t)
            p.wait("pe", tk, trguard)
            for j in range(4):
                t = nc.tensor.transpose(
                    R["ps_tr"][:, j * 128:(j + 1) * 128],
                    hb[:, 512 + j * 128:512 + (j + 1) * 128], R["ident"][:])
            ptk = p.tok("pe", t)
            eng = "dve" if i % 2 == 0 else "act"
            p.wait(eng, ptk)
            if eng == "dve":
                t = nc.vector.tensor_copy(
                    k2T[:, 0:4, i * 128:(i + 1) * 128],
                    R["ps_tr"][:, 0:512].rearrange("p (c q) -> p c q", q=128))
            else:
                t = nc.scalar.copy(
                    k2T[:, 0:4, i * 128:(i + 1) * 128],
                    R["ps_tr"][:, 0:512].rearrange("p (c q) -> p c q", q=128))
            trguard = p.tok(eng, t)
        for i in range(NT):
            til = slice(i * 128, (i + 1) * 128)
            for half in range(2):
                q = half
                osl = slice(half * 512, (half + 1) * 512)
                p.wait("pe", d_tok, trguard, ygq.get(q))
                for j in range(4):
                    t = nc.tensor.matmul(ps_cc[q][:], k2T[:, j, til],
                                         wq_d[:, j, osl], start=(j == 0),
                                         stop=(j == 3))
                mtok = p.tok("pe", t)
                p.wait("dve", mtok, xres_tok[i], sc2_tok)
                t = nc.vector.scalar_tensor_tensor(
                    R["x_res"][:, i, osl], ps_cc[q][:], bd_rs2[:, i:i + 1],
                    R["x_res"][:, i, osl], ALU.mult, ALU.add)
                xres_tok[i] = p.tok("dve", t)
                ygq[q] = xres_tok[i]
                yguard = xres_tok[i]
    p.barrier()


def emit_l0a(p, R, W, scr):
    nc = p.nc
    st = R["stats"]
    q_dram = scr["q_dram"]
    H = 16
    aux = R["aux"]
    auxf = aux.bitcast(F32)
    qselT = aux[:, 0:160]
    attnT = aux[:, 160:320]
    ao_sb = aux[:, 320:1344].rearrange("p (h d) -> p h d", d=64)
    t32i = auxf[0:32, 672:704]
    t32o = auxf[0:32, 704:736]
    wrf = R["Wr"].bitcast(F32)
    qs_byh = wrf[0:16, 0:2048]
    qs_scr = wrf[0:16, 2048:4096]
    iota2k = wrf[0:10, 0:2048]
    qs_all = R["bb2"][:, 0:256].rearrange("p (i h) -> p i h", h=16)

    ln_tok = emit_ln_to_xnT(p, R)
    xnT = R["xnT"]
    p.barrier()

    wsb = R["Wr"].bitcast(BF16)[:, 0:8192].rearrange("p (c n) -> p c n", n=1024)
    ps_a = R["ps_a"]
    ps_b = R["ps_b"]

    bq_tok = vec_to_bcast(p, R, W["l0_bq"], R["bb1"], 1024)
    p.wait("pool", p.last("pe"))
    wq_tok = p.dma("pool", wsb, W["l0_wqT"])
    eps9, eps9_tok = p.ccol(1e-9)
    dguard = [None, None]
    pguard = [None, None]
    Af = R["A"].bitcast(F32)
    ps_big = p.bps("ps_big_l0a", [128, 2048], F32)
    qdma = [None] * NT
    for i in range(NT):
        qq = i % 2
        til = slice(i * 128, (i + 1) * 128)
        ps = [ps_a, ps_b] if qq == 0 else [ps_big[:, 0:512], ps_big[:, 512:1024]]
        QF = R["tmpf"][:] if qq == 0 else Af[:, 0:1024]
        QS = R["tmpsq"][:] if qq == 0 else Af[:, 1024:2048]
        stq = R["stats"] if qq == 0 else R["bb2"][:, 256:336]
        cbf = R["tmpbf_a"] if qq == 0 else R["tmpbf_b"]
        p.wait("pe", wq_tok, ln_tok, dguard[qq], p.last("act") if i < 2 else None)
        for half in range(2):
            osl = slice(half * 512, (half + 1) * 512)
            for c in range(KC):
                t = nc.tensor.matmul(ps[half][:], xnT[:, c, til],
                                     wsb[:, c, osl], start=(c == 0),
                                     stop=(c == KC - 1))
        mtok = p.tok("pe", t)
        p.wait("dve", mtok, bq_tok, pguard[qq], dguard[qq])
        t = nc.vector.tensor_tensor(QF[:, 0:512], ps[0][:],
                                    R["bb1"][:, 0:512], ALU.add)
        p.tok("dve", t)
        t = nc.vector.tensor_tensor(QF[:, 512:1024], ps[1][:],
                                    R["bb1"][:, 512:1024], ALU.add)
        qtk = p.tok("dve", t)
        p.wait("pool", qtk, qdma[i - 2] if i >= 2 else None)
        t = nc.gpsimd.tensor_copy(cbf[:], QF[:])
        ctk = p.tok("pool", t)
        pguard[qq] = ctk
        p.wait("sp", ctk)
        qdma[i] = p.dma("sp", q_dram.rearrange("(i p) d -> i p d", p=128)[i],
                        cbf[:])
        q3 = QF[:].rearrange("p (h d) -> p h d", d=64)
        p.wait("dve", qtk)
        t = nc.vector.reduce_sum(stq[:, 0:16], q3, axis=AX.X)
        s1tk = p.tok("dve", t)
        p.wait("act", qtk)
        t = nc.scalar.activation(QS[:], QF[:], AF.Square)
        sqt = p.tok("act", t)
        p.wait("dve", sqt)
        t = nc.vector.reduce_sum(stq[:, 16:32],
                                 QS[:].rearrange("p (h d) -> p h d", d=64),
                                 axis=AX.X)
        s2tk = p.tok("dve", t)
        p.wait("dve", s2tk)
        t = nc.vector.reduce_max(stq[:, 32:48], q3, axis=AX.X)
        mtk = p.tok("dve", t)
        p.wait("dve", mtk)
        mb = stq[:, 32:48].unsqueeze(2).to_broadcast([128, 16, 64])
        t = nc.vector.tensor_tensor(
            QS[:].rearrange("p (h d) -> p h d", d=64), q3, mb,
            ALU.subtract)
        dtk = p.tok("dve", t)
        p.wait("act", dtk, pguard[qq])
        t = nc.scalar.activation(QF[:], QS[:], AF.Exp)
        etk = p.tok("act", t)
        p.wait("dve", etk)
        t = nc.vector.reduce_sum(stq[:, 48:64],
                                 QF[:].rearrange("p (h d) -> p h d", d=64),
                                 axis=AX.X)
        stk = p.tok("dve", t)
        p.wait("dve", stk)
        t = nc.vector.reciprocal(stq[:, 64:80], stq[:, 48:64])
        rtk = p.tok("dve", t)
        p.wait("dve", rtk)
        rb = stq[:, 64:80].unsqueeze(2).to_broadcast([128, 16, 64])
        t = nc.vector.tensor_tensor(
            QF[:].rearrange("p (h d) -> p h d", d=64),
            QF[:].rearrange("p (h d) -> p h d", d=64), rb, ALU.mult)
        ptk2 = p.tok("dve", t)
        p.wait("act", ptk2, eps9_tok)
        t = nc.scalar.activation(QS[:], QF[:], AF.Ln,
                                 bias=eps9)
        ltk = p.tok("act", t)
        p.wait("dve", ltk)
        t = nc.vector.tensor_tensor(QS[:], QF[:], QS[:],
                                    ALU.mult)
        ptk3 = p.tok("dve", t)
        p.wait("dve", ptk3)
        t = nc.vector.reduce_sum(stq[:, 48:64],
                                 QS[:].rearrange("p (h d) -> p h d", d=64),
                                 axis=AX.X, negate=True)
        enttk = p.tok("dve", t)
        p.wait("act", s2tk)
        t = nc.scalar.activation(stq[:, 64:80], stq[:, 16:32], AF.Sqrt)
        l2tk = p.tok("act", t)
        p.wait("dve", s1tk, l2tk, enttk)
        t = nc.vector.tensor_tensor(qs_all[:, i, :], stq[:, 0:16], stq[:, 0:16],
                                    ALU.mult)
        tk = p.tok("dve", t)
        p.wait("dve", tk)
        t = nc.vector.tensor_scalar(qs_all[:, i, :], qs_all[:, i, :],
                                    -1.0 / 64.0, None, ALU.mult)
        tk = p.tok("dve", t)
        p.wait("dve", tk)
        t = nc.vector.tensor_tensor(qs_all[:, i, :], stq[:, 16:32],
                                    qs_all[:, i, :], ALU.add)
        tk = p.tok("dve", t)
        p.wait("dve", tk)
        t = nc.vector.tensor_scalar(qs_all[:, i, :], qs_all[:, i, :],
                                    0.2 / 63.0, None, ALU.mult)
        tk = p.tok("dve", t)
        p.wait("dve", tk)
        t = nc.vector.scalar_tensor_tensor(stq[:, 64:80], stq[:, 64:80], 0.5,
                                           qs_all[:, i, :], ALU.mult, ALU.add)
        tk = p.tok("dve", t)
        p.wait("dve", tk)
        t = nc.vector.scalar_tensor_tensor(qs_all[:, i, :], stq[:, 48:64], 0.3,
                                           stq[:, 64:80], ALU.mult, ALU.add)
        dguard[qq] = p.tok("dve", t)

    K_fm = R["A"].bitcast(BF16).rearrange("p (c t) -> p c t", t=T)
    V_tm = R["B"].bitcast(BF16).rearrange("p (i d) -> p i d", d=D)
    bkc = st[:, 0:8]
    bk_tok = vec_to_cols(p, R, W["l0_bk"], bkc, 1024)
    p.wait("pool", p.last("pe"), dguard[0], dguard[1])
    wk_tok = p.dma("pool", wsb, W["l0_wkT"])
    kguard = None
    for oc in range(KC):
        for ts4 in range(4):
            tsl = slice(ts4 * 512, (ts4 + 1) * 512)
            p.wait("pe", wk_tok, kguard)
            for c in range(KC):
                t = nc.tensor.matmul(ps_a[:], wsb[:, c, oc * 128:(oc + 1) * 128],
                                     xnT[:, c, tsl], start=(c == 0),
                                     stop=(c == KC - 1))
            mtok = p.tok("pe", t)
            p.wait("dve", mtok, bk_tok)
            t = nc.vector.tensor_scalar_add(K_fm[:, oc, tsl], ps_a[:],
                                            bkc[:, oc:oc + 1])
            kguard = p.tok("dve", t)
    bv_tok = vec_to_bcast(p, R, W["l0_bv"], R["bb1"], 1024)
    p.wait("pool", p.last("pe"), kguard)
    wv_tok = p.dma("pool", wsb, W["l0_wvT"])
    vguard = None
    for i in range(NT):
        til = slice(i * 128, (i + 1) * 128)
        for half in range(2):
            osl = slice(half * 512, (half + 1) * 512)
            p.wait("pe", wv_tok, vguard, bv_tok)
            for c in range(KC):
                t = nc.tensor.matmul(ps_a[:], xnT[:, c, til],
                                     wsb[:, c, osl], start=(c == 0),
                                     stop=(c == KC - 1))
            mtok = p.tok("pe", t)
            p.wait("dve", mtok, bv_tok)
            t = nc.vector.tensor_tensor(V_tm[:, i, osl], ps_a[:],
                                        R["bb1"][:, osl], ALU.add)
            vguard = p.tok("dve", t)

    wr_guard = p.last("pe")
    tguard = None
    for i in range(NT):
        p.wait("pe", dguard[0], dguard[1], wr_guard, tguard)
        t = nc.tensor.transpose(R["ps_row"][0:16, 0:128], qs_all[:, i, :],
                                R["ident_f"][:])
        ptk = p.tok("pe", t)
        p.wait("dve", ptk)
        t = nc.vector.tensor_copy(qs_byh[:, i * 128:(i + 1) * 128],
                                  R["ps_row"][0:16, 0:128])
        tguard = p.tok("dve", t)
    top8 = st[0:16, 16:24]
    next8 = st[0:16, 24:32]
    p.wait("dve", tguard)
    t = nc.vector.max(top8, qs_byh)
    tk = p.tok("dve", t)
    p.wait("dve", tk)
    t = nc.vector.max_index(R["idx_u"][0:16, :], top8, qs_byh)
    tk = p.tok("dve", t)
    p.wait("dve", tk)
    t = nc.vector.tensor_copy(st[0:16, 32:40], R["idx_u"][0:16, :])
    tk1 = p.tok("dve", t)
    p.wait("dve", tk1)
    t = nc.vector.match_replace(qs_scr, top8, qs_byh, -1e30)
    tk = p.tok("dve", t)
    p.wait("dve", tk)
    t = nc.vector.max(next8, qs_scr)
    tk = p.tok("dve", t)
    p.wait("dve", tk)
    t = nc.vector.max_index(R["idx_u"][0:16, :], next8, qs_byh)
    tk = p.tok("dve", t)
    p.wait("dve", tk)
    t = nc.vector.tensor_copy(st[0:16, 40:42], R["idx_u"][0:16, 0:2])
    idxf_tok = p.tok("dve", t)
    t = nc.vector.reduce_sum(st[0:16, 44:45], qs_byh, axis=AX.X)
    s1tk = p.tok("dve", t)
    p.wait("act", tguard, idxf_tok)
    t = nc.scalar.activation(qs_scr, qs_byh, AF.Square,
                             accum_out=st[0:16, 45:46])
    s2tk = p.tok("act", t)
    p.wait("dve", s1tk, s2tk)
    t = nc.vector.tensor_scalar_mul(st[0:16, 46:47], st[0:16, 44:45], 1.0 / T)
    tk = p.tok("dve", t)
    p.wait("dve", tk)
    t = nc.vector.tensor_tensor(st[0:16, 47:48], st[0:16, 44:45],
                                st[0:16, 46:47], ALU.mult)
    tk = p.tok("dve", t)
    p.wait("dve", tk)
    t = nc.vector.tensor_tensor(st[0:16, 47:48], st[0:16, 45:46],
                                st[0:16, 47:48], ALU.subtract)
    tk = p.tok("dve", t)
    p.wait("dve", tk)
    t = nc.vector.tensor_scalar_mul(st[0:16, 47:48], st[0:16, 47:48],
                                    1.0 / (T - 1))
    tk = p.tok("dve", t)
    p.wait("act", tk)
    t = nc.scalar.activation(st[0:16, 47:48], st[0:16, 47:48], AF.Sqrt)
    tk = p.tok("act", t)
    p.wait("dve", tk)
    t = nc.vector.tensor_scalar_add(st[0:16, 46:47], st[0:16, 46:47], 1e-6)
    tk = p.tok("dve", t)
    p.wait("dve", tk)
    t = nc.vector.reciprocal(st[0:16, 46:47], st[0:16, 46:47])
    tk = p.tok("dve", t)
    p.wait("dve", tk)
    t = nc.vector.tensor_tensor(st[0:16, 47:48], st[0:16, 47:48],
                                st[0:16, 46:47], ALU.mult)
    tk = p.tok("dve", t)
    p.wait("dve", tk)
    t = nc.vector.tensor_scalar(st[0:16, 47:48], st[0:16, 47:48], 10.0, MAGIC,
                                ALU.mult, ALU.add)
    tk = p.tok("dve", t)
    p.wait("dve", tk)
    t = nc.vector.tensor_scalar(st[0:16, 47:48], st[0:16, 47:48], MAGIC, 10.0,
                                ALU.subtract, ALU.min)
    tk = p.tok("dve", t)
    p.wait("dve", tk)
    t = nc.vector.tensor_scalar_max(st[0:16, 47:48], st[0:16, 47:48], 3.0)
    ftok = p.tok("dve", t)
    p.wait("pe", ftok)
    t = nc.tensor.matmul(R["ps_small"][0:16, 0:1], R["ones_f"][0:1, 0:16],
                         st[0:1, 47:48], start=True, stop=True)
    ptk = p.tok("pe", t)
    p.wait("dve", ptk)
    t = nc.vector.tensor_copy(st[0:16, 46:47], R["ps_small"][0:16, 0:1])
    utok = p.tok("dve", t)
    t = nc.gpsimd.iota(st[0:16, 48:58], [[1, 10]], channel_multiplier=0,
                       allow_small_or_imprecise_dtypes=True)
    iotok = p.tok("pool", t)
    p.wait("dve", utok, iotok)
    t = nc.vector.tensor_single_scalar(st[0:16, 58:68], st[0:16, 48:58],
                                       st[0:16, 46:47], ALU.is_lt)
    mtok = p.tok("dve", t)
    p.wait("dve", mtok, idxf_tok, tk1)
    t = nc.vector.tensor_scalar_add(st[0:16, 68:78], st[0:16, 32:42], -2048.0)
    tk = p.tok("dve", t)
    p.wait("dve", tk)
    t = nc.vector.tensor_tensor(st[0:16, 68:78], st[0:16, 68:78],
                                st[0:16, 58:68], ALU.mult)
    tk = p.tok("dve", t)
    p.wait("dve", tk)
    t = nc.vector.tensor_scalar_add(st[0:16, 68:78], st[0:16, 68:78], 2048.0)
    sidx_tok = p.tok("dve", t)
    p.wait("dve", sidx_tok)
    t = nc.vector.memset(t32i, 2048.0)
    tk = p.tok("dve", t)
    p.wait("dve", tk)
    t = nc.vector.tensor_copy(t32i[0:16, 0:10], st[0:16, 68:78])
    tk = p.tok("dve", t)
    p.wait("dve", tk)
    t = nc.vector.transpose(t32o, t32i)
    tk = p.tok("dve", t)
    p.wait("dve", tk)
    t = nc.vector.tensor_copy(st[0:10, 48:64], t32o[0:10, 0:16])
    sidxT_tok = p.tok("dve", t)
    p.wait("dve", sidxT_tok)
    t = nc.vector.tensor_copy(t32i[0:16, 0:10], st[0:16, 32:42])
    tk = p.tok("dve", t)
    p.wait("dve", tk)
    t = nc.vector.transpose(t32o, t32i)
    tk = p.tok("dve", t)
    p.wait("dve", tk)
    t = nc.vector.tensor_copy(R["idx_c"][0:10, :], t32o[0:10, 0:16])
    gidx_tok = p.tok("dve", t)

    qsel_tok = None
    tr_tok = None
    for h in range(H):
        hb = (h % 2) * 64
        p.wait("pool", gidx_tok, tr_tok, *(qdma if h == 0 else []))
        qg = p.dma_gather_rows(R["smallbf"][0:10, 1024:2048], q_dram,
                               R["idx_c"][0:10, h:h + 1])
        p.wait("pe", qg, qsel_tok)
        t = nc.tensor.transpose(R["ps_tr"][0:64, 0:10],
                                R["smallbf"][0:10, 1024 + h * 64:1024 + (h + 1) * 64],
                                R["ident"][0:10, 0:10])
        tr_tok = p.tok("pe", t)
        p.wait("dve", tr_tok)
        t = nc.vector.tensor_copy(aux[hb:hb + 64, h * 10:(h + 1) * 10],
                                  R["ps_tr"][0:64, 0:10])
        qsel_tok = p.tok("dve", t)

    ps_sc = ps_big
    att = R["smallbf"][0:10, 0:2048]
    aoguard = None
    attT_guard = None
    for h in range(H):
        hb = (h % 2) * 64
        ch = h // 2
        p.wait("pe", qsel_tok, p.last("dve"), p.last("act"))
        for n4 in range(4):
            nsl = slice(n4 * 512, (n4 + 1) * 512)
            t = nc.tensor.matmul(ps_sc[0:10, nsl],
                                 aux[hb:hb + 64, h * 10:(h + 1) * 10],
                                 K_fm[hb:hb + 64, ch, nsl],
                                 start=True, stop=True)
        sctok = p.tok("pe", t)
        p.wait("dve", sctok)
        t = nc.vector.reduce_max(st[0:10, 78:79], ps_sc[0:10, :], axis=AX.X,
                                 negate=True)
        tk = p.tok("dve", t)
        p.wait("dve", tk)
        t = nc.vector.tensor_scalar_mul(st[0:10, 78:79], st[0:10, 78:79], 0.125)
        tk = p.tok("dve", t)
        p.wait("act", tk, aoguard)
        t = nc.scalar.activation(att, ps_sc[0:10, :], AF.Exp,
                                 bias=st[0:10, 78:79], scale=0.125,
                                 accum_out=st[0:10, 79:80])
        etk = p.tok("act", t)
        p.wait("dve", etk)
        t = nc.vector.reciprocal(st[0:10, 78:79], st[0:10, 79:80])
        rtk = p.tok("dve", t)
        p.wait("pe", etk, attT_guard)
        for j in range(NT):
            t = nc.tensor.transpose(R["ps_tr"][:, j * 10:(j + 1) * 10],
                                    att[:, j * 128:(j + 1) * 128],
                                    R["ident"][0:10, 0:10])
        ttk = p.tok("pe", t)
        p.wait("dve", ttk)
        t = nc.vector.tensor_copy(aux[:, 160:320], R["ps_tr"][:, 0:160])
        atk = p.tok("dve", t)
        attT_guard = atk
        p.wait("pe", atk, aoguard)
        for j in range(NT):
            t = nc.tensor.matmul(R["ps_row"][0:10, 0:64],
                                 aux[:, 160 + j * 10:160 + (j + 1) * 10],
                                 V_tm[:, j, h * 64:(h + 1) * 64],
                                 start=(j == 0), stop=(j == NT - 1))
        vtk = p.tok("pe", t)
        p.wait("dve", vtk, rtk)
        t = nc.vector.tensor_scalar_mul(ao_sb[0:10, h, :],
                                        R["ps_row"][0:10, 0:64],
                                        st[0:10, 78:79])
        aoguard = p.tok("dve", t)

    p.wait("pool", p.last("dve"), p.last("pe"))
    t = nc.gpsimd.iota(iota2k, [[1, 2048]], channel_multiplier=0,
                       allow_small_or_imprecise_dtypes=True)
    io_tok = p.tok("pool", t)
    full_fm = R["xnT"]
    ohguard = None
    evguard = None
    for pair in range(8):
        for par in range(2):
            h = 2 * pair + par
            psx = ps_a if par == 0 else ps_b
            p.wait("dve", io_tok, sidxT_tok, ohguard, aoguard)
            t = nc.vector.tensor_single_scalar(att, iota2k,
                                               st[0:10, 48 + h:49 + h],
                                               ALU.is_equal)
            ohtok = p.tok("dve", t)
            for n4 in range(4):
                nsl = slice(n4 * 512, (n4 + 1) * 512)
                p.wait("pe", ohtok, evguard)
                t = nc.tensor.matmul(psx[0:64, :], ao_sb[0:10, h, :],
                                     att[:, nsl], start=True, stop=True)
                mtok = p.tok("pe", t)
                eng = "dve" if n4 % 2 == 0 else "act"
                p.wait(eng, mtok)
                if eng == "dve":
                    t = nc.vector.tensor_copy(
                        full_fm[par * 64:par * 64 + 64, pair, nsl], psx[0:64, :])
                else:
                    t = nc.scalar.copy(
                        full_fm[par * 64:par * 64 + 64, pair, nsl], psx[0:64, :])
                evguard = p.tok(eng, t)
            ohguard = evguard

    bo_tok = vec_to_bcast(p, R, W["l0_bo"], R["bb1"], 1024)
    p.wait("pool", p.last("pe"), p.last("dve"), p.last("act"))
    wo_tok = p.dma("pool", wsb, W["l0_woT"])
    xres_tok = [None] * NT
    for i in range(NT):
        p.wait("dve", bo_tok)
        t = nc.vector.tensor_tensor(R["x_res"][:, i, :], R["x_res"][:, i, :],
                                    R["bb1"][:], ALU.add)
        xres_tok[i] = p.tok("dve", t)
    yguard = p.last("dve")
    for i in range(NT):
        til = slice(i * 128, (i + 1) * 128)
        for half in range(2):
            osl = slice(half * 512, (half + 1) * 512)
            p.wait("pe", wo_tok, evguard, yguard)
            for c in range(KC):
                t = nc.tensor.matmul(ps_a[:], full_fm[:, c, til],
                                     wsb[:, c, osl], start=(c == 0),
                                     stop=(c == KC - 1))
            mtok = p.tok("pe", t)
            p.wait("dve", mtok, xres_tok[i])
            t = nc.vector.scalar_tensor_tensor(
                R["x_res"][:, i, osl], ps_a[:], 1.0,
                R["x_res"][:, i, osl], ALU.mult, ALU.add)
            xres_tok[i] = p.tok("dve", t)
            yguard = xres_tok[i]
    p.barrier()


def emit_l2a(p, R, W, scr):
    nc = p.nc
    st = R["stats"]
    R["cmask"] = R["Wr"].bitcast(F32)[:, 1536:1664]
    v_dram = scr["v_dram"]
    ln_tok = emit_ln_to_xnT(p, R)
    xnT = R["xnT"]
    p.barrier()

    Q_fm = R["A"].bitcast(BF16).rearrange("p (c t) -> p c t", t=T)
    K_fm = R["B"].bitcast(BF16).rearrange("p (c t) -> p c t", t=T)
    wsl = R["Wr"].bitcast(BF16)
    w_sl = [wsl[:, 0:1024].rearrange("p (c n) -> p c n", n=128),
            wsl[:, 1024:2048].rearrange("p (c n) -> p c n", n=128)]
    ps_a = R["ps_a"]
    ps_b = R["ps_b"]

    bc_tok = vec_to_cols(p, R, W["l2_battn"][0:1024], st[:, 0:8], 1024)
    bc_tok = vec_to_cols(p, R, W["l2_battn"][1024:2048], st[:, 8:16], 1024)
    bc_tok = vec_to_cols(p, R, W["l2_battn"][2048:3072], st[:, 16:24], 1024)
    bcols = st[:, 0:24]

    wtok = [None, None]
    slot_use = [None, None]
    wtok[0] = p.dma("pool", w_sl[0], W["l2_wattn"][0])
    vstrip = R["tmpsq"].bitcast(BF16).rearrange("p (j d) -> p j d", d=128)
    eguard = None
    vguard = None
    vs_tok = None
    for oc in range(24):
        par = oc % 2
        if oc + 1 < 24:
            p.wait("pool", slot_use[(oc + 1) % 2])
            wtok[(oc + 1) % 2] = p.dma(
                "pool", w_sl[(oc + 1) % 2], W["l2_wattn"][oc + 1])
        for ts4 in range(4):
            tsl = slice(ts4 * 512, (ts4 + 1) * 512)
            p.wait("pe", wtok[par], ln_tok, eguard)
            for c in range(KC):
                t = nc.tensor.matmul(ps_a[:], w_sl[par][:, c, :],
                                     xnT[:, c, tsl], start=(c == 0),
                                     stop=(c == KC - 1))
            mtok = p.tok("pe", t)
            slot_use[par] = mtok
            p.wait("dve", mtok, bc_tok)
            if oc < 8:
                t = nc.vector.tensor_scalar_add(Q_fm[:, oc, tsl], ps_a[:],
                                                bcols[:, oc:oc + 1])
                eguard = p.tok("dve", t)
            elif oc < 16:
                t = nc.vector.tensor_scalar_add(K_fm[:, oc - 8, tsl], ps_a[:],
                                                bcols[:, oc:oc + 1])
                eguard = p.tok("dve", t)
            else:
                p.wait("dve", vguard)
                vb = R["tmpbf_a"]
                t = nc.vector.tensor_scalar_add(vb[:, 0:512], ps_a[:],
                                                bcols[:, oc:oc + 1])
                eguard = p.tok("dve", t)
                p.wait("pe", eguard, vs_tok)
                for jj in range(4):
                    t = nc.tensor.transpose(
                        R["ps_tr"][:, jj * 128:(jj + 1) * 128],
                        vb[:, jj * 128:(jj + 1) * 128], R["ident"][:])
                ptk = p.tok("pe", t)
                p.wait("dve", ptk)
                t = nc.vector.tensor_copy(vstrip[:, ts4 * 4:ts4 * 4 + 4, :],
                                          R["ps_tr"][:, 0:512]
                                          .rearrange("p (j q) -> p j q", q=128))
                vguard = p.tok("dve", t)
        if oc >= 16:
            voc = oc - 16
            p.wait("sp", vguard)
            vs_tok = p.dma(
                "sp",
                v_dram.rearrange("(j p) d -> p j d", p=128)[:, :, voc * 128:(voc + 1) * 128],
                vstrip[:])
    p.barrier()

    t = nc.gpsimd.memset(R["tmpf"][:, 0:128], 0.0)
    tk = p.tok("pool", t)
    p.wait("pool", tk)
    t = nc.gpsimd.affine_select(R["cmask"], R["tmpf"][:, 0:128], [[-1, 128]],
                                ALU.is_ge, -1e30, base=0, channel_multiplier=1)
    cm_tok = p.tok("pool", t)

    bo_tok = vec_to_bcast(p, R, W["l2_bproj"], R["bb1"], 1024)
    xres_tok = [None] * NT
    for i in range(NT):
        p.wait("dve", bo_tok)
        t = nc.vector.tensor_tensor(R["x_res"][:, i, :], R["x_res"][:, i, :],
                                    R["bb1"][:], ALU.add)
        xres_tok[i] = p.tok("dve", t)

    ps_s = [p.bps("ps_s1_l2a", [128, 2048], F32)]
    ps_v = R["ps_row"]
    ps_y = ps_b
    atts = [R["tmpf"].bitcast(BF16), R["bb2"].bitcast(BF16)]
    apair = R["tmpsq"].bitcast(BF16).rearrange("p (q c) -> p q c", c=128)
    vpair = R["aux"][:, 0:2048].rearrange("p (j d) -> p j d", d=128)
    wp_sb = wsl[:, 2048:3072]
    yguard = p.last("dve")
    vp_tok = None
    wp_tok = None
    apguard = None

    def emit_TV(prev, nxt_stok):
        h, qt, inst, etk, rtk, apar = prev
        att = atts[apar]
        hb = (h % 2) * 64
        nj = qt + 1
        vtk = None
        atg = None
        for j0 in range(0, nj, 8):
            jn = min(8, nj - j0)
            abuf = R["tmpbf_a"] if (j0 // 8) % 2 == 0 else R["tmpbf_b"]
            p.wait("pe", etk, R["_l2a_atg"][(j0 // 8) % 2], vp_tok)
            for jj in range(jn):
                t = nc.tensor.transpose(
                    R["ps_tr"][:, jj * 128:(jj + 1) * 128],
                    att[:, (j0 + jj) * 128:(j0 + jj + 1) * 128],
                    R["ident"][:])
            ttk = p.tok("pe", t)
            p.wait("dve", ttk, R["_l2a_veva"][0] if j0 == 0 else None)
            t = nc.vector.tensor_copy(abuf[:, 0:jn * 128],
                                      R["ps_tr"][:, 0:jn * 128])
            atg = p.tok("dve", t)
            R["_l2a_atg"][(j0 // 8) % 2] = atg
            p.wait("pe", atg)
            for jj in range(jn):
                t = nc.tensor.matmul(
                    ps_v[:, 0:64], abuf[:, jj * 128:(jj + 1) * 128],
                    vpair[:, j0 + jj, hb:hb + 64],
                    start=(j0 + jj == 0), stop=(j0 + jj == nj - 1))
            vtk = p.tok("pe", t)
        R["_l2a_T"][apar] = p.last("pe")
        p.wait("dve", vtk, rtk)
        t = nc.vector.tensor_scalar_mul(apair[:, qt, hb:hb + 64],
                                        ps_v[:, 0:64], st[:, 30 + inst:31 + inst])
        R["_l2a_veva"][0] = p.tok("dve", t)
        return R["_l2a_veva"][0]

    R["_l2a_atg"] = [None, None]
    R["_l2a_veva"] = [None]
    R["_l2a_T"] = [None, None]
    exp_done = [None, None]
    prev = None
    for pair in range(8):
        if prev is not None:
            emit_TV(prev, None)
            prev = None
        p.wait("act", apguard, R["_l2a_veva"][0], p.last("pe"))
        vp_tok = p.dma("act", vpair,
                       v_dram.rearrange("(j p) d -> p j d", p=128)
                       [:, :, pair * 128:(pair + 1) * 128])
        p.wait("pool", p.last("pe"))
        wp_tok = p.dma("pool", wp_sb,
                       W["l2_wproj"][pair * 128:(pair + 1) * 128, :])
        for par in range(2):
            h = 2 * pair + par
            ch = pair
            hb = par * 64
            for qt in range(NT):
                k = par * NT + qt
                inst = 0
                til = slice(qt * 128, (qt + 1) * 128)
                L = (qt + 1) * 128
                p.wait("pe", ln_tok, exp_done[inst], apguard)
                nb = (L + 511) // 512
                b_diag = (qt * 128) // 512
                order = [b_diag] + [b for b in range(nb) if b != b_diag]
                stoks = {}
                for b in order:
                    bw = min(512, L - b * 512)
                    t = nc.tensor.matmul(ps_s[inst][:, b * 512:b * 512 + bw],
                                         Q_fm[hb:hb + 64, ch, til],
                                         K_fm[hb:hb + 64, ch, b * 512:b * 512 + bw],
                                         start=True, stop=True)
                    stoks[b] = p.tok("pe", t)
                stok = p.last("pe")
                if prev is not None:
                    emit_TV(prev, stok)
                p.wait("dve", stoks[b_diag], cm_tok)
                t = nc.vector.tensor_tensor(ps_s[inst][:, til], ps_s[inst][:, til],
                                            R["cmask"], ALU.add)
                mstk = p.tok("dve", t)
                p.wait("dve", stok, mstk)
                t = nc.vector.reduce_max(st[:, 28 + inst:29 + inst],
                                         ps_s[inst][:, 0:L],
                                         axis=AX.X, negate=True)
                tk = p.tok("dve", t)
                p.wait("dve", tk)
                t = nc.vector.tensor_scalar_mul(st[:, 28 + inst:29 + inst],
                                                st[:, 28 + inst:29 + inst], 0.125)
                tk = p.tok("dve", t)
                apar = k % 2
                p.wait("act", tk, R["_l2a_T"][apar])
                t = nc.scalar.activation(atts[apar][:, 0:L], ps_s[inst][:, 0:L],
                                         AF.Exp,
                                         bias=st[:, 28 + inst:29 + inst],
                                         scale=0.125,
                                         accum_out=st[:, 26 + inst:27 + inst])
                etk = p.tok("act", t)
                exp_done[inst] = etk
                p.wait("dve", etk)
                t = nc.vector.reciprocal(st[:, 30 + inst:31 + inst],
                                         st[:, 26 + inst:27 + inst])
                rtk = p.tok("dve", t)
                prev = (h, qt, inst, etk, rtk, apar)
        emit_TV(prev, None)
        prev = None
        aguard = R["_l2a_veva"][0]
        for qg in range(0, NT, 8):
            p.wait("pe", aguard, apguard)
            for jj in range(8):
                t = nc.tensor.transpose(R["ps_tr"][:, jj * 128:(jj + 1) * 128],
                                        apair[:, qg + jj, :], R["ident"][:])
            ptk = p.tok("pe", t)
            p.wait("dve", ptk)
            t = nc.vector.tensor_copy(
                apair[:, qg:qg + 8, :],
                R["ps_tr"][:].rearrange("p (j q) -> p j q", q=128))
            apguard = p.tok("dve", t)
        ypp = [ps_y, ps_a]
        ygq = {}
        for qt in range(NT):
            for half in range(2):
                q2 = half
                osl = slice(half * 512, (half + 1) * 512)
                p.wait("pe", wp_tok, apguard, ygq.get(q2))
                t = nc.tensor.matmul(ypp[q2][:], apair[:, qt, :],
                                     wp_sb[:, osl], start=True, stop=True)
                mtok = p.tok("pe", t)
                p.wait("dve", mtok, xres_tok[qt])
                t = nc.vector.scalar_tensor_tensor(
                    R["x_res"][:, qt, osl], ypp[q2][:], 1.0,
                    R["x_res"][:, qt, osl], ALU.mult, ALU.add)
                xres_tok[qt] = p.tok("dve", t)
                ygq[q2] = xres_tok[qt]
                yguard = xres_tok[qt]
        apguard = yguard
    p.barrier()


EMITTERS = {}


def build(blocks, weight_specs):
    nc = bass.Bass("TRN2", target_bir_lowering=False)
    x_in = nc.dram_tensor("x", [128, NT, D], F32, kind="ExternalInput").ap()
    y_out = nc.dram_tensor("y", [128, NT, D], F32, kind="ExternalOutput").ap()
    W = {}
    for name, (shape, dt) in weight_specs.items():
        W[name] = nc.dram_tensor(name, list(shape), dt, kind="ExternalInput").ap()
    cc = None
    if "l3f" in blocks:
        cc = {"in": nc.dram_tensor("cc_in", [1, 1], F32, kind="Internal").ap(),
              "out": nc.dram_tensor("cc_out", [1, 1], F32, kind="Internal").ap()}
    scr = {}
    if "l0a" in blocks:
        scr["q_dram"] = nc.dram_tensor("q_dram", [T, D], BF16, kind="Internal").ap()
    if "l2a" in blocks:
        scr["v_dram"] = nc.dram_tensor("v_dram", [T, D], BF16, kind="Internal").ap()
    if "l2f" in blocks:
        scr["h_dram"] = nc.dram_tensor("h_dram", [T, 4096], BF16, kind="Internal").ap()
        scr["wqg_dram"] = nc.dram_tensor("wqg_dram", [1024, 4096], BF16, kind="Internal").ap()
        scr["wqu_dram"] = nc.dram_tensor("wqu_dram", [1024, 4096], BF16, kind="Internal").ap()
        scr["wqd_dram"] = nc.dram_tensor("wqd_dram", [4096, 1024], BF16, kind="Internal").ap()

    with ExitStack() as ctx:
        p = Prog(nc, ctx)
        R = {}
        R["x_res"] = p.sb("x_res", [128, NT, D], F32)[:]
        R["xnT"] = p.sb("xnT", [128, KC, T], BF16)[:]
        R["A"] = p.sb("regA", [128, 8192], F32)[:]
        R["B"] = p.sb("regB", [128, 8192], F32)[:]
        R["Wr"] = p.sb("regW", [128, 4096], F32)[:]
        R["tmpf"] = p.sb("tmpf", [128, 1024], F32)[:]
        R["tmpsq"] = p.sb("tmpsq", [128, 1024], F32)[:]
        R["tmpbf_a"] = p.sb("tmpbf_a", [128, 1024], BF16)[:]
        R["tmpbf_b"] = p.sb("tmpbf_b", [128, 1024], BF16)[:]
        R["smallbf"] = p.sb("smallbf", [16, 2048], BF16)[:]
        R["stats"] = p.sb("stats", [128, 80], F32)[:]
        R["bb1"] = p.sb("bb1", [128, 1024], F32)[:]
        R["bb2"] = p.sb("bb2", [128, 1024], F32)[:]
        R["row_f"] = p.sb("row_f", [16, 512], F32)[:]
        R["idx_u"] = p.sb("idx_u", [128, 8], U32)[:]
        R["idx_c"] = p.sb("idx_c", [128, 16], U32)[:]
        R["aux"] = p.sb("aux", [128, 2048], BF16)[:]
        R["kv_bf"] = R["aux"][:, 0:1024].rearrange("p (h d) -> p h d", d=64)
        R["ks_bf"] = R["aux"][:, 1024:1040]
        R["ones_f"] = p.sb("ones_f", [1, 128], F32)[:]
        R["ones_col"] = p.sb("ones_col", [128, 1], F32)[:]
        R["ones_bf"] = R["tmpbf_a"][:, 0:128]
        R["ones_bfc"] = p.sb("ones_bfc", [128, 1], BF16)[:]
        R["ones_f128"] = R["tmpf"][:, 0:128]
        R["ident"] = p.sb("ident", [128, 128], BF16)[:]
        R["ident_f"] = p.sb("ident_f", [128, 128], F32)[:]
        R["ps_a"] = p.ps("ps_a", [128, 512], F32)[:]
        R["ps_b"] = p.ps("ps_b", [128, 512], F32)[:]
        R["ps_tr"] = p.ps("ps_tr", [128, 1024], BF16)[:]
        R["ps_row"] = p.ps("ps_row", [128, 512], F32)[:]
        R["ps_small"] = R["ps_row"][:, 496:512]
        R["_fm_evac_tok"] = [None]

        emit_common(p, R)
        tk = p.dma("sp", R["x_res"], x_in)
        p.wait("dve", tk)
        p.wait("act", tk)
        p.wait("pe", tk)
        p.wait("pool", tk)

        for b in blocks:
            with ExitStack() as bctx:
                p.bctx = bctx
                EMITTERS[b](p, R, W, cc=cc, scr=scr)
            p.barrier()

        tk = p.dma("sp", y_out, R["x_res"])
        p.wait("sp", tk)
    return nc


EMITTERS["l0f"] = lambda p, R, W, cc=None, scr=None: emit_l0f(p, R, W)
EMITTERS["l1f"] = lambda p, R, W, cc=None, scr=None: emit_l1f(p, R, W)
EMITTERS["l2f"] = lambda p, R, W, cc=None, scr=None: emit_l2f(p, R, W, scr)
EMITTERS["l0a"] = lambda p, R, W, cc=None, scr=None: emit_l0a(p, R, W, scr)
EMITTERS["l2a"] = lambda p, R, W, cc=None, scr=None: emit_l2a(p, R, W, scr)
EMITTERS["l1a"] = lambda p, R, W, cc=None, scr=None: emit_l1a(p, R, W)
EMITTERS["l3a"] = lambda p, R, W, cc=None, scr=None: emit_l3a(p, R, W)
EMITTERS["l3f"] = lambda p, R, W, cc=None, scr=None: emit_l3f(p, R, W, cc)


def _to32(a):
    return np.ascontiguousarray(a, dtype=np.float32)


def _pcn(WT):
    K, N = WT.shape
    return np.ascontiguousarray(WT.reshape(K // 128, 128, N).transpose(1, 0, 2))


def _tiles(WT, tn):
    K, N = WT.shape
    a = WT.reshape(K // 128, 128, N // tn, tn).transpose(2, 1, 0, 3)
    return np.ascontiguousarray(a)


def host_weights(inputs):
    I = inputs
    w = {}
    w["l0_wqT"] = (_pcn(_to32(I["L0_attn_Wq"].T)), F32)
    w["l0_wkT"] = (_pcn(_to32(I["L0_attn_Wk"].T)), F32)
    w["l0_wvT"] = (_pcn(_to32(I["L0_attn_Wv"].T)), F32)
    w["l0_woT"] = (_pcn(_to32(I["L0_attn_Wo"].T)), F32)
    for b in ["bq", "bk", "bv", "bo"]:
        w["l0_" + b] = (_to32(I["L0_attn_" + b]), F32)
    w["l0_wg4"] = (_tiles(np.ascontiguousarray(I["L0_ffn_w_gate"].T), 128), I32)
    w["l0_wu4"] = (_tiles(np.ascontiguousarray(I["L0_ffn_w_up"].T), 128), I32)
    w["l0_wdT"] = (np.ascontiguousarray(I["L0_ffn_w_down"].T), I32)
    w["l1_wqT"] = (_pcn(_to32(I["L1_attn_Wq"].T)), F32)
    w["l1_wkT"] = (_pcn(_to32(I["L1_attn_Wk"].T)), F32)
    w["l1_wvT"] = (_pcn(_to32(I["L1_attn_Wv"].T)), F32)
    w["l1_woT"] = (_pcn(_to32(I["L1_attn_Wo"].T)), F32)
    for b in ["bq", "bk", "bv", "bo"]:
        w["l1_" + b] = (_to32(I["L1_attn_" + b]), F32)
    w["l1_Wg"] = (_pcn(_to32(I["L1_ffn_Wg"])), F32)
    w["l1_bg"] = (_to32(I["L1_ffn_bg"]), F32)
    w["l1_W1"] = (np.stack([_pcn(_to32(I["L1_ffn_W1"][e])) for e in range(8)]), F32)
    w["l1_b1"] = (_to32(I["L1_ffn_b1"]), F32)
    w["l1_W2"] = (np.stack([_pcn(_to32(I["L1_ffn_W2"][e])) for e in range(8)]), F32)
    w["l1_b2"] = (_to32(I["L1_ffn_b2"]), F32)
    w["l2_wattn"] = (_tiles(_to32(I["L2_attn_w_attn"]), 128), F32)
    w["l2_battn"] = (_to32(I["L2_attn_b_attn"]), F32)
    w["l2_wproj"] = (_to32(I["L2_attn_w_proj"]), F32)
    w["l2_bproj"] = (_to32(I["L2_attn_b_proj"]), F32)
    w["l2_wgT"] = (_to32(I["L2_ffn_w_gate"].T), F32)
    w["l2_wuT"] = (_to32(I["L2_ffn_w_up"].T), F32)
    w["l2_wdT"] = (_to32(I["L2_ffn_w_down"].T), F32)
    win = _to32(I["L3_attn_W_in"])
    win_p = np.zeros((1024, 1152), np.float32)
    win_p[:, 0:1032] = win
    w["l3_Win"] = (_tiles(win_p, 128), F32)
    w["l3_b_in"] = (_to32(I["L3_attn_b_in"]), F32)
    wg3 = _to32(I["L3_attn_W_g"])
    wg3_p = np.zeros((1152, 8), np.float32)
    wg3_p[0:1032] = wg3
    w["l3_Wg"] = (_pcn(wg3_p), F32)
    w["l3_b_g"] = (_to32(I["L3_attn_b_g"]), F32)
    w["l3_Wout"] = (_to32(I["L3_attn_W_out"]), F32)
    w["l3_bout"] = (_to32(I["L3_attn_b_out"]), F32)
    w["l3f_w"] = (_to32(I["L3_ffn_weight"]), F32)
    w["l3f_gamma"] = (_to32(I["L3_ffn_ln_gamma"]), F32)
    w["l3f_beta"] = (_to32(I["L3_ffn_ln_beta"]), F32)
    return w


BLOCK_WEIGHTS = {
    "l0a": ["l0_wqT", "l0_wkT", "l0_wvT", "l0_woT", "l0_bq", "l0_bk", "l0_bv", "l0_bo"],
    "l0f": ["l0_wg4", "l0_wu4", "l0_wdT"],
    "l1a": ["l1_wqT", "l1_wkT", "l1_wvT", "l1_woT", "l1_bq", "l1_bk", "l1_bv", "l1_bo"],
    "l1f": ["l1_Wg", "l1_bg", "l1_W1", "l1_b1", "l1_W2", "l1_b2"],
    "l2a": ["l2_wattn", "l2_battn", "l2_wproj", "l2_bproj"],
    "l2f": ["l2_wgT", "l2_wuT", "l2_wdT"],
    "l3a": ["l3_Win", "l3_b_in", "l3_Wg", "l3_b_g", "l3_Wout", "l3_bout"],
    "l3f": ["l3f_w", "l3f_gamma", "l3f_beta"],
}

LAUNCH_GROUPS = [ALL_BLOCKS]


def run_groups(x_full, wdict, groups):
    x = np.ascontiguousarray(x_full, dtype=np.float32)
    for blocks in groups:
        names = []
        for b in blocks:
            names += BLOCK_WEIGHTS[b]
        spec = {n: (wdict[n][0].shape, wdict[n][1]) for n in names}
        nc = build(blocks, spec)
        in_maps = []
        for c in range(NCORES):
            xc = np.ascontiguousarray(
                x[c].reshape(NT, 128, D).transpose(1, 0, 2))
            m = {"x": xc}
            for n in names:
                m[n] = wdict[n][0]
            in_maps.append(m)
        res = run_bass_kernel_spmd(nc, in_maps, core_ids=list(range(NCORES)))
        x = np.stack([np.asarray(res.results[c]["y"])
                      .transpose(1, 0, 2).reshape(T, D) for c in range(NCORES)])
    return x


def kernel(**inputs):
    w = host_weights(inputs)
    x = np.ascontiguousarray(np.asarray(inputs["L0_attn_x"], dtype=np.float32))
    out = run_groups(x, w, LAUNCH_GROUPS)
    return out.astype(np.float32)
